# version 02
# baseline (speedup 1.0000x reference)
import numpy as np
import ml_dtypes
import concourse.bass as bass
import concourse.mybir as mybir
from concourse.bass_utils import run_bass_kernel_spmd

F32 = mybir.dt.float32
BF16 = mybir.dt.bfloat16
AF = mybir.ActivationFunctionType
ALU = mybir.AluOpType
AX = mybir.AxisListType
BF = ml_dtypes.bfloat16

T = 4096
D = 1024
NCH = 8
WIN = 512
NW = 8
S = 64
KCH = 16
LCH = 256
WARM = 320
STEPS = LCH + WARM
EPS = 1e-5
SHIFT = 12.0

NF = 18600
HIST = 0
OBF = 0
UVU, UVV, UVU2, UVV2, FACC, FTMP = 4096, 4608, 5120, 5632, 6144, 6656
SBMP = 9216
STAT = 13824
XWIN = 13888
CONVT = 17984
RTR = 18496

NB = 48200
XHT = 0
XHAT = 8224
XN2 = 12320
XGW = 20512
SGV = 24608
GSO = 25120
WBC = 25632
GWV = 26656
SCMB = 43040
YB = 0
GSL = 4096
SDW = 8224
XHAT4 = 0
XT4 = 4096
MW1, MW2, MW3 = 8192, 16384, 24576
WS1, WS2, WS3, SWR = 32768, 34816, 36864, 38912
HST = 38976
H2SC = 44096
XHAT5, XT5 = 0, 4096
WQ, WK, WV = 8192, 16384, 24576
QWB, KWB, VWB, VTB = 32768, 33792, 34816, 35840
QTC, KTC = 0, 8192
SVC = 16384
PT = 24704
ATA, ATB = 25728, 33920
AWIN, SWO = 0, 4096
XHAT6, XT6, XT6N, WTR, WTI = 0, 4096, 8192, 10240, 12288


class Prog:
    def __init__(self, nc):
        self.nc = nc
        self.engines = ("tensor", "vector", "scalar", "sync", "gpsimd")
        self.ops = {e: [] for e in self.engines}
        self.semname = {"tensor": "s_t", "vector": "s_v", "scalar": "s_s",
                        "sync": "s_y", "gpsimd": "s_g"}
        self.sems = {}
        self.cnt = {}
        for s in self.semname.values():
            self.sems[s] = nc.semaphore(s).__enter__()
            self.cnt[s] = 0

    def op(self, engine, fn, deps=(), inc=1):
        sem = self.semname[engine]
        waits = {}
        for dd in deps:
            if dd is None:
                continue
            sn, val = dd
            waits[sn] = max(waits.get(sn, 0), val)
        self.cnt[sem] += inc
        tok = (sem, self.cnt[sem])
        self.ops[engine].append((fn, sorted(waits.items()), (sem, inc)))
        return tok

    def dma(self, out, in_, deps=(), queue="sync"):
        return self.op(queue, lambda e, o=out, i=in_: e.dma_start(out=o, in_=i),
                       deps=deps, inc=16)

    def dma_tr(self, out, in_, deps=(), queue="scalar"):
        return self.op(queue, lambda e, o=out, i=in_: e.dma_start(out=o, in_=i, transpose=True),
                       deps=deps, inc=16)

    def barrier(self):
        return [(self.semname[e], self.cnt[self.semname[e]]) for e in self.engines]

    def emit(self, block):
        def make(engine):
            def body(eng):
                for fn, waits, inc in self.ops[engine]:
                    for sn, val in waits:
                        eng.wait_ge(self.sems[sn], val)
                    inst = fn(eng)
                    sn, n = inc
                    inst.then_inc(self.sems[sn], n)
            return body
        block.tensor(make("tensor"))
        block.vector(make("vector"))
        block.scalar(make("scalar"))
        block.sync(make("sync"))
        block.gpsimd(make("gpsimd"))


def build(nc, dbg=False):
    dram = lambda name, shape, d=F32, kind="ExternalInput": nc.dram_tensor(name, shape, d, kind=kind).ap()
    x0 = dram("x0", [T, D])
    gate_wt = dram("gate_wt", [D, 2 * D])
    B_wt = dram("B_wt", [D, S]); C_wt = dram("C_wt", [D, S]); D_wt_h = dram("D_wt", [S, D])
    Bb_h = dram("B_b", [S, 1]); Cb_h = dram("C_b", [S, 1]); Db_h = dram("D_b", [128, NCH])
    gb_h = dram("gate_b", [128, 2 * NCH])
    taps_h = dram("conv_taps", [NCH, 128, 3])
    cvb_h = dram("conv_b", [128, NCH])
    A_h = dram("A_m", [S, S]); At_h = dram("A_mt", [S, S]); dtA_h = dram("dtA", [1, 1])
    g_h = dram("g_col", [S, 1]); beta_h = dram("beta_col", [S, 1]); grep_h = dram("g_rep64", [S, S])
    lngc_h = dram("lngc", [128, NCH]); lnbc_h = dram("lnbc", [128, NCH])
    Pc_h = dram("Pc", [S, S]); id_h = dram("ident64", [S, S])
    cst_h = dram("consts", [128, 8])
    od_h = dram("onesdiv", [S, S])
    hpr_h = dram("hp_rows", [1, 128])
    id128_h = dram("ident128", [128, 128])
    cm_h = dram("cmask", [4, 128, WIN], BF16)
    w1_h = dram("moe_w1", [8, D, 128]); w2_h = dram("moe_w2", [8, D, 128]); w3_h = dram("moe_w3", [8, 128, D])
    ws1_h = dram("ws1", [D, 256]); ws2_h = dram("ws2", [D, 256]); ws3_h = dram("ws3", [256, D]); wr_h = dram("wr", [D, 8])
    Wq_h = dram("Wq_t", [D, D]); Wk_h = dram("Wk_t", [D, D]); Wv_h = dram("Wv_t", [D, D]); Wo_h = dram("Wo_t", [D, D])
    uvU_h = dram("uvU", [2, 4, 4, 128, 512]); uvV_h = dram("uvV", [2, 4, 4, 128, 512])
    ttbr_h = dram("tt_bias", [128, NCH])
    out_h = dram("out", [T, D], kind="ExternalOutput")
    kd = "ExternalOutput" if dbg else "Internal"
    x1h = dram("x1h", [T, D], kind=kd)
    x2h = dram("x2h", [T, D], kind=kd)
    x3h = dram("x3h", [T, D], kind=kd)
    gatesh = dram("gatesh", [NCH, 128, T], d=BF16, kind=("ExternalOutput" if dbg else "Internal"))
    if dbg:
        dbpp_h = dram("dbg_bpp", [S, 4608], kind="ExternalOutput")
        dzh_h = dram("dbg_zh", [S, STEPS * 16], kind="ExternalOutput")
        dcm_h = dram("dbg_cm", [S, T], d=BF16, kind="ExternalOutput")
        dyb_h = dram("dbg_yb", [S, T], d=BF16, kind="ExternalOutput")
        dxht_h = dram("dbg_xht", [128, 8192], d=BF16, kind="ExternalOutput")
        dxn2_h = dram("dbg_xn2", [128, 4096], d=BF16, kind="ExternalOutput")
    qth = dram("qth", [NCH, 128, T], d=BF16, kind="Internal")
    kth = dram("kth", [NCH, 128, T], d=BF16, kind="Internal")
    vth = dram("vth", [NCH, 32, 128, 128], d=BF16, kind="Internal")
    ath = dram("ath", [NCH, 128, T], d=BF16, kind="Internal")

    sbuf = lambda name, shape, d=F32: nc.sbuf_tensor(name, shape, d).__enter__()

    cC = sbuf("cC", [128, 8])
    cOD = sbuf("cOD", [S, S])
    cID = sbuf("cID", [S, S]); cPc = sbuf("cPc", [S, S]); cID128 = sbuf("cID128", [128, 128])
    cA = sbuf("cA", [S, S]); cAt = sbuf("cAt", [S, S]); cdt = sbuf("cdt", [1, 1])
    cg = sbuf("cg", [S, 1]); cbeta = sbuf("cbeta", [S, 1]); cgrep = sbuf("cgrep", [S, S])
    cHPR = sbuf("cHPR", [1, 128])
    lngc = sbuf("lngc_s", [128, NCH]); lnbc = sbuf("lnbc_s", [128, NCH])
    staps = sbuf("staps", [128, NCH * 4])
    scvb = sbuf("scvb", [128, NCH])
    sBb = sbuf("sBb", [S, 1]); sCb = sbuf("sCb", [S, 1])
    sDb = sbuf("sDb", [128, NCH]); sgbias = sbuf("sgbias", [128, 2 * NCH])
    sttbr = sbuf("sttbr", [128, NCH])
    scanM = sbuf("scanM", [S, S])
    bacol = sbuf("bacol", [S, 1])
    eE = sbuf("eE", [S, S]); eF = sbuf("eF", [S, S]); eH = sbuf("eH", [S, S])
    eAd = sbuf("eAd", [S, S]); eAdt = sbuf("eAdt", [S, S])
    eT1 = sbuf("eT1", [S, S]); eT2 = sbuf("eT2", [S, S])
    sdtc = sbuf("sdtc", [S, 1])
    rowtmp = sbuf("rowtmp", [1, S])
    cmb = sbuf("cmb", [128, 4 * WIN], BF16)
    szh2t = sbuf("szh2t", [S, LCH])
    hcell = sbuf("hcell", [128, 64], BF16)
    craw = [sbuf(f"craw{h}", [S, 8]) for h in range(2)]
    csqb = [sbuf(f"csqb{h}", [S, 8]) for h in range(2)]
    wfT = sbuf("wfT", [1, 8 * WIN])
    lrow = sbuf("lrow", [128, WIN])

    FP = sbuf("FP", [128, NF])
    BB = sbuf("BB", [128, NB], BF16)
    f = lambda a, b: FP[:, a:b]
    f6 = lambda a, b: FP[0:S, a:b]
    bb = lambda a, b: BB[:, a:b]
    b6 = lambda a, b: BB[0:S, a:b]

    p = Prog(nc)
    pb = [nc.psum_tensor(f"pb{i}", [128, 512], F32).__enter__() for i in range(8)]
    pfree = [()] * 8
    _rr = [0]

    def nb():
        i = _rr[0]; _rr[0] = (i + 1) % 8
        return i

    def bank_done(i, tok):
        pfree[i] = (tok,)

    def mm(bi, lhsT, rhs, deps=(), start=True, stop=True, tp=None, psl=None):
        dst = pb[bi][psl[0]:psl[1], psl[2]:psl[3]] if psl else pb[bi][:, :]
        dd = (list(pfree[bi]) if start else []) + list(deps)
        return p.op("tensor", lambda e, o=dst, l=lhsT, r=rhs, s0=start, s1=stop, t_=tp: e.matmul(
            o, l, r, start=s0, stop=s1, tile_position=t_, skip_group_check=True), deps=dd)

    def ptr(bi, cols, src, deps=(), rows=128):
        dst = pb[bi][0:rows, cols[0]:cols[1]]
        dd = list(pfree[bi]) + list(deps)
        return p.op("tensor", lambda e, o=dst, i=src: e.transpose(o, i, cID128[:, :]), deps=dd)

    def act(out, in_, func, deps=(), bias=None, scale=1.0):
        if bias is None:
            return p.op("scalar", lambda e, o=out, i=in_, sc=scale, fu=func: e.activation(
                out=o, in_=i, func=fu, scale=sc), deps=deps)
        return p.op("scalar", lambda e, o=out, i=in_, b=bias, sc=scale, fu=func: e.activation(
            out=o, in_=i, func=fu, bias=b, scale=sc), deps=deps)

    def vtt(out, a, b2, opn, deps=()):
        return p.op("vector", lambda e, o=out, x=a, y=b2, oo=opn: e.tensor_tensor(o, x, y, oo),
                    deps=deps)

    def vts(out, in0, s1, s2, op0, op1=None, deps=()):
        if s2 is None:
            return p.op("vector", lambda e, o=out, i=in0, a=s1, oo=op0: e.tensor_scalar(
                out=o, in0=i, scalar1=a, scalar2=None, op0=oo), deps=deps)
        return p.op("vector", lambda e, o=out, i=in0, a=s1, b2=s2, oo=op0, o1=op1: e.tensor_scalar(
            out=o, in0=i, scalar1=a, scalar2=b2, op0=oo, op1=o1), deps=deps)

    def vstt(out, in0, s1, in1, op0, op1, deps=()):
        return p.op("vector", lambda e, o=out, i=in0, a=s1, j=in1, oo=op0, o1=op1:
                    e.scalar_tensor_tensor(out=o, in0=i, scalar=a, in1=j, op0=oo, op1=o1),
                    deps=deps)

    def vcopy(out, in_, deps=()):
        return p.op("vector", lambda e, o=out, i=in_: e.tensor_copy(o, i), deps=deps)

    def vred(out, in_, opn, deps=()):
        return p.op("vector", lambda e, o=out, i=in_, oo=opn: e.tensor_reduce(
            out=o, in_=i, axis=AX.X, op=oo), deps=deps)

    def vrecip(out, in_, deps=()):
        return p.op("vector", lambda e, o=out, i=in_: e.reciprocal(out=o, in_=i), deps=deps)

    def vmemset(ap_, val, deps=()):
        return p.op("vector", lambda e, o=ap_, v=val: e.memset(o, v), deps=deps)

    def phase():
        bar = p.barrier()
        for i in range(8):
            pfree[i] = tuple(bar)
        return bar

    epsc = cC[:, 0:1]; eps2c = cC[:, 1:2]; mshiftc = cC[:, 2:3]

    ld = [p.dma(cC[:, :], cst_h), p.dma(cOD[:, :], od_h),
          p.dma(cID[:, :], id_h), p.dma(cPc[:, :], Pc_h), p.dma(cID128[:, :], id128_h),
          p.dma(cA[:, :], A_h), p.dma(cAt[:, :], At_h), p.dma(cdt[:, :], dtA_h),
          p.dma(cg[:, :], g_h), p.dma(cbeta[:, :], beta_h), p.dma(cgrep[:, :], grep_h),
          p.dma(cHPR[:, :], hpr_h),
          p.dma(lngc[:, :], lngc_h), p.dma(lnbc[:, :], lnbc_h),
          p.dma(staps.ap().rearrange("p (c k) -> p c k", k=4)[:, :, 0:3], taps_h.rearrange("c p k -> p c k")),
          p.dma(scvb[:, :], cvb_h),
          p.dma(sBb[:, :], Bb_h), p.dma(sCb[:, :], Cb_h),
          p.dma(sDb[:, :], Db_h),
          p.dma(sgbias[:, :], gb_h),
          p.dma(sttbr[:, :], ttbr_h),
          p.dma(cmb.ap().rearrange("p (j n) -> p j n", n=WIN), cm_h.rearrange("j p n -> p j n"))]

    t_t1 = []
    for c in range(NCH):
        t_t1.append(vts(staps[:, c * 4 + 3:c * 4 + 4], staps[:, c * 4 + 1:c * 4 + 2],
                        1.0, None, ALU.add, deps=ld))

    tok = mm(0, cOD[0:1, :], cdt[:, :], deps=ld, psl=(0, S, 0, 1))
    t_dt = vts(sdtc[:, :], pb[0][0:S, 0:1], float(S) * (2.0 ** -8), None, ALU.mult, deps=[tok])
    bank_done(0, t_dt)
    tA1 = vts(eAd[:, :], cA[:, :], sdtc[:, :], None, ALU.mult, deps=[t_dt] + ld)
    tA2 = vts(eAdt[:, :], cAt[:, :], sdtc[:, :], None, ALU.mult, deps=[t_dt] + ld)

    def horner(Xts, Hout, tdep):
        tH = vcopy(Hout[:, :], cID[:, :], deps=tdep + ld)
        for k in range(6, 0, -1):
            tm_ = mm(1, Xts[:, :], Hout[:, :], deps=[tH], psl=(0, S, 0, S))
            te = act(eT1[:, :], pb[1][0:S, 0:S], AF.Identity, deps=[tm_], scale=1.0 / k)
            bank_done(1, te)
            tH = vtt(Hout[:, :], eT1[:, :], cID[:, :], ALU.add, deps=[te])
        return tH

    tE = horner(eAdt, eE, [tA1, tA2])
    tF = horner(eAd, eF, [tA1, tA2])
    for _ in range(8):
        tm1 = mm(1, eF[:, :], eE[:, :], deps=[tE, tF], psl=(0, S, 0, S))
        te1 = act(eT1[:, :], pb[1][0:S, 0:S], AF.Identity, deps=[tm1])
        bank_done(1, te1)
        tm2 = mm(2, eE[:, :], eF[:, :], deps=[tE, tF], psl=(0, S, 0, S))
        te2 = act(eT2[:, :], pb[2][0:S, 0:S], AF.Identity, deps=[tm2])
        bank_done(2, te2)
        tE = vcopy(eE[:, :], eT1[:, :], deps=[te1, tm2])
        tF = vcopy(eF[:, :], eT2[:, :], deps=[te2, tE])
    tM = vtt(eH[:, :], eF[:, :], cgrep[:, :], ALU.mult, deps=[tE, tF])
    tsm = mm(1, eH[:, :], cPc[:, :], deps=[tM], psl=(0, S, 0, S))
    tsme = act(scanM[:, :], pb[1][0:S, 0:S], AF.Identity, deps=[tsm])
    bank_done(1, tsme)
    tba = mm(2, cbeta[:, :], eE[:, :], deps=[tE], psl=(0, 1, 0, S))
    tbae = vcopy(rowtmp[:, :], pb[2][0:1, 0:S], deps=[tba])
    bank_done(2, tbae)
    tbat = p.op("tensor", lambda e: e.transpose(pb[3][0:S, 0:1], rowtmp[:, :], cID128[0:1, 0:1]),
                deps=[tbae] + list(pfree[3]))
    tbac = vcopy(bacol[:, :], pb[3][0:S, 0:1], deps=[tbat])
    bank_done(3, tbac)

    tgw = [p.dma(bb(GWV + c * 2048, GWV + (c + 1) * 2048), gate_wt[c * 128:(c + 1) * 128, :],
                 queue="gpsimd", deps=ld) for c in range(NCH)]
    tbc = []
    for c in range(NCH):
        tbc.append(p.dma(bb(WBC + c * 2 * S, WBC + c * 2 * S + S), B_wt[c * 128:(c + 1) * 128, :],
                         queue="gpsimd", deps=ld))
        tbc.append(p.dma(bb(WBC + c * 2 * S + S, WBC + (c + 1) * 2 * S), C_wt[c * 128:(c + 1) * 128, :],
                         queue="gpsimd", deps=ld))
    tz = [vmemset(hcell[:, c * 4:c * 4 + 1], 0.0) for c in range(NCH)]
    tzp = [vmemset(f6(SBMP, SBMP + WARM), 0.0), vmemset(f6(SBMP + WARM + T, SBMP + 4608), 0.0)]

    def ln_window(xsrc, woff, deps, xhat_off, bf16_out=True):
        tl = p.dma(f(XWIN, XWIN + 4 * D).rearrange("p (i d) -> p i d", d=D),
                   xsrc[woff:woff + WIN, :].rearrange("(i p) d -> p i d", p=128), deps=deps)
        apl = []
        for i in range(4):
            xt = f(XWIN + i * D, XWIN + (i + 1) * D)
            st12 = f(STAT + i * 16, STAT + i * 16 + 12)
            mv = f(STAT + i * 16 + 12, STAT + i * 16 + 14)
            sc = f(STAT + i * 16 + 14, STAT + i * 16 + 15)
            t1 = p.op("vector", lambda e, o=st12[:, 0:6], i_=xt[:, 0:512]: e.bn_stats(out=o, in_=i_),
                      deps=[tl])
            t2 = p.op("vector", lambda e, o=st12[:, 6:12], i_=xt[:, 512:1024]: e.bn_stats(out=o, in_=i_),
                      deps=[t1])
            t3 = p.op("vector", lambda e, o=mv, i_=st12: e.bn_aggr(out=o, in_=i_), deps=[t2])
            if bf16_out == "composite":
                t4 = act(sc, mv[:, 1:2], AF.Sqrt, deps=[t3], bias=eps2c, scale=1.0 + EPS)
            else:
                t4 = act(sc, mv[:, 1:2], AF.Sqrt, deps=[t3], bias=epsc, scale=1.0)
            t5 = vrecip(sc, sc, deps=[t4])
            t6 = vts(bb(xhat_off + i * D, xhat_off + (i + 1) * D), xt, mv[:, 0:1], sc,
                     ALU.subtract, ALU.mult, deps=[t5])
            apl.append(t6)
        return tl, apl

    def tr_window(xhat_off, dst_fn, apl, deps=()):
        tts = [[None] * NCH for _ in range(4)]
        for i in range(4):
            for c in range(NCH):
                tts[i][c] = p.dma_tr(dst_fn(c, i), bb(xhat_off + i * D + c * 128, xhat_off + i * D + (c + 1) * 128),
                                     deps=[apl[i]] + list(deps))
        return tts


    conv_done = {}
    aff_done = {}
    gates_done = {-1: []}
    bm_evac = []
    ln_prev = []

    def do_conv(w, deps_extra):
        s = w % 2
        xn2o = XN2 + s * 4096
        toks = []
        prev = None
        for c in range(NCH):
            base = XHT + (c * 2 + s) * WIN
            dat = lambda a, b: bb(base + a, base + b)
            hL = hcell[:, c * 4 + s * 2:c * 4 + s * 2 + 1]
            hR = hcell[:, c * 4 + s * 2 + 32:c * 4 + s * 2 + 33]
            cv = lambda a, b: f(CONVT + a, CONVT + b)
            dd = aff_done[w] + deps_extra + ([prev] if prev else []) + gates_done.get(w - 2, [])
            t1 = vts(cv(0, WIN), dat(0, WIN), staps[:, c * 4 + 3:c * 4 + 4], None, ALU.mult, deps=dd)
            t2 = vstt(cv(1, WIN), dat(0, WIN - 1), staps[:, c * 4 + 0:c * 4 + 1],
                      cv(1, WIN), ALU.mult, ALU.add, deps=[t1])
            t2b = vstt(cv(0, 1), hL, staps[:, c * 4 + 0:c * 4 + 1],
                       cv(0, 1), ALU.mult, ALU.add, deps=[t2])
            t3 = vstt(cv(0, WIN - 1), dat(1, WIN), staps[:, c * 4 + 2:c * 4 + 3],
                      cv(0, WIN - 1), ALU.mult, ALU.add, deps=[t2b])
            t3b = vstt(cv(WIN - 1, WIN), hR, staps[:, c * 4 + 2:c * 4 + 3],
                       cv(WIN - 1, WIN), ALU.mult, ALU.add, deps=[t3])
            t4 = vts(bb(xn2o + c * WIN, xn2o + (c + 1) * WIN), cv(0, WIN),
                     scvb[:, c:c + 1], None, ALU.add, deps=[t3b])
            prev = t4
            toks.append(t4)
        conv_done[w] = toks
        return toks

    def do_gates(w):
        s = w % 2
        xn2o = XN2 + s * 4096
        ct = conv_done[w]
        igev = []
        prev_sg = None
        for oc in range(NCH):
            bi = nb()
            tmm = None
            for c in range(NCH):
                tmm = mm(bi, bb(GWV + c * 2048 + 1024 + oc * 128, GWV + c * 2048 + 1024 + (oc + 1) * 128),
                         bb(xn2o + c * WIN, xn2o + (c + 1) * WIN),
                         deps=(ct + tgw) if c == 0 else [tmm], start=(c == 0), stop=(c == NCH - 1))
            tsg = act(bb(SGV, SGV + WIN), pb[bi][:, :], AF.Sigmoid, deps=[tmm] + ([prev_sg] if prev_sg else []),
                      bias=sgbias[:, NCH + oc:NCH + oc + 1])
            bank_done(bi, tsg)
            txg = vtt(bb(XGW + oc * WIN, XGW + (oc + 1) * WIN), bb(xn2o + oc * WIN, xn2o + (oc + 1) * WIN),
                      bb(SGV, SGV + WIN), ALU.mult, deps=[tsg])
            prev_sg = txg
            igev.append(txg)
        biB = nb(); biC = nb()
        tmB = None; tmC = None
        for c in range(NCH):
            tmB = mm(biB, bb(WBC + c * 2 * S, WBC + c * 2 * S + S), bb(XGW + c * WIN, XGW + (c + 1) * WIN),
                     deps=(igev + tbc) if c == 0 else [tmB], start=(c == 0), stop=(c == NCH - 1),
                     psl=(0, S, 0, WIN))
        for c in range(NCH):
            tmC = mm(biC, bb(WBC + c * 2 * S + S, WBC + (c + 1) * 2 * S), bb(xn2o + c * WIN, xn2o + (c + 1) * WIN),
                     deps=(ct + tbc) if c == 0 else [tmC], start=(c == 0), stop=(c == NCH - 1),
                     psl=(0, S, 0, WIN))
        tBe = act(f6(HIST + w * WIN, HIST + (w + 1) * WIN), pb[biB][0:S, 0:WIN], AF.Identity,
                  deps=[tmB], bias=sBb[:, :])
        bank_done(biB, tBe)
        tCe = act(b6(SCMB + w * WIN, SCMB + (w + 1) * WIN), pb[biC][0:S, 0:WIN], AF.Identity,
                  deps=[tmC], bias=sCb[:, :])
        bank_done(biC, tCe)
        bm_evac.extend([tBe, tCe])
        gtoks = []
        prev_st = None
        for oc in range(NCH):
            bi = nb()
            tmm = None
            for c in range(NCH):
                tmm = mm(bi, bb(GWV + c * 2048 + oc * 128, GWV + c * 2048 + (oc + 1) * 128),
                         bb(xn2o + c * WIN, xn2o + (c + 1) * WIN),
                         deps=(ct + tgw) if c == 0 else [tmm], start=(c == 0), stop=(c == NCH - 1))
            tsg = act(bb(GSO, GSO + WIN), pb[bi][:, :], AF.Sigmoid,
                      deps=[tmm] + ([prev_st] if prev_st else []), bias=sgbias[:, oc:oc + 1])
            bank_done(bi, tsg)
            tst = p.dma(gatesh[oc, :, w * WIN:(w + 1) * WIN], bb(GSO, GSO + WIN), deps=[tsg])
            prev_st = tst
            gtoks.append(tst)
        gates_done[w] = gtoks + [tBe, tCe]
        return gtoks

    halo_toks = {}
    for w in range(NW):
        ldeps = tz + t_t1 + [tA1] + (ln_prev if ln_prev else [])
        tl, apl = ln_window(x0, w * WIN, ldeps, XHAT, bf16_out="composite")
        hl = []
        if w >= 2:
            s_src = w % 2; s_dst = (w - 1) % 2
            for c in range(NCH):
                base = XHT + (c * 2 + s_src) * WIN
                hl.append(vcopy(hcell[:, c * 4 + s_dst * 2:c * 4 + s_dst * 2 + 1],
                                bb(base + WIN - 1, base + WIN),
                                deps=aff_done[w - 2] + conv_done.get(w - 2, [])))
        s = w % 2
        tts = tr_window(XHAT, lambda c, i, s=s: bb(XHT + (c * 2 + s) * WIN + i * 128,
                                                   XHT + (c * 2 + s) * WIN + (i + 1) * 128),
                        apl, deps=hl + conv_done.get(w - 2, []) + gates_done.get(w - 2, []))
        ln_prev = [t for row in tts for t in row]
        aff = []
        for c in range(NCH):
            base = XHT + (c * 2 + s) * WIN
            aff.append(vts(bb(base, base + WIN), bb(base, base + WIN),
                           lngc[:, c:c + 1], lnbc[:, c:c + 1], ALU.mult, ALU.add,
                           deps=[tts[i][c] for i in range(4)]))
        aff_done[w] = aff
        hr = []
        if w >= 1:
            s_dst = (w - 1) % 2
            for c in range(NCH):
                base = XHT + (c * 2 + s) * WIN
                hr.append(vcopy(hcell[:, c * 4 + s_dst * 2 + 32:c * 4 + s_dst * 2 + 33],
                                bb(base, base + 1), deps=[aff[c]]))
            halo_toks[w - 1] = hl + hr
            do_conv(w - 1, hl + hr)
            do_gates(w - 1)
            if dbg and w == 1:
                td1 = p.dma(dxht_h, bb(XHT, XHT + 8192), deps=aff_done[0] + conv_done[0])
                td2 = p.dma(dxn2_h, bb(XN2, XN2 + 4096), deps=conv_done[0])
                gates_done[0] = gates_done[0] + [td1, td2]
    hl = []
    s_dst = 1
    for c in range(NCH):
        base = XHT + (c * 2 + 0) * WIN
        hl.append(vcopy(hcell[:, c * 4 + s_dst * 2:c * 4 + s_dst * 2 + 1],
                        bb(base + WIN - 1, base + WIN),
                        deps=aff_done[6] + conv_done.get(6, [])))
        hl.append(vmemset(hcell[:, c * 4 + s_dst * 2 + 32:c * 4 + s_dst * 2 + 33], 0.0,
                          deps=conv_done.get(5, [])))
    do_conv(7, hl)
    do_gates(7)

    P2B = phase()
    tb2 = vts(f6(HIST + 1, HIST + T), f6(HIST + 1, HIST + T), bacol[:, :], None, ALU.add,
              deps=P2B)
    pps = []
    for i in range(NW):
        bi = nb()
        tm_ = mm(bi, cPc[:, :], f6(HIST + i * WIN, HIST + (i + 1) * WIN), deps=[tb2],
                 psl=(0, S, 0, WIN))
        te = act(f6(SBMP + WARM + i * WIN, SBMP + WARM + (i + 1) * WIN), pb[bi][0:S, 0:WIN],
                 AF.Identity, deps=[tm_])
        bank_done(bi, te)
        pps.append(te)
    SBW = FP[0:S, SBMP:SBMP + 4608].rearrange("p (j l) -> p j l", l=512)
    scan_end = []
    for h in range(2):
        pA = [h * 4 + 0, h * 4 + 1]
        pB = [h * 4 + 2, h * 4 + 3]
        tV = tq = tM2 = tR = tZ = tM1 = None
        for t in range(STEPS):
            s0 = h * 256 + t
            bAP = SBW[:, s0 // 512: s0 // 512 + 8, s0 % 512]
            zt = f6(HIST + t * 16 + h * 8, HIST + t * 16 + h * 8 + 8)
            if t == 0:
                tV = vcopy(craw[h][:, :], bAP, deps=pps + P2B + tzp)
            else:
                tV = vtt(craw[h][:, :], pb[pA[t % 2]][0:S, 0:8], bAP, ALU.add, deps=[tM1])
            tq = act(csqb[h][:, :], craw[h][:, :], AF.Square, deps=[tV])
            tM2 = mm(pB[t % 2], cOD[:, :], csqb[h][:, :], deps=[tq] + ([tR] if t >= 2 else []),
                     psl=(0, S, 0, 8))
            tR = act(craw[h][:, 0:8] if False else csqb[h][:, :], pb[pB[t % 2]][0:S, 0:8],
                     AF.Sqrt, deps=[tM2], bias=epsc[0:S, :])
            tRc = vrecip(csqb[h][:, :], csqb[h][:, :], deps=[tR])
            tZ = vtt(zt, craw[h][:, :], csqb[h][:, :], ALU.mult, deps=[tRc])
            if t < STEPS - 1:
                tM1 = mm(pA[(t + 1) % 2], scanM[:, :], zt, deps=[tZ] + ([tV] if t >= 1 else []),
                         psl=(0, S, 0, 8))
            else:
                scan_end.append(tZ)

    if dbg:
        p.dma(dbpp_h, f6(SBMP, SBMP + 4608), deps=scan_end)
        p.dma(dzh_h, f6(HIST, HIST + STEPS * 16), deps=scan_end)
        p.dma(dcm_h, b6(SCMB, SCMB + T), deps=scan_end)

    P3B = phase()
    HISTV = FP[0:S, HIST:HIST + STEPS * 16].rearrange("p (t k) -> p t k", k=16)
    ty_prev = None
    ty_toks = []
    for j in range(KCH):
        kidx = (j % 2) * 8 + j // 2
        src = HISTV[:, WARM:STEPS, kidx]
        t1 = vts(szh2t[:, :], src, cg[:, :], cbeta[:, :], ALU.mult, ALU.add,
                 deps=P3B + ([ty_prev] if ty_prev else []))
        t2 = vtt(b6(YB + j * LCH, YB + (j + 1) * LCH), szh2t[:, :],
                 b6(SCMB + j * LCH, SCMB + (j + 1) * LCH), ALU.mult, deps=[t1])
        ty_prev = t2
        ty_toks.append(t2)
    tDld = p.dma(b6(SDW, SDW + D), D_wt_h, queue="gpsimd", deps=P3B)

    def out_window(w, obf_toks, xsrc, xdst, extra_ld_deps):
        tl = p.dma(f(XWIN, XWIN + 4 * D).rearrange("p (i d) -> p i d", d=D),
                   xsrc[w * WIN:(w + 1) * WIN, :].rearrange("(i p) d -> p i d", p=128),
                   deps=extra_ld_deps)
        adds = []
        for i in range(4):
            for oc in range(NCH):
                bi = nb()
                tt = ptr(bi, (0, 128), f(OBF + oc * WIN + i * 128, OBF + oc * WIN + (i + 1) * 128),
                         deps=obf_toks)
                ta = vtt(f(XWIN + i * D + oc * 128, XWIN + i * D + (oc + 1) * 128),
                         f(XWIN + i * D + oc * 128, XWIN + i * D + (oc + 1) * 128),
                         pb[bi][0:128, 0:128], ALU.add, deps=[tt, tl])
                bank_done(bi, ta)
                adds.append(ta)
        ts_ = p.dma(xdst[w * WIN:(w + 1) * WIN, :].rearrange("(i p) d -> p i d", p=128),
                    f(XWIN, XWIN + 4 * D).rearrange("p (i d) -> p i d", d=D), deps=adds)
        return ts_, adds

    prev_st = []
    tgl_prev = None
    for w in range(NW):
        tgl = p.dma(bb(GSL, GSL + 4096).rearrange("p (c n) -> p c n", n=WIN),
                    gatesh[:, :, w * WIN:(w + 1) * WIN].rearrange("c p n -> p c n"),
                    deps=P3B + ([tgl_prev] if tgl_prev else []))
        evs = []
        for oc in range(NCH):
            bi = nb()
            tm_ = mm(bi, b6(SDW + oc * 128, SDW + (oc + 1) * 128), b6(YB + w * WIN, YB + (w + 1) * WIN),
                     deps=ty_toks + [tDld], psl=(0, 128, 0, WIN))
            te = act(f(OBF + oc * WIN, OBF + (oc + 1) * WIN), pb[bi][:, 0:WIN], AF.Identity,
                     deps=[tm_] + prev_st, bias=sDb[:, oc:oc + 1])
            tg = vtt(f(OBF + oc * WIN, OBF + (oc + 1) * WIN), f(OBF + oc * WIN, OBF + (oc + 1) * WIN),
                     bb(GSL + oc * WIN, GSL + (oc + 1) * WIN), ALU.mult, deps=[te, tgl])
            bank_done(bi, tg)
            evs.append(tg)
        ts_, adds = out_window(w, evs, x0, x1h, prev_st)
        tgl_prev = evs[-1]
        prev_st = [ts_] + adds

    if dbg:
        p.dma(dyb_h, b6(YB, YB + T), deps=ty_toks + prev_st)

    P4B = phase()
    wld = []
    for e_i in range(8):
        for c in range(NCH):
            wld.append(p.dma(bb(MW1 + e_i * 1024 + c * 128, MW1 + e_i * 1024 + (c + 1) * 128),
                             w1_h[e_i, c * 128:(c + 1) * 128, :], queue="gpsimd", deps=P4B))
            wld.append(p.dma(bb(MW2 + e_i * 1024 + c * 128, MW2 + e_i * 1024 + (c + 1) * 128),
                             w2_h[e_i, c * 128:(c + 1) * 128, :], queue="gpsimd", deps=P4B))
        wld.append(p.dma(bb(MW3 + e_i * 1024, MW3 + (e_i + 1) * 1024), w3_h[e_i], queue="gpsimd", deps=P4B))
    for c in range(NCH):
        wld.append(p.dma(bb(WS1 + c * 256, WS1 + (c + 1) * 256), ws1_h[c * 128:(c + 1) * 128, :],
                         queue="gpsimd", deps=P4B))
        wld.append(p.dma(bb(WS2 + c * 256, WS2 + (c + 1) * 256), ws2_h[c * 128:(c + 1) * 128, :],
                         queue="gpsimd", deps=P4B))
        wld.append(p.dma(bb(SWR + c * 8, SWR + (c + 1) * 8), wr_h[c * 128:(c + 1) * 128, :],
                         queue="gpsimd", deps=P4B))
    for k in range(2):
        wld.append(p.dma(bb(WS3 + k * 1024, WS3 + (k + 1) * 1024), ws3_h[k * 128:(k + 1) * 128, :],
                         queue="gpsimd", deps=P4B))

    prev_w = P4B
    for w in range(NW):
        tl, apl = ln_window(x1h, w * WIN, prev_w, XHAT4)
        tts = tr_window(XHAT4, lambda c, i: bb(XT4 + c * WIN + i * 128, XT4 + c * WIN + (i + 1) * 128),
                        apl, deps=prev_w)
        trt = [t for row in tts for t in row]
        wf_toks = []
        for i in range(4):
            bi = nb()
            tmm = None
            for c in range(NCH):
                tmm = mm(bi, bb(XT4 + c * WIN + i * 128, XT4 + c * WIN + (i + 1) * 128),
                         bb(SWR + c * 8, SWR + (c + 1) * 8),
                         deps=(trt + wld) if c == 0 else [tmm], start=(c == 0), stop=(c == NCH - 1),
                         psl=(0, 128, 0, 8))
            pr = f(RTR + i * 24, RTR + i * 24 + 8)
            rx = f(RTR + i * 24 + 8, RTR + i * 24 + 16)
            mx = f(RTR + i * 24 + 16, RTR + i * 24 + 17)
            ms = f(RTR + i * 24 + 17, RTR + i * 24 + 18)
            m1 = f(RTR + i * 24 + 18, RTR + i * 24 + 19)
            m2 = f(RTR + i * 24 + 19, RTR + i * 24 + 20)
            t1 = vred(mx, pb[bi][:, 0:8], ALU.max, deps=[tmm])
            t2 = vts(mx, mx, -1.0, None, ALU.mult, deps=[t1])
            t3 = act(pr, pb[bi][:, 0:8], AF.Exp, deps=[t2], bias=mx)
            bank_done(bi, t3)
            t4 = vred(ms, pr, ALU.add, deps=[t3])
            t5 = vrecip(ms, ms, deps=[t4])
            t6 = vts(pr, pr, ms, None, ALU.mult, deps=[t5])
            t7 = vred(m1, pr, ALU.min, deps=[t6])
            t8 = vts(rx, pr, m1, None, ALU.is_le, deps=[t7])
            t9 = vstt(rx, rx, 2.0, pr, ALU.mult, ALU.add, deps=[t8])
            t11 = vred(m2, rx, ALU.min, deps=[t9])
            t12 = vts(rx, pr, m2, None, ALU.is_gt, deps=[t11])
            t13 = vtt(rx, rx, pr, ALU.mult, deps=[t12])
            t14 = vred(ms, rx, ALU.add, deps=[t13])
            t15 = vrecip(ms, ms, deps=[t14])
            t16 = vts(rx, rx, ms, None, ALU.mult, deps=[t15])
            for e_i in range(8):
                bj = nb()
                t17 = ptr(bj, (0, 128), rx[:, e_i:e_i + 1], deps=[t16], rows=1)
                t18 = vcopy(wfT[0:1, e_i * WIN + i * 128:e_i * WIN + (i + 1) * 128],
                            pb[bj][0:1, 0:128], deps=[t17])
                bank_done(bj, t18)
                wf_toks.append(t18)
        hw_toks = {}
        prev_h2 = None
        for e_i in range(10):
            bi1 = nb(); bi2 = nb()
            tm1 = tm2 = None
            for c in range(NCH):
                if e_i < 8:
                    l1 = bb(MW1 + e_i * 1024 + c * 128, MW1 + e_i * 1024 + (c + 1) * 128)
                    l2 = bb(MW2 + e_i * 1024 + c * 128, MW2 + e_i * 1024 + (c + 1) * 128)
                else:
                    k = e_i - 8
                    l1 = bb(WS1 + c * 256 + k * 128, WS1 + c * 256 + (k + 1) * 128)
                    l2 = bb(WS2 + c * 256 + k * 128, WS2 + c * 256 + (k + 1) * 128)
                rhs = bb(XT4 + c * WIN, XT4 + (c + 1) * WIN)
                tm1 = mm(bi1, l1, rhs, deps=(trt + wld) if c == 0 else [tm1],
                         start=(c == 0), stop=(c == NCH - 1))
                tm2 = mm(bi2, l2, rhs, deps=(trt + wld) if c == 0 else [tm2],
                         start=(c == 0), stop=(c == NCH - 1))
            hslot = bb(HST + e_i * WIN, HST + (e_i + 1) * WIN)
            ts1 = act(hslot, pb[bi1][:, :], AF.Silu, deps=[tm1])
            bank_done(bi1, ts1)
            ts2 = act(bb(H2SC, H2SC + WIN), pb[bi2][:, :], AF.Identity,
                      deps=[tm2] + ([prev_h2] if prev_h2 else []))
            bank_done(bi2, ts2)
            th = vtt(hslot, hslot, bb(H2SC, H2SC + WIN), ALU.mult, deps=[ts1, ts2])
            prev_h2 = th
            if e_i < 8:
                bj = nb()
                tbr = mm(bj, cHPR[0:1, :], wfT[0:1, e_i * WIN:(e_i + 1) * WIN], deps=wf_toks,
                         psl=(0, 128, 0, WIN))
                tw = vtt(hslot, hslot, pb[bj][:, :], ALU.mult, deps=[th, tbr])
                bank_done(bj, tw)
                hw_toks[e_i] = tw
            else:
                hw_toks[e_i] = th
        evs = []
        for oc in range(NCH):
            bo = nb()
            tmm = None
            first = True
            for k in range(2):
                tmm = mm(bo, bb(WS3 + k * 1024 + oc * 128, WS3 + k * 1024 + (oc + 1) * 128),
                         bb(HST + (8 + k) * WIN, HST + (9 + k) * WIN),
                         deps=([hw_toks[8], hw_toks[9]] + wld) if first else [tmm],
                         start=first, stop=False)
                first = False
            for e_i in range(8):
                tmm = mm(bo, bb(MW3 + e_i * 1024 + oc * 128, MW3 + e_i * 1024 + (oc + 1) * 128),
                         bb(HST + e_i * WIN, HST + (e_i + 1) * WIN),
                         deps=[tmm, hw_toks[e_i]], start=False, stop=(e_i == 7))
            te = act(f(OBF + oc * WIN, OBF + (oc + 1) * WIN), pb[bo][:, :], AF.Identity, deps=[tmm])
            bank_done(bo, te)
            evs.append(te)
        ts_, adds = out_window(w, evs, x1h, x2h, [])
        prev_w = [ts_] + adds

    P5aB = phase()
    wld5 = []
    for c in range(NCH):
        wld5.append(p.dma(bb(WQ + c * 1024, WQ + (c + 1) * 1024), Wq_h[c * 128:(c + 1) * 128, :],
                          queue="gpsimd", deps=P5aB))
        wld5.append(p.dma(bb(WK + c * 1024, WK + (c + 1) * 1024), Wk_h[c * 128:(c + 1) * 128, :],
                          queue="gpsimd", deps=P5aB))
        wld5.append(p.dma(bb(WV + c * 1024, WV + (c + 1) * 1024), Wv_h[c * 128:(c + 1) * 128, :],
                          queue="gpsimd", deps=P5aB))
    prev_w = P5aB
    last_use = {}
    for w in range(NW):
        tl, apl = ln_window(x2h, w * WIN, prev_w, XHAT5)
        tts = tr_window(XHAT5, lambda c, i: bb(XT5 + c * WIN + i * 128, XT5 + c * WIN + (i + 1) * 128),
                        apl, deps=prev_w)
        trt = [t for row in tts for t in row]
        wtoks = []
        for oc in range(NCH):
            for (Woff, BUF, hbm) in ((WQ, QWB, qth), (WK, KWB, kth)):
                bi = nb()
                tmm = None
                for c in range(NCH):
                    tmm = mm(bi, bb(Woff + c * 1024 + oc * 128, Woff + c * 1024 + (oc + 1) * 128),
                             bb(XT5 + c * WIN, XT5 + (c + 1) * WIN),
                             deps=(trt + wld5) if c == 0 else [tmm], start=(c == 0), stop=(c == NCH - 1))
                slot = BUF + (oc % 2) * WIN
                dd = [last_use[slot]] if slot in last_use else []
                te = act(bb(slot, slot + WIN), pb[bi][:, :], AF.Identity, deps=[tmm] + dd)
                bank_done(bi, te)
                tst = p.dma(hbm[oc, :, w * WIN:(w + 1) * WIN], bb(slot, slot + WIN), deps=[te])
                last_use[slot] = tst
                wtoks.append(tst)
            bi = nb()
            tmm = None
            for c in range(NCH):
                tmm = mm(bi, bb(WV + c * 1024 + oc * 128, WV + c * 1024 + (oc + 1) * 128),
                         bb(XT5 + c * WIN, XT5 + (c + 1) * WIN),
                         deps=(trt + wld5) if c == 0 else [tmm], start=(c == 0), stop=(c == NCH - 1))
            slot = VWB + (oc % 2) * WIN
            dd = [last_use[slot]] if slot in last_use else []
            te = act(bb(slot, slot + WIN), pb[bi][:, :], AF.Identity, deps=[tmm] + dd)
            bank_done(bi, te)
            vslot = VTB + (oc % 2) * WIN
            dv = [last_use[vslot]] if vslot in last_use else []
            ttr = [p.dma_tr(bb(vslot + i * 128, vslot + (i + 1) * 128),
                            bb(slot + i * 128, slot + (i + 1) * 128), deps=[te] + dv) for i in range(4)]
            last_use[slot] = ttr[-1]
            tst = p.dma(vth[oc, w * 4:(w + 1) * 4].rearrange("i p d -> p i d"),
                        bb(vslot, vslot + WIN).rearrange("p (i d) -> p i d", d=128), deps=ttr)
            last_use[vslot] = tst
            wtoks.append(tst)
        prev_w = wtoks

    P5bB = phase()
    chunk_done = {-1: P5bB, -2: P5bB}
    load_toks = {}

    def p5_loads(c):
        s = c % 2
        dd = list(chunk_done[c - 2])
        tq_ = p.dma(bb(QTC + s * 4096, QTC + (s + 1) * 4096), qth[c], deps=dd)
        tk_ = p.dma(bb(KTC + s * 4096, KTC + (s + 1) * 4096), kth[c], deps=dd)
        svo = SVC + s * 4160
        tms = vmemset(bb(svo, svo + 4160), 1.0, deps=dd)
        tvs = []
        for t in range(32):
            tvs.append(p.dma(bb(svo + t * 130, svo + t * 130 + 64), vth[c, t][:, 0:64], deps=[tms]))
            tvs.append(p.dma(bb(svo + t * 130 + 65, svo + t * 130 + 129), vth[c, t][:, 64:128], deps=[tms]))
        load_toks[c] = [tq_, tk_] + tvs

    p5_loads(0)
    lrow_free = None
    sc_free = {0: None, 1: None}
    pt_free = {0: None, 1: None}
    for c in range(NCH):
        if c + 1 < NCH:
            p5_loads(c + 1)
        s = c % 2
        svo = SVC + s * 4160
        ctoks = []
        for ha in range(2):
            hoff = ha * 64
            tp = (64, 0) if ha else None
            po = 2 + ha
            po_free = chunk_done[c - 1] if c > 0 else P5bB
            for qw in range(NW):
                nkt = 4 * (qw + 1)
                for kt in range(nkt):
                    sb_ = kt % 2
                    tsc = mm(sb_, BB[hoff:hoff + 64, KTC + s * 4096 + kt * 128: KTC + s * 4096 + (kt + 1) * 128],
                             BB[hoff:hoff + 64, QTC + s * 4096 + qw * WIN: QTC + s * 4096 + (qw + 1) * WIN],
                             deps=load_toks[c] + ([sc_free[sb_]] if sc_free[sb_] else []), tp=tp)
                    dd = [tsc] + ([pt_free[sb_]] if pt_free[sb_] else [])
                    te = act(bb(PT + sb_ * WIN, PT + (sb_ + 1) * WIN), pb[sb_][:, :], AF.Exp,
                             deps=dd, scale=0.125, bias=mshiftc)
                    sc_free[sb_] = te
                    if kt >= 4 * qw:
                        i = kt - 4 * qw
                        te = vtt(bb(PT + sb_ * WIN, PT + (sb_ + 1) * WIN),
                                 bb(PT + sb_ * WIN, PT + (sb_ + 1) * WIN),
                                 cmb[:, i * WIN:(i + 1) * WIN], ALU.mult, deps=[te])
                    tpv_ = mm(po, bb(svo + kt * 130 + ha * 65, svo + kt * 130 + (ha + 1) * 65),
                              bb(PT + sb_ * WIN, PT + (sb_ + 1) * WIN),
                              deps=[te] + (list(po_free) if kt == 0 else []),
                              start=(kt == 0), stop=(kt == nkt - 1), psl=(0, 65, 0, WIN))
                    pt_free[sb_] = tpv_
                tlc = vcopy(lrow[64:65, :], pb[po][64:65, 0:WIN],
                            deps=[tpv_] + ([lrow_free] if lrow_free else []))
                tsh = p.dma(lrow[0:1, :], lrow[64:65, :], deps=[tlc])
                trc = vrecip(lrow[0:1, :], lrow[0:1, :], deps=[tsh])
                tbc_ = mm(4, cHPR[0:1, 0:64], lrow[0:1, :], deps=[trc] + list(pfree[4]),
                          psl=(0, 64, 0, WIN))
                tev = act(lrow[0:64, :], pb[po][0:64, 0:WIN], AF.Identity, deps=[tbc_])
                AT = ATA if ha == 0 else ATB
                tno = vtt(BB[0:64, AT + s * 4096 + qw * WIN: AT + s * 4096 + (qw + 1) * WIN],
                          lrow[0:64, :], pb[4][0:64, 0:WIN], ALU.mult, deps=[tev])
                pfree[4] = (tno,)
                po_free = [tev]
                lrow_free = tno
                ctoks.append(tno)
        tsa = p.dma(ath[c, 0:64, :], BB[0:64, ATA + s * 4096: ATA + (s + 1) * 4096], deps=ctoks)
        tsb = p.dma(ath[c, 64:128, :], BB[0:64, ATB + s * 4096: ATB + (s + 1) * 4096], deps=ctoks)
        chunk_done[c] = [tsa, tsb] + ctoks

    P5wB = phase()
    wldo = [p.dma(bb(SWO + c * 1024, SWO + (c + 1) * 1024), Wo_h[c * 128:(c + 1) * 128, :],
                  queue="gpsimd", deps=P5wB) for c in range(NCH)]
    prev_w = P5wB
    for w in range(NW):
        tla = [p.dma(bb(AWIN + c * WIN, AWIN + (c + 1) * WIN), ath[c][:, w * WIN:(w + 1) * WIN],
                     deps=prev_w) for c in range(NCH)]
        evs = []
        for oc in range(NCH):
            bo = nb()
            tmm = None
            for c in range(NCH):
                tmm = mm(bo, bb(SWO + c * 1024 + oc * 128, SWO + c * 1024 + (oc + 1) * 128),
                         bb(AWIN + c * WIN, AWIN + (c + 1) * WIN),
                         deps=(tla + wldo) if c == 0 else [tmm], start=(c == 0), stop=(c == NCH - 1))
            te = act(f(OBF + oc * WIN, OBF + (oc + 1) * WIN), pb[bo][:, :], AF.Identity, deps=[tmm])
            bank_done(bo, te)
            evs.append(te)
        ts_, adds = out_window(w, evs, x2h, x3h, prev_w)
        prev_w = [ts_] + adds

    P6B = phase()
    wb_toks = []
    tprev = None
    for ri in range(2):
        WTo = WTR if ri == 0 else WTI
        for r in range(4):
            for cc in range(4):
                slot = cc % 2
                uo = UVU if slot == 0 else UVU2
                vo = UVV if slot == 0 else UVV2
                dd = [tprev] if tprev else list(P6B)
                tU = p.dma(f(uo, uo + 512), uvU_h[ri, r, cc], deps=dd)
                tV = p.dma(f(vo, vo + 512), uvV_h[ri, r, cc], deps=dd)
                if cc == 0:
                    tprev = vtt(f(FACC, FACC + 512), f(uo, uo + 512), f(vo, vo + 512), ALU.mult,
                                deps=[tU, tV])
                else:
                    tm_ = vtt(f(FTMP, FTMP + 512), f(uo, uo + 512), f(vo, vo + 512), ALU.mult,
                              deps=[tU, tV, tprev])
                    tprev = vtt(f(FACC, FACC + 512), f(FACC, FACC + 512), f(FTMP, FTMP + 512),
                                ALU.add, deps=[tm_])
            tw = act(bb(WTo + r * 512, WTo + (r + 1) * 512), f(FACC, FACC + 512), AF.Identity,
                     deps=[tprev])
            tprev = tw
            wb_toks.append(tw)

    prev_w = P6B
    final_stores = []
    for w in range(NW):
        tl, apl = ln_window(x3h, w * WIN, prev_w + wb_toks, XHAT6)
        tts = tr_window(XHAT6, lambda c, i: bb(XT6 + c * WIN + i * 128, XT6 + c * WIN + (i + 1) * 128),
                        apl, deps=prev_w)
        trt = [t for row in tts for t in row]
        tneg = [vts(bb(XT6N + k * WIN, XT6N + (k + 1) * WIN), bb(XT6 + (4 + k) * WIN, XT6 + (5 + k) * WIN),
                    -1.0, None, ALU.mult, deps=[tts[i][4 + k] for i in range(4)]) for k in range(4)]
        evs = []
        for oc in range(NCH):
            bo = nb()
            tmm = None
            first = True
            for k in range(4):
                if oc < 4:
                    l1 = bb(WTR + k * 512 + oc * 128, WTR + k * 512 + (oc + 1) * 128)
                    l2 = bb(WTI + k * 512 + oc * 128, WTI + k * 512 + (oc + 1) * 128)
                    r1 = bb(XT6 + k * WIN, XT6 + (k + 1) * WIN)
                    r2 = bb(XT6N + k * WIN, XT6N + (k + 1) * WIN)
                else:
                    l1 = bb(WTI + k * 512 + (oc - 4) * 128, WTI + k * 512 + (oc - 3) * 128)
                    l2 = bb(WTR + k * 512 + (oc - 4) * 128, WTR + k * 512 + (oc - 3) * 128)
                    r1 = bb(XT6 + k * WIN, XT6 + (k + 1) * WIN)
                    r2 = bb(XT6 + (4 + k) * WIN, XT6 + (5 + k) * WIN)
                tmm = mm(bo, l1, r1, deps=(trt + wb_toks + tneg) if first else [tmm],
                         start=first, stop=False)
                first = False
                tmm = mm(bo, l2, r2, deps=[tmm], start=False, stop=(k == 3))
            te = act(f(OBF + oc * WIN, OBF + (oc + 1) * WIN), pb[bo][:, :], AF.Identity,
                     deps=[tmm], bias=sttbr[:, oc:oc + 1])
            bank_done(bo, te)
            evs.append(te)
        ts_, adds = out_window(w, evs, x3h, out_h, prev_w)
        prev_w = [ts_] + adds
        final_stores.append(ts_)

    vmemset(f(CONVT, CONVT + 1), 0.0, deps=final_stores)

    with nc.Block() as block:
        p.emit(block)
    return nc


def host_prep(inp):
    g = {k: np.ascontiguousarray(np.asarray(v, np.float32)) for k, v in inp.items()}
    d = {}
    d["gate_wt"] = np.ascontiguousarray(g["L0_attn_gate_w"].T)
    d["B_wt"] = np.ascontiguousarray(g["L0_attn_B_w"].T)
    d["C_wt"] = np.ascontiguousarray(g["L0_attn_C_w"].T)
    d["D_wt"] = np.ascontiguousarray(g["L0_attn_D_w"].T)
    d["B_b"] = g["L0_attn_B_b"].reshape(S, 1)
    d["C_b"] = g["L0_attn_C_b"].reshape(S, 1)
    d["D_b"] = np.ascontiguousarray(g["L0_attn_D_b"].reshape(NCH, 128).T)
    d["gate_b"] = np.ascontiguousarray(g["L0_attn_gate_b"].reshape(2 * NCH, 128).T)
    d["conv_taps"] = g["L0_attn_conv_w"].reshape(D, 3).reshape(NCH, 128, 3)
    d["conv_b"] = np.ascontiguousarray(g["L0_attn_conv_b"].reshape(NCH, 128).T)
    d["A_m"] = g["L0_attn_A"]
    d["A_mt"] = np.ascontiguousarray(g["L0_attn_A"].T)
    d["dtA"] = np.asarray(g["L0_attn_dt_A"], np.float32).reshape(1, 1)
    d["g_col"] = g["L0_attn_ln_st_g"].reshape(S, 1)
    d["beta_col"] = g["L0_attn_ln_st_b"].reshape(S, 1)
    d["g_rep64"] = np.ascontiguousarray(np.tile(g["L0_attn_ln_st_g"].reshape(1, S), (S, 1)))
    d["lngc"] = np.ascontiguousarray(g["L0_attn_ln_in_g"].reshape(NCH, 128).T)
    d["lnbc"] = np.ascontiguousarray(g["L0_attn_ln_in_b"].reshape(NCH, 128).T)
    d["Pc"] = (np.eye(S) - 1.0 / S).astype(np.float32)
    d["ident64"] = np.eye(S, dtype=np.float32)
    d["ident128"] = np.eye(128, dtype=np.float32)
    consts = np.zeros((128, 8), np.float32)
    consts[:, 0] = EPS
    consts[:, 1] = EPS * EPS
    consts[:, 2] = -SHIFT
    d["consts"] = consts
    d["onesdiv"] = np.full((S, S), 1.0 / S, np.float32)
    d["hp_rows"] = np.ones((1, 128), np.float32)
    cmask = np.zeros((4, 128, WIN), np.float32)
    for i in range(4):
        for kp in range(128):
            cmask[i, kp, i * 128 + kp:] = 1.0
    d["cmask"] = cmask.astype(BF)
    d["moe_w1"] = g["L0_ffn_w1"]
    d["moe_w2"] = g["L0_ffn_w2"]
    d["moe_w3"] = g["L0_ffn_w3"]
    d["ws1"] = g["L0_ffn_ws1"]
    d["ws2"] = g["L0_ffn_ws2"]
    d["ws3"] = g["L0_ffn_ws3"]
    d["wr"] = g["L0_ffn_wr"]
    d["Wq_t"] = np.ascontiguousarray(g["L1_attn_Wq"].T)
    d["Wk_t"] = np.ascontiguousarray(g["L1_attn_Wk"].T)
    d["Wv_t"] = np.ascontiguousarray(g["L1_attn_Wv"].T)
    d["Wo_t"] = np.ascontiguousarray(g["L1_attn_Wo"].T)
    uvU = np.zeros((2, 4, 4, 128, 512), np.float32)
    uvV = np.zeros((2, 4, 4, 128, 512), np.float32)
    for ri, (G1k, G2k) in enumerate((("L1_ffn_G1_R", "L1_ffn_G2_R"), ("L1_ffn_G1_I", "L1_ffn_G2_I"))):
        for cc in range(4):
            Ac = g[G1k][0, :, cc, :]
            Bc = g[G2k][cc, :, 0, :]
            U = np.broadcast_to(Ac.T[:, None, :, None], (32, 16, 32, 16)).reshape(512, 512)
            V = np.broadcast_to(Bc.T[None, :, None, :], (32, 16, 32, 16)).reshape(512, 512)
            for r in range(4):
                uvU[ri, r, cc] = U[r * 128:(r + 1) * 128]
                uvV[ri, r, cc] = V[r * 128:(r + 1) * 128]
    d["uvU"] = uvU
    d["uvV"] = uvV
    d["tt_bias"] = np.ascontiguousarray(np.concatenate([g["L1_ffn_bias_real"], g["L1_ffn_bias_imag"]]).reshape(NCH, 128).T)
    return d


def kernel(**inp):
    nc = bass.Bass("TRN2", target_bir_lowering=False)
    build(nc, dbg=False)
    base = host_prep(inp)
    x = np.asarray(inp["L0_attn_x"], np.float32)
    in_maps = []
    for b in range(8):
        m = dict(base)
        m["x0"] = np.ascontiguousarray(x[b])
        in_maps.append(m)
    res = run_bass_kernel_spmd(nc, in_maps, core_ids=list(range(8)))
    return np.stack([np.asarray(res.results[i]["out"], np.float32) for i in range(8)])


# version 04
# speedup vs baseline: 1.1105x; 1.1105x over previous
import numpy as np
import ml_dtypes
import concourse.bass as bass
import concourse.mybir as mybir
from concourse.bass_utils import run_bass_kernel_spmd

F32 = mybir.dt.float32
BF16 = mybir.dt.bfloat16
AF = mybir.ActivationFunctionType
ALU = mybir.AluOpType
AX = mybir.AxisListType
BF = ml_dtypes.bfloat16

T = 4096
D = 1024
NCH = 8
WIN = 512
NW = 8
S = 64
KCH = 64
LCH = T // KCH
WARM = 320
STEPS = LCH + WARM
EPS = 1e-5
SHIFT = 12.0

NF = 18600
HIST = 0
OBF = 0
UVU, UVV, UVU2, UVV2, FACC, FTMP = 4096, 4608, 5120, 5632, 6144, 6656
SBMP = 9216
STAT = 13824
XWIN = 13888
CONVT = 17984
RTR = 18496

NB = 48200
XHT = 0
XHAT = 8224
XN2 = 12320
XGW = 20512
SGV = 24608
GSO = 25120
WBC = 25632
GWV = 26656
SCMB = 43040
YB = 0
GSL = 4096
SDW = 8224
XHAT4 = 0
XT4 = 4096
MW1, MW2, MW3 = 8192, 16384, 24576
WS1, WS2, WS3, SWR = 32768, 34816, 36864, 38912
HST = 38976
H2SC = 44096
XHAT5, XT5 = 0, 4096
WQ, WK, WV = 8192, 16384, 24576
QWB, KWB, VWB, VTB = 32768, 33792, 34816, 35840
QTC, KTC = 0, 8192
SVC = 16384
PT = 24704
ATA, ATB = 26752, 34944
AWIN, SWO = 0, 4096
XHAT6, XT6, XT6N, WTR, WTI = 0, 4096, 8192, 10240, 12288


class Prog:
    def __init__(self, nc):
        self.nc = nc
        self.engines = ("tensor", "vector", "scalar", "sync", "gpsimd")
        self.ops = {e: [] for e in self.engines}
        self.semname = {"tensor": "s_t", "vector": "s_v", "scalar": "s_s",
                        "sync": "s_y", "gpsimd": "s_g"}
        self.sems = {}
        self.cnt = {}
        for s in self.semname.values():
            self.sems[s] = nc.semaphore(s).__enter__()
            self.cnt[s] = 0

    def op(self, engine, fn, deps=(), inc=1):
        sem = self.semname[engine]
        waits = {}
        for dd in deps:
            if dd is None:
                continue
            sn, val = dd
            waits[sn] = max(waits.get(sn, 0), val)
        self.cnt[sem] += inc
        tok = (sem, self.cnt[sem])
        self.ops[engine].append((fn, sorted(waits.items()), (sem, inc)))
        return tok

    def dma(self, out, in_, deps=(), queue="sync"):
        return self.op(queue, lambda e, o=out, i=in_: e.dma_start(out=o, in_=i),
                       deps=deps, inc=16)

    def dma_tr(self, out, in_, deps=(), queue="scalar"):
        return self.op(queue, lambda e, o=out, i=in_: e.dma_start(out=o, in_=i, transpose=True),
                       deps=deps, inc=16)

    def barrier(self):
        return [(self.semname[e], self.cnt[self.semname[e]]) for e in self.engines]

    def emit(self, block):
        def make(engine):
            def body(eng):
                for fn, waits, inc in self.ops[engine]:
                    for sn, val in waits:
                        eng.wait_ge(self.sems[sn], val)
                    inst = fn(eng)
                    sn, n = inc
                    inst.then_inc(self.sems[sn], n)
            return body
        block.tensor(make("tensor"))
        block.vector(make("vector"))
        block.scalar(make("scalar"))
        block.sync(make("sync"))
        block.gpsimd(make("gpsimd"))


def build(nc, dbg=False):
    dram = lambda name, shape, d=F32, kind="ExternalInput": nc.dram_tensor(name, shape, d, kind=kind).ap()
    x0 = dram("x0", [T, D])
    gate_wt = dram("gate_wt", [D, 2 * D])
    B_wt = dram("B_wt", [D, S]); C_wt = dram("C_wt", [D, S]); D_wt_h = dram("D_wt", [S, D])
    Bb_h = dram("B_b", [S, 1]); Cb_h = dram("C_b", [S, 1]); Db_h = dram("D_b", [128, NCH])
    gb_h = dram("gate_b", [128, 2 * NCH])
    taps_h = dram("conv_taps", [NCH, 128, 3])
    cvb_h = dram("conv_b", [128, NCH])
    A_h = dram("A_m", [S, S]); At_h = dram("A_mt", [S, S]); dtA_h = dram("dtA", [1, 1])
    g_h = dram("g_col", [S, 1]); beta_h = dram("beta_col", [S, 1]); grep_h = dram("g_rep64", [S, S])
    lngc_h = dram("lngc", [128, NCH]); lnbc_h = dram("lnbc", [128, NCH])
    Pc_h = dram("Pc", [S, S]); id_h = dram("ident64", [S, S])
    cst_h = dram("consts", [128, 8])
    od_h = dram("onesdiv", [S, S])
    hpr_h = dram("hp_rows", [1, 128])
    id128_h = dram("ident128", [128, 128])
    cm_h = dram("cmask", [4, 128, WIN], BF16)
    w1_h = dram("moe_w1", [8, D, 128]); w2_h = dram("moe_w2", [8, D, 128]); w3_h = dram("moe_w3", [8, 128, D])
    ws1_h = dram("ws1", [D, 256]); ws2_h = dram("ws2", [D, 256]); ws3_h = dram("ws3", [256, D]); wr_h = dram("wr", [D, 8])
    Wq_h = dram("Wq_t", [D, D]); Wk_h = dram("Wk_t", [D, D]); Wv_h = dram("Wv_t", [D, D]); Wo_h = dram("Wo_t", [D, D])
    uvU_h = dram("uvU", [2, 4, 4, 128, 512]); uvV_h = dram("uvV", [2, 4, 4, 128, 512])
    ttbr_h = dram("tt_bias", [128, NCH])
    out_h = dram("out", [T, D], kind="ExternalOutput")
    kd = "ExternalOutput" if dbg else "Internal"
    x1h = dram("x1h", [T, D], kind=kd)
    x2h = dram("x2h", [T, D], kind=kd)
    x3h = dram("x3h", [T, D], kind=kd)
    gatesh = dram("gatesh", [NCH, 128, T], d=BF16, kind=("ExternalOutput" if dbg else "Internal"))
    if dbg:
        dbpp_h = dram("dbg_bpp", [S, 4608], kind="ExternalOutput")
        dzh_h = dram("dbg_zh", [S, T], kind="ExternalOutput")
        dcm_h = dram("dbg_cm", [S, T], d=BF16, kind="ExternalOutput")
        dyb_h = dram("dbg_yb", [S, T], d=BF16, kind="ExternalOutput")
        dxht_h = dram("dbg_xht", [128, 8192], d=BF16, kind="ExternalOutput")
        dxn2_h = dram("dbg_xn2", [128, 4096], d=BF16, kind="ExternalOutput")
    qth = dram("qth", [NCH, 128, T], d=BF16, kind="Internal")
    kth = dram("kth", [NCH, 128, T], d=BF16, kind="Internal")
    vth = dram("vth", [NCH, 32, 128, 128], d=BF16, kind="Internal")
    ath = dram("ath", [NCH, 128, T], d=BF16, kind="Internal")

    sbuf = lambda name, shape, d=F32: nc.sbuf_tensor(name, shape, d).__enter__()

    cC = sbuf("cC", [128, 8])
    cOD = sbuf("cOD", [S, S])
    cID = sbuf("cID", [S, S]); cPc = sbuf("cPc", [S, S]); cID128 = sbuf("cID128", [128, 128])
    cA = sbuf("cA", [S, S]); cAt = sbuf("cAt", [S, S]); cdt = sbuf("cdt", [1, 1])
    cg = sbuf("cg", [S, 1]); cbeta = sbuf("cbeta", [S, 1]); cgrep = sbuf("cgrep", [S, S])
    cHPR = sbuf("cHPR", [1, 128])
    lngc = sbuf("lngc_s", [128, NCH]); lnbc = sbuf("lnbc_s", [128, NCH])
    staps = sbuf("staps", [128, NCH * 4])
    scvb = sbuf("scvb", [128, NCH])
    sBb = sbuf("sBb", [S, 1]); sCb = sbuf("sCb", [S, 1])
    sDb = sbuf("sDb", [128, NCH]); sgbias = sbuf("sgbias", [128, 2 * NCH])
    sttbr = sbuf("sttbr", [128, NCH])
    scanM = sbuf("scanM", [S, S])
    bacol = sbuf("bacol", [S, 1])
    eE = sbuf("eE", [S, S]); eF = sbuf("eF", [S, S]); eH = sbuf("eH", [S, S])
    eAd = sbuf("eAd", [S, S]); eAdt = sbuf("eAdt", [S, S])
    eT1 = sbuf("eT1", [S, S]); eT2 = sbuf("eT2", [S, S])
    sdtc = sbuf("sdtc", [S, 1])
    rowtmp = sbuf("rowtmp", [1, S])
    cmb = sbuf("cmb", [128, 4 * WIN], BF16)
    szh2t = sbuf("szh2t", [S, LCH])
    hcell = sbuf("hcell", [128, 64], BF16)
    craw = [sbuf("craw0", [S, KCH])]
    csqb = [sbuf("csqb0", [S, KCH])]
    ztp = [sbuf(f"ztp{h}", [S, KCH]) for h in range(2)]
    wfT = sbuf("wfT", [1, 8 * WIN])
    lrow = sbuf("lrow", [128, WIN])

    FP = sbuf("FP", [128, NF])
    BB = sbuf("BB", [128, NB], BF16)
    f = lambda a, b: FP[:, a:b]
    f6 = lambda a, b: FP[0:S, a:b]
    bb = lambda a, b: BB[:, a:b]
    b6 = lambda a, b: BB[0:S, a:b]

    p = Prog(nc)
    pb = [nc.psum_tensor(f"pb{i}", [128, 512], F32).__enter__() for i in range(8)]
    pfree = [()] * 8
    _rr = [0]

    def nb():
        i = _rr[0]; _rr[0] = (i + 1) % 8
        return i

    def bank_done(i, tok):
        pfree[i] = (tok,)

    def mm(bi, lhsT, rhs, deps=(), start=True, stop=True, tp=None, psl=None):
        dst = pb[bi][psl[0]:psl[1], psl[2]:psl[3]] if psl else pb[bi][:, :]
        dd = (list(pfree[bi]) if start else []) + list(deps)
        return p.op("tensor", lambda e, o=dst, l=lhsT, r=rhs, s0=start, s1=stop, t_=tp: e.matmul(
            o, l, r, start=s0, stop=s1, tile_position=t_, skip_group_check=True), deps=dd)

    def ptr(bi, cols, src, deps=(), rows=128):
        dst = pb[bi][0:rows, cols[0]:cols[1]]
        dd = list(pfree[bi]) + list(deps)
        return p.op("tensor", lambda e, o=dst, i=src: e.transpose(o, i, cID128[:, :]), deps=dd)

    def act(out, in_, func, deps=(), bias=None, scale=1.0):
        if bias is None:
            return p.op("scalar", lambda e, o=out, i=in_, sc=scale, fu=func: e.activation(
                out=o, in_=i, func=fu, scale=sc), deps=deps)
        return p.op("scalar", lambda e, o=out, i=in_, b=bias, sc=scale, fu=func: e.activation(
            out=o, in_=i, func=fu, bias=b, scale=sc), deps=deps)

    def vtt(out, a, b2, opn, deps=()):
        return p.op("vector", lambda e, o=out, x=a, y=b2, oo=opn: e.tensor_tensor(o, x, y, oo),
                    deps=deps)

    def vts(out, in0, s1, s2, op0, op1=None, deps=()):
        if s2 is None:
            return p.op("vector", lambda e, o=out, i=in0, a=s1, oo=op0: e.tensor_scalar(
                out=o, in0=i, scalar1=a, scalar2=None, op0=oo), deps=deps)
        return p.op("vector", lambda e, o=out, i=in0, a=s1, b2=s2, oo=op0, o1=op1: e.tensor_scalar(
            out=o, in0=i, scalar1=a, scalar2=b2, op0=oo, op1=o1), deps=deps)

    def vstt(out, in0, s1, in1, op0, op1, deps=()):
        return p.op("vector", lambda e, o=out, i=in0, a=s1, j=in1, oo=op0, o1=op1:
                    e.scalar_tensor_tensor(out=o, in0=i, scalar=a, in1=j, op0=oo, op1=o1),
                    deps=deps)

    def vcopy(out, in_, deps=()):
        return p.op("vector", lambda e, o=out, i=in_: e.tensor_copy(o, i), deps=deps)

    def vred(out, in_, opn, deps=()):
        return p.op("vector", lambda e, o=out, i=in_, oo=opn: e.tensor_reduce(
            out=o, in_=i, axis=AX.X, op=oo), deps=deps)

    def vrecip(out, in_, deps=()):
        return p.op("vector", lambda e, o=out, i=in_: e.reciprocal(out=o, in_=i), deps=deps)

    def vmemset(ap_, val, deps=()):
        return p.op("vector", lambda e, o=ap_, v=val: e.memset(o, v), deps=deps)

    def phase():
        bar = p.barrier()
        for i in range(8):
            pfree[i] = tuple(bar)
        return bar

    epsc = cC[:, 0:1]; eps2c = cC[:, 1:2]; mshiftc = cC[:, 2:3]

    ld = [p.dma(cC[:, :], cst_h), p.dma(cOD[:, :], od_h),
          p.dma(cID[:, :], id_h), p.dma(cPc[:, :], Pc_h), p.dma(cID128[:, :], id128_h),
          p.dma(cA[:, :], A_h), p.dma(cAt[:, :], At_h), p.dma(cdt[:, :], dtA_h),
          p.dma(cg[:, :], g_h), p.dma(cbeta[:, :], beta_h), p.dma(cgrep[:, :], grep_h),
          p.dma(cHPR[:, :], hpr_h),
          p.dma(lngc[:, :], lngc_h), p.dma(lnbc[:, :], lnbc_h),
          p.dma(staps.ap().rearrange("p (c k) -> p c k", k=4)[:, :, 0:3], taps_h.rearrange("c p k -> p c k")),
          p.dma(scvb[:, :], cvb_h),
          p.dma(sBb[:, :], Bb_h), p.dma(sCb[:, :], Cb_h),
          p.dma(sDb[:, :], Db_h),
          p.dma(sgbias[:, :], gb_h),
          p.dma(sttbr[:, :], ttbr_h),
          p.dma(cmb.ap().rearrange("p (j n) -> p j n", n=WIN), cm_h.rearrange("j p n -> p j n"))]

    t_t1 = []
    for c in range(NCH):
        t_t1.append(vts(staps[:, c * 4 + 3:c * 4 + 4], staps[:, c * 4 + 1:c * 4 + 2],
                        1.0, None, ALU.add, deps=ld))

    tok = mm(0, cOD[0:1, :], cdt[:, :], deps=ld, psl=(0, S, 0, 1))
    t_dt = vts(sdtc[:, :], pb[0][0:S, 0:1], float(S) * (2.0 ** -8), None, ALU.mult, deps=[tok])
    bank_done(0, t_dt)
    tA1 = vts(eAd[:, :], cA[:, :], sdtc[:, :], None, ALU.mult, deps=[t_dt] + ld)
    tA2 = vts(eAdt[:, :], cAt[:, :], sdtc[:, :], None, ALU.mult, deps=[t_dt] + ld)

    def horner(Xts, Hout, tdep):
        tH = vcopy(Hout[:, :], cID[:, :], deps=tdep + ld)
        for k in range(6, 0, -1):
            tm_ = mm(1, Xts[:, :], Hout[:, :], deps=[tH], psl=(0, S, 0, S))
            te = act(eT1[:, :], pb[1][0:S, 0:S], AF.Identity, deps=[tm_], scale=1.0 / k)
            bank_done(1, te)
            tH = vtt(Hout[:, :], eT1[:, :], cID[:, :], ALU.add, deps=[te])
        return tH

    tE = horner(eAdt, eE, [tA1, tA2])
    tF = horner(eAd, eF, [tA1, tA2])
    for _ in range(8):
        tm1 = mm(1, eF[:, :], eE[:, :], deps=[tE, tF], psl=(0, S, 0, S))
        te1 = act(eT1[:, :], pb[1][0:S, 0:S], AF.Identity, deps=[tm1])
        bank_done(1, te1)
        tm2 = mm(2, eE[:, :], eF[:, :], deps=[tE, tF], psl=(0, S, 0, S))
        te2 = act(eT2[:, :], pb[2][0:S, 0:S], AF.Identity, deps=[tm2])
        bank_done(2, te2)
        tE = vcopy(eE[:, :], eT1[:, :], deps=[te1, tm2])
        tF = vcopy(eF[:, :], eT2[:, :], deps=[te2, tE])
    tM = vtt(eH[:, :], eF[:, :], cgrep[:, :], ALU.mult, deps=[tE, tF])
    tsm = mm(1, eH[:, :], cPc[:, :], deps=[tM], psl=(0, S, 0, S))
    tsme = act(scanM[:, :], pb[1][0:S, 0:S], AF.Identity, deps=[tsm])
    bank_done(1, tsme)
    tba = mm(2, cbeta[:, :], eE[:, :], deps=[tE], psl=(0, 1, 0, S))
    tbae = vcopy(rowtmp[:, :], pb[2][0:1, 0:S], deps=[tba])
    bank_done(2, tbae)
    tbat = p.op("tensor", lambda e: e.transpose(pb[3][0:S, 0:1], rowtmp[:, :], cID128[0:1, 0:1]),
                deps=[tbae] + list(pfree[3]))
    tbac = vcopy(bacol[:, :], pb[3][0:S, 0:1], deps=[tbat])
    bank_done(3, tbac)

    tgw = [p.dma(bb(GWV + c * 2048, GWV + (c + 1) * 2048), gate_wt[c * 128:(c + 1) * 128, :],
                 queue="gpsimd", deps=ld) for c in range(NCH)]
    tbc = []
    for c in range(NCH):
        tbc.append(p.dma(bb(WBC + c * 2 * S, WBC + c * 2 * S + S), B_wt[c * 128:(c + 1) * 128, :],
                         queue="gpsimd", deps=ld))
        tbc.append(p.dma(bb(WBC + c * 2 * S + S, WBC + (c + 1) * 2 * S), C_wt[c * 128:(c + 1) * 128, :],
                         queue="gpsimd", deps=ld))
    tz = [vmemset(hcell[:, c * 4:c * 4 + 1], 0.0) for c in range(NCH)]
    tzp = [vmemset(f6(SBMP, SBMP + WARM), 0.0), vmemset(f6(SBMP + WARM + T, SBMP + 4608), 0.0)]

    def ln_window(xsrc, woff, deps, xhat_off, bf16_out=True):
        tl = p.dma(f(XWIN, XWIN + 4 * D).rearrange("p (i d) -> p i d", d=D),
                   xsrc[woff:woff + WIN, :].rearrange("(i p) d -> p i d", p=128), deps=deps)
        apl = []
        for i in range(4):
            xt = f(XWIN + i * D, XWIN + (i + 1) * D)
            st12 = f(STAT + i * 16, STAT + i * 16 + 12)
            mv = f(STAT + i * 16 + 12, STAT + i * 16 + 14)
            sc = f(STAT + i * 16 + 14, STAT + i * 16 + 15)
            t1 = p.op("vector", lambda e, o=st12[:, 0:6], i_=xt[:, 0:512]: e.bn_stats(out=o, in_=i_),
                      deps=[tl])
            t2 = p.op("vector", lambda e, o=st12[:, 6:12], i_=xt[:, 512:1024]: e.bn_stats(out=o, in_=i_),
                      deps=[t1])
            t3 = p.op("vector", lambda e, o=mv, i_=st12: e.bn_aggr(out=o, in_=i_), deps=[t2])
            if bf16_out == "composite":
                t4 = act(sc, mv[:, 1:2], AF.Sqrt, deps=[t3], bias=eps2c, scale=1.0 + EPS)
            else:
                t4 = act(sc, mv[:, 1:2], AF.Sqrt, deps=[t3], bias=epsc, scale=1.0)
            t5 = vrecip(sc, sc, deps=[t4])
            t6 = vts(bb(xhat_off + i * D, xhat_off + (i + 1) * D), xt, mv[:, 0:1], sc,
                     ALU.subtract, ALU.mult, deps=[t5])
            apl.append(t6)
        return tl, apl

    def tr_window(xhat_off, dst_fn, apl, deps=()):
        tts = [[None] * NCH for _ in range(4)]
        for i in range(4):
            for c in range(NCH):
                tts[i][c] = p.dma_tr(dst_fn(c, i), bb(xhat_off + i * D + c * 128, xhat_off + i * D + (c + 1) * 128),
                                     deps=[apl[i]] + list(deps))
        return tts


    conv_done = {}
    aff_done = {}
    gates_done = {-1: []}
    bm_evac = []
    ln_prev = []

    def do_conv(w, deps_extra):
        s = w % 2
        xn2o = XN2 + s * 4096
        toks = []
        prev = None
        for c in range(NCH):
            base = XHT + (c * 2 + s) * WIN
            dat = lambda a, b: bb(base + a, base + b)
            hL = hcell[:, c * 4 + s * 2:c * 4 + s * 2 + 1]
            hR = hcell[:, c * 4 + s * 2 + 32:c * 4 + s * 2 + 33]
            cv = lambda a, b: f(CONVT + a, CONVT + b)
            dd = aff_done[w] + deps_extra + ([prev] if prev else []) + gates_done.get(w - 2, [])
            t1 = vts(cv(0, WIN), dat(0, WIN), staps[:, c * 4 + 3:c * 4 + 4], None, ALU.mult, deps=dd)
            t2 = vstt(cv(1, WIN), dat(0, WIN - 1), staps[:, c * 4 + 0:c * 4 + 1],
                      cv(1, WIN), ALU.mult, ALU.add, deps=[t1])
            t2b = vstt(cv(0, 1), hL, staps[:, c * 4 + 0:c * 4 + 1],
                       cv(0, 1), ALU.mult, ALU.add, deps=[t2])
            t3 = vstt(cv(0, WIN - 1), dat(1, WIN), staps[:, c * 4 + 2:c * 4 + 3],
                      cv(0, WIN - 1), ALU.mult, ALU.add, deps=[t2b])
            t3b = vstt(cv(WIN - 1, WIN), hR, staps[:, c * 4 + 2:c * 4 + 3],
                       cv(WIN - 1, WIN), ALU.mult, ALU.add, deps=[t3])
            t4 = vts(bb(xn2o + c * WIN, xn2o + (c + 1) * WIN), cv(0, WIN),
                     scvb[:, c:c + 1], None, ALU.add, deps=[t3b])
            prev = t4
            toks.append(t4)
        conv_done[w] = toks
        return toks

    def do_gates(w):
        s = w % 2
        xn2o = XN2 + s * 4096
        ct = conv_done[w]
        igev = []
        prev_sg = None
        for oc in range(NCH):
            bi = nb()
            tmm = None
            for c in range(NCH):
                tmm = mm(bi, bb(GWV + c * 2048 + 1024 + oc * 128, GWV + c * 2048 + 1024 + (oc + 1) * 128),
                         bb(xn2o + c * WIN, xn2o + (c + 1) * WIN),
                         deps=(ct + tgw) if c == 0 else [tmm], start=(c == 0), stop=(c == NCH - 1))
            tsg = act(bb(SGV, SGV + WIN), pb[bi][:, :], AF.Sigmoid, deps=[tmm] + ([prev_sg] if prev_sg else []),
                      bias=sgbias[:, NCH + oc:NCH + oc + 1])
            bank_done(bi, tsg)
            txg = vtt(bb(XGW + oc * WIN, XGW + (oc + 1) * WIN), bb(xn2o + oc * WIN, xn2o + (oc + 1) * WIN),
                      bb(SGV, SGV + WIN), ALU.mult, deps=[tsg])
            prev_sg = txg
            igev.append(txg)
        biB = nb(); biC = nb()
        tmB = None; tmC = None
        for c in range(NCH):
            tmB = mm(biB, bb(WBC + c * 2 * S, WBC + c * 2 * S + S), bb(XGW + c * WIN, XGW + (c + 1) * WIN),
                     deps=(igev + tbc) if c == 0 else [tmB], start=(c == 0), stop=(c == NCH - 1),
                     psl=(0, S, 0, WIN))
        for c in range(NCH):
            tmC = mm(biC, bb(WBC + c * 2 * S + S, WBC + (c + 1) * 2 * S), bb(xn2o + c * WIN, xn2o + (c + 1) * WIN),
                     deps=(ct + tbc) if c == 0 else [tmC], start=(c == 0), stop=(c == NCH - 1),
                     psl=(0, S, 0, WIN))
        tBe = act(f6(HIST + w * WIN, HIST + (w + 1) * WIN), pb[biB][0:S, 0:WIN], AF.Identity,
                  deps=[tmB], bias=sBb[:, :])
        bank_done(biB, tBe)
        tCe = act(b6(SCMB + w * WIN, SCMB + (w + 1) * WIN), pb[biC][0:S, 0:WIN], AF.Identity,
                  deps=[tmC], bias=sCb[:, :])
        bank_done(biC, tCe)
        bm_evac.extend([tBe, tCe])
        gtoks = []
        prev_st = None
        for oc in range(NCH):
            bi = nb()
            tmm = None
            for c in range(NCH):
                tmm = mm(bi, bb(GWV + c * 2048 + oc * 128, GWV + c * 2048 + (oc + 1) * 128),
                         bb(xn2o + c * WIN, xn2o + (c + 1) * WIN),
                         deps=(ct + tgw) if c == 0 else [tmm], start=(c == 0), stop=(c == NCH - 1))
            tsg = act(bb(GSO, GSO + WIN), pb[bi][:, :], AF.Sigmoid,
                      deps=[tmm] + ([prev_st] if prev_st else []), bias=sgbias[:, oc:oc + 1])
            bank_done(bi, tsg)
            tst = p.dma(gatesh[oc, :, w * WIN:(w + 1) * WIN], bb(GSO, GSO + WIN), deps=[tsg])
            prev_st = tst
            gtoks.append(tst)
        gates_done[w] = gtoks + [tBe, tCe]
        return gtoks

    halo_toks = {}
    for w in range(NW):
        ldeps = tz + t_t1 + [tA1] + (ln_prev if ln_prev else [])
        tl, apl = ln_window(x0, w * WIN, ldeps, XHAT, bf16_out="composite")
        hl = []
        if w >= 2:
            s_src = w % 2; s_dst = (w - 1) % 2
            for c in range(NCH):
                base = XHT + (c * 2 + s_src) * WIN
                hl.append(vcopy(hcell[:, c * 4 + s_dst * 2:c * 4 + s_dst * 2 + 1],
                                bb(base + WIN - 1, base + WIN),
                                deps=aff_done[w - 2] + conv_done.get(w - 2, [])))
        s = w % 2
        tts = tr_window(XHAT, lambda c, i, s=s: bb(XHT + (c * 2 + s) * WIN + i * 128,
                                                   XHT + (c * 2 + s) * WIN + (i + 1) * 128),
                        apl, deps=hl + conv_done.get(w - 2, []) + gates_done.get(w - 2, []))
        ln_prev = [t for row in tts for t in row]
        aff = []
        for c in range(NCH):
            base = XHT + (c * 2 + s) * WIN
            aff.append(vts(bb(base, base + WIN), bb(base, base + WIN),
                           lngc[:, c:c + 1], lnbc[:, c:c + 1], ALU.mult, ALU.add,
                           deps=[tts[i][c] for i in range(4)]))
        aff_done[w] = aff
        hr = []
        if w >= 1:
            s_dst = (w - 1) % 2
            for c in range(NCH):
                base = XHT + (c * 2 + s) * WIN
                hr.append(vcopy(hcell[:, c * 4 + s_dst * 2 + 32:c * 4 + s_dst * 2 + 33],
                                bb(base, base + 1), deps=[aff[c]]))
            halo_toks[w - 1] = hl + hr
            do_conv(w - 1, hl + hr)
            do_gates(w - 1)
            if dbg and w == 1:
                td1 = p.dma(dxht_h, bb(XHT, XHT + 8192), deps=aff_done[0] + conv_done[0])
                td2 = p.dma(dxn2_h, bb(XN2, XN2 + 4096), deps=conv_done[0])
                gates_done[0] = gates_done[0] + [td1, td2]
    hl = []
    s_dst = 1
    for c in range(NCH):
        base = XHT + (c * 2 + 0) * WIN
        hl.append(vcopy(hcell[:, c * 4 + s_dst * 2:c * 4 + s_dst * 2 + 1],
                        bb(base + WIN - 1, base + WIN),
                        deps=aff_done[6] + conv_done.get(6, [])))
        hl.append(vmemset(hcell[:, c * 4 + s_dst * 2 + 32:c * 4 + s_dst * 2 + 33], 0.0,
                          deps=conv_done.get(5, [])))
    do_conv(7, hl)
    do_gates(7)

    P2B = phase()
    tb2 = vts(f6(HIST + 1, HIST + T), f6(HIST + 1, HIST + T), bacol[:, :], None, ALU.add,
              deps=P2B)
    pps = []
    for i in range(NW):
        bi = nb()
        tm_ = mm(bi, cPc[:, :], f6(HIST + i * WIN, HIST + (i + 1) * WIN), deps=[tb2],
                 psl=(0, S, 0, WIN))
        te = act(f6(SBMP + WARM + i * WIN, SBMP + WARM + (i + 1) * WIN), pb[bi][0:S, 0:WIN],
                 AF.Identity, deps=[tm_])
        bank_done(bi, te)
        pps.append(te)
    SBW = FP[0:S, SBMP:SBMP + 4608].rearrange("p (j l) -> p j l", l=KCH)
    scan_end = []
    pA = [0, 1]
    pB = [2, 3]
    tV = tq = tM2 = tR = tZ = tM1 = None
    for t in range(STEPS):
        bAP = SBW[:, t // KCH: t // KCH + KCH, t % KCH]
        if t < WARM:
            zt = ztp[t % 2][:, :]
        else:
            zt = f6(HIST + (t - WARM) * KCH, HIST + (t - WARM + 1) * KCH)
        if t == 0:
            tV = vcopy(craw[0][:, :], bAP, deps=pps + P2B + tzp)
        else:
            tV = vtt(craw[0][:, :], pb[pA[t % 2]][0:S, 0:KCH], bAP, ALU.add, deps=[tM1])
        tq = act(csqb[0][:, :], craw[0][:, :], AF.Square, deps=[tV])
        tM2 = mm(pB[t % 2], cOD[:, :], csqb[0][:, :], deps=[tq] + ([tR] if t >= 2 else []),
                 psl=(0, S, 0, KCH))
        tR = act(csqb[0][:, :], pb[pB[t % 2]][0:S, 0:KCH],
                 AF.Sqrt, deps=[tM2], bias=epsc[0:S, :])
        tRc = vrecip(csqb[0][:, :], csqb[0][:, :], deps=[tR])
        tZ = vtt(zt, craw[0][:, :], csqb[0][:, :], ALU.mult, deps=[tRc])
        if t < STEPS - 1:
            tM1 = mm(pA[(t + 1) % 2], scanM[:, :], zt, deps=[tZ] + ([tV] if t >= 1 else []),
                     psl=(0, S, 0, KCH))
        else:
            scan_end.append(tZ)

    if dbg:
        p.dma(dbpp_h, f6(SBMP, SBMP + 4608), deps=scan_end)
        p.dma(dzh_h, f6(HIST, HIST + T), deps=scan_end)
        p.dma(dcm_h, b6(SCMB, SCMB + T), deps=scan_end)

    P3B = phase()
    HISTV = FP[0:S, HIST:HIST + T].rearrange("p (t k) -> p t k", k=KCH)
    ty_prev = None
    ty_toks = []
    for j in range(KCH):
        zsrc = HISTV[:, :, j]
        t1 = vts(szh2t[:, 0:LCH], zsrc, cg[:, :], cbeta[:, :], ALU.mult, ALU.add,
                 deps=P3B + ([ty_prev] if ty_prev else []))
        t2 = vtt(b6(YB + j * LCH, YB + (j + 1) * LCH), szh2t[:, 0:LCH],
                 b6(SCMB + j * LCH, SCMB + (j + 1) * LCH), ALU.mult, deps=[t1])
        ty_prev = t2
        ty_toks.append(t2)
    tDld = p.dma(b6(SDW, SDW + D), D_wt_h, queue="gpsimd", deps=P3B)

    def out_window(w, obf_toks, xsrc, xdst, extra_ld_deps):
        tl = p.dma(f(XWIN, XWIN + 4 * D).rearrange("p (i d) -> p i d", d=D),
                   xsrc[w * WIN:(w + 1) * WIN, :].rearrange("(i p) d -> p i d", p=128),
                   deps=extra_ld_deps)
        adds = []
        for i in range(4):
            for oc in range(NCH):
                bi = nb()
                tt = ptr(bi, (0, 128), f(OBF + oc * WIN + i * 128, OBF + oc * WIN + (i + 1) * 128),
                         deps=obf_toks)
                ta = vtt(f(XWIN + i * D + oc * 128, XWIN + i * D + (oc + 1) * 128),
                         f(XWIN + i * D + oc * 128, XWIN + i * D + (oc + 1) * 128),
                         pb[bi][0:128, 0:128], ALU.add, deps=[tt, tl])
                bank_done(bi, ta)
                adds.append(ta)
        ts_ = p.dma(xdst[w * WIN:(w + 1) * WIN, :].rearrange("(i p) d -> p i d", p=128),
                    f(XWIN, XWIN + 4 * D).rearrange("p (i d) -> p i d", d=D), deps=adds)
        return ts_, adds

    prev_st = []
    tgl_prev = None
    for w in range(NW):
        tgl = p.dma(bb(GSL, GSL + 4096).rearrange("p (c n) -> p c n", n=WIN),
                    gatesh[:, :, w * WIN:(w + 1) * WIN].rearrange("c p n -> p c n"),
                    deps=P3B + ([tgl_prev] if tgl_prev else []))
        evs = []
        for oc in range(NCH):
            bi = nb()
            tm_ = mm(bi, b6(SDW + oc * 128, SDW + (oc + 1) * 128), b6(YB + w * WIN, YB + (w + 1) * WIN),
                     deps=ty_toks + [tDld], psl=(0, 128, 0, WIN))
            te = act(f(OBF + oc * WIN, OBF + (oc + 1) * WIN), pb[bi][:, 0:WIN], AF.Identity,
                     deps=[tm_] + prev_st, bias=sDb[:, oc:oc + 1])
            tg = vtt(f(OBF + oc * WIN, OBF + (oc + 1) * WIN), f(OBF + oc * WIN, OBF + (oc + 1) * WIN),
                     bb(GSL + oc * WIN, GSL + (oc + 1) * WIN), ALU.mult, deps=[te, tgl])
            bank_done(bi, tg)
            evs.append(tg)
        ts_, adds = out_window(w, evs, x0, x1h, prev_st)
        tgl_prev = evs[-1]
        prev_st = [ts_] + adds

    if dbg:
        p.dma(dyb_h, b6(YB, YB + T), deps=ty_toks + prev_st)

    P4B = phase()
    wld = []
    for e_i in range(8):
        for c in range(NCH):
            wld.append(p.dma(bb(MW1 + e_i * 1024 + c * 128, MW1 + e_i * 1024 + (c + 1) * 128),
                             w1_h[e_i, c * 128:(c + 1) * 128, :], queue="gpsimd", deps=P4B))
            wld.append(p.dma(bb(MW2 + e_i * 1024 + c * 128, MW2 + e_i * 1024 + (c + 1) * 128),
                             w2_h[e_i, c * 128:(c + 1) * 128, :], queue="gpsimd", deps=P4B))
        wld.append(p.dma(bb(MW3 + e_i * 1024, MW3 + (e_i + 1) * 1024), w3_h[e_i], queue="gpsimd", deps=P4B))
    for c in range(NCH):
        wld.append(p.dma(bb(WS1 + c * 256, WS1 + (c + 1) * 256), ws1_h[c * 128:(c + 1) * 128, :],
                         queue="gpsimd", deps=P4B))
        wld.append(p.dma(bb(WS2 + c * 256, WS2 + (c + 1) * 256), ws2_h[c * 128:(c + 1) * 128, :],
                         queue="gpsimd", deps=P4B))
        wld.append(p.dma(bb(SWR + c * 8, SWR + (c + 1) * 8), wr_h[c * 128:(c + 1) * 128, :],
                         queue="gpsimd", deps=P4B))
    for k in range(2):
        wld.append(p.dma(bb(WS3 + k * 1024, WS3 + (k + 1) * 1024), ws3_h[k * 128:(k + 1) * 128, :],
                         queue="gpsimd", deps=P4B))

    prev_w = P4B
    for w in range(NW):
        tl, apl = ln_window(x1h, w * WIN, prev_w, XHAT4)
        tts = tr_window(XHAT4, lambda c, i: bb(XT4 + c * WIN + i * 128, XT4 + c * WIN + (i + 1) * 128),
                        apl, deps=prev_w)
        trt = [t for row in tts for t in row]
        wf_toks = []
        for i in range(4):
            bi = nb()
            tmm = None
            for c in range(NCH):
                tmm = mm(bi, bb(XT4 + c * WIN + i * 128, XT4 + c * WIN + (i + 1) * 128),
                         bb(SWR + c * 8, SWR + (c + 1) * 8),
                         deps=(trt + wld) if c == 0 else [tmm], start=(c == 0), stop=(c == NCH - 1),
                         psl=(0, 128, 0, 8))
            pr = f(RTR + i * 24, RTR + i * 24 + 8)
            rx = f(RTR + i * 24 + 8, RTR + i * 24 + 16)
            mx = f(RTR + i * 24 + 16, RTR + i * 24 + 17)
            ms = f(RTR + i * 24 + 17, RTR + i * 24 + 18)
            m1 = f(RTR + i * 24 + 18, RTR + i * 24 + 19)
            m2 = f(RTR + i * 24 + 19, RTR + i * 24 + 20)
            t1 = vred(mx, pb[bi][:, 0:8], ALU.max, deps=[tmm])
            t2 = vts(mx, mx, -1.0, None, ALU.mult, deps=[t1])
            t3 = act(pr, pb[bi][:, 0:8], AF.Exp, deps=[t2], bias=mx)
            bank_done(bi, t3)
            t4 = vred(ms, pr, ALU.add, deps=[t3])
            t5 = vrecip(ms, ms, deps=[t4])
            t6 = vts(pr, pr, ms, None, ALU.mult, deps=[t5])
            t7 = vred(m1, pr, ALU.min, deps=[t6])
            t8 = vts(rx, pr, m1, None, ALU.is_le, deps=[t7])
            t9 = vstt(rx, rx, 2.0, pr, ALU.mult, ALU.add, deps=[t8])
            t11 = vred(m2, rx, ALU.min, deps=[t9])
            t12 = vts(rx, pr, m2, None, ALU.is_gt, deps=[t11])
            t13 = vtt(rx, rx, pr, ALU.mult, deps=[t12])
            t14 = vred(ms, rx, ALU.add, deps=[t13])
            t15 = vrecip(ms, ms, deps=[t14])
            t16 = vts(rx, rx, ms, None, ALU.mult, deps=[t15])
            for e_i in range(8):
                bj = nb()
                t17 = ptr(bj, (0, 128), rx[:, e_i:e_i + 1], deps=[t16], rows=1)
                t18 = vcopy(wfT[0:1, e_i * WIN + i * 128:e_i * WIN + (i + 1) * 128],
                            pb[bj][0:1, 0:128], deps=[t17])
                bank_done(bj, t18)
                wf_toks.append(t18)
        hw_toks = {}
        prev_h2 = None
        for e_i in range(10):
            bi1 = nb(); bi2 = nb()
            tm1 = tm2 = None
            for c in range(NCH):
                if e_i < 8:
                    l1 = bb(MW1 + e_i * 1024 + c * 128, MW1 + e_i * 1024 + (c + 1) * 128)
                    l2 = bb(MW2 + e_i * 1024 + c * 128, MW2 + e_i * 1024 + (c + 1) * 128)
                else:
                    k = e_i - 8
                    l1 = bb(WS1 + c * 256 + k * 128, WS1 + c * 256 + (k + 1) * 128)
                    l2 = bb(WS2 + c * 256 + k * 128, WS2 + c * 256 + (k + 1) * 128)
                rhs = bb(XT4 + c * WIN, XT4 + (c + 1) * WIN)
                tm1 = mm(bi1, l1, rhs, deps=(trt + wld) if c == 0 else [tm1],
                         start=(c == 0), stop=(c == NCH - 1))
                tm2 = mm(bi2, l2, rhs, deps=(trt + wld) if c == 0 else [tm2],
                         start=(c == 0), stop=(c == NCH - 1))
            hslot = bb(HST + e_i * WIN, HST + (e_i + 1) * WIN)
            ts1 = act(hslot, pb[bi1][:, :], AF.Silu, deps=[tm1])
            bank_done(bi1, ts1)
            ts2 = act(bb(H2SC, H2SC + WIN), pb[bi2][:, :], AF.Identity,
                      deps=[tm2] + ([prev_h2] if prev_h2 else []))
            bank_done(bi2, ts2)
            th = vtt(hslot, hslot, bb(H2SC, H2SC + WIN), ALU.mult, deps=[ts1, ts2])
            prev_h2 = th
            if e_i < 8:
                bj = nb()
                tbr = mm(bj, cHPR[0:1, :], wfT[0:1, e_i * WIN:(e_i + 1) * WIN], deps=wf_toks,
                         psl=(0, 128, 0, WIN))
                tw = vtt(hslot, hslot, pb[bj][:, :], ALU.mult, deps=[th, tbr])
                bank_done(bj, tw)
                hw_toks[e_i] = tw
            else:
                hw_toks[e_i] = th
        evs = []
        for oc in range(NCH):
            bo = nb()
            tmm = None
            first = True
            for k in range(2):
                tmm = mm(bo, bb(WS3 + k * 1024 + oc * 128, WS3 + k * 1024 + (oc + 1) * 128),
                         bb(HST + (8 + k) * WIN, HST + (9 + k) * WIN),
                         deps=([hw_toks[8], hw_toks[9]] + wld) if first else [tmm],
                         start=first, stop=False)
                first = False
            for e_i in range(8):
                tmm = mm(bo, bb(MW3 + e_i * 1024 + oc * 128, MW3 + e_i * 1024 + (oc + 1) * 128),
                         bb(HST + e_i * WIN, HST + (e_i + 1) * WIN),
                         deps=[tmm, hw_toks[e_i]], start=False, stop=(e_i == 7))
            te = act(f(OBF + oc * WIN, OBF + (oc + 1) * WIN), pb[bo][:, :], AF.Identity, deps=[tmm])
            bank_done(bo, te)
            evs.append(te)
        ts_, adds = out_window(w, evs, x1h, x2h, [])
        prev_w = [ts_] + adds

    P5aB = phase()
    wld5 = []
    for c in range(NCH):
        wld5.append(p.dma(bb(WQ + c * 1024, WQ + (c + 1) * 1024), Wq_h[c * 128:(c + 1) * 128, :],
                          queue="gpsimd", deps=P5aB))
        wld5.append(p.dma(bb(WK + c * 1024, WK + (c + 1) * 1024), Wk_h[c * 128:(c + 1) * 128, :],
                          queue="gpsimd", deps=P5aB))
        wld5.append(p.dma(bb(WV + c * 1024, WV + (c + 1) * 1024), Wv_h[c * 128:(c + 1) * 128, :],
                          queue="gpsimd", deps=P5aB))
    prev_w = P5aB
    last_use = {}
    for w in range(NW):
        tl, apl = ln_window(x2h, w * WIN, prev_w, XHAT5)
        tts = tr_window(XHAT5, lambda c, i: bb(XT5 + c * WIN + i * 128, XT5 + c * WIN + (i + 1) * 128),
                        apl, deps=prev_w)
        trt = [t for row in tts for t in row]
        wtoks = []
        for oc in range(NCH):
            for (Woff, BUF, hbm) in ((WQ, QWB, qth), (WK, KWB, kth)):
                bi = nb()
                tmm = None
                for c in range(NCH):
                    tmm = mm(bi, bb(Woff + c * 1024 + oc * 128, Woff + c * 1024 + (oc + 1) * 128),
                             bb(XT5 + c * WIN, XT5 + (c + 1) * WIN),
                             deps=(trt + wld5) if c == 0 else [tmm], start=(c == 0), stop=(c == NCH - 1))
                slot = BUF + (oc % 2) * WIN
                dd = [last_use[slot]] if slot in last_use else []
                te = act(bb(slot, slot + WIN), pb[bi][:, :], AF.Identity, deps=[tmm] + dd)
                bank_done(bi, te)
                tst = p.dma(hbm[oc, :, w * WIN:(w + 1) * WIN], bb(slot, slot + WIN), deps=[te])
                last_use[slot] = tst
                wtoks.append(tst)
            bi = nb()
            tmm = None
            for c in range(NCH):
                tmm = mm(bi, bb(WV + c * 1024 + oc * 128, WV + c * 1024 + (oc + 1) * 128),
                         bb(XT5 + c * WIN, XT5 + (c + 1) * WIN),
                         deps=(trt + wld5) if c == 0 else [tmm], start=(c == 0), stop=(c == NCH - 1))
            slot = VWB + (oc % 2) * WIN
            dd = [last_use[slot]] if slot in last_use else []
            te = act(bb(slot, slot + WIN), pb[bi][:, :], AF.Identity, deps=[tmm] + dd)
            bank_done(bi, te)
            vslot = VTB + (oc % 2) * WIN
            dv = [last_use[vslot]] if vslot in last_use else []
            ttr = [p.dma_tr(bb(vslot + i * 128, vslot + (i + 1) * 128),
                            bb(slot + i * 128, slot + (i + 1) * 128), deps=[te] + dv) for i in range(4)]
            last_use[slot] = ttr[-1]
            tst = p.dma(vth[oc, w * 4:(w + 1) * 4].rearrange("i p d -> p i d"),
                        bb(vslot, vslot + WIN).rearrange("p (i d) -> p i d", d=128), deps=ttr)
            last_use[vslot] = tst
            wtoks.append(tst)
        prev_w = wtoks

    P5bB = phase()
    chunk_done = {-1: P5bB, -2: P5bB}
    load_toks = {}

    def p5_loads(c):
        s = c % 2
        dd = list(chunk_done[c - 2])
        tq_ = p.dma(bb(QTC + s * 4096, QTC + (s + 1) * 4096), qth[c], deps=dd)
        tk_ = p.dma(bb(KTC + s * 4096, KTC + (s + 1) * 4096), kth[c], deps=dd)
        svo = SVC + s * 4160
        tms = vmemset(bb(svo, svo + 4160), 1.0, deps=dd)
        tvs = []
        for t in range(32):
            tvs.append(p.dma(bb(svo + t * 130, svo + t * 130 + 64), vth[c, t][:, 0:64], deps=[tms]))
            tvs.append(p.dma(bb(svo + t * 130 + 65, svo + t * 130 + 129), vth[c, t][:, 64:128], deps=[tms]))
        load_toks[c] = [tq_, tk_] + tvs

    p5_loads(0)
    lrow_free = None
    SCB = [0, 1, 5, 6]
    sc_free = {0: None, 1: None, 2: None, 3: None}
    pt_free = {0: None, 1: None, 2: None, 3: None}
    for c in range(NCH):
        if c + 1 < NCH:
            p5_loads(c + 1)
        s = c % 2
        svo = SVC + s * 4160
        ctoks = []
        for ha in range(2):
            hoff = ha * 64
            tp = (64, 0) if ha else None
            po = 2 + ha
            po_free = chunk_done[c - 1] if c > 0 else P5bB
            for qw in range(NW):
                nkt = 4 * (qw + 1)
                for kt in range(nkt):
                    sb_ = kt % 4
                    tsc = mm(SCB[sb_], BB[hoff:hoff + 64, KTC + s * 4096 + kt * 128: KTC + s * 4096 + (kt + 1) * 128],
                             BB[hoff:hoff + 64, QTC + s * 4096 + qw * WIN: QTC + s * 4096 + (qw + 1) * WIN],
                             deps=load_toks[c] + ([sc_free[sb_]] if sc_free[sb_] else []), tp=tp)
                    dd = [tsc] + ([pt_free[sb_]] if pt_free[sb_] else [])
                    te = act(bb(PT + sb_ * WIN, PT + (sb_ + 1) * WIN), pb[SCB[sb_]][:, :], AF.Exp,
                             deps=dd, scale=0.125, bias=mshiftc)
                    sc_free[sb_] = te
                    if kt >= 4 * qw:
                        i = kt - 4 * qw
                        te = vtt(bb(PT + sb_ * WIN, PT + (sb_ + 1) * WIN),
                                 bb(PT + sb_ * WIN, PT + (sb_ + 1) * WIN),
                                 cmb[:, i * WIN:(i + 1) * WIN], ALU.mult, deps=[te])
                    tpv_ = mm(po, bb(svo + kt * 130 + ha * 65, svo + kt * 130 + (ha + 1) * 65),
                              bb(PT + sb_ * WIN, PT + (sb_ + 1) * WIN),
                              deps=[te] + (list(po_free) if kt == 0 else []),
                              start=(kt == 0), stop=(kt == nkt - 1), psl=(0, 65, 0, WIN))
                    pt_free[sb_] = tpv_
                tlc = vcopy(lrow[64:65, :], pb[po][64:65, 0:WIN],
                            deps=[tpv_] + ([lrow_free] if lrow_free else []))
                tsh = p.dma(lrow[0:1, :], lrow[64:65, :], deps=[tlc])
                trc = vrecip(lrow[0:1, :], lrow[0:1, :], deps=[tsh])
                tbc_ = mm(4, cHPR[0:1, 0:64], lrow[0:1, :], deps=[trc] + list(pfree[4]),
                          psl=(0, 64, 0, WIN))
                tev = act(lrow[0:64, :], pb[po][0:64, 0:WIN], AF.Identity, deps=[tbc_])
                AT = ATA if ha == 0 else ATB
                tno = vtt(BB[0:64, AT + s * 4096 + qw * WIN: AT + s * 4096 + (qw + 1) * WIN],
                          lrow[0:64, :], pb[4][0:64, 0:WIN], ALU.mult, deps=[tev])
                pfree[4] = (tno,)
                po_free = [tev]
                lrow_free = tno
                ctoks.append(tno)
        tsa = p.dma(ath[c, 0:64, :], BB[0:64, ATA + s * 4096: ATA + (s + 1) * 4096], deps=ctoks)
        tsb = p.dma(ath[c, 64:128, :], BB[0:64, ATB + s * 4096: ATB + (s + 1) * 4096], deps=ctoks)
        chunk_done[c] = [tsa, tsb] + ctoks

    P5wB = phase()
    wldo = [p.dma(bb(SWO + c * 1024, SWO + (c + 1) * 1024), Wo_h[c * 128:(c + 1) * 128, :],
                  queue="gpsimd", deps=P5wB) for c in range(NCH)]
    prev_w = P5wB
    for w in range(NW):
        tla = [p.dma(bb(AWIN + c * WIN, AWIN + (c + 1) * WIN), ath[c][:, w * WIN:(w + 1) * WIN],
                     deps=prev_w) for c in range(NCH)]
        evs = []
        for oc in range(NCH):
            bo = nb()
            tmm = None
            for c in range(NCH):
                tmm = mm(bo, bb(SWO + c * 1024 + oc * 128, SWO + c * 1024 + (oc + 1) * 128),
                         bb(AWIN + c * WIN, AWIN + (c + 1) * WIN),
                         deps=(tla + wldo) if c == 0 else [tmm], start=(c == 0), stop=(c == NCH - 1))
            te = act(f(OBF + oc * WIN, OBF + (oc + 1) * WIN), pb[bo][:, :], AF.Identity, deps=[tmm])
            bank_done(bo, te)
            evs.append(te)
        ts_, adds = out_window(w, evs, x2h, x3h, prev_w)
        prev_w = [ts_] + adds

    P6B = phase()
    wb_toks = []
    tprev = None
    for ri in range(2):
        WTo = WTR if ri == 0 else WTI
        for r in range(4):
            for cc in range(4):
                slot = cc % 2
                uo = UVU if slot == 0 else UVU2
                vo = UVV if slot == 0 else UVV2
                dd = [tprev] if tprev else list(P6B)
                tU = p.dma(f(uo, uo + 512), uvU_h[ri, r, cc], deps=dd)
                tV = p.dma(f(vo, vo + 512), uvV_h[ri, r, cc], deps=dd)
                if cc == 0:
                    tprev = vtt(f(FACC, FACC + 512), f(uo, uo + 512), f(vo, vo + 512), ALU.mult,
                                deps=[tU, tV])
                else:
                    tm_ = vtt(f(FTMP, FTMP + 512), f(uo, uo + 512), f(vo, vo + 512), ALU.mult,
                              deps=[tU, tV, tprev])
                    tprev = vtt(f(FACC, FACC + 512), f(FACC, FACC + 512), f(FTMP, FTMP + 512),
                                ALU.add, deps=[tm_])
            tw = act(bb(WTo + r * 512, WTo + (r + 1) * 512), f(FACC, FACC + 512), AF.Identity,
                     deps=[tprev])
            tprev = tw
            wb_toks.append(tw)

    prev_w = P6B
    final_stores = []
    for w in range(NW):
        tl, apl = ln_window(x3h, w * WIN, prev_w + wb_toks, XHAT6)
        tts = tr_window(XHAT6, lambda c, i: bb(XT6 + c * WIN + i * 128, XT6 + c * WIN + (i + 1) * 128),
                        apl, deps=prev_w)
        trt = [t for row in tts for t in row]
        tneg = [vts(bb(XT6N + k * WIN, XT6N + (k + 1) * WIN), bb(XT6 + (4 + k) * WIN, XT6 + (5 + k) * WIN),
                    -1.0, None, ALU.mult, deps=[tts[i][4 + k] for i in range(4)]) for k in range(4)]
        evs = []
        for oc in range(NCH):
            bo = nb()
            tmm = None
            first = True
            for k in range(4):
                if oc < 4:
                    l1 = bb(WTR + k * 512 + oc * 128, WTR + k * 512 + (oc + 1) * 128)
                    l2 = bb(WTI + k * 512 + oc * 128, WTI + k * 512 + (oc + 1) * 128)
                    r1 = bb(XT6 + k * WIN, XT6 + (k + 1) * WIN)
                    r2 = bb(XT6N + k * WIN, XT6N + (k + 1) * WIN)
                else:
                    l1 = bb(WTI + k * 512 + (oc - 4) * 128, WTI + k * 512 + (oc - 3) * 128)
                    l2 = bb(WTR + k * 512 + (oc - 4) * 128, WTR + k * 512 + (oc - 3) * 128)
                    r1 = bb(XT6 + k * WIN, XT6 + (k + 1) * WIN)
                    r2 = bb(XT6 + (4 + k) * WIN, XT6 + (5 + k) * WIN)
                tmm = mm(bo, l1, r1, deps=(trt + wb_toks + tneg) if first else [tmm],
                         start=first, stop=False)
                first = False
                tmm = mm(bo, l2, r2, deps=[tmm], start=False, stop=(k == 3))
            te = act(f(OBF + oc * WIN, OBF + (oc + 1) * WIN), pb[bo][:, :], AF.Identity,
                     deps=[tmm], bias=sttbr[:, oc:oc + 1])
            bank_done(bo, te)
            evs.append(te)
        ts_, adds = out_window(w, evs, x3h, out_h, prev_w)
        prev_w = [ts_] + adds
        final_stores.append(ts_)

    vmemset(f(CONVT, CONVT + 1), 0.0, deps=final_stores)

    with nc.Block() as block:
        p.emit(block)
    return nc


def host_prep(inp):
    g = {k: np.ascontiguousarray(np.asarray(v, np.float32)) for k, v in inp.items()}
    d = {}
    d["gate_wt"] = np.ascontiguousarray(g["L0_attn_gate_w"].T)
    d["B_wt"] = np.ascontiguousarray(g["L0_attn_B_w"].T)
    d["C_wt"] = np.ascontiguousarray(g["L0_attn_C_w"].T)
    d["D_wt"] = np.ascontiguousarray(g["L0_attn_D_w"].T)
    d["B_b"] = g["L0_attn_B_b"].reshape(S, 1)
    d["C_b"] = g["L0_attn_C_b"].reshape(S, 1)
    d["D_b"] = np.ascontiguousarray(g["L0_attn_D_b"].reshape(NCH, 128).T)
    d["gate_b"] = np.ascontiguousarray(g["L0_attn_gate_b"].reshape(2 * NCH, 128).T)
    d["conv_taps"] = g["L0_attn_conv_w"].reshape(D, 3).reshape(NCH, 128, 3)
    d["conv_b"] = np.ascontiguousarray(g["L0_attn_conv_b"].reshape(NCH, 128).T)
    d["A_m"] = g["L0_attn_A"]
    d["A_mt"] = np.ascontiguousarray(g["L0_attn_A"].T)
    d["dtA"] = np.asarray(g["L0_attn_dt_A"], np.float32).reshape(1, 1)
    d["g_col"] = g["L0_attn_ln_st_g"].reshape(S, 1)
    d["beta_col"] = g["L0_attn_ln_st_b"].reshape(S, 1)
    d["g_rep64"] = np.ascontiguousarray(np.tile(g["L0_attn_ln_st_g"].reshape(1, S), (S, 1)))
    d["lngc"] = np.ascontiguousarray(g["L0_attn_ln_in_g"].reshape(NCH, 128).T)
    d["lnbc"] = np.ascontiguousarray(g["L0_attn_ln_in_b"].reshape(NCH, 128).T)
    d["Pc"] = (np.eye(S) - 1.0 / S).astype(np.float32)
    d["ident64"] = np.eye(S, dtype=np.float32)
    d["ident128"] = np.eye(128, dtype=np.float32)
    consts = np.zeros((128, 8), np.float32)
    consts[:, 0] = EPS
    consts[:, 1] = EPS * EPS
    consts[:, 2] = -SHIFT
    d["consts"] = consts
    d["onesdiv"] = np.full((S, S), 1.0 / S, np.float32)
    d["hp_rows"] = np.ones((1, 128), np.float32)
    cmask = np.zeros((4, 128, WIN), np.float32)
    for i in range(4):
        for kp in range(128):
            cmask[i, kp, i * 128 + kp:] = 1.0
    d["cmask"] = cmask.astype(BF)
    d["moe_w1"] = g["L0_ffn_w1"]
    d["moe_w2"] = g["L0_ffn_w2"]
    d["moe_w3"] = g["L0_ffn_w3"]
    d["ws1"] = g["L0_ffn_ws1"]
    d["ws2"] = g["L0_ffn_ws2"]
    d["ws3"] = g["L0_ffn_ws3"]
    d["wr"] = g["L0_ffn_wr"]
    d["Wq_t"] = np.ascontiguousarray(g["L1_attn_Wq"].T)
    d["Wk_t"] = np.ascontiguousarray(g["L1_attn_Wk"].T)
    d["Wv_t"] = np.ascontiguousarray(g["L1_attn_Wv"].T)
    d["Wo_t"] = np.ascontiguousarray(g["L1_attn_Wo"].T)
    uvU = np.zeros((2, 4, 4, 128, 512), np.float32)
    uvV = np.zeros((2, 4, 4, 128, 512), np.float32)
    for ri, (G1k, G2k) in enumerate((("L1_ffn_G1_R", "L1_ffn_G2_R"), ("L1_ffn_G1_I", "L1_ffn_G2_I"))):
        for cc in range(4):
            Ac = g[G1k][0, :, cc, :]
            Bc = g[G2k][cc, :, 0, :]
            U = np.broadcast_to(Ac.T[:, None, :, None], (32, 16, 32, 16)).reshape(512, 512)
            V = np.broadcast_to(Bc.T[None, :, None, :], (32, 16, 32, 16)).reshape(512, 512)
            for r in range(4):
                uvU[ri, r, cc] = U[r * 128:(r + 1) * 128]
                uvV[ri, r, cc] = V[r * 128:(r + 1) * 128]
    d["uvU"] = uvU
    d["uvV"] = uvV
    d["tt_bias"] = np.ascontiguousarray(np.concatenate([g["L1_ffn_bias_real"], g["L1_ffn_bias_imag"]]).reshape(NCH, 128).T)
    return d


def kernel(**inp):
    nc = bass.Bass("TRN2", target_bir_lowering=False)
    build(nc, dbg=False)
    base = host_prep(inp)
    x = np.asarray(inp["L0_attn_x"], np.float32)
    in_maps = []
    for b in range(8):
        m = dict(base)
        m["x0"] = np.ascontiguousarray(x[b])
        in_maps.append(m)
    res = run_bass_kernel_spmd(nc, in_maps, core_ids=list(range(8)))
    return np.stack([np.asarray(res.results[i]["out"], np.float32) for i in range(8)])


# version 06
# speedup vs baseline: 1.4056x; 1.2657x over previous
import numpy as np
import ml_dtypes
import concourse.bass as bass
import concourse.mybir as mybir
from concourse.bass_utils import run_bass_kernel_spmd

F32 = mybir.dt.float32
BF16 = mybir.dt.bfloat16
AF = mybir.ActivationFunctionType
ALU = mybir.AluOpType
AX = mybir.AxisListType
BF = ml_dtypes.bfloat16

T = 4096
D = 1024
NCH = 8
WIN = 512
NW = 8
S = 64
KCH = 64
LCH = T // KCH
WARM = 320
STEPS = LCH + WARM
EPS = 1e-5
SHIFT = 12.0

NF = 18600
HIST = 0
OBF = 0
UVU, UVV, UVU2, UVV2, FACC, FTMP = 4096, 4608, 5120, 5632, 6144, 6656
SBMP = 9216
STAT = 13824
XWIN = 13888
CONVT = 17984
RTR = 18496

NB = 48200
XHT = 0
XHAT = 8224
XN2 = 12320
XGW = 20512
SGV = 24608
GSO = 25120
WBC = 25632
GWV = 26656
SCMB = 43040
YB = 0
GSL = 4096
SDW = 8224
XHAT4 = 0
XT4 = 4096
MW1, MW2, MW3 = 8192, 16384, 24576
WS1, WS2, WS3, SWR = 32768, 34816, 36864, 38912
HST = 38976
H2SC = 44096
XHAT5, XT5 = 0, 4096
WQ, WK, WV = 8192, 16384, 24576
QWB, KWB, VWB, VTB = 32768, 33792, 34816, 35840
QTC, KTC = 0, 8192
SVC = 16384
PT = 24704
ATA, ATB = 26752, 34944
AWIN, SWO = 0, 4096
XHAT6, XT6, XT6N, WTR, WTI = 0, 4096, 8192, 10240, 12288


class Prog:
    def __init__(self, nc):
        self.nc = nc
        self.engines = ("tensor", "vector", "scalar", "sync", "gpsimd")
        self.ops = {e: [] for e in self.engines}
        self.semname = {"tensor": "s_t", "vector": "s_v", "scalar": "s_s",
                        "sync": "s_y", "gpsimd": "s_g"}
        self.sems = {}
        self.cnt = {}
        for s in self.semname.values():
            self.sems[s] = nc.semaphore(s).__enter__()
            self.cnt[s] = 0

    def op(self, engine, fn, deps=(), inc=1):
        sem = self.semname[engine]
        waits = {}
        for dd in deps:
            if dd is None:
                continue
            sn, val = dd
            waits[sn] = max(waits.get(sn, 0), val)
        self.cnt[sem] += inc
        tok = (sem, self.cnt[sem])
        self.ops[engine].append((fn, sorted(waits.items()), (sem, inc)))
        return tok

    def dma(self, out, in_, deps=(), queue="sync"):
        return self.op(queue, lambda e, o=out, i=in_: e.dma_start(out=o, in_=i),
                       deps=deps, inc=16)

    def dma_tr(self, out, in_, deps=(), queue="scalar"):
        return self.op(queue, lambda e, o=out, i=in_: e.dma_start(out=o, in_=i, transpose=True),
                       deps=deps, inc=16)

    def barrier(self):
        return [(self.semname[e], self.cnt[self.semname[e]]) for e in self.engines]

    def emit(self, block):
        def make(engine):
            def body(eng):
                for fn, waits, inc in self.ops[engine]:
                    for sn, val in waits:
                        eng.wait_ge(self.sems[sn], val)
                    inst = fn(eng)
                    sn, n = inc
                    inst.then_inc(self.sems[sn], n)
            return body
        block.tensor(make("tensor"))
        block.vector(make("vector"))
        block.scalar(make("scalar"))
        block.sync(make("sync"))
        block.gpsimd(make("gpsimd"))


def build(nc, dbg=False):
    dram = lambda name, shape, d=F32, kind="ExternalInput": nc.dram_tensor(name, shape, d, kind=kind).ap()
    x0 = dram("x0", [T, D])
    gate_wt = dram("gate_wt", [D, 2 * D])
    B_wt = dram("B_wt", [D, S]); C_wt = dram("C_wt", [D, S]); D_wt_h = dram("D_wt", [S, D])
    Bb_h = dram("B_b", [S, 1]); Cb_h = dram("C_b", [S, 1]); Db_h = dram("D_b", [128, NCH])
    gb_h = dram("gate_b", [128, 2 * NCH])
    taps_h = dram("conv_taps", [NCH, 128, 3])
    cvb_h = dram("conv_b", [128, NCH])
    A_h = dram("A_m", [S, S]); At_h = dram("A_mt", [S, S]); dtA_h = dram("dtA", [1, 1])
    g_h = dram("g_col", [S, 1]); beta_h = dram("beta_col", [S, 1]); grep_h = dram("g_rep64", [S, S])
    lngc_h = dram("lngc", [128, NCH]); lnbc_h = dram("lnbc", [128, NCH])
    Pc_h = dram("Pc", [S, S]); id_h = dram("ident64", [S, S])
    cst_h = dram("consts", [128, 8])
    od_h = dram("onesdiv", [S, S])
    hpr_h = dram("hp_rows", [1, 128])
    id128_h = dram("ident128", [128, 128])
    cm_h = dram("cmask", [4, 128, WIN], BF16)
    w1_h = dram("moe_w1", [8, D, 128]); w2_h = dram("moe_w2", [8, D, 128]); w3_h = dram("moe_w3", [8, 128, D])
    ws1_h = dram("ws1", [D, 256]); ws2_h = dram("ws2", [D, 256]); ws3_h = dram("ws3", [256, D]); wr_h = dram("wr", [D, 8])
    Wq_h = dram("Wq_t", [D, D]); Wk_h = dram("Wk_t", [D, D]); Wv_h = dram("Wv_t", [D, D]); Wo_h = dram("Wo_t", [D, D])
    uvU_h = dram("uvU", [2, 4, 4, 128, 512]); uvV_h = dram("uvV", [2, 4, 4, 128, 512])
    ttbr_h = dram("tt_bias", [128, NCH])
    out_h = dram("out", [T, D], kind="ExternalOutput")
    kd = "ExternalOutput" if dbg else "Internal"
    x1h = dram("x1h", [T, D], kind=kd)
    x2h = dram("x2h", [T, D], kind=kd)
    x3h = dram("x3h", [T, D], kind=kd)
    gatesh = dram("gatesh", [NCH, 128, T], d=BF16, kind=("ExternalOutput" if dbg else "Internal"))
    if dbg:
        dbpp_h = dram("dbg_bpp", [S, 4608], kind="ExternalOutput")
        dzh_h = dram("dbg_zh", [S, T], kind="ExternalOutput")
        dcm_h = dram("dbg_cm", [S, T], d=BF16, kind="ExternalOutput")
        dyb_h = dram("dbg_yb", [S, T], d=BF16, kind="ExternalOutput")
        dxht_h = dram("dbg_xht", [128, 8192], d=BF16, kind="ExternalOutput")
        dxn2_h = dram("dbg_xn2", [128, 4096], d=BF16, kind="ExternalOutput")
    qth = dram("qth", [NCH, 128, T], d=BF16, kind="Internal")
    kth = dram("kth", [NCH, 128, T], d=BF16, kind="Internal")
    vth = dram("vth", [NCH, 32, 128, 128], d=BF16, kind="Internal")
    ath = dram("ath", [NCH, 128, T], d=BF16, kind="Internal")

    sbuf = lambda name, shape, d=F32: nc.sbuf_tensor(name, shape, d).__enter__()

    cC = sbuf("cC", [128, 8])
    cOD = sbuf("cOD", [S, S])
    cID = sbuf("cID", [S, S]); cPc = sbuf("cPc", [S, S]); cID128 = sbuf("cID128", [128, 128])
    cA = sbuf("cA", [S, S]); cAt = sbuf("cAt", [S, S]); cdt = sbuf("cdt", [1, 1])
    cg = sbuf("cg", [S, 1]); cbeta = sbuf("cbeta", [S, 1]); cgrep = sbuf("cgrep", [S, S])
    cHPR = sbuf("cHPR", [1, 128])
    lngc = sbuf("lngc_s", [128, NCH]); lnbc = sbuf("lnbc_s", [128, NCH])
    staps = sbuf("staps", [128, NCH * 4])
    scvb = sbuf("scvb", [128, NCH])
    sBb = sbuf("sBb", [S, 1]); sCb = sbuf("sCb", [S, 1])
    sDb = sbuf("sDb", [128, NCH]); sgbias = sbuf("sgbias", [128, 2 * NCH])
    sttbr = sbuf("sttbr", [128, NCH])
    scanM = sbuf("scanM", [S, S])
    bacol = sbuf("bacol", [S, 1])
    eE = sbuf("eE", [S, S]); eF = sbuf("eF", [S, S]); eH = sbuf("eH", [S, S])
    eAd = sbuf("eAd", [S, S]); eAdt = sbuf("eAdt", [S, S])
    eT1 = sbuf("eT1", [S, S]); eT2 = sbuf("eT2", [S, S])
    sdtc = sbuf("sdtc", [S, 1])
    rowtmp = sbuf("rowtmp", [1, S])
    cmb = sbuf("cmb", [128, 4 * WIN], BF16)
    szh2t = sbuf("szh2t", [S, LCH])
    hcell = sbuf("hcell", [128, 64], BF16)
    craw = [sbuf("craw0", [S, KCH])]
    csqb = [sbuf("csqb0", [S, KCH])]
    ztp = [sbuf(f"ztp{h}", [S, KCH]) for h in range(2)]
    wfT = sbuf("wfT", [1, 8 * WIN])
    lrow = sbuf("lrow", [128, 2 * WIN])

    FP = sbuf("FP", [128, NF])
    BB = sbuf("BB", [128, NB], BF16)
    f = lambda a, b: FP[:, a:b]
    f6 = lambda a, b: FP[0:S, a:b]
    bb = lambda a, b: BB[:, a:b]
    b6 = lambda a, b: BB[0:S, a:b]

    p = Prog(nc)
    pb = [nc.psum_tensor(f"pb{i}", [128, 512], F32).__enter__() for i in range(8)]
    pfree = [()] * 8
    _rr = [0]

    def nb():
        i = _rr[0]; _rr[0] = (i + 1) % 8
        return i

    def bank_done(i, tok):
        pfree[i] = (tok,)

    def mm(bi, lhsT, rhs, deps=(), start=True, stop=True, tp=None, psl=None):
        dst = pb[bi][psl[0]:psl[1], psl[2]:psl[3]] if psl else pb[bi][:, :]
        dd = (list(pfree[bi]) if start else []) + list(deps)
        return p.op("tensor", lambda e, o=dst, l=lhsT, r=rhs, s0=start, s1=stop, t_=tp: e.matmul(
            o, l, r, start=s0, stop=s1, tile_position=t_, skip_group_check=True), deps=dd)

    def ptr(bi, cols, src, deps=(), rows=128):
        dst = pb[bi][0:rows, cols[0]:cols[1]]
        dd = list(pfree[bi]) + list(deps)
        return p.op("tensor", lambda e, o=dst, i=src: e.transpose(o, i, cID128[:, :]), deps=dd)

    def act(out, in_, func, deps=(), bias=None, scale=1.0):
        if bias is None:
            return p.op("scalar", lambda e, o=out, i=in_, sc=scale, fu=func: e.activation(
                out=o, in_=i, func=fu, scale=sc), deps=deps)
        return p.op("scalar", lambda e, o=out, i=in_, b=bias, sc=scale, fu=func: e.activation(
            out=o, in_=i, func=fu, bias=b, scale=sc), deps=deps)

    def vtt(out, a, b2, opn, deps=()):
        return p.op("vector", lambda e, o=out, x=a, y=b2, oo=opn: e.tensor_tensor(o, x, y, oo),
                    deps=deps)

    def vts(out, in0, s1, s2, op0, op1=None, deps=()):
        if s2 is None:
            return p.op("vector", lambda e, o=out, i=in0, a=s1, oo=op0: e.tensor_scalar(
                out=o, in0=i, scalar1=a, scalar2=None, op0=oo), deps=deps)
        return p.op("vector", lambda e, o=out, i=in0, a=s1, b2=s2, oo=op0, o1=op1: e.tensor_scalar(
            out=o, in0=i, scalar1=a, scalar2=b2, op0=oo, op1=o1), deps=deps)

    def vstt(out, in0, s1, in1, op0, op1, deps=()):
        return p.op("vector", lambda e, o=out, i=in0, a=s1, j=in1, oo=op0, o1=op1:
                    e.scalar_tensor_tensor(out=o, in0=i, scalar=a, in1=j, op0=oo, op1=o1),
                    deps=deps)

    def vcopy(out, in_, deps=()):
        return p.op("vector", lambda e, o=out, i=in_: e.tensor_copy(o, i), deps=deps)

    def vred(out, in_, opn, deps=()):
        return p.op("vector", lambda e, o=out, i=in_, oo=opn: e.tensor_reduce(
            out=o, in_=i, axis=AX.X, op=oo), deps=deps)

    def vrecip(out, in_, deps=()):
        return p.op("vector", lambda e, o=out, i=in_: e.reciprocal(out=o, in_=i), deps=deps)

    def vmemset(ap_, val, deps=()):
        return p.op("vector", lambda e, o=ap_, v=val: e.memset(o, v), deps=deps)

    def phase():
        bar = p.barrier()
        for i in range(8):
            pfree[i] = tuple(bar)
        return bar

    epsc = cC[:, 0:1]; eps2c = cC[:, 1:2]; mshiftc = cC[:, 2:3]

    ld = [p.dma(cC[:, :], cst_h), p.dma(cOD[:, :], od_h),
          p.dma(cID[:, :], id_h), p.dma(cPc[:, :], Pc_h), p.dma(cID128[:, :], id128_h),
          p.dma(cA[:, :], A_h), p.dma(cAt[:, :], At_h), p.dma(cdt[:, :], dtA_h),
          p.dma(cg[:, :], g_h), p.dma(cbeta[:, :], beta_h), p.dma(cgrep[:, :], grep_h),
          p.dma(cHPR[:, :], hpr_h),
          p.dma(lngc[:, :], lngc_h), p.dma(lnbc[:, :], lnbc_h),
          p.dma(staps.ap().rearrange("p (c k) -> p c k", k=4)[:, :, 0:3], taps_h.rearrange("c p k -> p c k")),
          p.dma(scvb[:, :], cvb_h),
          p.dma(sBb[:, :], Bb_h), p.dma(sCb[:, :], Cb_h),
          p.dma(sDb[:, :], Db_h),
          p.dma(sgbias[:, :], gb_h),
          p.dma(sttbr[:, :], ttbr_h),
          p.dma(cmb.ap().rearrange("p (j n) -> p j n", n=WIN), cm_h.rearrange("j p n -> p j n"))]

    t_t1 = []
    for c in range(NCH):
        t_t1.append(vts(staps[:, c * 4 + 3:c * 4 + 4], staps[:, c * 4 + 1:c * 4 + 2],
                        1.0, None, ALU.add, deps=ld))

    tok = mm(0, cOD[0:1, :], cdt[:, :], deps=ld, psl=(0, S, 0, 1))
    t_dt = vts(sdtc[:, :], pb[0][0:S, 0:1], float(S) * (2.0 ** -8), None, ALU.mult, deps=[tok])
    bank_done(0, t_dt)
    tA1 = vts(eAd[:, :], cA[:, :], sdtc[:, :], None, ALU.mult, deps=[t_dt] + ld)
    tA2 = vts(eAdt[:, :], cAt[:, :], sdtc[:, :], None, ALU.mult, deps=[t_dt] + ld)

    def horner(Xts, Hout, tdep):
        tH = vcopy(Hout[:, :], cID[:, :], deps=tdep + ld)
        for k in range(6, 0, -1):
            tm_ = mm(1, Xts[:, :], Hout[:, :], deps=[tH], psl=(0, S, 0, S))
            te = act(eT1[:, :], pb[1][0:S, 0:S], AF.Identity, deps=[tm_], scale=1.0 / k)
            bank_done(1, te)
            tH = vtt(Hout[:, :], eT1[:, :], cID[:, :], ALU.add, deps=[te])
        return tH

    tE = horner(eAdt, eE, [tA1, tA2])
    tF = horner(eAd, eF, [tA1, tA2])
    for _ in range(8):
        tm1 = mm(1, eF[:, :], eE[:, :], deps=[tE, tF], psl=(0, S, 0, S))
        te1 = act(eT1[:, :], pb[1][0:S, 0:S], AF.Identity, deps=[tm1])
        bank_done(1, te1)
        tm2 = mm(2, eE[:, :], eF[:, :], deps=[tE, tF], psl=(0, S, 0, S))
        te2 = act(eT2[:, :], pb[2][0:S, 0:S], AF.Identity, deps=[tm2])
        bank_done(2, te2)
        tE = vcopy(eE[:, :], eT1[:, :], deps=[te1, tm2])
        tF = vcopy(eF[:, :], eT2[:, :], deps=[te2, tE])
    tM = vtt(eH[:, :], eF[:, :], cgrep[:, :], ALU.mult, deps=[tE, tF])
    tsm = mm(1, eH[:, :], cPc[:, :], deps=[tM], psl=(0, S, 0, S))
    tsme = act(scanM[:, :], pb[1][0:S, 0:S], AF.Identity, deps=[tsm])
    bank_done(1, tsme)
    tba = mm(2, cbeta[:, :], eE[:, :], deps=[tE], psl=(0, 1, 0, S))
    tbae = vcopy(rowtmp[:, :], pb[2][0:1, 0:S], deps=[tba])
    bank_done(2, tbae)
    tbat = p.op("tensor", lambda e: e.transpose(pb[3][0:S, 0:1], rowtmp[:, :], cID128[0:1, 0:1]),
                deps=[tbae] + list(pfree[3]))
    tbac = vcopy(bacol[:, :], pb[3][0:S, 0:1], deps=[tbat])
    bank_done(3, tbac)

    tgw = [p.dma(bb(GWV + c * 2048, GWV + (c + 1) * 2048), gate_wt[c * 128:(c + 1) * 128, :],
                 queue="gpsimd", deps=ld) for c in range(NCH)]
    tbc = []
    for c in range(NCH):
        tbc.append(p.dma(bb(WBC + c * 2 * S, WBC + c * 2 * S + S), B_wt[c * 128:(c + 1) * 128, :],
                         queue="gpsimd", deps=ld))
        tbc.append(p.dma(bb(WBC + c * 2 * S + S, WBC + (c + 1) * 2 * S), C_wt[c * 128:(c + 1) * 128, :],
                         queue="gpsimd", deps=ld))
    tz = [vmemset(hcell[:, c * 4:c * 4 + 1], 0.0) for c in range(NCH)]
    tzp = [vmemset(f6(SBMP, SBMP + WARM), 0.0), vmemset(f6(SBMP + WARM + T, SBMP + 4608), 0.0)]

    def ln_window(xsrc, woff, deps, xhat_off, bf16_out=True):
        tl = p.dma(f(XWIN, XWIN + 4 * D).rearrange("p (i d) -> p i d", d=D),
                   xsrc[woff:woff + WIN, :].rearrange("(i p) d -> p i d", p=128), deps=deps)
        apl = []
        for i in range(4):
            xt = f(XWIN + i * D, XWIN + (i + 1) * D)
            st12 = f(STAT + i * 16, STAT + i * 16 + 12)
            mv = f(STAT + i * 16 + 12, STAT + i * 16 + 14)
            sc = f(STAT + i * 16 + 14, STAT + i * 16 + 15)
            t1 = p.op("vector", lambda e, o=st12[:, 0:6], i_=xt[:, 0:512]: e.bn_stats(out=o, in_=i_),
                      deps=[tl])
            t2 = p.op("vector", lambda e, o=st12[:, 6:12], i_=xt[:, 512:1024]: e.bn_stats(out=o, in_=i_),
                      deps=[t1])
            t3 = p.op("vector", lambda e, o=mv, i_=st12: e.bn_aggr(out=o, in_=i_), deps=[t2])
            if bf16_out == "composite":
                t4 = act(sc, mv[:, 1:2], AF.Sqrt, deps=[t3], bias=eps2c, scale=1.0 + EPS)
            else:
                t4 = act(sc, mv[:, 1:2], AF.Sqrt, deps=[t3], bias=epsc, scale=1.0)
            t5 = vrecip(sc, sc, deps=[t4])
            t6 = vts(bb(xhat_off + i * D, xhat_off + (i + 1) * D), xt, mv[:, 0:1], sc,
                     ALU.subtract, ALU.mult, deps=[t5])
            apl.append(t6)
        return tl, apl

    def tr_window(xhat_off, dst_fn, apl, deps=()):
        tts = [[None] * NCH for _ in range(4)]
        for i in range(4):
            for c in range(NCH):
                tts[i][c] = p.dma_tr(dst_fn(c, i), bb(xhat_off + i * D + c * 128, xhat_off + i * D + (c + 1) * 128),
                                     deps=[apl[i]] + list(deps))
        return tts


    conv_done = {}
    aff_done = {}
    gates_done = {-1: []}
    bm_evac = []
    ln_prev = []

    def do_conv(w, deps_extra):
        s = w % 2
        xn2o = XN2 + s * 4096
        toks = []
        prev = None
        for c in range(NCH):
            base = XHT + (c * 2 + s) * WIN
            dat = lambda a, b: bb(base + a, base + b)
            hL = hcell[:, c * 4 + s * 2:c * 4 + s * 2 + 1]
            hR = hcell[:, c * 4 + s * 2 + 32:c * 4 + s * 2 + 33]
            cv = lambda a, b: f(CONVT + a, CONVT + b)
            dd = aff_done[w] + deps_extra + ([prev] if prev else []) + gates_done.get(w - 2, [])
            t1 = vts(cv(0, WIN), dat(0, WIN), staps[:, c * 4 + 3:c * 4 + 4], None, ALU.mult, deps=dd)
            t2 = vstt(cv(1, WIN), dat(0, WIN - 1), staps[:, c * 4 + 0:c * 4 + 1],
                      cv(1, WIN), ALU.mult, ALU.add, deps=[t1])
            t2b = vstt(cv(0, 1), hL, staps[:, c * 4 + 0:c * 4 + 1],
                       cv(0, 1), ALU.mult, ALU.add, deps=[t2])
            t3 = vstt(cv(0, WIN - 1), dat(1, WIN), staps[:, c * 4 + 2:c * 4 + 3],
                      cv(0, WIN - 1), ALU.mult, ALU.add, deps=[t2b])
            t3b = vstt(cv(WIN - 1, WIN), hR, staps[:, c * 4 + 2:c * 4 + 3],
                       cv(WIN - 1, WIN), ALU.mult, ALU.add, deps=[t3])
            t4 = vts(bb(xn2o + c * WIN, xn2o + (c + 1) * WIN), cv(0, WIN),
                     scvb[:, c:c + 1], None, ALU.add, deps=[t3b])
            prev = t4
            toks.append(t4)
        conv_done[w] = toks
        return toks

    def do_gates(w):
        s = w % 2
        xn2o = XN2 + s * 4096
        ct = conv_done[w]
        igev = []
        prev_sg = None
        for oc in range(NCH):
            bi = nb()
            tmm = None
            for c in range(NCH):
                tmm = mm(bi, bb(GWV + c * 2048 + 1024 + oc * 128, GWV + c * 2048 + 1024 + (oc + 1) * 128),
                         bb(xn2o + c * WIN, xn2o + (c + 1) * WIN),
                         deps=(ct + tgw) if c == 0 else [tmm], start=(c == 0), stop=(c == NCH - 1))
            tsg = act(bb(SGV, SGV + WIN), pb[bi][:, :], AF.Sigmoid, deps=[tmm] + ([prev_sg] if prev_sg else []),
                      bias=sgbias[:, NCH + oc:NCH + oc + 1])
            bank_done(bi, tsg)
            txg = vtt(bb(XGW + oc * WIN, XGW + (oc + 1) * WIN), bb(xn2o + oc * WIN, xn2o + (oc + 1) * WIN),
                      bb(SGV, SGV + WIN), ALU.mult, deps=[tsg])
            prev_sg = txg
            igev.append(txg)
        biB = nb(); biC = nb()
        tmB = None; tmC = None
        for c in range(NCH):
            tmB = mm(biB, bb(WBC + c * 2 * S, WBC + c * 2 * S + S), bb(XGW + c * WIN, XGW + (c + 1) * WIN),
                     deps=(igev + tbc) if c == 0 else [tmB], start=(c == 0), stop=(c == NCH - 1),
                     psl=(0, S, 0, WIN))
        for c in range(NCH):
            tmC = mm(biC, bb(WBC + c * 2 * S + S, WBC + (c + 1) * 2 * S), bb(xn2o + c * WIN, xn2o + (c + 1) * WIN),
                     deps=(ct + tbc) if c == 0 else [tmC], start=(c == 0), stop=(c == NCH - 1),
                     psl=(0, S, 0, WIN))
        tBe = act(f6(HIST + w * WIN, HIST + (w + 1) * WIN), pb[biB][0:S, 0:WIN], AF.Identity,
                  deps=[tmB], bias=sBb[:, :])
        bank_done(biB, tBe)
        tCe = act(b6(SCMB + w * WIN, SCMB + (w + 1) * WIN), pb[biC][0:S, 0:WIN], AF.Identity,
                  deps=[tmC], bias=sCb[:, :])
        bank_done(biC, tCe)
        bm_evac.extend([tBe, tCe])
        gtoks = []
        prev_st = None
        for oc in range(NCH):
            bi = nb()
            tmm = None
            for c in range(NCH):
                tmm = mm(bi, bb(GWV + c * 2048 + oc * 128, GWV + c * 2048 + (oc + 1) * 128),
                         bb(xn2o + c * WIN, xn2o + (c + 1) * WIN),
                         deps=(ct + tgw) if c == 0 else [tmm], start=(c == 0), stop=(c == NCH - 1))
            tsg = act(bb(GSO, GSO + WIN), pb[bi][:, :], AF.Sigmoid,
                      deps=[tmm] + ([prev_st] if prev_st else []), bias=sgbias[:, oc:oc + 1])
            bank_done(bi, tsg)
            tst = p.dma(gatesh[oc, :, w * WIN:(w + 1) * WIN], bb(GSO, GSO + WIN), deps=[tsg])
            prev_st = tst
            gtoks.append(tst)
        gates_done[w] = gtoks + [tBe, tCe]
        return gtoks

    halo_toks = {}
    for w in range(NW):
        ldeps = tz + t_t1 + [tA1] + (ln_prev if ln_prev else [])
        tl, apl = ln_window(x0, w * WIN, ldeps, XHAT, bf16_out="composite")
        hl = []
        if w >= 2:
            s_src = w % 2; s_dst = (w - 1) % 2
            for c in range(NCH):
                base = XHT + (c * 2 + s_src) * WIN
                hl.append(vcopy(hcell[:, c * 4 + s_dst * 2:c * 4 + s_dst * 2 + 1],
                                bb(base + WIN - 1, base + WIN),
                                deps=aff_done[w - 2] + conv_done.get(w - 2, [])))
        s = w % 2
        tts = tr_window(XHAT, lambda c, i, s=s: bb(XHT + (c * 2 + s) * WIN + i * 128,
                                                   XHT + (c * 2 + s) * WIN + (i + 1) * 128),
                        apl, deps=hl + conv_done.get(w - 2, []) + gates_done.get(w - 2, []))
        ln_prev = [t for row in tts for t in row]
        aff = []
        for c in range(NCH):
            base = XHT + (c * 2 + s) * WIN
            aff.append(vts(bb(base, base + WIN), bb(base, base + WIN),
                           lngc[:, c:c + 1], lnbc[:, c:c + 1], ALU.mult, ALU.add,
                           deps=[tts[i][c] for i in range(4)]))
        aff_done[w] = aff
        hr = []
        if w >= 1:
            s_dst = (w - 1) % 2
            for c in range(NCH):
                base = XHT + (c * 2 + s) * WIN
                hr.append(vcopy(hcell[:, c * 4 + s_dst * 2 + 32:c * 4 + s_dst * 2 + 33],
                                bb(base, base + 1), deps=[aff[c]]))
            halo_toks[w - 1] = hl + hr
            do_conv(w - 1, hl + hr)
            do_gates(w - 1)
            if dbg and w == 1:
                td1 = p.dma(dxht_h, bb(XHT, XHT + 8192), deps=aff_done[0] + conv_done[0])
                td2 = p.dma(dxn2_h, bb(XN2, XN2 + 4096), deps=conv_done[0])
                gates_done[0] = gates_done[0] + [td1, td2]
    hl = []
    s_dst = 1
    for c in range(NCH):
        base = XHT + (c * 2 + 0) * WIN
        hl.append(vcopy(hcell[:, c * 4 + s_dst * 2:c * 4 + s_dst * 2 + 1],
                        bb(base + WIN - 1, base + WIN),
                        deps=aff_done[6] + conv_done.get(6, [])))
        hl.append(vmemset(hcell[:, c * 4 + s_dst * 2 + 32:c * 4 + s_dst * 2 + 33], 0.0,
                          deps=conv_done.get(5, [])))
    do_conv(7, hl)
    do_gates(7)

    P2B = phase()
    tb2 = vts(f6(HIST + 1, HIST + T), f6(HIST + 1, HIST + T), bacol[:, :], None, ALU.add,
              deps=P2B)
    pps = []
    for i in range(NW):
        bi = nb()
        tm_ = mm(bi, cPc[:, :], f6(HIST + i * WIN, HIST + (i + 1) * WIN), deps=[tb2],
                 psl=(0, S, 0, WIN))
        te = act(f6(SBMP + WARM + i * WIN, SBMP + WARM + (i + 1) * WIN), pb[bi][0:S, 0:WIN],
                 AF.Identity, deps=[tm_])
        bank_done(bi, te)
        pps.append(te)
    SBW = FP[0:S, SBMP:SBMP + 4608].rearrange("p (j l) -> p j l", l=KCH)
    scan_end = []
    pA = [0, 1]
    pB = [2, 3]
    tV = tq = tM2 = tR = tZ = tM1 = None
    for t in range(STEPS):
        bAP = SBW[:, t // KCH: t // KCH + KCH, t % KCH]
        if t < WARM:
            zt = ztp[t % 2][:, :]
        else:
            zt = f6(HIST + (t - WARM) * KCH, HIST + (t - WARM + 1) * KCH)
        if t == 0:
            tV = vcopy(craw[0][:, :], bAP, deps=pps + P2B + tzp)
        else:
            tV = vtt(craw[0][:, :], pb[pA[t % 2]][0:S, 0:KCH], bAP, ALU.add, deps=[tM1])
        tq = act(csqb[0][:, :], craw[0][:, :], AF.Square, deps=[tV])
        tM2 = mm(pB[t % 2], cOD[:, :], csqb[0][:, :], deps=[tq] + ([tR] if t >= 2 else []),
                 psl=(0, S, 0, KCH))
        tR = act(csqb[0][:, :], pb[pB[t % 2]][0:S, 0:KCH],
                 AF.Sqrt, deps=[tM2], bias=epsc[0:S, :])
        tRc = vrecip(csqb[0][:, :], csqb[0][:, :], deps=[tR])
        tZ = vtt(zt, craw[0][:, :], csqb[0][:, :], ALU.mult, deps=[tRc])
        if t < STEPS - 1:
            tM1 = mm(pA[(t + 1) % 2], scanM[:, :], zt, deps=[tZ] + ([tV] if t >= 1 else []),
                     psl=(0, S, 0, KCH))
        else:
            scan_end.append(tZ)

    if dbg:
        p.dma(dbpp_h, f6(SBMP, SBMP + 4608), deps=scan_end)
        p.dma(dzh_h, f6(HIST, HIST + T), deps=scan_end)
        p.dma(dcm_h, b6(SCMB, SCMB + T), deps=scan_end)

    P3B = phase()
    HISTV = FP[0:S, HIST:HIST + T].rearrange("p (t k) -> p t k", k=KCH)
    ty_prev = None
    ty_toks = []
    for j in range(KCH):
        zsrc = HISTV[:, :, j]
        t1 = vts(szh2t[:, 0:LCH], zsrc, cg[:, :], cbeta[:, :], ALU.mult, ALU.add,
                 deps=P3B + ([ty_prev] if ty_prev else []))
        t2 = vtt(b6(YB + j * LCH, YB + (j + 1) * LCH), szh2t[:, 0:LCH],
                 b6(SCMB + j * LCH, SCMB + (j + 1) * LCH), ALU.mult, deps=[t1])
        ty_prev = t2
        ty_toks.append(t2)
    tDld = p.dma(b6(SDW, SDW + D), D_wt_h, queue="gpsimd", deps=P3B)

    def out_window(w, obf_toks, xsrc, xdst, extra_ld_deps):
        tl = p.dma(f(XWIN, XWIN + 4 * D).rearrange("p (i d) -> p i d", d=D),
                   xsrc[w * WIN:(w + 1) * WIN, :].rearrange("(i p) d -> p i d", p=128),
                   deps=extra_ld_deps)
        adds = []
        for i in range(4):
            for oc in range(NCH):
                bi = nb()
                tt = ptr(bi, (0, 128), f(OBF + oc * WIN + i * 128, OBF + oc * WIN + (i + 1) * 128),
                         deps=obf_toks)
                ta = vtt(f(XWIN + i * D + oc * 128, XWIN + i * D + (oc + 1) * 128),
                         f(XWIN + i * D + oc * 128, XWIN + i * D + (oc + 1) * 128),
                         pb[bi][0:128, 0:128], ALU.add, deps=[tt, tl])
                bank_done(bi, ta)
                adds.append(ta)
        ts_ = p.dma(xdst[w * WIN:(w + 1) * WIN, :].rearrange("(i p) d -> p i d", p=128),
                    f(XWIN, XWIN + 4 * D).rearrange("p (i d) -> p i d", d=D), deps=adds)
        return ts_, adds

    prev_st = []
    tgl_prev = None
    for w in range(NW):
        tgl = p.dma(bb(GSL, GSL + 4096).rearrange("p (c n) -> p c n", n=WIN),
                    gatesh[:, :, w * WIN:(w + 1) * WIN].rearrange("c p n -> p c n"),
                    deps=P3B + ([tgl_prev] if tgl_prev else []))
        evs = []
        for oc in range(NCH):
            bi = nb()
            tm_ = mm(bi, b6(SDW + oc * 128, SDW + (oc + 1) * 128), b6(YB + w * WIN, YB + (w + 1) * WIN),
                     deps=ty_toks + [tDld], psl=(0, 128, 0, WIN))
            te = act(f(OBF + oc * WIN, OBF + (oc + 1) * WIN), pb[bi][:, 0:WIN], AF.Identity,
                     deps=[tm_] + prev_st, bias=sDb[:, oc:oc + 1])
            tg = vtt(f(OBF + oc * WIN, OBF + (oc + 1) * WIN), f(OBF + oc * WIN, OBF + (oc + 1) * WIN),
                     bb(GSL + oc * WIN, GSL + (oc + 1) * WIN), ALU.mult, deps=[te, tgl])
            bank_done(bi, tg)
            evs.append(tg)
        ts_, adds = out_window(w, evs, x0, x1h, prev_st)
        tgl_prev = evs[-1]
        prev_st = [ts_] + adds

    if dbg:
        p.dma(dyb_h, b6(YB, YB + T), deps=ty_toks + prev_st)

    P4B = phase()
    wld = []
    for e_i in range(8):
        for c in range(NCH):
            wld.append(p.dma(bb(MW1 + e_i * 1024 + c * 128, MW1 + e_i * 1024 + (c + 1) * 128),
                             w1_h[e_i, c * 128:(c + 1) * 128, :], queue="gpsimd", deps=P4B))
            wld.append(p.dma(bb(MW2 + e_i * 1024 + c * 128, MW2 + e_i * 1024 + (c + 1) * 128),
                             w2_h[e_i, c * 128:(c + 1) * 128, :], queue="gpsimd", deps=P4B))
        wld.append(p.dma(bb(MW3 + e_i * 1024, MW3 + (e_i + 1) * 1024), w3_h[e_i], queue="gpsimd", deps=P4B))
    for c in range(NCH):
        wld.append(p.dma(bb(WS1 + c * 256, WS1 + (c + 1) * 256), ws1_h[c * 128:(c + 1) * 128, :],
                         queue="gpsimd", deps=P4B))
        wld.append(p.dma(bb(WS2 + c * 256, WS2 + (c + 1) * 256), ws2_h[c * 128:(c + 1) * 128, :],
                         queue="gpsimd", deps=P4B))
        wld.append(p.dma(bb(SWR + c * 8, SWR + (c + 1) * 8), wr_h[c * 128:(c + 1) * 128, :],
                         queue="gpsimd", deps=P4B))
    for k in range(2):
        wld.append(p.dma(bb(WS3 + k * 1024, WS3 + (k + 1) * 1024), ws3_h[k * 128:(k + 1) * 128, :],
                         queue="gpsimd", deps=P4B))

    prev_w = P4B
    for w in range(NW):
        tl, apl = ln_window(x1h, w * WIN, prev_w, XHAT4)
        tts = tr_window(XHAT4, lambda c, i: bb(XT4 + c * WIN + i * 128, XT4 + c * WIN + (i + 1) * 128),
                        apl, deps=prev_w)
        trt = [t for row in tts for t in row]
        wf_toks = []
        for i in range(4):
            bi = nb()
            tmm = None
            for c in range(NCH):
                tmm = mm(bi, bb(XT4 + c * WIN + i * 128, XT4 + c * WIN + (i + 1) * 128),
                         bb(SWR + c * 8, SWR + (c + 1) * 8),
                         deps=(trt + wld) if c == 0 else [tmm], start=(c == 0), stop=(c == NCH - 1),
                         psl=(0, 128, 0, 8))
            pr = f(RTR + i * 24, RTR + i * 24 + 8)
            rx = f(RTR + i * 24 + 8, RTR + i * 24 + 16)
            mx = f(RTR + i * 24 + 16, RTR + i * 24 + 17)
            ms = f(RTR + i * 24 + 17, RTR + i * 24 + 18)
            m1 = f(RTR + i * 24 + 18, RTR + i * 24 + 19)
            m2 = f(RTR + i * 24 + 19, RTR + i * 24 + 20)
            t1 = vred(mx, pb[bi][:, 0:8], ALU.max, deps=[tmm])
            t2 = vts(mx, mx, -1.0, None, ALU.mult, deps=[t1])
            t3 = act(pr, pb[bi][:, 0:8], AF.Exp, deps=[t2], bias=mx)
            bank_done(bi, t3)
            t4 = vred(ms, pr, ALU.add, deps=[t3])
            t5 = vrecip(ms, ms, deps=[t4])
            t6 = vts(pr, pr, ms, None, ALU.mult, deps=[t5])
            t7 = vred(m1, pr, ALU.min, deps=[t6])
            t8 = vts(rx, pr, m1, None, ALU.is_le, deps=[t7])
            t9 = vstt(rx, rx, 2.0, pr, ALU.mult, ALU.add, deps=[t8])
            t11 = vred(m2, rx, ALU.min, deps=[t9])
            t12 = vts(rx, pr, m2, None, ALU.is_gt, deps=[t11])
            t13 = vtt(rx, rx, pr, ALU.mult, deps=[t12])
            t14 = vred(ms, rx, ALU.add, deps=[t13])
            t15 = vrecip(ms, ms, deps=[t14])
            t16 = vts(rx, rx, ms, None, ALU.mult, deps=[t15])
            for e_i in range(8):
                bj = nb()
                t17 = ptr(bj, (0, 128), rx[:, e_i:e_i + 1], deps=[t16], rows=1)
                t18 = vcopy(wfT[0:1, e_i * WIN + i * 128:e_i * WIN + (i + 1) * 128],
                            pb[bj][0:1, 0:128], deps=[t17])
                bank_done(bj, t18)
                wf_toks.append(t18)
        hw_toks = {}
        prev_h2 = None
        for e_i in range(10):
            bi1 = nb(); bi2 = nb()
            tm1 = tm2 = None
            for c in range(NCH):
                if e_i < 8:
                    l1 = bb(MW1 + e_i * 1024 + c * 128, MW1 + e_i * 1024 + (c + 1) * 128)
                    l2 = bb(MW2 + e_i * 1024 + c * 128, MW2 + e_i * 1024 + (c + 1) * 128)
                else:
                    k = e_i - 8
                    l1 = bb(WS1 + c * 256 + k * 128, WS1 + c * 256 + (k + 1) * 128)
                    l2 = bb(WS2 + c * 256 + k * 128, WS2 + c * 256 + (k + 1) * 128)
                rhs = bb(XT4 + c * WIN, XT4 + (c + 1) * WIN)
                tm1 = mm(bi1, l1, rhs, deps=(trt + wld) if c == 0 else [tm1],
                         start=(c == 0), stop=(c == NCH - 1))
                tm2 = mm(bi2, l2, rhs, deps=(trt + wld) if c == 0 else [tm2],
                         start=(c == 0), stop=(c == NCH - 1))
            hslot = bb(HST + e_i * WIN, HST + (e_i + 1) * WIN)
            ts1 = act(hslot, pb[bi1][:, :], AF.Silu, deps=[tm1])
            bank_done(bi1, ts1)
            ts2 = act(bb(H2SC, H2SC + WIN), pb[bi2][:, :], AF.Identity,
                      deps=[tm2] + ([prev_h2] if prev_h2 else []))
            bank_done(bi2, ts2)
            th = vtt(hslot, hslot, bb(H2SC, H2SC + WIN), ALU.mult, deps=[ts1, ts2])
            prev_h2 = th
            if e_i < 8:
                bj = nb()
                tbr = mm(bj, cHPR[0:1, :], wfT[0:1, e_i * WIN:(e_i + 1) * WIN], deps=wf_toks,
                         psl=(0, 128, 0, WIN))
                tw = vtt(hslot, hslot, pb[bj][:, :], ALU.mult, deps=[th, tbr])
                bank_done(bj, tw)
                hw_toks[e_i] = tw
            else:
                hw_toks[e_i] = th
        evs = []
        for oc in range(NCH):
            bo = nb()
            tmm = None
            first = True
            for k in range(2):
                tmm = mm(bo, bb(WS3 + k * 1024 + oc * 128, WS3 + k * 1024 + (oc + 1) * 128),
                         bb(HST + (8 + k) * WIN, HST + (9 + k) * WIN),
                         deps=([hw_toks[8], hw_toks[9]] + wld) if first else [tmm],
                         start=first, stop=False)
                first = False
            for e_i in range(8):
                tmm = mm(bo, bb(MW3 + e_i * 1024 + oc * 128, MW3 + e_i * 1024 + (oc + 1) * 128),
                         bb(HST + e_i * WIN, HST + (e_i + 1) * WIN),
                         deps=[tmm, hw_toks[e_i]], start=False, stop=(e_i == 7))
            te = act(f(OBF + oc * WIN, OBF + (oc + 1) * WIN), pb[bo][:, :], AF.Identity, deps=[tmm])
            bank_done(bo, te)
            evs.append(te)
        ts_, adds = out_window(w, evs, x1h, x2h, [])
        prev_w = [ts_] + adds

    P5aB = phase()
    wld5 = []
    for c in range(NCH):
        wld5.append(p.dma(bb(WQ + c * 1024, WQ + (c + 1) * 1024), Wq_h[c * 128:(c + 1) * 128, :],
                          queue="gpsimd", deps=P5aB))
        wld5.append(p.dma(bb(WK + c * 1024, WK + (c + 1) * 1024), Wk_h[c * 128:(c + 1) * 128, :],
                          queue="gpsimd", deps=P5aB))
        wld5.append(p.dma(bb(WV + c * 1024, WV + (c + 1) * 1024), Wv_h[c * 128:(c + 1) * 128, :],
                          queue="gpsimd", deps=P5aB))
    prev_w = P5aB
    last_use = {}
    for w in range(NW):
        tl, apl = ln_window(x2h, w * WIN, prev_w, XHAT5)
        tts = tr_window(XHAT5, lambda c, i: bb(XT5 + c * WIN + i * 128, XT5 + c * WIN + (i + 1) * 128),
                        apl, deps=prev_w)
        trt = [t for row in tts for t in row]
        wtoks = []
        for oc in range(NCH):
            for (Woff, BUF, hbm) in ((WQ, QWB, qth), (WK, KWB, kth)):
                bi = nb()
                tmm = None
                for c in range(NCH):
                    tmm = mm(bi, bb(Woff + c * 1024 + oc * 128, Woff + c * 1024 + (oc + 1) * 128),
                             bb(XT5 + c * WIN, XT5 + (c + 1) * WIN),
                             deps=(trt + wld5) if c == 0 else [tmm], start=(c == 0), stop=(c == NCH - 1))
                slot = BUF + (oc % 2) * WIN
                dd = [last_use[slot]] if slot in last_use else []
                te = act(bb(slot, slot + WIN), pb[bi][:, :], AF.Identity, deps=[tmm] + dd)
                bank_done(bi, te)
                tst = p.dma(hbm[oc, :, w * WIN:(w + 1) * WIN], bb(slot, slot + WIN), deps=[te])
                last_use[slot] = tst
                wtoks.append(tst)
            bi = nb()
            tmm = None
            for c in range(NCH):
                tmm = mm(bi, bb(WV + c * 1024 + oc * 128, WV + c * 1024 + (oc + 1) * 128),
                         bb(XT5 + c * WIN, XT5 + (c + 1) * WIN),
                         deps=(trt + wld5) if c == 0 else [tmm], start=(c == 0), stop=(c == NCH - 1))
            slot = VWB + (oc % 2) * WIN
            dd = [last_use[slot]] if slot in last_use else []
            te = act(bb(slot, slot + WIN), pb[bi][:, :], AF.Identity, deps=[tmm] + dd)
            bank_done(bi, te)
            vslot = VTB + (oc % 2) * WIN
            dv = [last_use[vslot]] if vslot in last_use else []
            ttr = [p.dma_tr(bb(vslot + i * 128, vslot + (i + 1) * 128),
                            bb(slot + i * 128, slot + (i + 1) * 128), deps=[te] + dv) for i in range(4)]
            last_use[slot] = ttr[-1]
            tst = p.dma(vth[oc, w * 4:(w + 1) * 4].rearrange("i p d -> p i d"),
                        bb(vslot, vslot + WIN).rearrange("p (i d) -> p i d", d=128), deps=ttr)
            last_use[vslot] = tst
            wtoks.append(tst)
        prev_w = wtoks

    P5bB = phase()
    chunk_done = {-1: P5bB, -2: P5bB}
    load_toks = {}

    def p5_loads(c):
        s = c % 2
        dd = list(chunk_done[c - 2])
        tq_ = p.dma(bb(QTC + s * 4096, QTC + (s + 1) * 4096), qth[c], deps=dd)
        tk_ = p.dma(bb(KTC + s * 4096, KTC + (s + 1) * 4096), kth[c], deps=dd)
        svo = SVC + s * 4160
        tms = vmemset(bb(svo, svo + 4160), 1.0, deps=dd)
        tvs = []
        for t in range(32):
            tvs.append(p.dma(bb(svo + t * 130, svo + t * 130 + 64), vth[c, t][:, 0:64], deps=[tms]))
            tvs.append(p.dma(bb(svo + t * 130 + 65, svo + t * 130 + 129), vth[c, t][:, 64:128], deps=[tms]))
        load_toks[c] = [tq_, tk_] + tvs

    p5_loads(0)
    lrow_free = {0: None, 1: None}
    SCB = [0, 1, 5, 6]
    sc_free = {0: None, 1: None, 2: None, 3: None}
    pt_free = {0: None, 1: None, 2: None, 3: None}
    for c in range(NCH):
        if c + 1 < NCH:
            p5_loads(c + 1)
        s = c % 2
        svo = SVC + s * 4160
        ctoks = []
        po_free = {0: (chunk_done[c - 1] if c > 0 else P5bB), 1: (chunk_done[c - 1] if c > 0 else P5bB)}
        for qw in range(NW):
            nkt = 4 * (qw + 1)
            tpv_l = {}
            pend = None
            for kt in range(nkt + 1):
                cur = []
                if kt < nkt:
                    for ha in range(2):
                        hoff = ha * 64
                        tp = (64, 0) if ha else None
                        sb_ = (kt % 2) * 2 + ha
                        tsc = mm(SCB[sb_], BB[hoff:hoff + 64, KTC + s * 4096 + kt * 128: KTC + s * 4096 + (kt + 1) * 128],
                                 BB[hoff:hoff + 64, QTC + s * 4096 + qw * WIN: QTC + s * 4096 + (qw + 1) * WIN],
                                 deps=load_toks[c] + ([sc_free[sb_]] if sc_free[sb_] else []), tp=tp)
                        dd = [tsc] + ([pt_free[sb_]] if pt_free[sb_] else [])
                        te = act(bb(PT + sb_ * WIN, PT + (sb_ + 1) * WIN), pb[SCB[sb_]][:, :], AF.Exp,
                                 deps=dd, scale=0.125, bias=mshiftc)
                        sc_free[sb_] = te
                        if kt >= 4 * qw:
                            i = kt - 4 * qw
                            te = vtt(bb(PT + sb_ * WIN, PT + (sb_ + 1) * WIN),
                                     bb(PT + sb_ * WIN, PT + (sb_ + 1) * WIN),
                                     cmb[:, i * WIN:(i + 1) * WIN], ALU.mult, deps=[te])
                        cur.append((ha, sb_, te))
                if pend is not None:
                    pkt = kt - 1
                    for ha, sb_, te in pend:
                        tpv_ = mm(2 + ha, bb(svo + pkt * 130 + ha * 65, svo + pkt * 130 + (ha + 1) * 65),
                                  bb(PT + sb_ * WIN, PT + (sb_ + 1) * WIN),
                                  deps=[te] + (list(po_free[ha]) if pkt == 0 else []),
                                  start=(pkt == 0), stop=(pkt == nkt - 1), psl=(0, 65, 0, WIN))
                        pt_free[sb_] = tpv_
                        tpv_l[ha] = tpv_
                pend = cur
            tno_l = {}
            for ha in range(2):
                po = 2 + ha
                lo = ha * WIN
                tlc = vcopy(lrow[64:65, lo:lo + WIN], pb[po][64:65, 0:WIN],
                            deps=[tpv_l[ha]] + ([lrow_free[ha]] if lrow_free[ha] else []))
                tsh = p.dma(lrow[0:1, lo:lo + WIN], lrow[64:65, lo:lo + WIN], deps=[tlc])
                trc = vrecip(lrow[0:1, lo:lo + WIN], lrow[0:1, lo:lo + WIN], deps=[tsh])
                bb_ = 4 + ha * 3
                tbc_ = mm(bb_, cHPR[0:1, 0:64], lrow[0:1, lo:lo + WIN], deps=[trc] + list(pfree[bb_]),
                          psl=(0, 64, 0, WIN))
                tev = act(lrow[0:64, lo:lo + WIN], pb[po][0:64, 0:WIN], AF.Identity, deps=[tbc_])
                AT = ATA if ha == 0 else ATB
                tno = vtt(BB[0:64, AT + s * 4096 + qw * WIN: AT + s * 4096 + (qw + 1) * WIN],
                          lrow[0:64, lo:lo + WIN], pb[bb_][0:64, 0:WIN], ALU.mult, deps=[tev])
                pfree[bb_] = (tno,)
                po_free[ha] = [tev]
                lrow_free[ha] = tno
                ctoks.append(tno)
        tsa = p.dma(ath[c, 0:64, :], BB[0:64, ATA + s * 4096: ATA + (s + 1) * 4096], deps=ctoks)
        tsb = p.dma(ath[c, 64:128, :], BB[0:64, ATB + s * 4096: ATB + (s + 1) * 4096], deps=ctoks)
        chunk_done[c] = [tsa, tsb] + ctoks

    P5wB = phase()
    wldo = [p.dma(bb(SWO + c * 1024, SWO + (c + 1) * 1024), Wo_h[c * 128:(c + 1) * 128, :],
                  queue="gpsimd", deps=P5wB) for c in range(NCH)]
    prev_w = P5wB
    for w in range(NW):
        tla = [p.dma(bb(AWIN + c * WIN, AWIN + (c + 1) * WIN), ath[c][:, w * WIN:(w + 1) * WIN],
                     deps=prev_w) for c in range(NCH)]
        evs = []
        for oc in range(NCH):
            bo = nb()
            tmm = None
            for c in range(NCH):
                tmm = mm(bo, bb(SWO + c * 1024 + oc * 128, SWO + c * 1024 + (oc + 1) * 128),
                         bb(AWIN + c * WIN, AWIN + (c + 1) * WIN),
                         deps=(tla + wldo) if c == 0 else [tmm], start=(c == 0), stop=(c == NCH - 1))
            te = act(f(OBF + oc * WIN, OBF + (oc + 1) * WIN), pb[bo][:, :], AF.Identity, deps=[tmm])
            bank_done(bo, te)
            evs.append(te)
        ts_, adds = out_window(w, evs, x2h, x3h, prev_w)
        prev_w = [ts_] + adds

    P6B = phase()
    wb_toks = []
    tprev = None
    for ri in range(2):
        WTo = WTR if ri == 0 else WTI
        for r in range(4):
            for cc in range(4):
                slot = cc % 2
                uo = UVU if slot == 0 else UVU2
                vo = UVV if slot == 0 else UVV2
                dd = [tprev] if tprev else list(P6B)
                tU = p.dma(f(uo, uo + 512), uvU_h[ri, r, cc], deps=dd)
                tV = p.dma(f(vo, vo + 512), uvV_h[ri, r, cc], deps=dd)
                if cc == 0:
                    tprev = vtt(f(FACC, FACC + 512), f(uo, uo + 512), f(vo, vo + 512), ALU.mult,
                                deps=[tU, tV])
                else:
                    tm_ = vtt(f(FTMP, FTMP + 512), f(uo, uo + 512), f(vo, vo + 512), ALU.mult,
                              deps=[tU, tV, tprev])
                    tprev = vtt(f(FACC, FACC + 512), f(FACC, FACC + 512), f(FTMP, FTMP + 512),
                                ALU.add, deps=[tm_])
            tw = act(bb(WTo + r * 512, WTo + (r + 1) * 512), f(FACC, FACC + 512), AF.Identity,
                     deps=[tprev])
            tprev = tw
            wb_toks.append(tw)

    prev_w = P6B
    final_stores = []
    for w in range(NW):
        tl, apl = ln_window(x3h, w * WIN, prev_w + wb_toks, XHAT6)
        tts = tr_window(XHAT6, lambda c, i: bb(XT6 + c * WIN + i * 128, XT6 + c * WIN + (i + 1) * 128),
                        apl, deps=prev_w)
        trt = [t for row in tts for t in row]
        tneg = [vts(bb(XT6N + k * WIN, XT6N + (k + 1) * WIN), bb(XT6 + (4 + k) * WIN, XT6 + (5 + k) * WIN),
                    -1.0, None, ALU.mult, deps=[tts[i][4 + k] for i in range(4)]) for k in range(4)]
        evs = []
        for oc in range(NCH):
            bo = nb()
            tmm = None
            first = True
            for k in range(4):
                if oc < 4:
                    l1 = bb(WTR + k * 512 + oc * 128, WTR + k * 512 + (oc + 1) * 128)
                    l2 = bb(WTI + k * 512 + oc * 128, WTI + k * 512 + (oc + 1) * 128)
                    r1 = bb(XT6 + k * WIN, XT6 + (k + 1) * WIN)
                    r2 = bb(XT6N + k * WIN, XT6N + (k + 1) * WIN)
                else:
                    l1 = bb(WTI + k * 512 + (oc - 4) * 128, WTI + k * 512 + (oc - 3) * 128)
                    l2 = bb(WTR + k * 512 + (oc - 4) * 128, WTR + k * 512 + (oc - 3) * 128)
                    r1 = bb(XT6 + k * WIN, XT6 + (k + 1) * WIN)
                    r2 = bb(XT6 + (4 + k) * WIN, XT6 + (5 + k) * WIN)
                tmm = mm(bo, l1, r1, deps=(trt + wb_toks + tneg) if first else [tmm],
                         start=first, stop=False)
                first = False
                tmm = mm(bo, l2, r2, deps=[tmm], start=False, stop=(k == 3))
            te = act(f(OBF + oc * WIN, OBF + (oc + 1) * WIN), pb[bo][:, :], AF.Identity,
                     deps=[tmm], bias=sttbr[:, oc:oc + 1])
            bank_done(bo, te)
            evs.append(te)
        ts_, adds = out_window(w, evs, x3h, out_h, prev_w)
        prev_w = [ts_] + adds
        final_stores.append(ts_)

    vmemset(f(CONVT, CONVT + 1), 0.0, deps=final_stores)

    with nc.Block() as block:
        p.emit(block)
    return nc


def host_prep(inp):
    g = {k: np.ascontiguousarray(np.asarray(v, np.float32)) for k, v in inp.items()}
    d = {}
    d["gate_wt"] = np.ascontiguousarray(g["L0_attn_gate_w"].T)
    d["B_wt"] = np.ascontiguousarray(g["L0_attn_B_w"].T)
    d["C_wt"] = np.ascontiguousarray(g["L0_attn_C_w"].T)
    d["D_wt"] = np.ascontiguousarray(g["L0_attn_D_w"].T)
    d["B_b"] = g["L0_attn_B_b"].reshape(S, 1)
    d["C_b"] = g["L0_attn_C_b"].reshape(S, 1)
    d["D_b"] = np.ascontiguousarray(g["L0_attn_D_b"].reshape(NCH, 128).T)
    d["gate_b"] = np.ascontiguousarray(g["L0_attn_gate_b"].reshape(2 * NCH, 128).T)
    d["conv_taps"] = g["L0_attn_conv_w"].reshape(D, 3).reshape(NCH, 128, 3)
    d["conv_b"] = np.ascontiguousarray(g["L0_attn_conv_b"].reshape(NCH, 128).T)
    d["A_m"] = g["L0_attn_A"]
    d["A_mt"] = np.ascontiguousarray(g["L0_attn_A"].T)
    d["dtA"] = np.asarray(g["L0_attn_dt_A"], np.float32).reshape(1, 1)
    d["g_col"] = g["L0_attn_ln_st_g"].reshape(S, 1)
    d["beta_col"] = g["L0_attn_ln_st_b"].reshape(S, 1)
    d["g_rep64"] = np.ascontiguousarray(np.tile(g["L0_attn_ln_st_g"].reshape(1, S), (S, 1)))
    d["lngc"] = np.ascontiguousarray(g["L0_attn_ln_in_g"].reshape(NCH, 128).T)
    d["lnbc"] = np.ascontiguousarray(g["L0_attn_ln_in_b"].reshape(NCH, 128).T)
    d["Pc"] = (np.eye(S) - 1.0 / S).astype(np.float32)
    d["ident64"] = np.eye(S, dtype=np.float32)
    d["ident128"] = np.eye(128, dtype=np.float32)
    consts = np.zeros((128, 8), np.float32)
    consts[:, 0] = EPS
    consts[:, 1] = EPS * EPS
    consts[:, 2] = -SHIFT
    d["consts"] = consts
    d["onesdiv"] = np.full((S, S), 1.0 / S, np.float32)
    d["hp_rows"] = np.ones((1, 128), np.float32)
    cmask = np.zeros((4, 128, WIN), np.float32)
    for i in range(4):
        for kp in range(128):
            cmask[i, kp, i * 128 + kp:] = 1.0
    d["cmask"] = cmask.astype(BF)
    d["moe_w1"] = g["L0_ffn_w1"]
    d["moe_w2"] = g["L0_ffn_w2"]
    d["moe_w3"] = g["L0_ffn_w3"]
    d["ws1"] = g["L0_ffn_ws1"]
    d["ws2"] = g["L0_ffn_ws2"]
    d["ws3"] = g["L0_ffn_ws3"]
    d["wr"] = g["L0_ffn_wr"]
    d["Wq_t"] = np.ascontiguousarray(g["L1_attn_Wq"].T)
    d["Wk_t"] = np.ascontiguousarray(g["L1_attn_Wk"].T)
    d["Wv_t"] = np.ascontiguousarray(g["L1_attn_Wv"].T)
    d["Wo_t"] = np.ascontiguousarray(g["L1_attn_Wo"].T)
    uvU = np.zeros((2, 4, 4, 128, 512), np.float32)
    uvV = np.zeros((2, 4, 4, 128, 512), np.float32)
    for ri, (G1k, G2k) in enumerate((("L1_ffn_G1_R", "L1_ffn_G2_R"), ("L1_ffn_G1_I", "L1_ffn_G2_I"))):
        for cc in range(4):
            Ac = g[G1k][0, :, cc, :]
            Bc = g[G2k][cc, :, 0, :]
            U = np.broadcast_to(Ac.T[:, None, :, None], (32, 16, 32, 16)).reshape(512, 512)
            V = np.broadcast_to(Bc.T[None, :, None, :], (32, 16, 32, 16)).reshape(512, 512)
            for r in range(4):
                uvU[ri, r, cc] = U[r * 128:(r + 1) * 128]
                uvV[ri, r, cc] = V[r * 128:(r + 1) * 128]
    d["uvU"] = uvU
    d["uvV"] = uvV
    d["tt_bias"] = np.ascontiguousarray(np.concatenate([g["L1_ffn_bias_real"], g["L1_ffn_bias_imag"]]).reshape(NCH, 128).T)
    return d


def kernel(**inp):
    nc = bass.Bass("TRN2", target_bir_lowering=False)
    build(nc, dbg=False)
    base = host_prep(inp)
    x = np.asarray(inp["L0_attn_x"], np.float32)
    in_maps = []
    for b in range(8):
        m = dict(base)
        m["x0"] = np.ascontiguousarray(x[b])
        in_maps.append(m)
    res = run_bass_kernel_spmd(nc, in_maps, core_ids=list(range(8)))
    return np.stack([np.asarray(res.results[i]["out"], np.float32) for i in range(8)])


# version 07
# speedup vs baseline: 1.4305x; 1.0177x over previous
import numpy as np
import ml_dtypes
import concourse.bass as bass
import concourse.mybir as mybir
from concourse.bass_utils import run_bass_kernel_spmd

F32 = mybir.dt.float32
BF16 = mybir.dt.bfloat16
AF = mybir.ActivationFunctionType
ALU = mybir.AluOpType
AX = mybir.AxisListType
BF = ml_dtypes.bfloat16

T = 4096
D = 1024
NCH = 8
WIN = 512
NW = 8
S = 64
KCH = 64
LCH = T // KCH
WARM = 256
STEPS = LCH + WARM
EPS = 1e-5
SHIFT = 12.0

NF = 18600
HIST = 0
OBF = 0
UVU, UVV, UVU2, UVV2, FACC, FTMP = 4096, 4608, 5120, 5632, 6144, 6656
SBMP = 9216
STAT = 13824
XWIN = 13888
CONVT = 17984
RTR = 18496

NB = 48200
XHT = 0
XHAT = 8224
XN2 = 12320
XGW = 20512
SGV = 24608
GSO = 25632
WBC = 26656
GWV = 27680
SCMB = 44064
YB = 0
GSL = 4096
SDW = 8224
XHAT4 = 0
XT4 = 4096
MW1, MW2, MW3 = 8192, 16384, 24576
WS1, WS2, WS3, SWR = 32768, 34816, 36864, 38912
HST = 38976
H2SC = 44096
XHAT5, XT5 = 0, 4096
WQ, WK, WV = 8192, 16384, 24576
QWB, KWB, VWB, VTB = 32768, 33792, 34816, 35840
QTC, KTC = 0, 8192
SVC = 16384
PT = 24704
ATA, ATB = 26752, 34944
AWIN, SWO = 0, 4096
XHAT6, XT6, XT6N, WTR, WTI = 0, 4096, 8192, 10240, 12288


class Prog:
    def __init__(self, nc):
        self.nc = nc
        self.engines = ("tensor", "vector", "scalar", "sync", "gpsimd")
        self.ops = {e: [] for e in self.engines}
        self.semname = {"tensor": "s_t", "vector": "s_v", "scalar": "s_s",
                        "sync": "s_y", "gpsimd": "s_g"}
        self.sems = {}
        self.cnt = {}
        for s in self.semname.values():
            self.sems[s] = nc.semaphore(s).__enter__()
            self.cnt[s] = 0

    def op(self, engine, fn, deps=(), inc=1):
        sem = self.semname[engine]
        waits = {}
        for dd in deps:
            if dd is None:
                continue
            sn, val = dd
            waits[sn] = max(waits.get(sn, 0), val)
        self.cnt[sem] += inc
        tok = (sem, self.cnt[sem])
        self.ops[engine].append((fn, sorted(waits.items()), (sem, inc)))
        return tok

    def dma(self, out, in_, deps=(), queue="sync"):
        return self.op(queue, lambda e, o=out, i=in_: e.dma_start(out=o, in_=i),
                       deps=deps, inc=16)

    def dma_tr(self, out, in_, deps=(), queue="scalar"):
        return self.op(queue, lambda e, o=out, i=in_: e.dma_start(out=o, in_=i, transpose=True),
                       deps=deps, inc=16)

    def barrier(self):
        return [(self.semname[e], self.cnt[self.semname[e]]) for e in self.engines]

    def emit(self, block):
        def make(engine):
            def body(eng):
                for fn, waits, inc in self.ops[engine]:
                    for sn, val in waits:
                        eng.wait_ge(self.sems[sn], val)
                    inst = fn(eng)
                    sn, n = inc
                    inst.then_inc(self.sems[sn], n)
            return body
        block.tensor(make("tensor"))
        block.vector(make("vector"))
        block.scalar(make("scalar"))
        block.sync(make("sync"))
        block.gpsimd(make("gpsimd"))


def build(nc, dbg=False):
    dram = lambda name, shape, d=F32, kind="ExternalInput": nc.dram_tensor(name, shape, d, kind=kind).ap()
    x0 = dram("x0", [T, D])
    gate_wt = dram("gate_wt", [D, 2 * D])
    B_wt = dram("B_wt", [D, S]); C_wt = dram("C_wt", [D, S]); D_wt_h = dram("D_wt", [S, D])
    Bb_h = dram("B_b", [S, 1]); Cb_h = dram("C_b", [S, 1]); Db_h = dram("D_b", [128, NCH])
    gb_h = dram("gate_b", [128, 2 * NCH])
    taps_h = dram("conv_taps", [NCH, 128, 3])
    cvb_h = dram("conv_b", [128, NCH])
    A_h = dram("A_m", [S, S]); At_h = dram("A_mt", [S, S]); dtA_h = dram("dtA", [1, 1])
    g_h = dram("g_col", [S, 1]); beta_h = dram("beta_col", [S, 1]); grep_h = dram("g_rep64", [S, S])
    lngc_h = dram("lngc", [128, NCH]); lnbc_h = dram("lnbc", [128, NCH])
    Pc_h = dram("Pc", [S, S]); id_h = dram("ident64", [S, S])
    cst_h = dram("consts", [128, 8])
    od_h = dram("onesdiv", [S, S])
    hpr_h = dram("hp_rows", [1, 128])
    id128_h = dram("ident128", [128, 128])
    cm_h = dram("cmask", [4, 128, WIN], BF16)
    w1_h = dram("moe_w1", [8, D, 128]); w2_h = dram("moe_w2", [8, D, 128]); w3_h = dram("moe_w3", [8, 128, D])
    ws1_h = dram("ws1", [D, 256]); ws2_h = dram("ws2", [D, 256]); ws3_h = dram("ws3", [256, D]); wr_h = dram("wr", [D, 8])
    Wq_h = dram("Wq_t", [D, D]); Wk_h = dram("Wk_t", [D, D]); Wv_h = dram("Wv_t", [D, D]); Wo_h = dram("Wo_t", [D, D])
    uvU_h = dram("uvU", [2, 4, 4, 128, 512]); uvV_h = dram("uvV", [2, 4, 4, 128, 512])
    ttbr_h = dram("tt_bias", [128, NCH])
    out_h = dram("out", [T, D], kind="ExternalOutput")
    kd = "ExternalOutput" if dbg else "Internal"
    x1h = dram("x1h", [T, D], kind=kd)
    x2h = dram("x2h", [T, D], kind=kd)
    x3h = dram("x3h", [T, D], kind=kd)
    gatesh = dram("gatesh", [NCH, 128, T], d=BF16, kind=("ExternalOutput" if dbg else "Internal"))
    if dbg:
        dbpp_h = dram("dbg_bpp", [S, 4608], kind="ExternalOutput")
        dzh_h = dram("dbg_zh", [S, T], kind="ExternalOutput")
        dcm_h = dram("dbg_cm", [S, T], d=BF16, kind="ExternalOutput")
        dyb_h = dram("dbg_yb", [S, T], d=BF16, kind="ExternalOutput")
        dxht_h = dram("dbg_xht", [128, 8192], d=BF16, kind="ExternalOutput")
        dxn2_h = dram("dbg_xn2", [128, 4096], d=BF16, kind="ExternalOutput")
    qth = dram("qth", [NCH, 128, T], d=BF16, kind="Internal")
    kth = dram("kth", [NCH, 128, T], d=BF16, kind="Internal")
    vth = dram("vth", [NCH, 32, 128, 128], d=BF16, kind="Internal")
    ath = dram("ath", [NCH, 128, T], d=BF16, kind="Internal")

    sbuf = lambda name, shape, d=F32: nc.sbuf_tensor(name, shape, d).__enter__()

    cC = sbuf("cC", [128, 8])
    cOD = sbuf("cOD", [S, S])
    cID = sbuf("cID", [S, S]); cPc = sbuf("cPc", [S, S]); cID128 = sbuf("cID128", [128, 128])
    cA = sbuf("cA", [S, S]); cAt = sbuf("cAt", [S, S]); cdt = sbuf("cdt", [1, 1])
    cg = sbuf("cg", [S, 1]); cbeta = sbuf("cbeta", [S, 1]); cgrep = sbuf("cgrep", [S, S])
    cHPR = sbuf("cHPR", [1, 128])
    lngc = sbuf("lngc_s", [128, NCH]); lnbc = sbuf("lnbc_s", [128, NCH])
    staps = sbuf("staps", [128, NCH * 4])
    scvb = sbuf("scvb", [128, NCH])
    sBb = sbuf("sBb", [S, 1]); sCb = sbuf("sCb", [S, 1])
    sDb = sbuf("sDb", [128, NCH]); sgbias = sbuf("sgbias", [128, 2 * NCH])
    sttbr = sbuf("sttbr", [128, NCH])
    scanM = sbuf("scanM", [S, S])
    bacol = sbuf("bacol", [S, 1])
    eE = sbuf("eE", [S, S]); eF = sbuf("eF", [S, S]); eH = sbuf("eH", [S, S])
    eAd = sbuf("eAd", [S, S]); eAdt = sbuf("eAdt", [S, S])
    eT1 = sbuf("eT1", [S, S]); eT2 = sbuf("eT2", [S, S])
    sdtc = sbuf("sdtc", [S, 1])
    rowtmp = sbuf("rowtmp", [1, S])
    cmb = sbuf("cmb", [128, 4 * WIN], BF16)
    szh2t = sbuf("szh2t", [S, LCH])
    hcell = sbuf("hcell", [128, 64], BF16)
    craw = [sbuf("craw0", [S, KCH])]
    csqb = [sbuf("csqb0", [S, KCH])]
    ztp = [sbuf(f"ztp{h}", [S, KCH]) for h in range(2)]
    wfT = sbuf("wfT", [1, 8 * WIN])
    lrow = sbuf("lrow", [128, 2 * WIN])

    FP = sbuf("FP", [128, NF])
    BB = sbuf("BB", [128, NB], BF16)
    f = lambda a, b: FP[:, a:b]
    f6 = lambda a, b: FP[0:S, a:b]
    bb = lambda a, b: BB[:, a:b]
    b6 = lambda a, b: BB[0:S, a:b]

    p = Prog(nc)
    pb = [nc.psum_tensor(f"pb{i}", [128, 512], F32).__enter__() for i in range(8)]
    pfree = [()] * 8
    _rr = [0]

    def nb():
        i = _rr[0]; _rr[0] = (i + 1) % 8
        return i

    def bank_done(i, tok):
        pfree[i] = (tok,)

    def mm(bi, lhsT, rhs, deps=(), start=True, stop=True, tp=None, psl=None):
        dst = pb[bi][psl[0]:psl[1], psl[2]:psl[3]] if psl else pb[bi][:, :]
        dd = (list(pfree[bi]) if start else []) + list(deps)
        return p.op("tensor", lambda e, o=dst, l=lhsT, r=rhs, s0=start, s1=stop, t_=tp: e.matmul(
            o, l, r, start=s0, stop=s1, tile_position=t_, skip_group_check=True), deps=dd)

    def ptr(bi, cols, src, deps=(), rows=128):
        dst = pb[bi][0:rows, cols[0]:cols[1]]
        dd = list(pfree[bi]) + list(deps)
        return p.op("tensor", lambda e, o=dst, i=src: e.transpose(o, i, cID128[:, :]), deps=dd)

    def act(out, in_, func, deps=(), bias=None, scale=1.0):
        if bias is None:
            return p.op("scalar", lambda e, o=out, i=in_, sc=scale, fu=func: e.activation(
                out=o, in_=i, func=fu, scale=sc), deps=deps)
        return p.op("scalar", lambda e, o=out, i=in_, b=bias, sc=scale, fu=func: e.activation(
            out=o, in_=i, func=fu, bias=b, scale=sc), deps=deps)

    def vtt(out, a, b2, opn, deps=()):
        return p.op("vector", lambda e, o=out, x=a, y=b2, oo=opn: e.tensor_tensor(o, x, y, oo),
                    deps=deps)

    def vts(out, in0, s1, s2, op0, op1=None, deps=()):
        if s2 is None:
            return p.op("vector", lambda e, o=out, i=in0, a=s1, oo=op0: e.tensor_scalar(
                out=o, in0=i, scalar1=a, scalar2=None, op0=oo), deps=deps)
        return p.op("vector", lambda e, o=out, i=in0, a=s1, b2=s2, oo=op0, o1=op1: e.tensor_scalar(
            out=o, in0=i, scalar1=a, scalar2=b2, op0=oo, op1=o1), deps=deps)

    def vstt(out, in0, s1, in1, op0, op1, deps=()):
        return p.op("vector", lambda e, o=out, i=in0, a=s1, j=in1, oo=op0, o1=op1:
                    e.scalar_tensor_tensor(out=o, in0=i, scalar=a, in1=j, op0=oo, op1=o1),
                    deps=deps)

    def vcopy(out, in_, deps=()):
        return p.op("vector", lambda e, o=out, i=in_: e.tensor_copy(o, i), deps=deps)

    def vred(out, in_, opn, deps=()):
        return p.op("vector", lambda e, o=out, i=in_, oo=opn: e.tensor_reduce(
            out=o, in_=i, axis=AX.X, op=oo), deps=deps)

    def vrecip(out, in_, deps=()):
        return p.op("vector", lambda e, o=out, i=in_: e.reciprocal(out=o, in_=i), deps=deps)

    def vmemset(ap_, val, deps=()):
        return p.op("vector", lambda e, o=ap_, v=val: e.memset(o, v), deps=deps)

    def phase():
        bar = p.barrier()
        for i in range(8):
            pfree[i] = tuple(bar)
        return bar

    epsc = cC[:, 0:1]; eps2c = cC[:, 1:2]; mshiftc = cC[:, 2:3]

    ld = [p.dma(cC[:, :], cst_h), p.dma(cOD[:, :], od_h),
          p.dma(cID[:, :], id_h), p.dma(cPc[:, :], Pc_h), p.dma(cID128[:, :], id128_h),
          p.dma(cA[:, :], A_h), p.dma(cAt[:, :], At_h), p.dma(cdt[:, :], dtA_h),
          p.dma(cg[:, :], g_h), p.dma(cbeta[:, :], beta_h), p.dma(cgrep[:, :], grep_h),
          p.dma(cHPR[:, :], hpr_h),
          p.dma(lngc[:, :], lngc_h), p.dma(lnbc[:, :], lnbc_h),
          p.dma(staps.ap().rearrange("p (c k) -> p c k", k=4)[:, :, 0:3], taps_h.rearrange("c p k -> p c k")),
          p.dma(scvb[:, :], cvb_h),
          p.dma(sBb[:, :], Bb_h), p.dma(sCb[:, :], Cb_h),
          p.dma(sDb[:, :], Db_h),
          p.dma(sgbias[:, :], gb_h),
          p.dma(sttbr[:, :], ttbr_h),
          p.dma(cmb.ap().rearrange("p (j n) -> p j n", n=WIN), cm_h.rearrange("j p n -> p j n"))]

    t_t1 = []
    for c in range(NCH):
        t_t1.append(vts(staps[:, c * 4 + 3:c * 4 + 4], staps[:, c * 4 + 1:c * 4 + 2],
                        1.0, None, ALU.add, deps=ld))

    tok = mm(0, cOD[0:1, :], cdt[:, :], deps=ld, psl=(0, S, 0, 1))
    t_dt = vts(sdtc[:, :], pb[0][0:S, 0:1], float(S) * (2.0 ** -8), None, ALU.mult, deps=[tok])
    bank_done(0, t_dt)
    tA1 = vts(eAd[:, :], cA[:, :], sdtc[:, :], None, ALU.mult, deps=[t_dt] + ld)
    tA2 = vts(eAdt[:, :], cAt[:, :], sdtc[:, :], None, ALU.mult, deps=[t_dt] + ld)

    def horner(Xts, Hout, tdep):
        tH = vcopy(Hout[:, :], cID[:, :], deps=tdep + ld)
        for k in range(6, 0, -1):
            tm_ = mm(1, Xts[:, :], Hout[:, :], deps=[tH], psl=(0, S, 0, S))
            te = act(eT1[:, :], pb[1][0:S, 0:S], AF.Identity, deps=[tm_], scale=1.0 / k)
            bank_done(1, te)
            tH = vtt(Hout[:, :], eT1[:, :], cID[:, :], ALU.add, deps=[te])
        return tH

    tE = horner(eAdt, eE, [tA1, tA2])
    tF = horner(eAd, eF, [tA1, tA2])
    for _ in range(8):
        tm1 = mm(1, eF[:, :], eE[:, :], deps=[tE, tF], psl=(0, S, 0, S))
        te1 = act(eT1[:, :], pb[1][0:S, 0:S], AF.Identity, deps=[tm1])
        bank_done(1, te1)
        tm2 = mm(2, eE[:, :], eF[:, :], deps=[tE, tF], psl=(0, S, 0, S))
        te2 = act(eT2[:, :], pb[2][0:S, 0:S], AF.Identity, deps=[tm2])
        bank_done(2, te2)
        tE = vcopy(eE[:, :], eT1[:, :], deps=[te1, tm2])
        tF = vcopy(eF[:, :], eT2[:, :], deps=[te2, tE])
    tM = vtt(eH[:, :], eF[:, :], cgrep[:, :], ALU.mult, deps=[tE, tF])
    tsm = mm(1, eH[:, :], cPc[:, :], deps=[tM], psl=(0, S, 0, S))
    tsme = act(scanM[:, :], pb[1][0:S, 0:S], AF.Identity, deps=[tsm])
    bank_done(1, tsme)
    tba = mm(2, cbeta[:, :], eE[:, :], deps=[tE], psl=(0, 1, 0, S))
    tbae = vcopy(rowtmp[:, :], pb[2][0:1, 0:S], deps=[tba])
    bank_done(2, tbae)
    tbat = p.op("tensor", lambda e: e.transpose(pb[3][0:S, 0:1], rowtmp[:, :], cID128[0:1, 0:1]),
                deps=[tbae] + list(pfree[3]))
    tbac = vcopy(bacol[:, :], pb[3][0:S, 0:1], deps=[tbat])
    bank_done(3, tbac)

    tgw = [p.dma(bb(GWV + c * 2048, GWV + (c + 1) * 2048), gate_wt[c * 128:(c + 1) * 128, :],
                 queue="gpsimd", deps=ld) for c in range(NCH)]
    tbc = []
    for c in range(NCH):
        tbc.append(p.dma(bb(WBC + c * 2 * S, WBC + c * 2 * S + S), B_wt[c * 128:(c + 1) * 128, :],
                         queue="gpsimd", deps=ld))
        tbc.append(p.dma(bb(WBC + c * 2 * S + S, WBC + (c + 1) * 2 * S), C_wt[c * 128:(c + 1) * 128, :],
                         queue="gpsimd", deps=ld))
    tz = [vmemset(hcell[:, c * 4:c * 4 + 1], 0.0) for c in range(NCH)]
    tzp = [vmemset(f6(SBMP, SBMP + WARM), 0.0), vmemset(f6(SBMP + WARM + T, SBMP + 4608), 0.0)]

    def ln_window(xsrc, woff, deps, xhat_off, bf16_out=True):
        tl = p.dma(f(XWIN, XWIN + 4 * D).rearrange("p (i d) -> p i d", d=D),
                   xsrc[woff:woff + WIN, :].rearrange("(i p) d -> p i d", p=128), deps=deps)
        apl = []
        for i in range(4):
            xt = f(XWIN + i * D, XWIN + (i + 1) * D)
            st12 = f(STAT + i * 16, STAT + i * 16 + 12)
            mv = f(STAT + i * 16 + 12, STAT + i * 16 + 14)
            sc = f(STAT + i * 16 + 14, STAT + i * 16 + 15)
            t1 = p.op("vector", lambda e, o=st12[:, 0:6], i_=xt[:, 0:512]: e.bn_stats(out=o, in_=i_),
                      deps=[tl])
            t2 = p.op("vector", lambda e, o=st12[:, 6:12], i_=xt[:, 512:1024]: e.bn_stats(out=o, in_=i_),
                      deps=[t1])
            t3 = p.op("vector", lambda e, o=mv, i_=st12: e.bn_aggr(out=o, in_=i_), deps=[t2])
            if bf16_out == "composite":
                t4 = act(sc, mv[:, 1:2], AF.Sqrt, deps=[t3], bias=eps2c, scale=1.0 + EPS)
            else:
                t4 = act(sc, mv[:, 1:2], AF.Sqrt, deps=[t3], bias=epsc, scale=1.0)
            t5 = vrecip(sc, sc, deps=[t4])
            t6 = vts(bb(xhat_off + i * D, xhat_off + (i + 1) * D), xt, mv[:, 0:1], sc,
                     ALU.subtract, ALU.mult, deps=[t5])
            apl.append(t6)
        return tl, apl

    def tr_window(xhat_off, dst_fn, apl, deps=()):
        tts = [[None] * NCH for _ in range(4)]
        for i in range(4):
            for c in range(NCH):
                tts[i][c] = p.dma_tr(dst_fn(c, i), bb(xhat_off + i * D + c * 128, xhat_off + i * D + (c + 1) * 128),
                                     deps=[apl[i]] + list(deps))
        return tts


    conv_done = {}
    aff_done = {}
    gates_done = {-1: []}
    bm_evac = []
    ln_prev = []

    def do_conv(w, deps_extra):
        s = w % 2
        xn2o = XN2 + s * 4096
        toks = []
        prev = None
        for c in range(NCH):
            base = XHT + (c * 2 + s) * WIN
            dat = lambda a, b: bb(base + a, base + b)
            hL = hcell[:, c * 4 + s * 2:c * 4 + s * 2 + 1]
            hR = hcell[:, c * 4 + s * 2 + 32:c * 4 + s * 2 + 33]
            cv = lambda a, b: f(4096 + c * WIN + a, 4096 + c * WIN + b)
            dd = aff_done[w] + deps_extra + gates_done.get(w - 2, [])
            t1 = vts(cv(0, WIN), dat(0, WIN), staps[:, c * 4 + 3:c * 4 + 4], None, ALU.mult, deps=dd)
            t2 = vstt(cv(1, WIN), dat(0, WIN - 1), staps[:, c * 4 + 0:c * 4 + 1],
                      cv(1, WIN), ALU.mult, ALU.add, deps=[t1])
            t2b = vstt(cv(0, 1), hL, staps[:, c * 4 + 0:c * 4 + 1],
                       cv(0, 1), ALU.mult, ALU.add, deps=[t2])
            t3 = vstt(cv(0, WIN - 1), dat(1, WIN), staps[:, c * 4 + 2:c * 4 + 3],
                      cv(0, WIN - 1), ALU.mult, ALU.add, deps=[t2b])
            t3b = vstt(cv(WIN - 1, WIN), hR, staps[:, c * 4 + 2:c * 4 + 3],
                       cv(WIN - 1, WIN), ALU.mult, ALU.add, deps=[t3])
            t4 = vts(bb(xn2o + c * WIN, xn2o + (c + 1) * WIN), cv(0, WIN),
                     scvb[:, c:c + 1], None, ALU.add, deps=[t3b])
            prev = t4
            toks.append(t4)
        conv_done[w] = toks
        return toks

    def do_gates(w):
        s = w % 2
        xn2o = XN2 + s * 4096
        ct = conv_done[w]
        igev = []
        prev_sg = {0: None, 1: None}
        for oc in range(NCH):
            bi = nb()
            tmm = None
            for c in range(NCH):
                tmm = mm(bi, bb(GWV + c * 2048 + 1024 + oc * 128, GWV + c * 2048 + 1024 + (oc + 1) * 128),
                         bb(xn2o + c * WIN, xn2o + (c + 1) * WIN),
                         deps=([ct[c]] + tgw) if c == 0 else [tmm, ct[c]], start=(c == 0), stop=(c == NCH - 1))
            sgo = SGV + (oc % 2) * WIN
            tsg = act(bb(sgo, sgo + WIN), pb[bi][:, :], AF.Sigmoid,
                      deps=[tmm] + ([prev_sg[oc % 2]] if prev_sg[oc % 2] else []),
                      bias=sgbias[:, NCH + oc:NCH + oc + 1])
            bank_done(bi, tsg)
            txg = vtt(bb(XGW + oc * WIN, XGW + (oc + 1) * WIN), bb(xn2o + oc * WIN, xn2o + (oc + 1) * WIN),
                      bb(sgo, sgo + WIN), ALU.mult, deps=[tsg])
            prev_sg[oc % 2] = txg
            igev.append(txg)
        biB = nb(); biC = nb()
        tmB = None; tmC = None
        for c in range(NCH):
            tmB = mm(biB, bb(WBC + c * 2 * S, WBC + c * 2 * S + S), bb(XGW + c * WIN, XGW + (c + 1) * WIN),
                     deps=(igev + tbc) if c == 0 else [tmB], start=(c == 0), stop=(c == NCH - 1),
                     psl=(0, S, 0, WIN))
        for c in range(NCH):
            tmC = mm(biC, bb(WBC + c * 2 * S + S, WBC + (c + 1) * 2 * S), bb(xn2o + c * WIN, xn2o + (c + 1) * WIN),
                     deps=([ct[c]] + tbc) if c == 0 else [tmC, ct[c]], start=(c == 0), stop=(c == NCH - 1),
                     psl=(0, S, 0, WIN))
        tBe = act(f6(HIST + w * WIN, HIST + (w + 1) * WIN), pb[biB][0:S, 0:WIN], AF.Identity,
                  deps=[tmB], bias=sBb[:, :])
        bank_done(biB, tBe)
        tCe = act(b6(SCMB + w * WIN, SCMB + (w + 1) * WIN), pb[biC][0:S, 0:WIN], AF.Identity,
                  deps=[tmC], bias=sCb[:, :])
        bank_done(biC, tCe)
        bm_evac.extend([tBe, tCe])
        gtoks = []
        prev_st = {0: None, 1: None}
        for oc in range(NCH):
            bi = nb()
            tmm = None
            for c in range(NCH):
                tmm = mm(bi, bb(GWV + c * 2048 + oc * 128, GWV + c * 2048 + (oc + 1) * 128),
                         bb(xn2o + c * WIN, xn2o + (c + 1) * WIN),
                         deps=([ct[c]] + tgw) if c == 0 else [tmm, ct[c]], start=(c == 0), stop=(c == NCH - 1))
            go = GSO + (oc % 2) * WIN
            tsg = act(bb(go, go + WIN), pb[bi][:, :], AF.Sigmoid,
                      deps=[tmm] + ([prev_st[oc % 2]] if prev_st[oc % 2] else []), bias=sgbias[:, oc:oc + 1])
            bank_done(bi, tsg)
            tst = p.dma(gatesh[oc, :, w * WIN:(w + 1) * WIN], bb(go, go + WIN), deps=[tsg])
            prev_st[oc % 2] = tst
            gtoks.append(tst)
        gates_done[w] = gtoks + [tBe, tCe]
        return gtoks

    halo_toks = {}
    for w in range(NW):
        ldeps = tz + t_t1 + [tA1] + (ln_prev if ln_prev else [])
        tl, apl = ln_window(x0, w * WIN, ldeps, XHAT, bf16_out="composite")
        hl = []
        if w >= 2:
            s_src = w % 2; s_dst = (w - 1) % 2
            for c in range(NCH):
                base = XHT + (c * 2 + s_src) * WIN
                hl.append(vcopy(hcell[:, c * 4 + s_dst * 2:c * 4 + s_dst * 2 + 1],
                                bb(base + WIN - 1, base + WIN),
                                deps=aff_done[w - 2] + conv_done.get(w - 2, [])))
        s = w % 2
        tts = tr_window(XHAT, lambda c, i, s=s: bb(XHT + (c * 2 + s) * WIN + i * 128,
                                                   XHT + (c * 2 + s) * WIN + (i + 1) * 128),
                        apl, deps=hl + conv_done.get(w - 2, []) + gates_done.get(w - 2, []))
        ln_prev = [t for row in tts for t in row]
        aff = []
        for c in range(NCH):
            base = XHT + (c * 2 + s) * WIN
            aff.append(vts(bb(base, base + WIN), bb(base, base + WIN),
                           lngc[:, c:c + 1], lnbc[:, c:c + 1], ALU.mult, ALU.add,
                           deps=[tts[i][c] for i in range(4)]))
        aff_done[w] = aff
        hr = []
        if w >= 1:
            s_dst = (w - 1) % 2
            for c in range(NCH):
                base = XHT + (c * 2 + s) * WIN
                hr.append(vcopy(hcell[:, c * 4 + s_dst * 2 + 32:c * 4 + s_dst * 2 + 33],
                                bb(base, base + 1), deps=[aff[c]]))
            halo_toks[w - 1] = hl + hr
            do_conv(w - 1, hl + hr)
            do_gates(w - 1)
            if dbg and w == 1:
                td1 = p.dma(dxht_h, bb(XHT, XHT + 8192), deps=aff_done[0] + conv_done[0])
                td2 = p.dma(dxn2_h, bb(XN2, XN2 + 4096), deps=conv_done[0])
                gates_done[0] = gates_done[0] + [td1, td2]
    hl = []
    s_dst = 1
    for c in range(NCH):
        base = XHT + (c * 2 + 0) * WIN
        hl.append(vcopy(hcell[:, c * 4 + s_dst * 2:c * 4 + s_dst * 2 + 1],
                        bb(base + WIN - 1, base + WIN),
                        deps=aff_done[6] + conv_done.get(6, [])))
        hl.append(vmemset(hcell[:, c * 4 + s_dst * 2 + 32:c * 4 + s_dst * 2 + 33], 0.0,
                          deps=conv_done.get(5, [])))
    do_conv(7, hl)
    do_gates(7)

    P2B = phase()
    tb2 = vts(f6(HIST + 1, HIST + T), f6(HIST + 1, HIST + T), bacol[:, :], None, ALU.add,
              deps=P2B)
    pps = []
    for i in range(NW):
        bi = nb()
        tm_ = mm(bi, cPc[:, :], f6(HIST + i * WIN, HIST + (i + 1) * WIN), deps=[tb2],
                 psl=(0, S, 0, WIN))
        te = act(f6(SBMP + WARM + i * WIN, SBMP + WARM + (i + 1) * WIN), pb[bi][0:S, 0:WIN],
                 AF.Identity, deps=[tm_])
        bank_done(bi, te)
        pps.append(te)
    SBW = FP[0:S, SBMP:SBMP + 4608].rearrange("p (j l) -> p j l", l=KCH)
    scan_end = []
    pA = [0, 1]
    pB = [2, 3]
    tV = tq = tM2 = tR = tZ = tM1 = None
    for t in range(STEPS):
        bAP = SBW[:, t // KCH: t // KCH + KCH, t % KCH]
        if t < WARM:
            zt = ztp[t % 2][:, :]
        else:
            zt = f6(HIST + (t - WARM) * KCH, HIST + (t - WARM + 1) * KCH)
        if t == 0:
            tV = vcopy(craw[0][:, :], bAP, deps=pps + P2B + tzp)
        else:
            tV = vtt(craw[0][:, :], pb[pA[t % 2]][0:S, 0:KCH], bAP, ALU.add, deps=[tM1])
        tq = vtt(csqb[0][:, :], craw[0][:, :], craw[0][:, :], ALU.mult, deps=[tV])
        tM2 = mm(pB[t % 2], cOD[:, :], csqb[0][:, :], deps=[tq] + ([tR] if t >= 2 else []),
                 psl=(0, S, 0, KCH))
        tR = act(csqb[0][:, :], pb[pB[t % 2]][0:S, 0:KCH],
                 AF.Sqrt, deps=[tM2], bias=epsc[0:S, :])
        tRc = vrecip(csqb[0][:, :], csqb[0][:, :], deps=[tR])
        tZ = vtt(zt, craw[0][:, :], csqb[0][:, :], ALU.mult, deps=[tRc])
        if t < STEPS - 1:
            tM1 = mm(pA[(t + 1) % 2], scanM[:, :], zt, deps=[tZ] + ([tV] if t >= 1 else []),
                     psl=(0, S, 0, KCH))
        else:
            scan_end.append(tZ)

    if dbg:
        p.dma(dbpp_h, f6(SBMP, SBMP + 4608), deps=scan_end)
        p.dma(dzh_h, f6(HIST, HIST + T), deps=scan_end)
        p.dma(dcm_h, b6(SCMB, SCMB + T), deps=scan_end)

    P3B = phase()
    HISTV = FP[0:S, HIST:HIST + T].rearrange("p (t k) -> p t k", k=KCH)
    ty_prev = None
    ty_toks = []
    for j in range(KCH):
        zsrc = HISTV[:, :, j]
        t1 = vts(szh2t[:, 0:LCH], zsrc, cg[:, :], cbeta[:, :], ALU.mult, ALU.add,
                 deps=P3B + ([ty_prev] if ty_prev else []))
        t2 = vtt(b6(YB + j * LCH, YB + (j + 1) * LCH), szh2t[:, 0:LCH],
                 b6(SCMB + j * LCH, SCMB + (j + 1) * LCH), ALU.mult, deps=[t1])
        ty_prev = t2
        ty_toks.append(t2)
    tDld = p.dma(b6(SDW, SDW + D), D_wt_h, queue="gpsimd", deps=P3B)

    def out_window(w, obf_toks, xsrc, xdst, extra_ld_deps):
        tl = p.dma(f(XWIN, XWIN + 4 * D).rearrange("p (i d) -> p i d", d=D),
                   xsrc[w * WIN:(w + 1) * WIN, :].rearrange("(i p) d -> p i d", p=128),
                   deps=extra_ld_deps)
        adds = []
        for i in range(4):
            for oc in range(NCH):
                bi = nb()
                tt = ptr(bi, (0, 128), f(OBF + oc * WIN + i * 128, OBF + oc * WIN + (i + 1) * 128),
                         deps=obf_toks)
                ta = vtt(f(XWIN + i * D + oc * 128, XWIN + i * D + (oc + 1) * 128),
                         f(XWIN + i * D + oc * 128, XWIN + i * D + (oc + 1) * 128),
                         pb[bi][0:128, 0:128], ALU.add, deps=[tt, tl])
                bank_done(bi, ta)
                adds.append(ta)
        ts_ = p.dma(xdst[w * WIN:(w + 1) * WIN, :].rearrange("(i p) d -> p i d", p=128),
                    f(XWIN, XWIN + 4 * D).rearrange("p (i d) -> p i d", d=D), deps=adds)
        return ts_, adds

    prev_st = []
    tgl_prev = None
    for w in range(NW):
        tgl = p.dma(bb(GSL, GSL + 4096).rearrange("p (c n) -> p c n", n=WIN),
                    gatesh[:, :, w * WIN:(w + 1) * WIN].rearrange("c p n -> p c n"),
                    deps=P3B + ([tgl_prev] if tgl_prev else []))
        evs = []
        for oc in range(NCH):
            bi = nb()
            tm_ = mm(bi, b6(SDW + oc * 128, SDW + (oc + 1) * 128), b6(YB + w * WIN, YB + (w + 1) * WIN),
                     deps=ty_toks + [tDld], psl=(0, 128, 0, WIN))
            te = act(f(OBF + oc * WIN, OBF + (oc + 1) * WIN), pb[bi][:, 0:WIN], AF.Identity,
                     deps=[tm_] + prev_st, bias=sDb[:, oc:oc + 1])
            tg = vtt(f(OBF + oc * WIN, OBF + (oc + 1) * WIN), f(OBF + oc * WIN, OBF + (oc + 1) * WIN),
                     bb(GSL + oc * WIN, GSL + (oc + 1) * WIN), ALU.mult, deps=[te, tgl])
            bank_done(bi, tg)
            evs.append(tg)
        ts_, adds = out_window(w, evs, x0, x1h, prev_st)
        tgl_prev = evs[-1]
        prev_st = [ts_] + adds

    if dbg:
        p.dma(dyb_h, b6(YB, YB + T), deps=ty_toks + prev_st)

    P4B = phase()
    wld = []
    for e_i in range(8):
        for c in range(NCH):
            wld.append(p.dma(bb(MW1 + e_i * 1024 + c * 128, MW1 + e_i * 1024 + (c + 1) * 128),
                             w1_h[e_i, c * 128:(c + 1) * 128, :], queue="gpsimd", deps=P4B))
            wld.append(p.dma(bb(MW2 + e_i * 1024 + c * 128, MW2 + e_i * 1024 + (c + 1) * 128),
                             w2_h[e_i, c * 128:(c + 1) * 128, :], queue="gpsimd", deps=P4B))
        wld.append(p.dma(bb(MW3 + e_i * 1024, MW3 + (e_i + 1) * 1024), w3_h[e_i], queue="gpsimd", deps=P4B))
    for c in range(NCH):
        wld.append(p.dma(bb(WS1 + c * 256, WS1 + (c + 1) * 256), ws1_h[c * 128:(c + 1) * 128, :],
                         queue="gpsimd", deps=P4B))
        wld.append(p.dma(bb(WS2 + c * 256, WS2 + (c + 1) * 256), ws2_h[c * 128:(c + 1) * 128, :],
                         queue="gpsimd", deps=P4B))
        wld.append(p.dma(bb(SWR + c * 8, SWR + (c + 1) * 8), wr_h[c * 128:(c + 1) * 128, :],
                         queue="gpsimd", deps=P4B))
    for k in range(2):
        wld.append(p.dma(bb(WS3 + k * 1024, WS3 + (k + 1) * 1024), ws3_h[k * 128:(k + 1) * 128, :],
                         queue="gpsimd", deps=P4B))

    prev_w = P4B
    for w in range(NW):
        tl, apl = ln_window(x1h, w * WIN, prev_w, XHAT4)
        tts = tr_window(XHAT4, lambda c, i: bb(XT4 + c * WIN + i * 128, XT4 + c * WIN + (i + 1) * 128),
                        apl, deps=prev_w)
        trt = [t for row in tts for t in row]
        wf_toks = []
        for i in range(4):
            bi = nb()
            tmm = None
            for c in range(NCH):
                tmm = mm(bi, bb(XT4 + c * WIN + i * 128, XT4 + c * WIN + (i + 1) * 128),
                         bb(SWR + c * 8, SWR + (c + 1) * 8),
                         deps=(trt + wld) if c == 0 else [tmm], start=(c == 0), stop=(c == NCH - 1),
                         psl=(0, 128, 0, 8))
            pr = f(RTR + i * 24, RTR + i * 24 + 8)
            rx = f(RTR + i * 24 + 8, RTR + i * 24 + 16)
            mx = f(RTR + i * 24 + 16, RTR + i * 24 + 17)
            ms = f(RTR + i * 24 + 17, RTR + i * 24 + 18)
            m1 = f(RTR + i * 24 + 18, RTR + i * 24 + 19)
            m2 = f(RTR + i * 24 + 19, RTR + i * 24 + 20)
            t1 = vred(mx, pb[bi][:, 0:8], ALU.max, deps=[tmm])
            t2 = vts(mx, mx, -1.0, None, ALU.mult, deps=[t1])
            t3 = act(pr, pb[bi][:, 0:8], AF.Exp, deps=[t2], bias=mx)
            bank_done(bi, t3)
            t4 = vred(ms, pr, ALU.add, deps=[t3])
            t5 = vrecip(ms, ms, deps=[t4])
            t6 = vts(pr, pr, ms, None, ALU.mult, deps=[t5])
            t7 = vred(m1, pr, ALU.min, deps=[t6])
            t8 = vts(rx, pr, m1, None, ALU.is_le, deps=[t7])
            t9 = vstt(rx, rx, 2.0, pr, ALU.mult, ALU.add, deps=[t8])
            t11 = vred(m2, rx, ALU.min, deps=[t9])
            t12 = vts(rx, pr, m2, None, ALU.is_gt, deps=[t11])
            t13 = vtt(rx, rx, pr, ALU.mult, deps=[t12])
            t14 = vred(ms, rx, ALU.add, deps=[t13])
            t15 = vrecip(ms, ms, deps=[t14])
            t16 = vts(rx, rx, ms, None, ALU.mult, deps=[t15])
            for e_i in range(8):
                bj = nb()
                t17 = ptr(bj, (0, 128), rx[:, e_i:e_i + 1], deps=[t16], rows=1)
                t18 = vcopy(wfT[0:1, e_i * WIN + i * 128:e_i * WIN + (i + 1) * 128],
                            pb[bj][0:1, 0:128], deps=[t17])
                bank_done(bj, t18)
                wf_toks.append(t18)
        hw_toks = {}
        prev_h2 = None
        for e_i in range(10):
            bi1 = nb(); bi2 = nb()
            tm1 = tm2 = None
            for c in range(NCH):
                if e_i < 8:
                    l1 = bb(MW1 + e_i * 1024 + c * 128, MW1 + e_i * 1024 + (c + 1) * 128)
                    l2 = bb(MW2 + e_i * 1024 + c * 128, MW2 + e_i * 1024 + (c + 1) * 128)
                else:
                    k = e_i - 8
                    l1 = bb(WS1 + c * 256 + k * 128, WS1 + c * 256 + (k + 1) * 128)
                    l2 = bb(WS2 + c * 256 + k * 128, WS2 + c * 256 + (k + 1) * 128)
                rhs = bb(XT4 + c * WIN, XT4 + (c + 1) * WIN)
                tm1 = mm(bi1, l1, rhs, deps=(trt + wld) if c == 0 else [tm1],
                         start=(c == 0), stop=(c == NCH - 1))
                tm2 = mm(bi2, l2, rhs, deps=(trt + wld) if c == 0 else [tm2],
                         start=(c == 0), stop=(c == NCH - 1))
            hslot = bb(HST + e_i * WIN, HST + (e_i + 1) * WIN)
            ts1 = act(hslot, pb[bi1][:, :], AF.Silu, deps=[tm1])
            bank_done(bi1, ts1)
            ts2 = act(bb(H2SC, H2SC + WIN), pb[bi2][:, :], AF.Identity,
                      deps=[tm2] + ([prev_h2] if prev_h2 else []))
            bank_done(bi2, ts2)
            th = vtt(hslot, hslot, bb(H2SC, H2SC + WIN), ALU.mult, deps=[ts1, ts2])
            prev_h2 = th
            if e_i < 8:
                bj = nb()
                tbr = mm(bj, cHPR[0:1, :], wfT[0:1, e_i * WIN:(e_i + 1) * WIN], deps=wf_toks,
                         psl=(0, 128, 0, WIN))
                tw = vtt(hslot, hslot, pb[bj][:, :], ALU.mult, deps=[th, tbr])
                bank_done(bj, tw)
                hw_toks[e_i] = tw
            else:
                hw_toks[e_i] = th
        evs = []
        for oc in range(NCH):
            bo = nb()
            tmm = None
            first = True
            for k in range(2):
                tmm = mm(bo, bb(WS3 + k * 1024 + oc * 128, WS3 + k * 1024 + (oc + 1) * 128),
                         bb(HST + (8 + k) * WIN, HST + (9 + k) * WIN),
                         deps=([hw_toks[8], hw_toks[9]] + wld) if first else [tmm],
                         start=first, stop=False)
                first = False
            for e_i in range(8):
                tmm = mm(bo, bb(MW3 + e_i * 1024 + oc * 128, MW3 + e_i * 1024 + (oc + 1) * 128),
                         bb(HST + e_i * WIN, HST + (e_i + 1) * WIN),
                         deps=[tmm, hw_toks[e_i]], start=False, stop=(e_i == 7))
            te = act(f(OBF + oc * WIN, OBF + (oc + 1) * WIN), pb[bo][:, :], AF.Identity, deps=[tmm])
            bank_done(bo, te)
            evs.append(te)
        ts_, adds = out_window(w, evs, x1h, x2h, [])
        prev_w = [ts_] + adds

    P5aB = phase()
    wld5 = []
    for c in range(NCH):
        wld5.append(p.dma(bb(WQ + c * 1024, WQ + (c + 1) * 1024), Wq_h[c * 128:(c + 1) * 128, :],
                          queue="gpsimd", deps=P5aB))
        wld5.append(p.dma(bb(WK + c * 1024, WK + (c + 1) * 1024), Wk_h[c * 128:(c + 1) * 128, :],
                          queue="gpsimd", deps=P5aB))
        wld5.append(p.dma(bb(WV + c * 1024, WV + (c + 1) * 1024), Wv_h[c * 128:(c + 1) * 128, :],
                          queue="gpsimd", deps=P5aB))
    prev_w = P5aB
    last_use = {}
    for w in range(NW):
        tl, apl = ln_window(x2h, w * WIN, prev_w, XHAT5)
        tts = tr_window(XHAT5, lambda c, i: bb(XT5 + c * WIN + i * 128, XT5 + c * WIN + (i + 1) * 128),
                        apl, deps=prev_w)
        trt = [t for row in tts for t in row]
        wtoks = []
        for oc in range(NCH):
            for (Woff, BUF, hbm) in ((WQ, QWB, qth), (WK, KWB, kth)):
                bi = nb()
                tmm = None
                for c in range(NCH):
                    tmm = mm(bi, bb(Woff + c * 1024 + oc * 128, Woff + c * 1024 + (oc + 1) * 128),
                             bb(XT5 + c * WIN, XT5 + (c + 1) * WIN),
                             deps=(trt + wld5) if c == 0 else [tmm], start=(c == 0), stop=(c == NCH - 1))
                slot = BUF + (oc % 2) * WIN
                dd = [last_use[slot]] if slot in last_use else []
                te = act(bb(slot, slot + WIN), pb[bi][:, :], AF.Identity, deps=[tmm] + dd)
                bank_done(bi, te)
                tst = p.dma(hbm[oc, :, w * WIN:(w + 1) * WIN], bb(slot, slot + WIN), deps=[te])
                last_use[slot] = tst
                wtoks.append(tst)
            bi = nb()
            tmm = None
            for c in range(NCH):
                tmm = mm(bi, bb(WV + c * 1024 + oc * 128, WV + c * 1024 + (oc + 1) * 128),
                         bb(XT5 + c * WIN, XT5 + (c + 1) * WIN),
                         deps=(trt + wld5) if c == 0 else [tmm], start=(c == 0), stop=(c == NCH - 1))
            slot = VWB + (oc % 2) * WIN
            dd = [last_use[slot]] if slot in last_use else []
            te = act(bb(slot, slot + WIN), pb[bi][:, :], AF.Identity, deps=[tmm] + dd)
            bank_done(bi, te)
            vslot = VTB + (oc % 2) * WIN
            dv = [last_use[vslot]] if vslot in last_use else []
            ttr = [p.dma_tr(bb(vslot + i * 128, vslot + (i + 1) * 128),
                            bb(slot + i * 128, slot + (i + 1) * 128), deps=[te] + dv) for i in range(4)]
            last_use[slot] = ttr[-1]
            tst = p.dma(vth[oc, w * 4:(w + 1) * 4].rearrange("i p d -> p i d"),
                        bb(vslot, vslot + WIN).rearrange("p (i d) -> p i d", d=128), deps=ttr)
            last_use[vslot] = tst
            wtoks.append(tst)
        prev_w = wtoks

    P5bB = phase()
    chunk_done = {-1: P5bB, -2: P5bB}
    load_toks = {}

    def p5_loads(c):
        s = c % 2
        dd = list(chunk_done[c - 2])
        tq_ = p.dma(bb(QTC + s * 4096, QTC + (s + 1) * 4096), qth[c], deps=dd)
        tk_ = p.dma(bb(KTC + s * 4096, KTC + (s + 1) * 4096), kth[c], deps=dd)
        svo = SVC + s * 4160
        tms = vmemset(bb(svo, svo + 4160), 1.0, deps=dd)
        tvs = []
        for t in range(32):
            tvs.append(p.dma(bb(svo + t * 130, svo + t * 130 + 64), vth[c, t][:, 0:64], deps=[tms]))
            tvs.append(p.dma(bb(svo + t * 130 + 65, svo + t * 130 + 129), vth[c, t][:, 64:128], deps=[tms]))
        load_toks[c] = [tq_, tk_] + tvs

    p5_loads(0)
    lrow_free = {0: None, 1: None}
    SCB = [0, 1, 5, 6]
    sc_free = {0: None, 1: None, 2: None, 3: None}
    pt_free = {0: None, 1: None, 2: None, 3: None}
    for c in range(NCH):
        if c + 1 < NCH:
            p5_loads(c + 1)
        s = c % 2
        svo = SVC + s * 4160
        ctoks = []
        po_free = {0: (chunk_done[c - 1] if c > 0 else P5bB), 1: (chunk_done[c - 1] if c > 0 else P5bB)}
        for qw in range(NW):
            nkt = 4 * (qw + 1)
            tpv_l = {}
            pend = None
            for kt in range(nkt + 1):
                cur = []
                if kt < nkt:
                    for ha in range(2):
                        hoff = ha * 64
                        tp = (64, 0) if ha else None
                        sb_ = (kt % 2) * 2 + ha
                        tsc = mm(SCB[sb_], BB[hoff:hoff + 64, KTC + s * 4096 + kt * 128: KTC + s * 4096 + (kt + 1) * 128],
                                 BB[hoff:hoff + 64, QTC + s * 4096 + qw * WIN: QTC + s * 4096 + (qw + 1) * WIN],
                                 deps=load_toks[c] + ([sc_free[sb_]] if sc_free[sb_] else []), tp=tp)
                        dd = [tsc] + ([pt_free[sb_]] if pt_free[sb_] else [])
                        te = act(bb(PT + sb_ * WIN, PT + (sb_ + 1) * WIN), pb[SCB[sb_]][:, :], AF.Exp,
                                 deps=dd, scale=0.125, bias=mshiftc)
                        sc_free[sb_] = te
                        if kt >= 4 * qw:
                            i = kt - 4 * qw
                            te = vtt(bb(PT + sb_ * WIN, PT + (sb_ + 1) * WIN),
                                     bb(PT + sb_ * WIN, PT + (sb_ + 1) * WIN),
                                     cmb[:, i * WIN:(i + 1) * WIN], ALU.mult, deps=[te])
                        cur.append((ha, sb_, te))
                if pend is not None:
                    pkt = kt - 1
                    for ha, sb_, te in pend:
                        tpv_ = mm(2 + ha, bb(svo + pkt * 130 + ha * 65, svo + pkt * 130 + (ha + 1) * 65),
                                  bb(PT + sb_ * WIN, PT + (sb_ + 1) * WIN),
                                  deps=[te] + (list(po_free[ha]) if pkt == 0 else []),
                                  start=(pkt == 0), stop=(pkt == nkt - 1), psl=(0, 65, 0, WIN))
                        pt_free[sb_] = tpv_
                        tpv_l[ha] = tpv_
                pend = cur
            tno_l = {}
            for ha in range(2):
                po = 2 + ha
                lo = ha * WIN
                tlc = vcopy(lrow[64:65, lo:lo + WIN], pb[po][64:65, 0:WIN],
                            deps=[tpv_l[ha]] + ([lrow_free[ha]] if lrow_free[ha] else []))
                tsh = p.dma(lrow[0:1, lo:lo + WIN], lrow[64:65, lo:lo + WIN], deps=[tlc])
                trc = vrecip(lrow[0:1, lo:lo + WIN], lrow[0:1, lo:lo + WIN], deps=[tsh])
                bb_ = 4 + ha * 3
                tbc_ = mm(bb_, cHPR[0:1, 0:64], lrow[0:1, lo:lo + WIN], deps=[trc] + list(pfree[bb_]),
                          psl=(0, 64, 0, WIN))
                tev = act(lrow[0:64, lo:lo + WIN], pb[po][0:64, 0:WIN], AF.Identity, deps=[tbc_])
                AT = ATA if ha == 0 else ATB
                tno = vtt(BB[0:64, AT + s * 4096 + qw * WIN: AT + s * 4096 + (qw + 1) * WIN],
                          lrow[0:64, lo:lo + WIN], pb[bb_][0:64, 0:WIN], ALU.mult, deps=[tev])
                pfree[bb_] = (tno,)
                po_free[ha] = [tev]
                lrow_free[ha] = tno
                ctoks.append(tno)
        tsa = p.dma(ath[c, 0:64, :], BB[0:64, ATA + s * 4096: ATA + (s + 1) * 4096], deps=ctoks)
        tsb = p.dma(ath[c, 64:128, :], BB[0:64, ATB + s * 4096: ATB + (s + 1) * 4096], deps=ctoks)
        chunk_done[c] = [tsa, tsb] + ctoks

    P5wB = phase()
    wldo = [p.dma(bb(SWO + c * 1024, SWO + (c + 1) * 1024), Wo_h[c * 128:(c + 1) * 128, :],
                  queue="gpsimd", deps=P5wB) for c in range(NCH)]
    prev_w = P5wB
    for w in range(NW):
        tla = [p.dma(bb(AWIN + c * WIN, AWIN + (c + 1) * WIN), ath[c][:, w * WIN:(w + 1) * WIN],
                     deps=prev_w) for c in range(NCH)]
        evs = []
        for oc in range(NCH):
            bo = nb()
            tmm = None
            for c in range(NCH):
                tmm = mm(bo, bb(SWO + c * 1024 + oc * 128, SWO + c * 1024 + (oc + 1) * 128),
                         bb(AWIN + c * WIN, AWIN + (c + 1) * WIN),
                         deps=(tla + wldo) if c == 0 else [tmm], start=(c == 0), stop=(c == NCH - 1))
            te = act(f(OBF + oc * WIN, OBF + (oc + 1) * WIN), pb[bo][:, :], AF.Identity, deps=[tmm])
            bank_done(bo, te)
            evs.append(te)
        ts_, adds = out_window(w, evs, x2h, x3h, prev_w)
        prev_w = [ts_] + adds

    P6B = phase()
    wb_toks = []
    tprev = None
    for ri in range(2):
        WTo = WTR if ri == 0 else WTI
        for r in range(4):
            for cc in range(4):
                slot = cc % 2
                uo = UVU if slot == 0 else UVU2
                vo = UVV if slot == 0 else UVV2
                dd = [tprev] if tprev else list(P6B)
                tU = p.dma(f(uo, uo + 512), uvU_h[ri, r, cc], deps=dd)
                tV = p.dma(f(vo, vo + 512), uvV_h[ri, r, cc], deps=dd)
                if cc == 0:
                    tprev = vtt(f(FACC, FACC + 512), f(uo, uo + 512), f(vo, vo + 512), ALU.mult,
                                deps=[tU, tV])
                else:
                    tm_ = vtt(f(FTMP, FTMP + 512), f(uo, uo + 512), f(vo, vo + 512), ALU.mult,
                              deps=[tU, tV, tprev])
                    tprev = vtt(f(FACC, FACC + 512), f(FACC, FACC + 512), f(FTMP, FTMP + 512),
                                ALU.add, deps=[tm_])
            tw = act(bb(WTo + r * 512, WTo + (r + 1) * 512), f(FACC, FACC + 512), AF.Identity,
                     deps=[tprev])
            tprev = tw
            wb_toks.append(tw)

    prev_w = P6B
    final_stores = []
    for w in range(NW):
        tl, apl = ln_window(x3h, w * WIN, prev_w + wb_toks, XHAT6)
        tts = tr_window(XHAT6, lambda c, i: bb(XT6 + c * WIN + i * 128, XT6 + c * WIN + (i + 1) * 128),
                        apl, deps=prev_w)
        trt = [t for row in tts for t in row]
        tneg = [vts(bb(XT6N + k * WIN, XT6N + (k + 1) * WIN), bb(XT6 + (4 + k) * WIN, XT6 + (5 + k) * WIN),
                    -1.0, None, ALU.mult, deps=[tts[i][4 + k] for i in range(4)]) for k in range(4)]
        evs = []
        for oc in range(NCH):
            bo = nb()
            tmm = None
            first = True
            for k in range(4):
                if oc < 4:
                    l1 = bb(WTR + k * 512 + oc * 128, WTR + k * 512 + (oc + 1) * 128)
                    l2 = bb(WTI + k * 512 + oc * 128, WTI + k * 512 + (oc + 1) * 128)
                    r1 = bb(XT6 + k * WIN, XT6 + (k + 1) * WIN)
                    r2 = bb(XT6N + k * WIN, XT6N + (k + 1) * WIN)
                else:
                    l1 = bb(WTI + k * 512 + (oc - 4) * 128, WTI + k * 512 + (oc - 3) * 128)
                    l2 = bb(WTR + k * 512 + (oc - 4) * 128, WTR + k * 512 + (oc - 3) * 128)
                    r1 = bb(XT6 + k * WIN, XT6 + (k + 1) * WIN)
                    r2 = bb(XT6 + (4 + k) * WIN, XT6 + (5 + k) * WIN)
                tmm = mm(bo, l1, r1, deps=(trt + wb_toks + tneg) if first else [tmm],
                         start=first, stop=False)
                first = False
                tmm = mm(bo, l2, r2, deps=[tmm], start=False, stop=(k == 3))
            te = act(f(OBF + oc * WIN, OBF + (oc + 1) * WIN), pb[bo][:, :], AF.Identity,
                     deps=[tmm], bias=sttbr[:, oc:oc + 1])
            bank_done(bo, te)
            evs.append(te)
        ts_, adds = out_window(w, evs, x3h, out_h, prev_w)
        prev_w = [ts_] + adds
        final_stores.append(ts_)

    vmemset(f(CONVT, CONVT + 1), 0.0, deps=final_stores)

    with nc.Block() as block:
        p.emit(block)
    return nc


def host_prep(inp):
    g = {k: np.ascontiguousarray(np.asarray(v, np.float32)) for k, v in inp.items()}
    d = {}
    d["gate_wt"] = np.ascontiguousarray(g["L0_attn_gate_w"].T)
    d["B_wt"] = np.ascontiguousarray(g["L0_attn_B_w"].T)
    d["C_wt"] = np.ascontiguousarray(g["L0_attn_C_w"].T)
    d["D_wt"] = np.ascontiguousarray(g["L0_attn_D_w"].T)
    d["B_b"] = g["L0_attn_B_b"].reshape(S, 1)
    d["C_b"] = g["L0_attn_C_b"].reshape(S, 1)
    d["D_b"] = np.ascontiguousarray(g["L0_attn_D_b"].reshape(NCH, 128).T)
    d["gate_b"] = np.ascontiguousarray(g["L0_attn_gate_b"].reshape(2 * NCH, 128).T)
    d["conv_taps"] = g["L0_attn_conv_w"].reshape(D, 3).reshape(NCH, 128, 3)
    d["conv_b"] = np.ascontiguousarray(g["L0_attn_conv_b"].reshape(NCH, 128).T)
    d["A_m"] = g["L0_attn_A"]
    d["A_mt"] = np.ascontiguousarray(g["L0_attn_A"].T)
    d["dtA"] = np.asarray(g["L0_attn_dt_A"], np.float32).reshape(1, 1)
    d["g_col"] = g["L0_attn_ln_st_g"].reshape(S, 1)
    d["beta_col"] = g["L0_attn_ln_st_b"].reshape(S, 1)
    d["g_rep64"] = np.ascontiguousarray(np.tile(g["L0_attn_ln_st_g"].reshape(1, S), (S, 1)))
    d["lngc"] = np.ascontiguousarray(g["L0_attn_ln_in_g"].reshape(NCH, 128).T)
    d["lnbc"] = np.ascontiguousarray(g["L0_attn_ln_in_b"].reshape(NCH, 128).T)
    d["Pc"] = (np.eye(S) - 1.0 / S).astype(np.float32)
    d["ident64"] = np.eye(S, dtype=np.float32)
    d["ident128"] = np.eye(128, dtype=np.float32)
    consts = np.zeros((128, 8), np.float32)
    consts[:, 0] = EPS
    consts[:, 1] = EPS * EPS
    consts[:, 2] = -SHIFT
    d["consts"] = consts
    d["onesdiv"] = np.full((S, S), 1.0 / S, np.float32)
    d["hp_rows"] = np.ones((1, 128), np.float32)
    cmask = np.zeros((4, 128, WIN), np.float32)
    for i in range(4):
        for kp in range(128):
            cmask[i, kp, i * 128 + kp:] = 1.0
    d["cmask"] = cmask.astype(BF)
    d["moe_w1"] = g["L0_ffn_w1"]
    d["moe_w2"] = g["L0_ffn_w2"]
    d["moe_w3"] = g["L0_ffn_w3"]
    d["ws1"] = g["L0_ffn_ws1"]
    d["ws2"] = g["L0_ffn_ws2"]
    d["ws3"] = g["L0_ffn_ws3"]
    d["wr"] = g["L0_ffn_wr"]
    d["Wq_t"] = np.ascontiguousarray(g["L1_attn_Wq"].T)
    d["Wk_t"] = np.ascontiguousarray(g["L1_attn_Wk"].T)
    d["Wv_t"] = np.ascontiguousarray(g["L1_attn_Wv"].T)
    d["Wo_t"] = np.ascontiguousarray(g["L1_attn_Wo"].T)
    uvU = np.zeros((2, 4, 4, 128, 512), np.float32)
    uvV = np.zeros((2, 4, 4, 128, 512), np.float32)
    for ri, (G1k, G2k) in enumerate((("L1_ffn_G1_R", "L1_ffn_G2_R"), ("L1_ffn_G1_I", "L1_ffn_G2_I"))):
        for cc in range(4):
            Ac = g[G1k][0, :, cc, :]
            Bc = g[G2k][cc, :, 0, :]
            U = np.broadcast_to(Ac.T[:, None, :, None], (32, 16, 32, 16)).reshape(512, 512)
            V = np.broadcast_to(Bc.T[None, :, None, :], (32, 16, 32, 16)).reshape(512, 512)
            for r in range(4):
                uvU[ri, r, cc] = U[r * 128:(r + 1) * 128]
                uvV[ri, r, cc] = V[r * 128:(r + 1) * 128]
    d["uvU"] = uvU
    d["uvV"] = uvV
    d["tt_bias"] = np.ascontiguousarray(np.concatenate([g["L1_ffn_bias_real"], g["L1_ffn_bias_imag"]]).reshape(NCH, 128).T)
    return d


def kernel(**inp):
    nc = bass.Bass("TRN2", target_bir_lowering=False)
    build(nc, dbg=False)
    base = host_prep(inp)
    x = np.asarray(inp["L0_attn_x"], np.float32)
    in_maps = []
    for b in range(8):
        m = dict(base)
        m["x0"] = np.ascontiguousarray(x[b])
        in_maps.append(m)
    res = run_bass_kernel_spmd(nc, in_maps, core_ids=list(range(8)))
    return np.stack([np.asarray(res.results[i]["out"], np.float32) for i in range(8)])


# version 08
# speedup vs baseline: 1.5048x; 1.0519x over previous
import numpy as np
import ml_dtypes
import concourse.bass as bass
import concourse.mybir as mybir
from concourse.bass_utils import run_bass_kernel_spmd

F32 = mybir.dt.float32
BF16 = mybir.dt.bfloat16
AF = mybir.ActivationFunctionType
ALU = mybir.AluOpType
AX = mybir.AxisListType
BF = ml_dtypes.bfloat16

T = 4096
D = 1024
NCH = 8
WIN = 512
NW = 8
S = 64
KCH = 64
LCH = T // KCH
WARM = 256
STEPS = LCH + WARM
EPS = 1e-5
SHIFT = 12.0

NF = 18600
HIST = 0
OBF = 0
UVU, UVV, UVU2, UVV2, FACC, FTMP = 4096, 4608, 5120, 5632, 6144, 6656
SBMP = 9216
STAT = 13824
XWIN = 13888
CONVT = 17984
RTR = 18496

NB = 48200
XHT = 0
XHAT = 8224
XN2 = 12320
XGW = 20512
SGV = 24608
GSO = 25632
WBC = 26656
GWV = 27680
SCMB = 44064
YB = 0
GSL = 4096
SDW = 8224
XHAT4 = 0
XT4 = 4096
MW1, MW2, MW3 = 8192, 16384, 24576
WS1, WS2, WS3, SWR = 32768, 34816, 36864, 38912
HST = 38976
H2SC = 44096
XHAT5, XT5 = 0, 4096
WQ, WK, WV = 8192, 16384, 24576
QWB, KWB, VWB, VTB = 32768, 33792, 34816, 35840
QTC, KTC = 0, 8192
SVC = 16384
PT = 24704
ATA, ATB = 26752, 34944
AWIN, SWO = 0, 4096
XHAT6, XT6, XT6N, WTR, WTI = 0, 4096, 8192, 10240, 12288


class Prog:
    def __init__(self, nc):
        self.nc = nc
        self.engines = ("tensor", "vector", "scalar", "sync", "gpsimd")
        self.ops = {e: [] for e in self.engines}
        self.semname = {"tensor": "s_t", "vector": "s_v", "scalar": "s_s",
                        "sync": "s_y", "gpsimd": "s_g"}
        self.sems = {}
        self.cnt = {}
        for s in self.semname.values():
            self.sems[s] = nc.semaphore(s).__enter__()
            self.cnt[s] = 0

    def op(self, engine, fn, deps=(), inc=1):
        sem = self.semname[engine]
        waits = {}
        for dd in deps:
            if dd is None:
                continue
            sn, val = dd
            waits[sn] = max(waits.get(sn, 0), val)
        self.cnt[sem] += inc
        tok = (sem, self.cnt[sem])
        self.ops[engine].append((fn, sorted(waits.items()), (sem, inc)))
        return tok

    def dma(self, out, in_, deps=(), queue="sync"):
        return self.op(queue, lambda e, o=out, i=in_: e.dma_start(out=o, in_=i),
                       deps=deps, inc=16)

    def dma_tr(self, out, in_, deps=(), queue="scalar"):
        return self.op(queue, lambda e, o=out, i=in_: e.dma_start(out=o, in_=i, transpose=True),
                       deps=deps, inc=16)

    def barrier(self):
        return [(self.semname[e], self.cnt[self.semname[e]]) for e in self.engines]

    def emit(self, block):
        def make(engine):
            def body(eng):
                for fn, waits, inc in self.ops[engine]:
                    for sn, val in waits:
                        eng.wait_ge(self.sems[sn], val)
                    inst = fn(eng)
                    sn, n = inc
                    inst.then_inc(self.sems[sn], n)
            return body
        block.tensor(make("tensor"))
        block.vector(make("vector"))
        block.scalar(make("scalar"))
        block.sync(make("sync"))
        block.gpsimd(make("gpsimd"))


def build(nc, dbg=False):
    dram = lambda name, shape, d=F32, kind="ExternalInput": nc.dram_tensor(name, shape, d, kind=kind).ap()
    x0 = dram("x0", [T, D])
    gate_wt = dram("gate_wt", [D, 2 * D])
    B_wt = dram("B_wt", [D, S]); C_wt = dram("C_wt", [D, S]); D_wt_h = dram("D_wt", [S, D])
    Bb_h = dram("B_b", [S, 1]); Cb_h = dram("C_b", [S, 1]); Db_h = dram("D_b", [128, NCH])
    gb_h = dram("gate_b", [128, 2 * NCH])
    taps_h = dram("conv_taps", [NCH, 128, 3])
    cvb_h = dram("conv_b", [128, NCH])
    A_h = dram("A_m", [S, S]); At_h = dram("A_mt", [S, S]); dtA_h = dram("dtA", [1, 1])
    g_h = dram("g_col", [S, 1]); beta_h = dram("beta_col", [S, 1]); grep_h = dram("g_rep64", [S, S])
    lngc_h = dram("lngc", [128, NCH]); lnbc_h = dram("lnbc", [128, NCH])
    Pc_h = dram("Pc", [S, S]); id_h = dram("ident64", [S, S])
    cst_h = dram("consts", [128, 8])
    od_h = dram("onesdiv", [S, S])
    hpr_h = dram("hp_rows", [128, 128])
    id128_h = dram("ident128", [128, 128])
    cm_h = dram("cmask", [4, 128, WIN], BF16)
    w1_h = dram("moe_w1", [8, D, 128]); w2_h = dram("moe_w2", [8, D, 128]); w3_h = dram("moe_w3", [8, 128, D])
    ws1_h = dram("ws1", [D, 256]); ws2_h = dram("ws2", [D, 256]); ws3_h = dram("ws3", [256, D]); wr_h = dram("wr", [D, 8])
    Wq_h = dram("Wq_t", [D, D]); Wk_h = dram("Wk_t", [D, D]); Wv_h = dram("Wv_t", [D, D]); Wo_h = dram("Wo_t", [D, D])
    uvU_h = dram("uvU", [2, 4, 4, 128, 512]); uvV_h = dram("uvV", [2, 4, 4, 128, 512])
    ttbr_h = dram("tt_bias", [128, NCH])
    out_h = dram("out", [T, D], kind="ExternalOutput")
    kd = "ExternalOutput" if dbg else "Internal"
    x1h = dram("x1h", [T, D], kind=kd)
    x2h = dram("x2h", [T, D], kind=kd)
    x3h = dram("x3h", [T, D], kind=kd)
    gatesh = dram("gatesh", [NCH, 128, T], d=BF16, kind=("ExternalOutput" if dbg else "Internal"))
    if dbg:
        dbpp_h = dram("dbg_bpp", [S, 4608], kind="ExternalOutput")
        dzh_h = dram("dbg_zh", [S, T], kind="ExternalOutput")
        dcm_h = dram("dbg_cm", [S, T], d=BF16, kind="ExternalOutput")
        dyb_h = dram("dbg_yb", [S, T], d=BF16, kind="ExternalOutput")
        dxht_h = dram("dbg_xht", [128, 8192], d=BF16, kind="ExternalOutput")
        dxn2_h = dram("dbg_xn2", [128, 4096], d=BF16, kind="ExternalOutput")
    qth = dram("qth", [NCH, 128, T], d=BF16, kind="Internal")
    kth = dram("kth", [NCH, 128, T], d=BF16, kind="Internal")
    vth = dram("vth", [NCH, 32, 128, 128], d=BF16, kind="Internal")
    ath = dram("ath", [NCH, 128, T], d=BF16, kind="Internal")

    sbuf = lambda name, shape, d=F32: nc.sbuf_tensor(name, shape, d).__enter__()

    cC = sbuf("cC", [128, 8])
    cOD = sbuf("cOD", [S, S])
    cID = sbuf("cID", [S, S]); cPc = sbuf("cPc", [S, S]); cID128 = sbuf("cID128", [128, 128])
    cA = sbuf("cA", [S, S]); cAt = sbuf("cAt", [S, S]); cdt = sbuf("cdt", [1, 1])
    cg = sbuf("cg", [S, 1]); cbeta = sbuf("cbeta", [S, 1]); cgrep = sbuf("cgrep", [S, S])
    cHPR = sbuf("cHPR", [128, 128])
    lngc = sbuf("lngc_s", [128, NCH]); lnbc = sbuf("lnbc_s", [128, NCH])
    staps = sbuf("staps", [128, NCH * 4])
    scvb = sbuf("scvb", [128, NCH])
    sBb = sbuf("sBb", [S, 1]); sCb = sbuf("sCb", [S, 1])
    sDb = sbuf("sDb", [128, NCH]); sgbias = sbuf("sgbias", [128, 2 * NCH])
    sttbr = sbuf("sttbr", [128, NCH])
    scanM = sbuf("scanM", [S, S])
    bacol = sbuf("bacol", [S, 1])
    eE = sbuf("eE", [S, S]); eF = sbuf("eF", [S, S]); eH = sbuf("eH", [S, S])
    eAd = sbuf("eAd", [S, S]); eAdt = sbuf("eAdt", [S, S])
    eT1 = sbuf("eT1", [S, S]); eT2 = sbuf("eT2", [S, S])
    sdtc = sbuf("sdtc", [S, 1])
    rowtmp = sbuf("rowtmp", [1, S])
    cmb = sbuf("cmb", [128, 4 * WIN], BF16)
    szh2t = sbuf("szh2t", [S, LCH])
    hcell = sbuf("hcell", [128, 64], BF16)
    craw = [sbuf("craw0", [S, KCH])]
    csqb = [sbuf("csqb0", [S, KCH])]
    ztp = [sbuf(f"ztp{h}", [S, KCH]) for h in range(2)]
    wfT = sbuf("wfT", [1, 8 * WIN])
    lrow = sbuf("lrow", [128, 2 * WIN])

    FP = sbuf("FP", [128, NF])
    BB = sbuf("BB", [128, NB], BF16)
    f = lambda a, b: FP[:, a:b]
    f6 = lambda a, b: FP[0:S, a:b]
    bb = lambda a, b: BB[:, a:b]
    b6 = lambda a, b: BB[0:S, a:b]

    p = Prog(nc)
    pb = [nc.psum_tensor(f"pb{i}", [128, 512], F32).__enter__() for i in range(8)]
    pfree = [()] * 8
    _rr = [0]

    def nb():
        i = _rr[0]; _rr[0] = (i + 1) % 8
        return i

    def bank_done(i, tok):
        pfree[i] = (tok,)

    def mm(bi, lhsT, rhs, deps=(), start=True, stop=True, tp=None, psl=None):
        dst = pb[bi][psl[0]:psl[1], psl[2]:psl[3]] if psl else pb[bi][:, :]
        dd = (list(pfree[bi]) if start else []) + list(deps)
        return p.op("tensor", lambda e, o=dst, l=lhsT, r=rhs, s0=start, s1=stop, t_=tp: e.matmul(
            o, l, r, start=s0, stop=s1, tile_position=t_, skip_group_check=True), deps=dd)

    def ptr(bi, cols, src, deps=(), rows=128):
        dst = pb[bi][0:rows, cols[0]:cols[1]]
        dd = list(pfree[bi]) + list(deps)
        return p.op("tensor", lambda e, o=dst, i=src: e.transpose(o, i, cID128[:, :]), deps=dd)

    def act(out, in_, func, deps=(), bias=None, scale=1.0):
        if bias is None:
            return p.op("scalar", lambda e, o=out, i=in_, sc=scale, fu=func: e.activation(
                out=o, in_=i, func=fu, scale=sc), deps=deps)
        return p.op("scalar", lambda e, o=out, i=in_, b=bias, sc=scale, fu=func: e.activation(
            out=o, in_=i, func=fu, bias=b, scale=sc), deps=deps)

    def vtt(out, a, b2, opn, deps=()):
        return p.op("vector", lambda e, o=out, x=a, y=b2, oo=opn: e.tensor_tensor(o, x, y, oo),
                    deps=deps)

    def vts(out, in0, s1, s2, op0, op1=None, deps=()):
        if s2 is None:
            return p.op("vector", lambda e, o=out, i=in0, a=s1, oo=op0: e.tensor_scalar(
                out=o, in0=i, scalar1=a, scalar2=None, op0=oo), deps=deps)
        return p.op("vector", lambda e, o=out, i=in0, a=s1, b2=s2, oo=op0, o1=op1: e.tensor_scalar(
            out=o, in0=i, scalar1=a, scalar2=b2, op0=oo, op1=o1), deps=deps)

    def vstt(out, in0, s1, in1, op0, op1, deps=()):
        return p.op("vector", lambda e, o=out, i=in0, a=s1, j=in1, oo=op0, o1=op1:
                    e.scalar_tensor_tensor(out=o, in0=i, scalar=a, in1=j, op0=oo, op1=o1),
                    deps=deps)

    def vcopy(out, in_, deps=()):
        return p.op("vector", lambda e, o=out, i=in_: e.tensor_copy(o, i), deps=deps)

    def vred(out, in_, opn, deps=()):
        return p.op("vector", lambda e, o=out, i=in_, oo=opn: e.tensor_reduce(
            out=o, in_=i, axis=AX.X, op=oo), deps=deps)

    def vrecip(out, in_, deps=()):
        return p.op("vector", lambda e, o=out, i=in_: e.reciprocal(out=o, in_=i), deps=deps)

    def vmemset(ap_, val, deps=()):
        return p.op("vector", lambda e, o=ap_, v=val: e.memset(o, v), deps=deps)

    def phase():
        bar = p.barrier()
        for i in range(8):
            pfree[i] = tuple(bar)
        return bar

    epsc = cC[:, 0:1]; eps2c = cC[:, 1:2]; mshiftc = cC[:, 2:3]

    ld = [p.dma(cC[:, :], cst_h), p.dma(cOD[:, :], od_h),
          p.dma(cID[:, :], id_h), p.dma(cPc[:, :], Pc_h), p.dma(cID128[:, :], id128_h),
          p.dma(cA[:, :], A_h), p.dma(cAt[:, :], At_h), p.dma(cdt[:, :], dtA_h),
          p.dma(cg[:, :], g_h), p.dma(cbeta[:, :], beta_h), p.dma(cgrep[:, :], grep_h),
          p.dma(cHPR[:, :], hpr_h),
          p.dma(lngc[:, :], lngc_h), p.dma(lnbc[:, :], lnbc_h),
          p.dma(staps.ap().rearrange("p (c k) -> p c k", k=4)[:, :, 0:3], taps_h.rearrange("c p k -> p c k")),
          p.dma(scvb[:, :], cvb_h),
          p.dma(sBb[:, :], Bb_h), p.dma(sCb[:, :], Cb_h),
          p.dma(sDb[:, :], Db_h),
          p.dma(sgbias[:, :], gb_h),
          p.dma(sttbr[:, :], ttbr_h),
          p.dma(cmb.ap().rearrange("p (j n) -> p j n", n=WIN), cm_h.rearrange("j p n -> p j n"))]

    t_t1 = []
    for c in range(NCH):
        t_t1.append(vts(staps[:, c * 4 + 3:c * 4 + 4], staps[:, c * 4 + 1:c * 4 + 2],
                        1.0, None, ALU.add, deps=ld))

    tok = mm(0, cOD[0:1, :], cdt[:, :], deps=ld, psl=(0, S, 0, 1))
    t_dt = vts(sdtc[:, :], pb[0][0:S, 0:1], float(S) * (2.0 ** -8), None, ALU.mult, deps=[tok])
    bank_done(0, t_dt)
    tA1 = vts(eAd[:, :], cA[:, :], sdtc[:, :], None, ALU.mult, deps=[t_dt] + ld)
    tA2 = vts(eAdt[:, :], cAt[:, :], sdtc[:, :], None, ALU.mult, deps=[t_dt] + ld)

    def horner(Xts, Hout, tdep):
        tH = vcopy(Hout[:, :], cID[:, :], deps=tdep + ld)
        for k in range(6, 0, -1):
            tm_ = mm(1, Xts[:, :], Hout[:, :], deps=[tH], psl=(0, S, 0, S))
            te = act(eT1[:, :], pb[1][0:S, 0:S], AF.Identity, deps=[tm_], scale=1.0 / k)
            bank_done(1, te)
            tH = vtt(Hout[:, :], eT1[:, :], cID[:, :], ALU.add, deps=[te])
        return tH

    tE = horner(eAdt, eE, [tA1, tA2])
    tF = horner(eAd, eF, [tA1, tA2])
    for _ in range(8):
        tm1 = mm(1, eF[:, :], eE[:, :], deps=[tE, tF], psl=(0, S, 0, S))
        te1 = act(eT1[:, :], pb[1][0:S, 0:S], AF.Identity, deps=[tm1])
        bank_done(1, te1)
        tm2 = mm(2, eE[:, :], eF[:, :], deps=[tE, tF], psl=(0, S, 0, S))
        te2 = act(eT2[:, :], pb[2][0:S, 0:S], AF.Identity, deps=[tm2])
        bank_done(2, te2)
        tE = vcopy(eE[:, :], eT1[:, :], deps=[te1, tm2])
        tF = vcopy(eF[:, :], eT2[:, :], deps=[te2, tE])
    tM = vtt(eH[:, :], eF[:, :], cgrep[:, :], ALU.mult, deps=[tE, tF])
    tsm = mm(1, eH[:, :], cPc[:, :], deps=[tM], psl=(0, S, 0, S))
    tsme = act(scanM[:, :], pb[1][0:S, 0:S], AF.Identity, deps=[tsm])
    bank_done(1, tsme)
    tba = mm(2, cbeta[:, :], eE[:, :], deps=[tE], psl=(0, 1, 0, S))
    tbae = vcopy(rowtmp[:, :], pb[2][0:1, 0:S], deps=[tba])
    bank_done(2, tbae)
    tbat = p.op("tensor", lambda e: e.transpose(pb[3][0:S, 0:1], rowtmp[:, :], cID128[0:1, 0:1]),
                deps=[tbae] + list(pfree[3]))
    tbac = vcopy(bacol[:, :], pb[3][0:S, 0:1], deps=[tbat])
    bank_done(3, tbac)

    tgw = [p.dma(bb(GWV + c * 2048, GWV + (c + 1) * 2048), gate_wt[c * 128:(c + 1) * 128, :],
                 queue="gpsimd", deps=ld) for c in range(NCH)]
    tbc = []
    for c in range(NCH):
        tbc.append(p.dma(bb(WBC + c * 2 * S, WBC + c * 2 * S + S), B_wt[c * 128:(c + 1) * 128, :],
                         queue="gpsimd", deps=ld))
        tbc.append(p.dma(bb(WBC + c * 2 * S + S, WBC + (c + 1) * 2 * S), C_wt[c * 128:(c + 1) * 128, :],
                         queue="gpsimd", deps=ld))
    tz = [vmemset(hcell[:, c * 4:c * 4 + 1], 0.0) for c in range(NCH)]
    tzp = [vmemset(f6(SBMP, SBMP + WARM), 0.0), vmemset(f6(SBMP + WARM + T, SBMP + 4608), 0.0)]

    def ln_window(xsrc, woff, deps, xhat_off, bf16_out=True):
        tl = p.dma(f(XWIN, XWIN + 4 * D).rearrange("p (i d) -> p i d", d=D),
                   xsrc[woff:woff + WIN, :].rearrange("(i p) d -> p i d", p=128), deps=deps)
        apl = []
        for i in range(4):
            xt = f(XWIN + i * D, XWIN + (i + 1) * D)
            st12 = f(STAT + i * 16, STAT + i * 16 + 12)
            mv = f(STAT + i * 16 + 12, STAT + i * 16 + 14)
            sc = f(STAT + i * 16 + 14, STAT + i * 16 + 15)
            t1 = p.op("vector", lambda e, o=st12[:, 0:6], i_=xt[:, 0:512]: e.bn_stats(out=o, in_=i_),
                      deps=[tl])
            t2 = p.op("vector", lambda e, o=st12[:, 6:12], i_=xt[:, 512:1024]: e.bn_stats(out=o, in_=i_),
                      deps=[t1])
            t3 = p.op("vector", lambda e, o=mv, i_=st12: e.bn_aggr(out=o, in_=i_), deps=[t2])
            if bf16_out == "composite":
                t4 = act(sc, mv[:, 1:2], AF.Sqrt, deps=[t3], bias=eps2c, scale=1.0 + EPS)
            else:
                t4 = act(sc, mv[:, 1:2], AF.Sqrt, deps=[t3], bias=epsc, scale=1.0)
            t5 = vrecip(sc, sc, deps=[t4])
            t6 = vts(bb(xhat_off + i * D, xhat_off + (i + 1) * D), xt, mv[:, 0:1], sc,
                     ALU.subtract, ALU.mult, deps=[t5])
            apl.append(t6)
        return tl, apl

    def tr_window(xhat_off, dst_fn, apl, deps=()):
        tts = [[None] * NCH for _ in range(4)]
        for i in range(4):
            for c in range(NCH):
                tts[i][c] = p.dma_tr(dst_fn(c, i), bb(xhat_off + i * D + c * 128, xhat_off + i * D + (c + 1) * 128),
                                     deps=[apl[i]] + list(deps))
        return tts


    conv_done = {}
    aff_done = {}
    gates_done = {-1: []}
    bm_evac = []
    ln_prev = []

    def do_conv(w, deps_extra):
        s = w % 2
        xn2o = XN2 + s * 4096
        toks = []
        prev = None
        for c in range(NCH):
            base = XHT + (c * 2 + s) * WIN
            dat = lambda a, b: bb(base + a, base + b)
            hL = hcell[:, c * 4 + s * 2:c * 4 + s * 2 + 1]
            hR = hcell[:, c * 4 + s * 2 + 32:c * 4 + s * 2 + 33]
            cv = lambda a, b: f(4096 + c * WIN + a, 4096 + c * WIN + b)
            dd = aff_done[w] + deps_extra + gates_done.get(w - 2, [])
            t1 = vts(cv(0, WIN), dat(0, WIN), staps[:, c * 4 + 3:c * 4 + 4], None, ALU.mult, deps=dd)
            t2 = vstt(cv(1, WIN), dat(0, WIN - 1), staps[:, c * 4 + 0:c * 4 + 1],
                      cv(1, WIN), ALU.mult, ALU.add, deps=[t1])
            t2b = vstt(cv(0, 1), hL, staps[:, c * 4 + 0:c * 4 + 1],
                       cv(0, 1), ALU.mult, ALU.add, deps=[t2])
            t3 = vstt(cv(0, WIN - 1), dat(1, WIN), staps[:, c * 4 + 2:c * 4 + 3],
                      cv(0, WIN - 1), ALU.mult, ALU.add, deps=[t2b])
            t3b = vstt(cv(WIN - 1, WIN), hR, staps[:, c * 4 + 2:c * 4 + 3],
                       cv(WIN - 1, WIN), ALU.mult, ALU.add, deps=[t3])
            t4 = vts(bb(xn2o + c * WIN, xn2o + (c + 1) * WIN), cv(0, WIN),
                     scvb[:, c:c + 1], None, ALU.add, deps=[t3b])
            prev = t4
            toks.append(t4)
        conv_done[w] = toks
        return toks

    def do_gates(w):
        s = w % 2
        xn2o = XN2 + s * 4096
        ct = conv_done[w]
        igev = []
        prev_sg = {0: None, 1: None}
        for oc in range(NCH):
            bi = nb()
            tmm = None
            for c in range(NCH):
                tmm = mm(bi, bb(GWV + c * 2048 + 1024 + oc * 128, GWV + c * 2048 + 1024 + (oc + 1) * 128),
                         bb(xn2o + c * WIN, xn2o + (c + 1) * WIN),
                         deps=([ct[c]] + tgw) if c == 0 else [tmm, ct[c]], start=(c == 0), stop=(c == NCH - 1))
            sgo = SGV + (oc % 2) * WIN
            tsg = act(bb(sgo, sgo + WIN), pb[bi][:, :], AF.Sigmoid,
                      deps=[tmm] + ([prev_sg[oc % 2]] if prev_sg[oc % 2] else []),
                      bias=sgbias[:, NCH + oc:NCH + oc + 1])
            bank_done(bi, tsg)
            txg = vtt(bb(XGW + oc * WIN, XGW + (oc + 1) * WIN), bb(xn2o + oc * WIN, xn2o + (oc + 1) * WIN),
                      bb(sgo, sgo + WIN), ALU.mult, deps=[tsg])
            prev_sg[oc % 2] = txg
            igev.append(txg)
        gtoks = []
        prev_st = {0: None, 1: None}
        for oc in range(NCH):
            bi = nb()
            tmm = None
            for c in range(NCH):
                tmm = mm(bi, bb(GWV + c * 2048 + oc * 128, GWV + c * 2048 + (oc + 1) * 128),
                         bb(xn2o + c * WIN, xn2o + (c + 1) * WIN),
                         deps=([ct[c]] + tgw) if c == 0 else [tmm, ct[c]], start=(c == 0), stop=(c == NCH - 1))
            go = GSO + (oc % 2) * WIN
            tsg = act(bb(go, go + WIN), pb[bi][:, :], AF.Sigmoid,
                      deps=[tmm] + ([prev_st[oc % 2]] if prev_st[oc % 2] else []), bias=sgbias[:, oc:oc + 1])
            bank_done(bi, tsg)
            tst = p.dma(gatesh[oc, :, w * WIN:(w + 1) * WIN], bb(go, go + WIN), deps=[tsg])
            prev_st[oc % 2] = tst
            gtoks.append(tst)
        biB = nb(); biC = nb()
        tmB = None; tmC = None
        for c in range(NCH):
            tmC = mm(biC, bb(WBC + c * 2 * S + S, WBC + (c + 1) * 2 * S), bb(xn2o + c * WIN, xn2o + (c + 1) * WIN),
                     deps=([ct[c]] + tbc) if c == 0 else [tmC, ct[c]], start=(c == 0), stop=(c == NCH - 1),
                     psl=(0, S, 0, WIN))
        for c in range(NCH):
            tmB = mm(biB, bb(WBC + c * 2 * S, WBC + c * 2 * S + S), bb(XGW + c * WIN, XGW + (c + 1) * WIN),
                     deps=(igev + tbc) if c == 0 else [tmB], start=(c == 0), stop=(c == NCH - 1),
                     psl=(0, S, 0, WIN))
        tBe = act(f6(HIST + w * WIN, HIST + (w + 1) * WIN), pb[biB][0:S, 0:WIN], AF.Identity,
                  deps=[tmB], bias=sBb[:, :])
        bank_done(biB, tBe)
        tCe = act(b6(SCMB + w * WIN, SCMB + (w + 1) * WIN), pb[biC][0:S, 0:WIN], AF.Identity,
                  deps=[tmC], bias=sCb[:, :])
        bank_done(biC, tCe)
        bm_evac.extend([tBe, tCe])
        gates_done[w] = gtoks + [tBe, tCe]
        return gtoks

    halo_toks = {}
    for w in range(NW):
        ldeps = tz + t_t1 + [tA1] + (ln_prev if ln_prev else [])
        tl, apl = ln_window(x0, w * WIN, ldeps, XHAT, bf16_out="composite")
        hl = []
        if w >= 2:
            s_src = w % 2; s_dst = (w - 1) % 2
            for c in range(NCH):
                base = XHT + (c * 2 + s_src) * WIN
                hl.append(vcopy(hcell[:, c * 4 + s_dst * 2:c * 4 + s_dst * 2 + 1],
                                bb(base + WIN - 1, base + WIN),
                                deps=aff_done[w - 2] + conv_done.get(w - 2, [])))
        s = w % 2
        tts = tr_window(XHAT, lambda c, i, s=s: bb(XHT + (c * 2 + s) * WIN + i * 128,
                                                   XHT + (c * 2 + s) * WIN + (i + 1) * 128),
                        apl, deps=hl + conv_done.get(w - 2, []) + gates_done.get(w - 2, []))
        ln_prev = [t for row in tts for t in row]
        aff = []
        for c in range(NCH):
            base = XHT + (c * 2 + s) * WIN
            aff.append(vts(bb(base, base + WIN), bb(base, base + WIN),
                           lngc[:, c:c + 1], lnbc[:, c:c + 1], ALU.mult, ALU.add,
                           deps=[tts[i][c] for i in range(4)]))
        aff_done[w] = aff
        hr = []
        if w >= 1:
            s_dst = (w - 1) % 2
            for c in range(NCH):
                base = XHT + (c * 2 + s) * WIN
                hr.append(vcopy(hcell[:, c * 4 + s_dst * 2 + 32:c * 4 + s_dst * 2 + 33],
                                bb(base, base + 1), deps=[aff[c]]))
            halo_toks[w - 1] = hl + hr
            do_conv(w - 1, hl + hr)
            do_gates(w - 1)
            if dbg and w == 1:
                td1 = p.dma(dxht_h, bb(XHT, XHT + 8192), deps=aff_done[0] + conv_done[0])
                td2 = p.dma(dxn2_h, bb(XN2, XN2 + 4096), deps=conv_done[0])
                gates_done[0] = gates_done[0] + [td1, td2]
    hl = []
    s_dst = 1
    for c in range(NCH):
        base = XHT + (c * 2 + 0) * WIN
        hl.append(vcopy(hcell[:, c * 4 + s_dst * 2:c * 4 + s_dst * 2 + 1],
                        bb(base + WIN - 1, base + WIN),
                        deps=aff_done[6] + conv_done.get(6, [])))
        hl.append(vmemset(hcell[:, c * 4 + s_dst * 2 + 32:c * 4 + s_dst * 2 + 33], 0.0,
                          deps=conv_done.get(5, [])))
    do_conv(7, hl)
    do_gates(7)

    P2B = phase()
    tb2 = vts(f6(HIST + 1, HIST + T), f6(HIST + 1, HIST + T), bacol[:, :], None, ALU.add,
              deps=P2B)
    pps = []
    for i in range(NW):
        bi = nb()
        tm_ = mm(bi, cPc[:, :], f6(HIST + i * WIN, HIST + (i + 1) * WIN), deps=[tb2],
                 psl=(0, S, 0, WIN))
        te = act(f6(SBMP + WARM + i * WIN, SBMP + WARM + (i + 1) * WIN), pb[bi][0:S, 0:WIN],
                 AF.Identity, deps=[tm_])
        bank_done(bi, te)
        pps.append(te)
    SBW = FP[0:S, SBMP:SBMP + 4608].rearrange("p (j l) -> p j l", l=KCH)
    scan_end = []
    pA = [0, 1]
    pB = [2, 3]
    tV = tq = tM2 = tR = tZ = tM1 = None
    for t in range(STEPS):
        bAP = SBW[:, t // KCH: t // KCH + KCH, t % KCH]
        if t < WARM:
            zt = ztp[t % 2][:, :]
        else:
            zt = f6(HIST + (t - WARM) * KCH, HIST + (t - WARM + 1) * KCH)
        if t == 0:
            tV = vcopy(craw[0][:, :], bAP, deps=pps + P2B + tzp)
        else:
            tV = vtt(craw[0][:, :], pb[pA[t % 2]][0:S, 0:KCH], bAP, ALU.add, deps=[tM1])
        tq = vtt(csqb[0][:, :], craw[0][:, :], craw[0][:, :], ALU.mult, deps=[tV])
        tM2 = mm(pB[t % 2], cOD[:, :], csqb[0][:, :], deps=[tq] + ([tR] if t >= 2 else []),
                 psl=(0, S, 0, KCH))
        tR = act(csqb[0][:, :], pb[pB[t % 2]][0:S, 0:KCH],
                 AF.Sqrt, deps=[tM2], bias=epsc[0:S, :])
        tRc = vrecip(csqb[0][:, :], csqb[0][:, :], deps=[tR])
        tZ = vtt(zt, craw[0][:, :], csqb[0][:, :], ALU.mult, deps=[tRc])
        if t < STEPS - 1:
            tM1 = mm(pA[(t + 1) % 2], scanM[:, :], zt, deps=[tZ] + ([tV] if t >= 1 else []),
                     psl=(0, S, 0, KCH))
        else:
            scan_end.append(tZ)

    if dbg:
        p.dma(dbpp_h, f6(SBMP, SBMP + 4608), deps=scan_end)
        p.dma(dzh_h, f6(HIST, HIST + T), deps=scan_end)
        p.dma(dcm_h, b6(SCMB, SCMB + T), deps=scan_end)

    P3B = phase()
    HISTV = FP[0:S, HIST:HIST + T].rearrange("p (t k) -> p t k", k=KCH)
    ty_prev = None
    ty_toks = []
    for j in range(KCH):
        zsrc = HISTV[:, :, j]
        t1 = vts(szh2t[:, 0:LCH], zsrc, cg[:, :], cbeta[:, :], ALU.mult, ALU.add,
                 deps=P3B + ([ty_prev] if ty_prev else []))
        t2 = vtt(b6(YB + j * LCH, YB + (j + 1) * LCH), szh2t[:, 0:LCH],
                 b6(SCMB + j * LCH, SCMB + (j + 1) * LCH), ALU.mult, deps=[t1])
        ty_prev = t2
        ty_toks.append(t2)
    tDld = p.dma(b6(SDW, SDW + D), D_wt_h, queue="gpsimd", deps=P3B)

    def out_window(w, obf_toks, xsrc, xdst, extra_ld_deps):
        tl = p.dma(f(XWIN, XWIN + 4 * D).rearrange("p (i d) -> p i d", d=D),
                   xsrc[w * WIN:(w + 1) * WIN, :].rearrange("(i p) d -> p i d", p=128),
                   deps=extra_ld_deps)
        adds = []
        for i in range(4):
            for oc in range(NCH):
                bi = nb()
                tt = ptr(bi, (0, 128), f(OBF + oc * WIN + i * 128, OBF + oc * WIN + (i + 1) * 128),
                         deps=obf_toks)
                ta = vtt(f(XWIN + i * D + oc * 128, XWIN + i * D + (oc + 1) * 128),
                         f(XWIN + i * D + oc * 128, XWIN + i * D + (oc + 1) * 128),
                         pb[bi][0:128, 0:128], ALU.add, deps=[tt, tl])
                bank_done(bi, ta)
                adds.append(ta)
        ts_ = p.dma(xdst[w * WIN:(w + 1) * WIN, :].rearrange("(i p) d -> p i d", p=128),
                    f(XWIN, XWIN + 4 * D).rearrange("p (i d) -> p i d", d=D), deps=adds)
        return ts_, adds

    prev_st = []
    tgl_prev = None
    for w in range(NW):
        tgl = p.dma(bb(GSL, GSL + 4096).rearrange("p (c n) -> p c n", n=WIN),
                    gatesh[:, :, w * WIN:(w + 1) * WIN].rearrange("c p n -> p c n"),
                    deps=P3B + ([tgl_prev] if tgl_prev else []))
        evs = []
        for oc in range(NCH):
            bi = nb()
            tm_ = mm(bi, b6(SDW + oc * 128, SDW + (oc + 1) * 128), b6(YB + w * WIN, YB + (w + 1) * WIN),
                     deps=ty_toks + [tDld], psl=(0, 128, 0, WIN))
            te = act(f(OBF + oc * WIN, OBF + (oc + 1) * WIN), pb[bi][:, 0:WIN], AF.Identity,
                     deps=[tm_] + prev_st, bias=sDb[:, oc:oc + 1])
            tg = vtt(f(OBF + oc * WIN, OBF + (oc + 1) * WIN), f(OBF + oc * WIN, OBF + (oc + 1) * WIN),
                     bb(GSL + oc * WIN, GSL + (oc + 1) * WIN), ALU.mult, deps=[te, tgl])
            bank_done(bi, tg)
            evs.append(tg)
        ts_, adds = out_window(w, evs, x0, x1h, prev_st)
        tgl_prev = evs[-1]
        prev_st = [ts_] + adds

    if dbg:
        p.dma(dyb_h, b6(YB, YB + T), deps=ty_toks + prev_st)

    P4B = phase()
    wld = []
    for e_i in range(8):
        for c in range(NCH):
            wld.append(p.dma(bb(MW1 + e_i * 1024 + c * 128, MW1 + e_i * 1024 + (c + 1) * 128),
                             w1_h[e_i, c * 128:(c + 1) * 128, :], queue="gpsimd", deps=P4B))
            wld.append(p.dma(bb(MW2 + e_i * 1024 + c * 128, MW2 + e_i * 1024 + (c + 1) * 128),
                             w2_h[e_i, c * 128:(c + 1) * 128, :], queue="gpsimd", deps=P4B))
        wld.append(p.dma(bb(MW3 + e_i * 1024, MW3 + (e_i + 1) * 1024), w3_h[e_i], queue="gpsimd", deps=P4B))
    for c in range(NCH):
        wld.append(p.dma(bb(WS1 + c * 256, WS1 + (c + 1) * 256), ws1_h[c * 128:(c + 1) * 128, :],
                         queue="gpsimd", deps=P4B))
        wld.append(p.dma(bb(WS2 + c * 256, WS2 + (c + 1) * 256), ws2_h[c * 128:(c + 1) * 128, :],
                         queue="gpsimd", deps=P4B))
        wld.append(p.dma(bb(SWR + c * 8, SWR + (c + 1) * 8), wr_h[c * 128:(c + 1) * 128, :],
                         queue="gpsimd", deps=P4B))
    for k in range(2):
        wld.append(p.dma(bb(WS3 + k * 1024, WS3 + (k + 1) * 1024), ws3_h[k * 128:(k + 1) * 128, :],
                         queue="gpsimd", deps=P4B))

    prev_w = P4B
    for w in range(NW):
        tl, apl = ln_window(x1h, w * WIN, prev_w, XHAT4)
        tts = tr_window(XHAT4, lambda c, i: bb(XT4 + c * WIN + i * 128, XT4 + c * WIN + (i + 1) * 128),
                        apl, deps=prev_w)
        trt = [t for row in tts for t in row]
        wf_toks = []
        for i in range(4):
            bi = nb()
            tmm = None
            for c in range(NCH):
                tmm = mm(bi, bb(XT4 + c * WIN + i * 128, XT4 + c * WIN + (i + 1) * 128),
                         bb(SWR + c * 8, SWR + (c + 1) * 8),
                         deps=(trt + wld) if c == 0 else [tmm], start=(c == 0), stop=(c == NCH - 1),
                         psl=(0, 128, 0, 8))
            pr = f(RTR + i * 24, RTR + i * 24 + 8)
            rx = f(RTR + i * 24 + 8, RTR + i * 24 + 16)
            mx = f(RTR + i * 24 + 16, RTR + i * 24 + 17)
            ms = f(RTR + i * 24 + 17, RTR + i * 24 + 18)
            m1 = f(RTR + i * 24 + 18, RTR + i * 24 + 19)
            m2 = f(RTR + i * 24 + 19, RTR + i * 24 + 20)
            t1 = vred(mx, pb[bi][:, 0:8], ALU.max, deps=[tmm])
            t2 = vts(mx, mx, -1.0, None, ALU.mult, deps=[t1])
            t3 = act(pr, pb[bi][:, 0:8], AF.Exp, deps=[t2], bias=mx)
            bank_done(bi, t3)
            t4 = vred(ms, pr, ALU.add, deps=[t3])
            t5 = vrecip(ms, ms, deps=[t4])
            t6 = vts(pr, pr, ms, None, ALU.mult, deps=[t5])
            t7 = vred(m1, pr, ALU.min, deps=[t6])
            t8 = vts(rx, pr, m1, None, ALU.is_le, deps=[t7])
            t9 = vstt(rx, rx, 2.0, pr, ALU.mult, ALU.add, deps=[t8])
            t11 = vred(m2, rx, ALU.min, deps=[t9])
            t12 = vts(rx, pr, m2, None, ALU.is_gt, deps=[t11])
            t13 = vtt(rx, rx, pr, ALU.mult, deps=[t12])
            t14 = vred(ms, rx, ALU.add, deps=[t13])
            t15 = vrecip(ms, ms, deps=[t14])
            t16 = vts(rx, rx, ms, None, ALU.mult, deps=[t15])
            for e_i in range(8):
                bj = nb()
                t17 = ptr(bj, (0, 128), rx[:, e_i:e_i + 1], deps=[t16], rows=1)
                t18 = vcopy(wfT[0:1, e_i * WIN + i * 128:e_i * WIN + (i + 1) * 128],
                            pb[bj][0:1, 0:128], deps=[t17])
                bank_done(bj, t18)
                wf_toks.append(t18)
        hw_toks = {}
        prev_h2 = {0: None, 1: None}
        for e_i in range(10):
            bi1 = nb(); bi2 = nb()
            tm1 = tm2 = None
            for c in range(NCH):
                if e_i < 8:
                    l1 = bb(MW1 + e_i * 1024 + c * 128, MW1 + e_i * 1024 + (c + 1) * 128)
                    l2 = bb(MW2 + e_i * 1024 + c * 128, MW2 + e_i * 1024 + (c + 1) * 128)
                else:
                    k = e_i - 8
                    l1 = bb(WS1 + c * 256 + k * 128, WS1 + c * 256 + (k + 1) * 128)
                    l2 = bb(WS2 + c * 256 + k * 128, WS2 + c * 256 + (k + 1) * 128)
                rhs = bb(XT4 + c * WIN, XT4 + (c + 1) * WIN)
                tm1 = mm(bi1, l1, rhs, deps=(trt + wld) if c == 0 else [tm1],
                         start=(c == 0), stop=(c == NCH - 1))
                tm2 = mm(bi2, l2, rhs, deps=(trt + wld) if c == 0 else [tm2],
                         start=(c == 0), stop=(c == NCH - 1))
            hslot = bb(HST + e_i * WIN, HST + (e_i + 1) * WIN)
            h2o = H2SC + (e_i % 2) * WIN
            ts1 = act(hslot, pb[bi1][:, :], AF.Silu, deps=[tm1])
            bank_done(bi1, ts1)
            ts2 = act(bb(h2o, h2o + WIN), pb[bi2][:, :], AF.Identity,
                      deps=[tm2] + ([prev_h2[e_i % 2]] if prev_h2[e_i % 2] else []))
            bank_done(bi2, ts2)
            th = vtt(hslot, hslot, bb(h2o, h2o + WIN), ALU.mult, deps=[ts1, ts2])
            prev_h2[e_i % 2] = th
            if e_i < 8:
                bj = nb()
                tbr = mm(bj, cHPR[0:1, :], wfT[0:1, e_i * WIN:(e_i + 1) * WIN], deps=wf_toks,
                         psl=(0, 128, 0, WIN))
                tw = vtt(hslot, hslot, pb[bj][:, :], ALU.mult, deps=[th, tbr])
                bank_done(bj, tw)
                hw_toks[e_i] = tw
            else:
                hw_toks[e_i] = th
        evs = []
        for oc in range(NCH):
            bo = nb()
            tmm = None
            first = True
            for k in range(2):
                tmm = mm(bo, bb(WS3 + k * 1024 + oc * 128, WS3 + k * 1024 + (oc + 1) * 128),
                         bb(HST + (8 + k) * WIN, HST + (9 + k) * WIN),
                         deps=([hw_toks[8], hw_toks[9]] + wld) if first else [tmm],
                         start=first, stop=False)
                first = False
            for e_i in range(8):
                tmm = mm(bo, bb(MW3 + e_i * 1024 + oc * 128, MW3 + e_i * 1024 + (oc + 1) * 128),
                         bb(HST + e_i * WIN, HST + (e_i + 1) * WIN),
                         deps=[tmm, hw_toks[e_i]], start=False, stop=(e_i == 7))
            te = act(f(OBF + oc * WIN, OBF + (oc + 1) * WIN), pb[bo][:, :], AF.Identity, deps=[tmm])
            bank_done(bo, te)
            evs.append(te)
        ts_, adds = out_window(w, evs, x1h, x2h, [])
        prev_w = [ts_] + adds

    P5aB = phase()
    wld5 = []
    for c in range(NCH):
        wld5.append(p.dma(bb(WQ + c * 1024, WQ + (c + 1) * 1024), Wq_h[c * 128:(c + 1) * 128, :],
                          queue="gpsimd", deps=P5aB))
        wld5.append(p.dma(bb(WK + c * 1024, WK + (c + 1) * 1024), Wk_h[c * 128:(c + 1) * 128, :],
                          queue="gpsimd", deps=P5aB))
        wld5.append(p.dma(bb(WV + c * 1024, WV + (c + 1) * 1024), Wv_h[c * 128:(c + 1) * 128, :],
                          queue="gpsimd", deps=P5aB))
    prev_w = P5aB
    last_use = {}
    for w in range(NW):
        tl, apl = ln_window(x2h, w * WIN, prev_w, XHAT5)
        tts = tr_window(XHAT5, lambda c, i: bb(XT5 + c * WIN + i * 128, XT5 + c * WIN + (i + 1) * 128),
                        apl, deps=prev_w)
        trt = [t for row in tts for t in row]
        wtoks = []
        for oc in range(NCH):
            for (Woff, BUF, hbm) in ((WQ, QWB, qth), (WK, KWB, kth)):
                bi = nb()
                tmm = None
                for c in range(NCH):
                    tmm = mm(bi, bb(Woff + c * 1024 + oc * 128, Woff + c * 1024 + (oc + 1) * 128),
                             bb(XT5 + c * WIN, XT5 + (c + 1) * WIN),
                             deps=(trt + wld5) if c == 0 else [tmm], start=(c == 0), stop=(c == NCH - 1))
                slot = BUF + (oc % 2) * WIN
                dd = [last_use[slot]] if slot in last_use else []
                te = act(bb(slot, slot + WIN), pb[bi][:, :], AF.Identity, deps=[tmm] + dd)
                bank_done(bi, te)
                tst = p.dma(hbm[oc, :, w * WIN:(w + 1) * WIN], bb(slot, slot + WIN), deps=[te])
                last_use[slot] = tst
                wtoks.append(tst)
            bi = nb()
            tmm = None
            for c in range(NCH):
                tmm = mm(bi, bb(WV + c * 1024 + oc * 128, WV + c * 1024 + (oc + 1) * 128),
                         bb(XT5 + c * WIN, XT5 + (c + 1) * WIN),
                         deps=(trt + wld5) if c == 0 else [tmm], start=(c == 0), stop=(c == NCH - 1))
            slot = VWB + (oc % 2) * WIN
            dd = [last_use[slot]] if slot in last_use else []
            te = act(bb(slot, slot + WIN), pb[bi][:, :], AF.Identity, deps=[tmm] + dd)
            bank_done(bi, te)
            vslot = VTB + (oc % 2) * WIN
            dv = [last_use[vslot]] if vslot in last_use else []
            ttr = [p.dma_tr(bb(vslot + i * 128, vslot + (i + 1) * 128),
                            bb(slot + i * 128, slot + (i + 1) * 128), deps=[te] + dv) for i in range(4)]
            last_use[slot] = ttr[-1]
            tst = p.dma(vth[oc, w * 4:(w + 1) * 4].rearrange("i p d -> p i d"),
                        bb(vslot, vslot + WIN).rearrange("p (i d) -> p i d", d=128), deps=ttr)
            last_use[vslot] = tst
            wtoks.append(tst)
        prev_w = wtoks

    P5bB = phase()
    chunk_done = {-1: P5bB, -2: P5bB}
    load_toks = {}

    def p5_loads(c):
        s = c % 2
        dd = list(chunk_done[c - 2])
        tq_ = p.dma(bb(QTC + s * 4096, QTC + (s + 1) * 4096), qth[c], deps=dd)
        tk_ = p.dma(bb(KTC + s * 4096, KTC + (s + 1) * 4096), kth[c], deps=dd)
        svo = SVC + s * 4160
        tms = vmemset(bb(svo, svo + 4160), 1.0, deps=dd)
        tvs = []
        for t in range(32):
            tvs.append(p.dma(bb(svo + t * 130, svo + t * 130 + 64), vth[c, t][:, 0:64], deps=[tms]))
            tvs.append(p.dma(bb(svo + t * 130 + 65, svo + t * 130 + 129), vth[c, t][:, 64:128], deps=[tms]))
        load_toks[c] = [tq_, tk_] + tvs

    p5_loads(0)
    lrow_free = {0: None, 1: None}
    SCB = [0, 1, 5, 6]
    sc_free = {0: None, 1: None, 2: None, 3: None}
    pt_free = {0: None, 1: None, 2: None, 3: None}
    for c in range(NCH):
        if c + 1 < NCH:
            p5_loads(c + 1)
        s = c % 2
        svo = SVC + s * 4160
        ctoks = []
        po_free = {0: (chunk_done[c - 1] if c > 0 else P5bB), 1: (chunk_done[c - 1] if c > 0 else P5bB)}
        for qw in range(NW):
            nkt = 4 * (qw + 1)
            tpv_l = {}
            pend = None
            for kt in range(nkt + 1):
                cur = []
                if kt < nkt:
                    for ha in range(2):
                        hoff = ha * 64
                        tp = (64, 0) if ha else None
                        sb_ = (kt % 2) * 2 + ha
                        tsc = mm(SCB[sb_], BB[hoff:hoff + 64, KTC + s * 4096 + kt * 128: KTC + s * 4096 + (kt + 1) * 128],
                                 BB[hoff:hoff + 64, QTC + s * 4096 + qw * WIN: QTC + s * 4096 + (qw + 1) * WIN],
                                 deps=load_toks[c] + ([sc_free[sb_]] if sc_free[sb_] else []), tp=tp)
                        dd = [tsc] + ([pt_free[sb_]] if pt_free[sb_] else [])
                        te = act(bb(PT + sb_ * WIN, PT + (sb_ + 1) * WIN), pb[SCB[sb_]][:, :], AF.Exp,
                                 deps=dd, scale=0.125, bias=mshiftc)
                        sc_free[sb_] = te
                        if kt >= 4 * qw:
                            i = kt - 4 * qw
                            te = vtt(bb(PT + sb_ * WIN, PT + (sb_ + 1) * WIN),
                                     bb(PT + sb_ * WIN, PT + (sb_ + 1) * WIN),
                                     cmb[:, i * WIN:(i + 1) * WIN], ALU.mult, deps=[te])
                        cur.append((ha, sb_, te))
                if pend is not None:
                    pkt = kt - 1
                    for ha, sb_, te in pend:
                        tpv_ = mm(2 + ha, bb(svo + pkt * 130 + ha * 65, svo + pkt * 130 + (ha + 1) * 65),
                                  bb(PT + sb_ * WIN, PT + (sb_ + 1) * WIN),
                                  deps=[te] + (list(po_free[ha]) if pkt == 0 else []),
                                  start=(pkt == 0), stop=(pkt == nkt - 1), psl=(0, 65, 0, WIN))
                        pt_free[sb_] = tpv_
                        tpv_l[ha] = tpv_
                pend = cur
            tno_l = {}
            for ha in range(2):
                po = 2 + ha
                lo = ha * WIN
                tlc = vcopy(lrow[64:65, lo:lo + WIN], pb[po][64:65, 0:WIN],
                            deps=[tpv_l[ha]] + ([lrow_free[ha]] if lrow_free[ha] else []))
                tev = act(lrow[0:64, lo:lo + WIN], pb[po][0:64, 0:WIN], AF.Identity, deps=[tlc])
                trc = vrecip(lrow[64:65, lo:lo + WIN], lrow[64:65, lo:lo + WIN], deps=[tlc])
                bb_ = 4 + ha * 3
                tbc_ = mm(bb_, cHPR[64:65, 0:64], lrow[64:65, lo:lo + WIN],
                          deps=[trc] + list(pfree[bb_]), psl=(0, 64, 0, WIN), tp=(64, 0))
                AT = ATA if ha == 0 else ATB
                tno = vtt(BB[0:64, AT + s * 4096 + qw * WIN: AT + s * 4096 + (qw + 1) * WIN],
                          lrow[0:64, lo:lo + WIN], pb[bb_][0:64, 0:WIN], ALU.mult, deps=[tev, tbc_])
                pfree[bb_] = (tno,)
                po_free[ha] = [tev]
                lrow_free[ha] = tno
                ctoks.append(tno)
        tsa = p.dma(ath[c, 0:64, :], BB[0:64, ATA + s * 4096: ATA + (s + 1) * 4096], deps=ctoks)
        tsb = p.dma(ath[c, 64:128, :], BB[0:64, ATB + s * 4096: ATB + (s + 1) * 4096], deps=ctoks)
        chunk_done[c] = [tsa, tsb] + ctoks

    P5wB = phase()
    wldo = [p.dma(bb(SWO + c * 1024, SWO + (c + 1) * 1024), Wo_h[c * 128:(c + 1) * 128, :],
                  queue="gpsimd", deps=P5wB) for c in range(NCH)]
    prev_w = P5wB
    for w in range(NW):
        tla = [p.dma(bb(AWIN + c * WIN, AWIN + (c + 1) * WIN), ath[c][:, w * WIN:(w + 1) * WIN],
                     deps=prev_w) for c in range(NCH)]
        evs = []
        for oc in range(NCH):
            bo = nb()
            tmm = None
            for c in range(NCH):
                tmm = mm(bo, bb(SWO + c * 1024 + oc * 128, SWO + c * 1024 + (oc + 1) * 128),
                         bb(AWIN + c * WIN, AWIN + (c + 1) * WIN),
                         deps=(tla + wldo) if c == 0 else [tmm], start=(c == 0), stop=(c == NCH - 1))
            te = act(f(OBF + oc * WIN, OBF + (oc + 1) * WIN), pb[bo][:, :], AF.Identity, deps=[tmm])
            bank_done(bo, te)
            evs.append(te)
        ts_, adds = out_window(w, evs, x2h, x3h, prev_w)
        prev_w = [ts_] + adds

    P6B = phase()
    wb_toks = []
    tprev = None
    for ri in range(2):
        WTo = WTR if ri == 0 else WTI
        for r in range(4):
            for cc in range(4):
                slot = cc % 2
                uo = UVU if slot == 0 else UVU2
                vo = UVV if slot == 0 else UVV2
                dd = [tprev] if tprev else list(P6B)
                tU = p.dma(f(uo, uo + 512), uvU_h[ri, r, cc], deps=dd)
                tV = p.dma(f(vo, vo + 512), uvV_h[ri, r, cc], deps=dd)
                if cc == 0:
                    tprev = vtt(f(FACC, FACC + 512), f(uo, uo + 512), f(vo, vo + 512), ALU.mult,
                                deps=[tU, tV])
                else:
                    tm_ = vtt(f(FTMP, FTMP + 512), f(uo, uo + 512), f(vo, vo + 512), ALU.mult,
                              deps=[tU, tV, tprev])
                    tprev = vtt(f(FACC, FACC + 512), f(FACC, FACC + 512), f(FTMP, FTMP + 512),
                                ALU.add, deps=[tm_])
            tw = act(bb(WTo + r * 512, WTo + (r + 1) * 512), f(FACC, FACC + 512), AF.Identity,
                     deps=[tprev])
            tprev = tw
            wb_toks.append(tw)

    prev_w = P6B
    final_stores = []
    for w in range(NW):
        tl, apl = ln_window(x3h, w * WIN, prev_w + wb_toks, XHAT6)
        tts = tr_window(XHAT6, lambda c, i: bb(XT6 + c * WIN + i * 128, XT6 + c * WIN + (i + 1) * 128),
                        apl, deps=prev_w)
        trt = [t for row in tts for t in row]
        tneg = [vts(bb(XT6N + k * WIN, XT6N + (k + 1) * WIN), bb(XT6 + (4 + k) * WIN, XT6 + (5 + k) * WIN),
                    -1.0, None, ALU.mult, deps=[tts[i][4 + k] for i in range(4)]) for k in range(4)]
        evs = []
        for oc in range(NCH):
            bo = nb()
            tmm = None
            first = True
            for k in range(4):
                if oc < 4:
                    l1 = bb(WTR + k * 512 + oc * 128, WTR + k * 512 + (oc + 1) * 128)
                    l2 = bb(WTI + k * 512 + oc * 128, WTI + k * 512 + (oc + 1) * 128)
                    r1 = bb(XT6 + k * WIN, XT6 + (k + 1) * WIN)
                    r2 = bb(XT6N + k * WIN, XT6N + (k + 1) * WIN)
                else:
                    l1 = bb(WTI + k * 512 + (oc - 4) * 128, WTI + k * 512 + (oc - 3) * 128)
                    l2 = bb(WTR + k * 512 + (oc - 4) * 128, WTR + k * 512 + (oc - 3) * 128)
                    r1 = bb(XT6 + k * WIN, XT6 + (k + 1) * WIN)
                    r2 = bb(XT6 + (4 + k) * WIN, XT6 + (5 + k) * WIN)
                tmm = mm(bo, l1, r1, deps=(trt + wb_toks + tneg) if first else [tmm],
                         start=first, stop=False)
                first = False
                tmm = mm(bo, l2, r2, deps=[tmm], start=False, stop=(k == 3))
            te = act(f(OBF + oc * WIN, OBF + (oc + 1) * WIN), pb[bo][:, :], AF.Identity,
                     deps=[tmm], bias=sttbr[:, oc:oc + 1])
            bank_done(bo, te)
            evs.append(te)
        ts_, adds = out_window(w, evs, x3h, out_h, prev_w)
        prev_w = [ts_] + adds
        final_stores.append(ts_)

    vmemset(f(CONVT, CONVT + 1), 0.0, deps=final_stores)

    with nc.Block() as block:
        p.emit(block)
    return nc


def host_prep(inp):
    g = {k: np.ascontiguousarray(np.asarray(v, np.float32)) for k, v in inp.items()}
    d = {}
    d["gate_wt"] = np.ascontiguousarray(g["L0_attn_gate_w"].T)
    d["B_wt"] = np.ascontiguousarray(g["L0_attn_B_w"].T)
    d["C_wt"] = np.ascontiguousarray(g["L0_attn_C_w"].T)
    d["D_wt"] = np.ascontiguousarray(g["L0_attn_D_w"].T)
    d["B_b"] = g["L0_attn_B_b"].reshape(S, 1)
    d["C_b"] = g["L0_attn_C_b"].reshape(S, 1)
    d["D_b"] = np.ascontiguousarray(g["L0_attn_D_b"].reshape(NCH, 128).T)
    d["gate_b"] = np.ascontiguousarray(g["L0_attn_gate_b"].reshape(2 * NCH, 128).T)
    d["conv_taps"] = g["L0_attn_conv_w"].reshape(D, 3).reshape(NCH, 128, 3)
    d["conv_b"] = np.ascontiguousarray(g["L0_attn_conv_b"].reshape(NCH, 128).T)
    d["A_m"] = g["L0_attn_A"]
    d["A_mt"] = np.ascontiguousarray(g["L0_attn_A"].T)
    d["dtA"] = np.asarray(g["L0_attn_dt_A"], np.float32).reshape(1, 1)
    d["g_col"] = g["L0_attn_ln_st_g"].reshape(S, 1)
    d["beta_col"] = g["L0_attn_ln_st_b"].reshape(S, 1)
    d["g_rep64"] = np.ascontiguousarray(np.tile(g["L0_attn_ln_st_g"].reshape(1, S), (S, 1)))
    d["lngc"] = np.ascontiguousarray(g["L0_attn_ln_in_g"].reshape(NCH, 128).T)
    d["lnbc"] = np.ascontiguousarray(g["L0_attn_ln_in_b"].reshape(NCH, 128).T)
    d["Pc"] = (np.eye(S) - 1.0 / S).astype(np.float32)
    d["ident64"] = np.eye(S, dtype=np.float32)
    d["ident128"] = np.eye(128, dtype=np.float32)
    consts = np.zeros((128, 8), np.float32)
    consts[:, 0] = EPS
    consts[:, 1] = EPS * EPS
    consts[:, 2] = -SHIFT
    d["consts"] = consts
    d["onesdiv"] = np.full((S, S), 1.0 / S, np.float32)
    d["hp_rows"] = np.ones((128, 128), np.float32)
    cmask = np.zeros((4, 128, WIN), np.float32)
    for i in range(4):
        for kp in range(128):
            cmask[i, kp, i * 128 + kp:] = 1.0
    d["cmask"] = cmask.astype(BF)
    d["moe_w1"] = g["L0_ffn_w1"]
    d["moe_w2"] = g["L0_ffn_w2"]
    d["moe_w3"] = g["L0_ffn_w3"]
    d["ws1"] = g["L0_ffn_ws1"]
    d["ws2"] = g["L0_ffn_ws2"]
    d["ws3"] = g["L0_ffn_ws3"]
    d["wr"] = g["L0_ffn_wr"]
    d["Wq_t"] = np.ascontiguousarray(g["L1_attn_Wq"].T)
    d["Wk_t"] = np.ascontiguousarray(g["L1_attn_Wk"].T)
    d["Wv_t"] = np.ascontiguousarray(g["L1_attn_Wv"].T)
    d["Wo_t"] = np.ascontiguousarray(g["L1_attn_Wo"].T)
    uvU = np.zeros((2, 4, 4, 128, 512), np.float32)
    uvV = np.zeros((2, 4, 4, 128, 512), np.float32)
    for ri, (G1k, G2k) in enumerate((("L1_ffn_G1_R", "L1_ffn_G2_R"), ("L1_ffn_G1_I", "L1_ffn_G2_I"))):
        for cc in range(4):
            Ac = g[G1k][0, :, cc, :]
            Bc = g[G2k][cc, :, 0, :]
            U = np.broadcast_to(Ac.T[:, None, :, None], (32, 16, 32, 16)).reshape(512, 512)
            V = np.broadcast_to(Bc.T[None, :, None, :], (32, 16, 32, 16)).reshape(512, 512)
            for r in range(4):
                uvU[ri, r, cc] = U[r * 128:(r + 1) * 128]
                uvV[ri, r, cc] = V[r * 128:(r + 1) * 128]
    d["uvU"] = uvU
    d["uvV"] = uvV
    d["tt_bias"] = np.ascontiguousarray(np.concatenate([g["L1_ffn_bias_real"], g["L1_ffn_bias_imag"]]).reshape(NCH, 128).T)
    return d


def kernel(**inp):
    nc = bass.Bass("TRN2", target_bir_lowering=False)
    build(nc, dbg=False)
    base = host_prep(inp)
    x = np.asarray(inp["L0_attn_x"], np.float32)
    in_maps = []
    for b in range(8):
        m = dict(base)
        m["x0"] = np.ascontiguousarray(x[b])
        in_maps.append(m)
    res = run_bass_kernel_spmd(nc, in_maps, core_ids=list(range(8)))
    return np.stack([np.asarray(res.results[i]["out"], np.float32) for i in range(8)])


# version 11
# speedup vs baseline: 1.7318x; 1.1508x over previous
import numpy as np
import ml_dtypes
import concourse.bass as bass
import concourse.mybir as mybir
from concourse.bass_utils import run_bass_kernel_spmd

F32 = mybir.dt.float32
BF16 = mybir.dt.bfloat16
AF = mybir.ActivationFunctionType
ALU = mybir.AluOpType
AX = mybir.AxisListType
BF = ml_dtypes.bfloat16

T = 4096
D = 1024
NCH = 8
WIN = 512
NW = 8
S = 64
KCH = 64
LCH = T // KCH
WARM = 256
STEPS = LCH + WARM
EPS = 1e-5
SHIFT = 12.0

NF = 18600
HIST = 0
OBF = 0
UVU, UVV, UVU2, UVV2, FACC, FTMP = 4096, 4608, 5120, 5632, 6144, 6656
SBMP = 9216
STAT = 13824
XWIN = 13888
CONVT = 17984
RTR = 18496

NB = 48200
XHT = 0
XHAT = 8224
XN2 = 12320
XGW = 20512
SGV = 24608
GSO = 25632
WBC = 26656
GWV = 27680
SCMB = 44064
YB = 0
GSL = 4096
SDW = 8224
XHAT4 = 0
XT4 = 4096
MW1, MW2, MW3 = 8192, 16384, 24576
WS1, WS2, WS3, SWR = 32768, 34816, 36864, 38912
HST = 38976
H2SC = 44096
XHAT5, XT5 = 0, 4096
WQ, WK, WV = 8192, 16384, 24576
QWB, KWB, VWB, VTB = 32768, 33792, 34816, 35840
QTC, KTC = 0, 8192
SVC = 16384
PT = 24704
ATA, ATB = 26752, 34944
AWIN, SWO = 0, 8192
XHAT6, XT6, XT6N, WTR, WTI = 0, 4096, 8192, 10240, 12288


class Prog:
    def __init__(self, nc):
        self.nc = nc
        self.engines = ("tensor", "vector", "scalar", "sync", "gpsimd")
        self.ops = {e: [] for e in self.engines}
        self.semname = {"tensor": "s_t", "vector": "s_v", "scalar": "s_s",
                        "sync": "s_y", "gpsimd": "s_g"}
        self.sems = {}
        self.cnt = {}
        for s in self.semname.values():
            self.sems[s] = nc.semaphore(s).__enter__()
            self.cnt[s] = 0

    def op(self, engine, fn, deps=(), inc=1):
        sem = self.semname[engine]
        waits = {}
        for dd in deps:
            if dd is None:
                continue
            sn, val = dd
            waits[sn] = max(waits.get(sn, 0), val)
        self.cnt[sem] += inc
        tok = (sem, self.cnt[sem])
        self.ops[engine].append((fn, sorted(waits.items()), (sem, inc)))
        return tok

    def dma(self, out, in_, deps=(), queue="sync"):
        return self.op(queue, lambda e, o=out, i=in_: e.dma_start(out=o, in_=i),
                       deps=deps, inc=16)

    def dma_tr(self, out, in_, deps=(), queue="scalar"):
        return self.op(queue, lambda e, o=out, i=in_: e.dma_start(out=o, in_=i, transpose=True),
                       deps=deps, inc=16)

    def barrier(self):
        return [(self.semname[e], self.cnt[self.semname[e]]) for e in self.engines]

    def emit(self, block):
        def make(engine):
            def body(eng):
                for fn, waits, inc in self.ops[engine]:
                    for sn, val in waits:
                        eng.wait_ge(self.sems[sn], val)
                    inst = fn(eng)
                    sn, n = inc
                    inst.then_inc(self.sems[sn], n)
            return body
        block.tensor(make("tensor"))
        block.vector(make("vector"))
        block.scalar(make("scalar"))
        block.sync(make("sync"))
        block.gpsimd(make("gpsimd"))


def build(nc, dbg=False):
    dram = lambda name, shape, d=F32, kind="ExternalInput": nc.dram_tensor(name, shape, d, kind=kind).ap()
    x0 = dram("x0", [T, D])
    gate_wt = dram("gate_wt", [D, 2 * D])
    B_wt = dram("B_wt", [D, S]); C_wt = dram("C_wt", [D, S]); D_wt_h = dram("D_wt", [S, D])
    Bb_h = dram("B_b", [S, 1]); Cb_h = dram("C_b", [S, 1]); Db_h = dram("D_b", [128, NCH])
    gb_h = dram("gate_b", [128, 2 * NCH])
    taps_h = dram("conv_taps", [NCH, 128, 3])
    cvb_h = dram("conv_b", [128, NCH])
    A_h = dram("A_m", [S, S]); At_h = dram("A_mt", [S, S]); dtA_h = dram("dtA", [1, 1])
    g_h = dram("g_col", [S, 1]); beta_h = dram("beta_col", [S, 1]); grep_h = dram("g_rep64", [S, S])
    lngc_h = dram("lngc", [128, NCH]); lnbc_h = dram("lnbc", [128, NCH])
    Pc_h = dram("Pc", [S, S]); id_h = dram("ident64", [S, S])
    cst_h = dram("consts", [128, 8])
    od_h = dram("onesdiv", [S, S])
    hpr_h = dram("hp_rows", [128, 128])
    id128_h = dram("ident128", [128, 128])
    cm_h = dram("cmask", [4, 128, WIN], BF16)
    w1_h = dram("moe_w1", [8, D, 128]); w2_h = dram("moe_w2", [8, D, 128]); w3_h = dram("moe_w3", [8, 128, D])
    ws1_h = dram("ws1", [D, 256]); ws2_h = dram("ws2", [D, 256]); ws3_h = dram("ws3", [256, D]); wr_h = dram("wr", [D, 8])
    Wq_h = dram("Wq_t", [D, D]); Wk_h = dram("Wk_t", [D, D]); Wv_h = dram("Wv_t", [D, D]); Wo_h = dram("Wo_t", [D, D])
    uvU_h = dram("uvU", [2, 4, 4, 128, 512]); uvV_h = dram("uvV", [2, 4, 4, 128, 512])
    ttbr_h = dram("tt_bias", [128, NCH])
    out_h = dram("out", [T, D], kind="ExternalOutput")
    kd = "ExternalOutput" if dbg else "Internal"
    x1h = dram("x1h", [T, D], kind=kd)
    x2h = dram("x2h", [T, D], kind=kd)
    x3h = dram("x3h", [T, D], kind=kd)
    gatesh = dram("gatesh", [NCH, 128, T], d=BF16, kind=("ExternalOutput" if dbg else "Internal"))
    if dbg:
        dbpp_h = dram("dbg_bpp", [S, 4608], kind="ExternalOutput")
        dzh_h = dram("dbg_zh", [S, T], kind="ExternalOutput")
        dcm_h = dram("dbg_cm", [S, T], d=BF16, kind="ExternalOutput")
        dyb_h = dram("dbg_yb", [S, T], d=BF16, kind="ExternalOutput")
        dxht_h = dram("dbg_xht", [128, 8192], d=BF16, kind="ExternalOutput")
        dxn2_h = dram("dbg_xn2", [128, 4096], d=BF16, kind="ExternalOutput")
    qth = dram("qth", [NCH, 128, T], d=BF16, kind="Internal")
    kth = dram("kth", [NCH, 128, T], d=BF16, kind="Internal")
    vth = dram("vth", [NCH, 32, 128, 128], d=BF16, kind="Internal")
    ath = dram("ath", [NCH, 128, T], d=BF16, kind="Internal")

    sbuf = lambda name, shape, d=F32: nc.sbuf_tensor(name, shape, d).__enter__()

    cC = sbuf("cC", [128, 8])
    cOD = sbuf("cOD", [S, S])
    cID = sbuf("cID", [S, S]); cPc = sbuf("cPc", [S, S]); cID128 = sbuf("cID128", [128, 128])
    cA = sbuf("cA", [S, S]); cAt = sbuf("cAt", [S, S]); cdt = sbuf("cdt", [1, 1])
    cg = sbuf("cg", [S, 1]); cbeta = sbuf("cbeta", [S, 1]); cgrep = sbuf("cgrep", [S, S])
    cHPR = sbuf("cHPR", [128, 128])
    lngc = sbuf("lngc_s", [128, NCH]); lnbc = sbuf("lnbc_s", [128, NCH])
    staps = sbuf("staps", [128, NCH * 4])
    scvb = sbuf("scvb", [128, NCH])
    sBb = sbuf("sBb", [S, 1]); sCb = sbuf("sCb", [S, 1])
    sDb = sbuf("sDb", [128, NCH]); sgbias = sbuf("sgbias", [128, 2 * NCH])
    sttbr = sbuf("sttbr", [128, NCH])
    scanM = sbuf("scanM", [S, S])
    bacol = sbuf("bacol", [S, 1])
    eE = sbuf("eE", [S, S]); eF = sbuf("eF", [S, S]); eH = sbuf("eH", [S, S])
    eAd = sbuf("eAd", [S, S]); eAdt = sbuf("eAdt", [S, S])
    eT1 = sbuf("eT1", [S, S]); eT2 = sbuf("eT2", [S, S])
    sdtc = sbuf("sdtc", [S, 1])
    rowtmp = sbuf("rowtmp", [1, S])
    cmb = sbuf("cmb", [128, 4 * WIN], BF16)
    szh2t = sbuf("szh2t", [S, LCH])
    hcell = sbuf("hcell", [128, 64], BF16)
    craw = [sbuf("craw0", [S, KCH])]
    csqb = [sbuf("csqb0", [S, KCH])]
    ztp = [sbuf(f"ztp{h}", [S, KCH]) for h in range(2)]
    wfT = sbuf("wfT", [1, 8 * WIN])
    lrow = sbuf("lrow", [128, 2 * WIN])

    FP = sbuf("FP", [128, NF])
    BB = sbuf("BB", [128, NB], BF16)
    f = lambda a, b: FP[:, a:b]
    f6 = lambda a, b: FP[0:S, a:b]
    bb = lambda a, b: BB[:, a:b]
    b6 = lambda a, b: BB[0:S, a:b]

    p = Prog(nc)
    pb = [nc.psum_tensor(f"pb{i}", [128, 512], F32).__enter__() for i in range(8)]
    pfree = [()] * 8
    _rr = [0]

    def nb():
        i = _rr[0]; _rr[0] = (i + 1) % 8
        return i

    def bank_done(i, tok):
        pfree[i] = (tok,)

    def mm(bi, lhsT, rhs, deps=(), start=True, stop=True, tp=None, psl=None):
        dst = pb[bi][psl[0]:psl[1], psl[2]:psl[3]] if psl else pb[bi][:, :]
        dd = (list(pfree[bi]) if start else []) + list(deps)
        return p.op("tensor", lambda e, o=dst, l=lhsT, r=rhs, s0=start, s1=stop, t_=tp: e.matmul(
            o, l, r, start=s0, stop=s1, tile_position=t_, skip_group_check=True), deps=dd)

    def ptr(bi, cols, src, deps=(), rows=128):
        dst = pb[bi][0:rows, cols[0]:cols[1]]
        dd = list(pfree[bi]) + list(deps)
        return p.op("tensor", lambda e, o=dst, i=src: e.transpose(o, i, cID128[:, :]), deps=dd)

    def act(out, in_, func, deps=(), bias=None, scale=1.0):
        if bias is None:
            return p.op("scalar", lambda e, o=out, i=in_, sc=scale, fu=func: e.activation(
                out=o, in_=i, func=fu, scale=sc), deps=deps)
        return p.op("scalar", lambda e, o=out, i=in_, b=bias, sc=scale, fu=func: e.activation(
            out=o, in_=i, func=fu, bias=b, scale=sc), deps=deps)

    def vtt(out, a, b2, opn, deps=()):
        return p.op("vector", lambda e, o=out, x=a, y=b2, oo=opn: e.tensor_tensor(o, x, y, oo),
                    deps=deps)

    def vts(out, in0, s1, s2, op0, op1=None, deps=()):
        if s2 is None:
            return p.op("vector", lambda e, o=out, i=in0, a=s1, oo=op0: e.tensor_scalar(
                out=o, in0=i, scalar1=a, scalar2=None, op0=oo), deps=deps)
        return p.op("vector", lambda e, o=out, i=in0, a=s1, b2=s2, oo=op0, o1=op1: e.tensor_scalar(
            out=o, in0=i, scalar1=a, scalar2=b2, op0=oo, op1=o1), deps=deps)

    def vstt(out, in0, s1, in1, op0, op1, deps=()):
        return p.op("vector", lambda e, o=out, i=in0, a=s1, j=in1, oo=op0, o1=op1:
                    e.scalar_tensor_tensor(out=o, in0=i, scalar=a, in1=j, op0=oo, op1=o1),
                    deps=deps)

    def vcopy(out, in_, deps=()):
        return p.op("vector", lambda e, o=out, i=in_: e.tensor_copy(o, i), deps=deps)

    def vred(out, in_, opn, deps=()):
        return p.op("vector", lambda e, o=out, i=in_, oo=opn: e.tensor_reduce(
            out=o, in_=i, axis=AX.X, op=oo), deps=deps)

    def vrecip(out, in_, deps=()):
        return p.op("vector", lambda e, o=out, i=in_: e.reciprocal(out=o, in_=i), deps=deps)

    def vmemset(ap_, val, deps=()):
        return p.op("vector", lambda e, o=ap_, v=val: e.memset(o, v), deps=deps)

    def phase():
        bar = p.barrier()
        for i in range(8):
            pfree[i] = tuple(bar)
        return bar

    epsc = cC[:, 0:1]; eps2c = cC[:, 1:2]; mshiftc = cC[:, 2:3]

    ld = [p.dma(cC[:, :], cst_h), p.dma(cOD[:, :], od_h),
          p.dma(cID[:, :], id_h), p.dma(cPc[:, :], Pc_h), p.dma(cID128[:, :], id128_h),
          p.dma(cA[:, :], A_h), p.dma(cAt[:, :], At_h), p.dma(cdt[:, :], dtA_h),
          p.dma(cg[:, :], g_h), p.dma(cbeta[:, :], beta_h), p.dma(cgrep[:, :], grep_h),
          p.dma(cHPR[:, :], hpr_h),
          p.dma(lngc[:, :], lngc_h), p.dma(lnbc[:, :], lnbc_h),
          p.dma(staps.ap().rearrange("p (c k) -> p c k", k=4)[:, :, 0:3], taps_h.rearrange("c p k -> p c k")),
          p.dma(scvb[:, :], cvb_h),
          p.dma(sBb[:, :], Bb_h), p.dma(sCb[:, :], Cb_h),
          p.dma(sDb[:, :], Db_h),
          p.dma(sgbias[:, :], gb_h),
          p.dma(sttbr[:, :], ttbr_h),
          p.dma(cmb.ap().rearrange("p (j n) -> p j n", n=WIN), cm_h.rearrange("j p n -> p j n"))]

    t_t1 = []
    for c in range(NCH):
        t_t1.append(vts(staps[:, c * 4 + 3:c * 4 + 4], staps[:, c * 4 + 1:c * 4 + 2],
                        1.0, None, ALU.add, deps=ld))

    tok = mm(0, cOD[0:1, :], cdt[:, :], deps=ld, psl=(0, S, 0, 1))
    t_dt = vts(sdtc[:, :], pb[0][0:S, 0:1], float(S) * (2.0 ** -8), None, ALU.mult, deps=[tok])
    bank_done(0, t_dt)
    tA1 = vts(eAd[:, :], cA[:, :], sdtc[:, :], None, ALU.mult, deps=[t_dt] + ld)
    tA2 = vts(eAdt[:, :], cAt[:, :], sdtc[:, :], None, ALU.mult, deps=[t_dt] + ld)

    def horner(Xts, Hout, tdep):
        tH = vcopy(Hout[:, :], cID[:, :], deps=tdep + ld)
        for k in range(6, 0, -1):
            tm_ = mm(1, Xts[:, :], Hout[:, :], deps=[tH], psl=(0, S, 0, S))
            te = act(eT1[:, :], pb[1][0:S, 0:S], AF.Identity, deps=[tm_], scale=1.0 / k)
            bank_done(1, te)
            tH = vtt(Hout[:, :], eT1[:, :], cID[:, :], ALU.add, deps=[te])
        return tH

    tE = horner(eAdt, eE, [tA1, tA2])
    tF = horner(eAd, eF, [tA1, tA2])
    for _ in range(8):
        tm1 = mm(1, eF[:, :], eE[:, :], deps=[tE, tF], psl=(0, S, 0, S))
        te1 = act(eT1[:, :], pb[1][0:S, 0:S], AF.Identity, deps=[tm1])
        bank_done(1, te1)
        tm2 = mm(2, eE[:, :], eF[:, :], deps=[tE, tF], psl=(0, S, 0, S))
        te2 = act(eT2[:, :], pb[2][0:S, 0:S], AF.Identity, deps=[tm2])
        bank_done(2, te2)
        tE = vcopy(eE[:, :], eT1[:, :], deps=[te1, tm2])
        tF = vcopy(eF[:, :], eT2[:, :], deps=[te2, tE])
    tM = vtt(eH[:, :], eF[:, :], cgrep[:, :], ALU.mult, deps=[tE, tF])
    tsm = mm(1, eH[:, :], cPc[:, :], deps=[tM], psl=(0, S, 0, S))
    tsme = act(scanM[:, :], pb[1][0:S, 0:S], AF.Identity, deps=[tsm])
    bank_done(1, tsme)
    tba = mm(2, cbeta[:, :], eE[:, :], deps=[tE], psl=(0, 1, 0, S))
    tbae = vcopy(rowtmp[:, :], pb[2][0:1, 0:S], deps=[tba])
    bank_done(2, tbae)
    tbat = p.op("tensor", lambda e: e.transpose(pb[3][0:S, 0:1], rowtmp[:, :], cID128[0:1, 0:1]),
                deps=[tbae] + list(pfree[3]))
    tbac = vcopy(bacol[:, :], pb[3][0:S, 0:1], deps=[tbat])
    bank_done(3, tbac)

    tgw = [p.dma(bb(GWV + c * 2048, GWV + (c + 1) * 2048), gate_wt[c * 128:(c + 1) * 128, :],
                 queue="gpsimd", deps=ld) for c in range(NCH)]
    tbc = []
    for c in range(NCH):
        tbc.append(p.dma(bb(WBC + c * 2 * S, WBC + c * 2 * S + S), B_wt[c * 128:(c + 1) * 128, :],
                         queue="gpsimd", deps=ld))
        tbc.append(p.dma(bb(WBC + c * 2 * S + S, WBC + (c + 1) * 2 * S), C_wt[c * 128:(c + 1) * 128, :],
                         queue="gpsimd", deps=ld))
    tz = [vmemset(hcell[:, c * 4:c * 4 + 1], 0.0) for c in range(NCH)]

    def ln_window(xsrc, woff, deps, xhat_off, bf16_out=True):
        tl = p.dma(f(XWIN, XWIN + 4 * D).rearrange("p (i d) -> p i d", d=D),
                   xsrc[woff:woff + WIN, :].rearrange("(i p) d -> p i d", p=128), deps=deps)
        apl = []
        for i in range(4):
            xt = f(XWIN + i * D, XWIN + (i + 1) * D)
            st12 = f(STAT + i * 16, STAT + i * 16 + 12)
            mv = f(STAT + i * 16 + 12, STAT + i * 16 + 14)
            sc = f(STAT + i * 16 + 14, STAT + i * 16 + 15)
            t1 = p.op("vector", lambda e, o=st12[:, 0:6], i_=xt[:, 0:512]: e.bn_stats(out=o, in_=i_),
                      deps=[tl])
            t2 = p.op("vector", lambda e, o=st12[:, 6:12], i_=xt[:, 512:1024]: e.bn_stats(out=o, in_=i_),
                      deps=[t1])
            t3 = p.op("vector", lambda e, o=mv, i_=st12: e.bn_aggr(out=o, in_=i_), deps=[t2])
            if bf16_out == "composite":
                t4 = act(sc, mv[:, 1:2], AF.Sqrt, deps=[t3], bias=eps2c, scale=1.0 + EPS)
            else:
                t4 = act(sc, mv[:, 1:2], AF.Sqrt, deps=[t3], bias=epsc, scale=1.0)
            t5 = vrecip(sc, sc, deps=[t4])
            t6 = vts(f(xhat_off + i * D, xhat_off + (i + 1) * D), xt, mv[:, 0:1], sc,
                     ALU.subtract, ALU.mult, deps=[t5])
            apl.append(t6)
        return tl, apl

    def tr_window(xhat_off, dst_fn, apl, deps=()):
        tts = [[None] * NCH for _ in range(4)]
        for i in range(4):
            for c in range(NCH):
                bi = nb()
                tt = p.op("tensor", lambda e, o=pb[bi][:, 0:128],
                          i_=f(xhat_off + i * D + c * 128, xhat_off + i * D + (c + 1) * 128):
                          e.transpose(o, i_, cID128[:, :]),
                          deps=[apl[i]] + list(deps) + list(pfree[bi]))
                tv = vcopy(dst_fn(c, i), pb[bi][:, 0:128], deps=[tt])
                bank_done(bi, tv)
                tts[i][c] = tv
        return tts


    conv_done = {}
    aff_done = {}
    gates_done = {-1: []}
    bm_evac = []
    ln_prev = []

    def do_conv(w, deps_extra):
        s = w % 2
        xn2o = XN2 + s * 4096
        toks = []
        prev = None
        for c in range(NCH):
            base = XHT + (c * 2 + s) * WIN
            dat = lambda a, b: bb(base + a, base + b)
            hL = hcell[:, c * 4 + s * 2:c * 4 + s * 2 + 1]
            hR = hcell[:, c * 4 + s * 2 + 32:c * 4 + s * 2 + 33]
            cv = lambda a, b: f(4096 + c * WIN + a, 4096 + c * WIN + b)
            dd = aff_done[w] + deps_extra + gates_done.get(w - 2, [])
            t1 = vts(cv(0, WIN), dat(0, WIN), staps[:, c * 4 + 3:c * 4 + 4], None, ALU.mult, deps=dd)
            t2 = vstt(cv(1, WIN), dat(0, WIN - 1), staps[:, c * 4 + 0:c * 4 + 1],
                      cv(1, WIN), ALU.mult, ALU.add, deps=[t1])
            t2b = vstt(cv(0, 1), hL, staps[:, c * 4 + 0:c * 4 + 1],
                       cv(0, 1), ALU.mult, ALU.add, deps=[t2])
            t3 = vstt(cv(0, WIN - 1), dat(1, WIN), staps[:, c * 4 + 2:c * 4 + 3],
                      cv(0, WIN - 1), ALU.mult, ALU.add, deps=[t2b])
            t3b = vstt(cv(WIN - 1, WIN), hR, staps[:, c * 4 + 2:c * 4 + 3],
                       cv(WIN - 1, WIN), ALU.mult, ALU.add, deps=[t3])
            t4 = vts(bb(xn2o + c * WIN, xn2o + (c + 1) * WIN), cv(0, WIN),
                     scvb[:, c:c + 1], None, ALU.add, deps=[t3b])
            prev = t4
            toks.append(t4)
        conv_done[w] = toks
        return toks

    def do_gates(w):
        s = w % 2
        xn2o = XN2 + s * 4096
        ct = conv_done[w]
        igev = []
        prev_sg = {0: None, 1: None}
        for oc in range(NCH):
            bi = nb()
            tmm = None
            for c in range(NCH):
                tmm = mm(bi, bb(GWV + c * 2048 + 1024 + oc * 128, GWV + c * 2048 + 1024 + (oc + 1) * 128),
                         bb(xn2o + c * WIN, xn2o + (c + 1) * WIN),
                         deps=([ct[c]] + tgw) if c == 0 else [tmm, ct[c]], start=(c == 0), stop=(c == NCH - 1))
            sgo = SGV + (oc % 2) * WIN
            tsg = act(bb(sgo, sgo + WIN), pb[bi][:, :], AF.Sigmoid,
                      deps=[tmm] + ([prev_sg[oc % 2]] if prev_sg[oc % 2] else []),
                      bias=sgbias[:, NCH + oc:NCH + oc + 1])
            bank_done(bi, tsg)
            txg = vtt(bb(XGW + oc * WIN, XGW + (oc + 1) * WIN), bb(xn2o + oc * WIN, xn2o + (oc + 1) * WIN),
                      bb(sgo, sgo + WIN), ALU.mult, deps=[tsg])
            prev_sg[oc % 2] = txg
            igev.append(txg)
        gtoks = []
        prev_st = {0: None, 1: None}
        for oc in range(NCH):
            bi = nb()
            tmm = None
            for c in range(NCH):
                tmm = mm(bi, bb(GWV + c * 2048 + oc * 128, GWV + c * 2048 + (oc + 1) * 128),
                         bb(xn2o + c * WIN, xn2o + (c + 1) * WIN),
                         deps=([ct[c]] + tgw) if c == 0 else [tmm, ct[c]], start=(c == 0), stop=(c == NCH - 1))
            go = GSO + (oc % 2) * WIN
            tsg = act(bb(go, go + WIN), pb[bi][:, :], AF.Sigmoid,
                      deps=[tmm] + ([prev_st[oc % 2]] if prev_st[oc % 2] else []), bias=sgbias[:, oc:oc + 1])
            bank_done(bi, tsg)
            tst = p.dma(gatesh[oc, :, w * WIN:(w + 1) * WIN], bb(go, go + WIN), deps=[tsg])
            prev_st[oc % 2] = tst
            gtoks.append(tst)
        biB = nb(); biC = nb()
        tmB = None; tmC = None
        for c in range(NCH):
            tmC = mm(biC, bb(WBC + c * 2 * S + S, WBC + (c + 1) * 2 * S), bb(xn2o + c * WIN, xn2o + (c + 1) * WIN),
                     deps=([ct[c]] + tbc) if c == 0 else [tmC, ct[c]], start=(c == 0), stop=(c == NCH - 1),
                     psl=(0, S, 0, WIN))
        for c in range(NCH):
            tmB = mm(biB, bb(WBC + c * 2 * S, WBC + c * 2 * S + S), bb(XGW + c * WIN, XGW + (c + 1) * WIN),
                     deps=(igev + tbc) if c == 0 else [tmB], start=(c == 0), stop=(c == NCH - 1),
                     psl=(0, S, 0, WIN))
        tBe = act(f6(HIST + w * WIN, HIST + (w + 1) * WIN), pb[biB][0:S, 0:WIN], AF.Identity,
                  deps=[tmB], bias=sBb[:, :])
        bank_done(biB, tBe)
        tCe = act(b6(SCMB + w * WIN, SCMB + (w + 1) * WIN), pb[biC][0:S, 0:WIN], AF.Identity,
                  deps=[tmC], bias=sCb[:, :])
        bank_done(biC, tCe)
        bm_evac.extend([tBe, tCe])
        gates_done[w] = gtoks + [tBe, tCe]
        return gtoks

    halo_toks = {}
    for w in range(NW):
        ldeps = tz + t_t1 + [tA1] + (ln_prev if ln_prev else [])
        tl, apl = ln_window(x0, w * WIN, ldeps, 9216, bf16_out="composite")
        hl = []
        if w >= 2:
            s_src = w % 2; s_dst = (w - 1) % 2
            for c in range(NCH):
                base = XHT + (c * 2 + s_src) * WIN
                hl.append(vcopy(hcell[:, c * 4 + s_dst * 2:c * 4 + s_dst * 2 + 1],
                                bb(base + WIN - 1, base + WIN),
                                deps=aff_done[w - 2] + conv_done.get(w - 2, [])))
        s = w % 2
        tts = tr_window(9216, lambda c, i, s=s: bb(XHT + (c * 2 + s) * WIN + i * 128,
                                                   XHT + (c * 2 + s) * WIN + (i + 1) * 128),
                        apl, deps=hl + conv_done.get(w - 2, []) + gates_done.get(w - 2, []))
        ln_prev = [t for row in tts for t in row]
        aff = []
        for c in range(NCH):
            base = XHT + (c * 2 + s) * WIN
            aff.append(vts(bb(base, base + WIN), bb(base, base + WIN),
                           lngc[:, c:c + 1], lnbc[:, c:c + 1], ALU.mult, ALU.add,
                           deps=[tts[i][c] for i in range(4)]))
        aff_done[w] = aff
        hr = []
        if w >= 1:
            s_dst = (w - 1) % 2
            for c in range(NCH):
                base = XHT + (c * 2 + s) * WIN
                hr.append(vcopy(hcell[:, c * 4 + s_dst * 2 + 32:c * 4 + s_dst * 2 + 33],
                                bb(base, base + 1), deps=[aff[c]]))
            halo_toks[w - 1] = hl + hr
            do_conv(w - 1, hl + hr)
            do_gates(w - 1)
            if dbg and w == 1:
                td1 = p.dma(dxht_h, bb(XHT, XHT + 8192), deps=aff_done[0] + conv_done[0])
                td2 = p.dma(dxn2_h, bb(XN2, XN2 + 4096), deps=conv_done[0])
                gates_done[0] = gates_done[0] + [td1, td2]
    hl = []
    s_dst = 1
    for c in range(NCH):
        base = XHT + (c * 2 + 0) * WIN
        hl.append(vcopy(hcell[:, c * 4 + s_dst * 2:c * 4 + s_dst * 2 + 1],
                        bb(base + WIN - 1, base + WIN),
                        deps=aff_done[6] + conv_done.get(6, [])))
        hl.append(vmemset(hcell[:, c * 4 + s_dst * 2 + 32:c * 4 + s_dst * 2 + 33], 0.0,
                          deps=conv_done.get(5, [])))
    do_conv(7, hl)
    do_gates(7)

    P2B = phase()
    tzp = [vmemset(f6(SBMP, SBMP + WARM), 0.0, deps=P2B),
           vmemset(f6(SBMP + WARM + T, SBMP + 4608), 0.0, deps=P2B)]
    tb2 = vts(f6(HIST + 1, HIST + T), f6(HIST + 1, HIST + T), bacol[:, :], None, ALU.add,
              deps=P2B)
    pps = []
    for i in range(NW):
        bi = nb()
        tm_ = mm(bi, cPc[:, :], f6(HIST + i * WIN, HIST + (i + 1) * WIN), deps=[tb2],
                 psl=(0, S, 0, WIN))
        te = act(f6(SBMP + WARM + i * WIN, SBMP + WARM + (i + 1) * WIN), pb[bi][0:S, 0:WIN],
                 AF.Identity, deps=[tm_])
        bank_done(bi, te)
        pps.append(te)
    SBW = FP[0:S, SBMP:SBMP + 4608].rearrange("p (j l) -> p j l", l=KCH)
    scan_end = []
    pA = [0, 1]
    pB = [2, 3]
    tV = tq = tM2 = tR = tZ = tM1 = None
    for t in range(STEPS):
        bAP = SBW[:, t // KCH: t // KCH + KCH, t % KCH]
        if t < WARM:
            zt = ztp[t % 2][:, :]
        else:
            zt = f6(HIST + (t - WARM) * KCH, HIST + (t - WARM + 1) * KCH)
        if t == 0:
            tV = vcopy(craw[0][:, :], bAP, deps=pps + P2B + tzp)
        else:
            tV = vtt(craw[0][:, :], pb[pA[t % 2]][0:S, 0:KCH], bAP, ALU.add, deps=[tM1])
        tq = vtt(csqb[0][:, :], craw[0][:, :], craw[0][:, :], ALU.mult, deps=[tV])
        tM2 = mm(pB[t % 2], cOD[:, :], csqb[0][:, :], deps=[tq] + ([tR] if t >= 2 else []),
                 psl=(0, S, 0, KCH))
        tR = act(csqb[0][:, :], pb[pB[t % 2]][0:S, 0:KCH],
                 AF.Sqrt, deps=[tM2], bias=epsc[0:S, :])
        tRc = vrecip(csqb[0][:, :], csqb[0][:, :], deps=[tR])
        tZ = vtt(zt, craw[0][:, :], csqb[0][:, :], ALU.mult, deps=[tRc])
        if t < STEPS - 1:
            tM1 = mm(pA[(t + 1) % 2], scanM[:, :], zt, deps=[tZ] + ([tV] if t >= 1 else []),
                     psl=(0, S, 0, KCH))
        else:
            scan_end.append(tZ)

    if dbg:
        p.dma(dbpp_h, f6(SBMP, SBMP + 4608), deps=scan_end)
        p.dma(dzh_h, f6(HIST, HIST + T), deps=scan_end)
        p.dma(dcm_h, b6(SCMB, SCMB + T), deps=scan_end)

    P3B = phase()
    HISTV = FP[0:S, HIST:HIST + T].rearrange("p (t k) -> p t k", k=KCH)
    ty_prev = None
    ty_toks = []
    for j in range(KCH):
        zsrc = HISTV[:, :, j]
        t1 = vts(szh2t[:, 0:LCH], zsrc, cg[:, :], cbeta[:, :], ALU.mult, ALU.add,
                 deps=P3B + ([ty_prev] if ty_prev else []))
        t2 = vtt(b6(YB + j * LCH, YB + (j + 1) * LCH), szh2t[:, 0:LCH],
                 b6(SCMB + j * LCH, SCMB + (j + 1) * LCH), ALU.mult, deps=[t1])
        ty_prev = t2
        ty_toks.append(t2)
    tDld = p.dma(b6(SDW, SDW + D), D_wt_h, queue="gpsimd", deps=P3B)

    def out_window(w, obf_toks, xsrc, xdst, extra_ld_deps):
        tl = p.dma(f(XWIN, XWIN + 4 * D).rearrange("p (i d) -> p i d", d=D),
                   xsrc[w * WIN:(w + 1) * WIN, :].rearrange("(i p) d -> p i d", p=128),
                   deps=extra_ld_deps)
        adds = []
        for i in range(4):
            for oc in range(NCH):
                bi = nb()
                tt = ptr(bi, (0, 128), f(OBF + oc * WIN + i * 128, OBF + oc * WIN + (i + 1) * 128),
                         deps=obf_toks)
                ta = vtt(f(XWIN + i * D + oc * 128, XWIN + i * D + (oc + 1) * 128),
                         f(XWIN + i * D + oc * 128, XWIN + i * D + (oc + 1) * 128),
                         pb[bi][0:128, 0:128], ALU.add, deps=[tt, tl])
                bank_done(bi, ta)
                adds.append(ta)
        ts_ = p.dma(xdst[w * WIN:(w + 1) * WIN, :].rearrange("(i p) d -> p i d", p=128),
                    f(XWIN, XWIN + 4 * D).rearrange("p (i d) -> p i d", d=D), deps=adds)
        return ts_, adds

    prev_st = []
    tgl_prev = None
    for w in range(NW):
        tgl = p.dma(bb(GSL, GSL + 4096).rearrange("p (c n) -> p c n", n=WIN),
                    gatesh[:, :, w * WIN:(w + 1) * WIN].rearrange("c p n -> p c n"),
                    deps=P3B + ([tgl_prev] if tgl_prev else []))
        evs = []
        for oc in range(NCH):
            bi = nb()
            tm_ = mm(bi, b6(SDW + oc * 128, SDW + (oc + 1) * 128), b6(YB + w * WIN, YB + (w + 1) * WIN),
                     deps=ty_toks + [tDld], psl=(0, 128, 0, WIN))
            te = act(f(OBF + oc * WIN, OBF + (oc + 1) * WIN), pb[bi][:, 0:WIN], AF.Identity,
                     deps=[tm_] + prev_st, bias=sDb[:, oc:oc + 1])
            tg = vtt(f(OBF + oc * WIN, OBF + (oc + 1) * WIN), f(OBF + oc * WIN, OBF + (oc + 1) * WIN),
                     bb(GSL + oc * WIN, GSL + (oc + 1) * WIN), ALU.mult, deps=[te, tgl])
            bank_done(bi, tg)
            evs.append(tg)
        ts_, adds = out_window(w, evs, x0, x1h, prev_st)
        tgl_prev = evs[-1]
        prev_st = [ts_] + adds

    if dbg:
        p.dma(dyb_h, b6(YB, YB + T), deps=ty_toks + prev_st)

    P4B = phase()
    wld = []
    for c in range(NCH):
        wld.append(p.dma(bb(SWR + c * 8, SWR + (c + 1) * 8), wr_h[c * 128:(c + 1) * 128, :],
                         queue="gpsimd", deps=P4B))
    for c in range(NCH):
        wld.append(p.dma(bb(WS1 + c * 256, WS1 + (c + 1) * 256), ws1_h[c * 128:(c + 1) * 128, :],
                         queue="gpsimd", deps=P4B))
        wld.append(p.dma(bb(WS2 + c * 256, WS2 + (c + 1) * 256), ws2_h[c * 128:(c + 1) * 128, :],
                         queue="gpsimd", deps=P4B))
    for e_i in range(8):
        for c in range(NCH):
            wld.append(p.dma(bb(MW1 + e_i * 1024 + c * 128, MW1 + e_i * 1024 + (c + 1) * 128),
                             w1_h[e_i, c * 128:(c + 1) * 128, :], queue="gpsimd", deps=P4B))
            wld.append(p.dma(bb(MW2 + e_i * 1024 + c * 128, MW2 + e_i * 1024 + (c + 1) * 128),
                             w2_h[e_i, c * 128:(c + 1) * 128, :], queue="gpsimd", deps=P4B))
    for k in range(2):
        wld.append(p.dma(bb(WS3 + k * 1024, WS3 + (k + 1) * 1024), ws3_h[k * 128:(k + 1) * 128, :],
                         queue="gpsimd", deps=P4B))
    for e_i in range(8):
        wld.append(p.dma(bb(MW3 + e_i * 1024, MW3 + (e_i + 1) * 1024), w3_h[e_i], queue="gpsimd", deps=P4B))

    prev_w = P4B
    for w in range(NW):
        tl, apl = ln_window(x1h, w * WIN, prev_w, OBF)
        tts = tr_window(OBF, lambda c, i: bb(XT4 + c * WIN + i * 128, XT4 + c * WIN + (i + 1) * 128),
                        apl, deps=prev_w)
        trt = [t for row in tts for t in row]
        wf_toks = []
        for i in range(4):
            bi = nb()
            tmm = None
            for c in range(NCH):
                tmm = mm(bi, bb(XT4 + c * WIN + i * 128, XT4 + c * WIN + (i + 1) * 128),
                         bb(SWR + c * 8, SWR + (c + 1) * 8),
                         deps=(trt + wld) if c == 0 else [tmm], start=(c == 0), stop=(c == NCH - 1),
                         psl=(0, 128, 0, 8))
            pr = f(RTR + i * 24, RTR + i * 24 + 8)
            rx = f(RTR + i * 24 + 8, RTR + i * 24 + 16)
            mx = f(RTR + i * 24 + 16, RTR + i * 24 + 17)
            ms = f(RTR + i * 24 + 17, RTR + i * 24 + 18)
            m1 = f(RTR + i * 24 + 18, RTR + i * 24 + 19)
            m2 = f(RTR + i * 24 + 19, RTR + i * 24 + 20)
            t1 = vred(mx, pb[bi][:, 0:8], ALU.max, deps=[tmm])
            t2 = vts(mx, mx, -1.0, None, ALU.mult, deps=[t1])
            t3 = act(pr, pb[bi][:, 0:8], AF.Exp, deps=[t2], bias=mx)
            bank_done(bi, t3)
            t4 = vred(ms, pr, ALU.add, deps=[t3])
            t5 = vrecip(ms, ms, deps=[t4])
            t6 = vts(pr, pr, ms, None, ALU.mult, deps=[t5])
            t7 = vred(m1, pr, ALU.min, deps=[t6])
            t8 = vts(rx, pr, m1, None, ALU.is_le, deps=[t7])
            t9 = vstt(rx, rx, 2.0, pr, ALU.mult, ALU.add, deps=[t8])
            t11 = vred(m2, rx, ALU.min, deps=[t9])
            t12 = vts(rx, pr, m2, None, ALU.is_gt, deps=[t11])
            t13 = vtt(rx, rx, pr, ALU.mult, deps=[t12])
            t14 = vred(ms, rx, ALU.add, deps=[t13])
            t15 = vrecip(ms, ms, deps=[t14])
            t16 = vts(rx, rx, ms, None, ALU.mult, deps=[t15])
            for e_i in range(8):
                bj = nb()
                t17 = ptr(bj, (0, 128), rx[:, e_i:e_i + 1], deps=[t16], rows=1)
                t18 = vcopy(wfT[0:1, e_i * WIN + i * 128:e_i * WIN + (i + 1) * 128],
                            pb[bj][0:1, 0:128], deps=[t17])
                bank_done(bj, t18)
                wf_toks.append(t18)
        hw_toks = {}
        prev_h2 = {0: None, 1: None}
        for e_i in range(10):
            bi1 = nb(); bi2 = nb()
            tm1 = tm2 = None
            for c in range(NCH):
                if e_i < 8:
                    l1 = bb(MW1 + e_i * 1024 + c * 128, MW1 + e_i * 1024 + (c + 1) * 128)
                    l2 = bb(MW2 + e_i * 1024 + c * 128, MW2 + e_i * 1024 + (c + 1) * 128)
                else:
                    k = e_i - 8
                    l1 = bb(WS1 + c * 256 + k * 128, WS1 + c * 256 + (k + 1) * 128)
                    l2 = bb(WS2 + c * 256 + k * 128, WS2 + c * 256 + (k + 1) * 128)
                rhs = bb(XT4 + c * WIN, XT4 + (c + 1) * WIN)
                tm1 = mm(bi1, l1, rhs, deps=(trt + wld) if c == 0 else [tm1],
                         start=(c == 0), stop=(c == NCH - 1))
                tm2 = mm(bi2, l2, rhs, deps=(trt + wld) if c == 0 else [tm2],
                         start=(c == 0), stop=(c == NCH - 1))
            hslot = bb(HST + e_i * WIN, HST + (e_i + 1) * WIN)
            h2o = H2SC + (e_i % 2) * WIN
            ts1 = act(hslot, pb[bi1][:, :], AF.Silu, deps=[tm1])
            bank_done(bi1, ts1)
            ts2 = act(bb(h2o, h2o + WIN), pb[bi2][:, :], AF.Identity,
                      deps=[tm2] + ([prev_h2[e_i % 2]] if prev_h2[e_i % 2] else []))
            bank_done(bi2, ts2)
            th = vtt(hslot, hslot, bb(h2o, h2o + WIN), ALU.mult, deps=[ts1, ts2])
            prev_h2[e_i % 2] = th
            if e_i < 8:
                bj = nb()
                tbr = mm(bj, cHPR[0:1, :], wfT[0:1, e_i * WIN:(e_i + 1) * WIN], deps=wf_toks,
                         psl=(0, 128, 0, WIN))
                tw = vtt(hslot, hslot, pb[bj][:, :], ALU.mult, deps=[th, tbr])
                bank_done(bj, tw)
                hw_toks[e_i] = tw
            else:
                hw_toks[e_i] = th
        evs = []
        for oc in range(NCH):
            bo = nb()
            tmm = None
            first = True
            for k in range(2):
                tmm = mm(bo, bb(WS3 + k * 1024 + oc * 128, WS3 + k * 1024 + (oc + 1) * 128),
                         bb(HST + (8 + k) * WIN, HST + (9 + k) * WIN),
                         deps=([hw_toks[8], hw_toks[9]] + wld) if first else [tmm],
                         start=first, stop=False)
                first = False
            for e_i in range(8):
                tmm = mm(bo, bb(MW3 + e_i * 1024 + oc * 128, MW3 + e_i * 1024 + (oc + 1) * 128),
                         bb(HST + e_i * WIN, HST + (e_i + 1) * WIN),
                         deps=[tmm, hw_toks[e_i]], start=False, stop=(e_i == 7))
            te = act(f(OBF + oc * WIN, OBF + (oc + 1) * WIN), pb[bo][:, :], AF.Identity, deps=[tmm])
            bank_done(bo, te)
            evs.append(te)
        ts_, adds = out_window(w, evs, x1h, x2h, [])
        prev_w = [ts_] + adds

    P5aB = phase()
    wld5 = []
    for c in range(NCH):
        wld5.append(p.dma(bb(WQ + c * 1024, WQ + (c + 1) * 1024), Wq_h[c * 128:(c + 1) * 128, :],
                          queue="gpsimd", deps=P5aB))
        wld5.append(p.dma(bb(WK + c * 1024, WK + (c + 1) * 1024), Wk_h[c * 128:(c + 1) * 128, :],
                          queue="gpsimd", deps=P5aB))
        wld5.append(p.dma(bb(WV + c * 1024, WV + (c + 1) * 1024), Wv_h[c * 128:(c + 1) * 128, :],
                          queue="gpsimd", deps=P5aB))
    prev_w = P5aB
    last_use = {}
    for w in range(NW):
        tl, apl = ln_window(x2h, w * WIN, prev_w, OBF)
        tts = tr_window(OBF, lambda c, i: bb(XT5 + c * WIN + i * 128, XT5 + c * WIN + (i + 1) * 128),
                        apl, deps=prev_w)
        trt = [t for row in tts for t in row]
        wtoks = []
        for oc in range(NCH):
            for (Woff, BUF, hbm) in ((WQ, QWB, qth), (WK, KWB, kth)):
                bi = nb()
                tmm = None
                for c in range(NCH):
                    tmm = mm(bi, bb(Woff + c * 1024 + oc * 128, Woff + c * 1024 + (oc + 1) * 128),
                             bb(XT5 + c * WIN, XT5 + (c + 1) * WIN),
                             deps=(trt + wld5) if c == 0 else [tmm], start=(c == 0), stop=(c == NCH - 1))
                slot = BUF + (oc % 2) * WIN
                dd = [last_use[slot]] if slot in last_use else []
                te = act(bb(slot, slot + WIN), pb[bi][:, :], AF.Identity, deps=[tmm] + dd)
                bank_done(bi, te)
                tst = p.dma(hbm[oc, :, w * WIN:(w + 1) * WIN], bb(slot, slot + WIN), deps=[te])
                last_use[slot] = tst
                wtoks.append(tst)
            bi = nb()
            tmm = None
            for c in range(NCH):
                tmm = mm(bi, bb(WV + c * 1024 + oc * 128, WV + c * 1024 + (oc + 1) * 128),
                         bb(XT5 + c * WIN, XT5 + (c + 1) * WIN),
                         deps=(trt + wld5) if c == 0 else [tmm], start=(c == 0), stop=(c == NCH - 1))
            slot = VWB + (oc % 2) * WIN
            dd = [last_use[slot]] if slot in last_use else []
            te = act(bb(slot, slot + WIN), pb[bi][:, :], AF.Identity, deps=[tmm] + dd)
            bank_done(bi, te)
            vslot = VTB + (oc % 2) * WIN
            dv = [last_use[vslot]] if vslot in last_use else []
            ttr = [p.dma_tr(bb(vslot + i * 128, vslot + (i + 1) * 128),
                            bb(slot + i * 128, slot + (i + 1) * 128), deps=[te] + dv) for i in range(4)]
            last_use[slot] = ttr[-1]
            tst = p.dma(vth[oc, w * 4:(w + 1) * 4].rearrange("i p d -> p i d"),
                        bb(vslot, vslot + WIN).rearrange("p (i d) -> p i d", d=128), deps=ttr)
            last_use[vslot] = tst
            wtoks.append(tst)
        prev_w = wtoks

    P5bB = phase()
    chunk_done = {-1: P5bB, -2: P5bB}
    load_toks = {}

    def p5_loads(c):
        s = c % 2
        dd = list(chunk_done[c - 2])
        tq_ = p.dma(bb(QTC + s * 4096, QTC + (s + 1) * 4096), qth[c], deps=dd)
        tk_ = p.dma(bb(KTC + s * 4096, KTC + (s + 1) * 4096), kth[c], deps=dd)
        svo = SVC + s * 4160
        tms = vmemset(bb(svo, svo + 4160), 1.0, deps=dd)
        svv = BB[:, svo:svo + 4160].rearrange("p (t u) -> p t u", u=130)
        tvs = [p.dma(svv[:, :, 0:64], vth[c, :, :, 0:64].rearrange("t p d -> p t d"), deps=[tms]),
               p.dma(svv[:, :, 65:129], vth[c, :, :, 64:128].rearrange("t p d -> p t d"), deps=[tms])]
        load_toks[c] = [tq_, tk_] + tvs

    p5_loads(0)
    lrow_free = {0: None, 1: None}
    SCB = [0, 1, 5, 6]
    sc_free = {0: None, 1: None, 2: None, 3: None}
    pt_free = {0: None, 1: None, 2: None, 3: None}
    for c in range(NCH):
        if c + 1 < NCH:
            p5_loads(c + 1)
        s = c % 2
        svo = SVC + s * 4160
        ctoks = []
        po_free = {0: (chunk_done[c - 1] if c > 0 else P5bB), 1: (chunk_done[c - 1] if c > 0 else P5bB)}
        for qw in range(NW):
            nkt = 4 * (qw + 1)
            tpv_l = {}
            pend = None
            for kt in range(nkt + 1):
                cur = []
                if kt < nkt:
                    for ha in range(2):
                        hoff = ha * 64
                        tp = (64, 0) if ha else None
                        sb_ = (kt % 2) * 2 + ha
                        tsc = mm(SCB[sb_], BB[hoff:hoff + 64, KTC + s * 4096 + kt * 128: KTC + s * 4096 + (kt + 1) * 128],
                                 BB[hoff:hoff + 64, QTC + s * 4096 + qw * WIN: QTC + s * 4096 + (qw + 1) * WIN],
                                 deps=load_toks[c] + ([sc_free[sb_]] if sc_free[sb_] else []), tp=tp)
                        dd = [tsc] + ([pt_free[sb_]] if pt_free[sb_] else [])
                        te = act(bb(PT + sb_ * WIN, PT + (sb_ + 1) * WIN), pb[SCB[sb_]][:, :], AF.Exp,
                                 deps=dd, scale=0.125, bias=mshiftc)
                        sc_free[sb_] = te
                        if kt >= 4 * qw:
                            i = kt - 4 * qw
                            te = vtt(bb(PT + sb_ * WIN, PT + (sb_ + 1) * WIN),
                                     bb(PT + sb_ * WIN, PT + (sb_ + 1) * WIN),
                                     cmb[:, i * WIN:(i + 1) * WIN], ALU.mult, deps=[te])
                        cur.append((ha, sb_, te))
                if pend is not None:
                    pkt = kt - 1
                    for ha, sb_, te in pend:
                        tpv_ = mm(2 + ha, bb(svo + pkt * 130 + ha * 65, svo + pkt * 130 + (ha + 1) * 65),
                                  bb(PT + sb_ * WIN, PT + (sb_ + 1) * WIN),
                                  deps=[te] + (list(po_free[ha]) if pkt == 0 else []),
                                  start=(pkt == 0), stop=(pkt == nkt - 1), psl=(0, 65, 0, WIN))
                        pt_free[sb_] = tpv_
                        tpv_l[ha] = tpv_
                pend = cur
            tno_l = {}
            for ha in range(2):
                po = 2 + ha
                lo = ha * WIN
                tlc = vcopy(lrow[64:65, lo:lo + WIN], pb[po][64:65, 0:WIN],
                            deps=[tpv_l[ha]] + ([lrow_free[ha]] if lrow_free[ha] else []))
                tev = act(lrow[0:64, lo:lo + WIN], pb[po][0:64, 0:WIN], AF.Identity, deps=[tlc])
                trc = vrecip(lrow[64:65, lo:lo + WIN], lrow[64:65, lo:lo + WIN], deps=[tlc])
                bb_ = 4 + ha * 3
                tbc_ = mm(bb_, cHPR[64:65, 0:64], lrow[64:65, lo:lo + WIN],
                          deps=[trc] + list(pfree[bb_]), psl=(0, 64, 0, WIN), tp=(64, 0))
                AT = ATA if ha == 0 else ATB
                tno = vtt(BB[0:64, AT + s * 4096 + qw * WIN: AT + s * 4096 + (qw + 1) * WIN],
                          lrow[0:64, lo:lo + WIN], pb[bb_][0:64, 0:WIN], ALU.mult, deps=[tev, tbc_])
                pfree[bb_] = (tno,)
                po_free[ha] = [tev]
                lrow_free[ha] = tno
                ctoks.append(tno)
        tsa = p.dma(ath[c, 0:64, :], BB[0:64, ATA + s * 4096: ATA + (s + 1) * 4096], deps=ctoks)
        tsb = p.dma(ath[c, 64:128, :], BB[0:64, ATB + s * 4096: ATB + (s + 1) * 4096], deps=ctoks)
        chunk_done[c] = [tsa, tsb] + ctoks

    P5wB = phase()
    wldo = [p.dma(bb(SWO + c * 1024, SWO + (c + 1) * 1024), Wo_h[c * 128:(c + 1) * 128, :],
                  queue="gpsimd", deps=P5wB) for c in range(NCH)]
    awin_last = None
    prev_w = P5wB
    awin_last = {0: None, 1: None}
    tla_w = {}
    def awin_loads(w):
        ao = AWIN + (w % 2) * 4096
        dd = ([awin_last[w % 2]] if awin_last[w % 2] else []) + list(P5wB)
        tla_w[w] = [p.dma(bb(ao + c * WIN, ao + (c + 1) * WIN), ath[c][:, w * WIN:(w + 1) * WIN],
                          deps=dd) for c in range(NCH)]
    awin_loads(0)
    for w in range(NW):
        if w + 1 < NW:
            awin_loads(w + 1)
        ao = AWIN + (w % 2) * 4096
        evs = []
        last_mm = None
        for oc in range(NCH):
            bo = nb()
            tmm = None
            for c in range(NCH):
                tmm = mm(bo, bb(SWO + c * 1024 + oc * 128, SWO + c * 1024 + (oc + 1) * 128),
                         bb(ao + c * WIN, ao + (c + 1) * WIN),
                         deps=(tla_w[w] + wldo) if c == 0 else [tmm], start=(c == 0), stop=(c == NCH - 1))
            te = act(f(OBF + oc * WIN, OBF + (oc + 1) * WIN), pb[bo][:, :], AF.Identity, deps=[tmm])
            bank_done(bo, te)
            last_mm = tmm
            evs.append(te)
        awin_last[w % 2] = last_mm
        ts_, adds = out_window(w, evs, x2h, x3h, prev_w)
        prev_w = [ts_] + adds

    P6B = phase()
    wb_toks = []
    tprev = None
    slot_last = {0: None, 1: None}
    for ri in range(2):
        WTo = WTR if ri == 0 else WTI
        for r in range(4):
            for cc in range(4):
                slot = cc % 2
                uo = UVU if slot == 0 else UVU2
                vo = UVV if slot == 0 else UVV2
                dd = [slot_last[slot]] if slot_last[slot] else list(P6B)
                tU = p.dma(f(uo, uo + 512), uvU_h[ri, r, cc], deps=dd)
                tV = p.dma(f(vo, vo + 512), uvV_h[ri, r, cc], deps=dd)
                if cc == 0:
                    tprev = vtt(f(FACC, FACC + 512), f(uo, uo + 512), f(vo, vo + 512), ALU.mult,
                                deps=[tU, tV] + ([tprev] if tprev else []))
                else:
                    tm_ = vtt(f(FTMP, FTMP + 512), f(uo, uo + 512), f(vo, vo + 512), ALU.mult,
                              deps=[tU, tV, tprev])
                    tprev = vtt(f(FACC, FACC + 512), f(FACC, FACC + 512), f(FTMP, FTMP + 512),
                                ALU.add, deps=[tm_])
                slot_last[slot] = tprev
            tw = act(bb(WTo + r * 512, WTo + (r + 1) * 512), f(FACC, FACC + 512), AF.Identity,
                     deps=[tprev])
            tprev = tw
            wb_toks.append(tw)

    prev_w = P6B
    final_stores = []
    for w in range(NW):
        tl, apl = ln_window(x3h, w * WIN, prev_w, OBF)
        tts = tr_window(OBF, lambda c, i: bb(XT6 + c * WIN + i * 128, XT6 + c * WIN + (i + 1) * 128),
                        apl, deps=prev_w)
        trt = [t for row in tts for t in row]
        tneg = [vts(bb(XT6N + k * WIN, XT6N + (k + 1) * WIN), bb(XT6 + (4 + k) * WIN, XT6 + (5 + k) * WIN),
                    -1.0, None, ALU.mult, deps=[tts[i][4 + k] for i in range(4)]) for k in range(4)]
        evs = []
        for oc in range(NCH):
            bo = nb()
            tmm = None
            first = True
            for k in range(4):
                if oc < 4:
                    l1 = bb(WTR + k * 512 + oc * 128, WTR + k * 512 + (oc + 1) * 128)
                    l2 = bb(WTI + k * 512 + oc * 128, WTI + k * 512 + (oc + 1) * 128)
                    r1 = bb(XT6 + k * WIN, XT6 + (k + 1) * WIN)
                    r2 = bb(XT6N + k * WIN, XT6N + (k + 1) * WIN)
                else:
                    l1 = bb(WTI + k * 512 + (oc - 4) * 128, WTI + k * 512 + (oc - 3) * 128)
                    l2 = bb(WTR + k * 512 + (oc - 4) * 128, WTR + k * 512 + (oc - 3) * 128)
                    r1 = bb(XT6 + k * WIN, XT6 + (k + 1) * WIN)
                    r2 = bb(XT6 + (4 + k) * WIN, XT6 + (5 + k) * WIN)
                tmm = mm(bo, l1, r1, deps=(trt + wb_toks + tneg) if first else [tmm],
                         start=first, stop=False)
                first = False
                tmm = mm(bo, l2, r2, deps=[tmm], start=False, stop=(k == 3))
            te = act(f(OBF + oc * WIN, OBF + (oc + 1) * WIN), pb[bo][:, :], AF.Identity,
                     deps=[tmm], bias=sttbr[:, oc:oc + 1])
            bank_done(bo, te)
            evs.append(te)
        ts_, adds = out_window(w, evs, x3h, out_h, prev_w)
        prev_w = [ts_] + adds
        final_stores.append(ts_)

    vmemset(f(CONVT, CONVT + 1), 0.0, deps=final_stores)

    with nc.Block() as block:
        p.emit(block)
    return nc


def host_prep(inp):
    g = {k: np.ascontiguousarray(np.asarray(v, np.float32)) for k, v in inp.items()}
    d = {}
    d["gate_wt"] = np.ascontiguousarray(g["L0_attn_gate_w"].T)
    d["B_wt"] = np.ascontiguousarray(g["L0_attn_B_w"].T)
    d["C_wt"] = np.ascontiguousarray(g["L0_attn_C_w"].T)
    d["D_wt"] = np.ascontiguousarray(g["L0_attn_D_w"].T)
    d["B_b"] = g["L0_attn_B_b"].reshape(S, 1)
    d["C_b"] = g["L0_attn_C_b"].reshape(S, 1)
    d["D_b"] = np.ascontiguousarray(g["L0_attn_D_b"].reshape(NCH, 128).T)
    d["gate_b"] = np.ascontiguousarray(g["L0_attn_gate_b"].reshape(2 * NCH, 128).T)
    d["conv_taps"] = g["L0_attn_conv_w"].reshape(D, 3).reshape(NCH, 128, 3)
    d["conv_b"] = np.ascontiguousarray(g["L0_attn_conv_b"].reshape(NCH, 128).T)
    d["A_m"] = g["L0_attn_A"]
    d["A_mt"] = np.ascontiguousarray(g["L0_attn_A"].T)
    d["dtA"] = np.asarray(g["L0_attn_dt_A"], np.float32).reshape(1, 1)
    d["g_col"] = g["L0_attn_ln_st_g"].reshape(S, 1)
    d["beta_col"] = g["L0_attn_ln_st_b"].reshape(S, 1)
    d["g_rep64"] = np.ascontiguousarray(np.tile(g["L0_attn_ln_st_g"].reshape(1, S), (S, 1)))
    d["lngc"] = np.ascontiguousarray(g["L0_attn_ln_in_g"].reshape(NCH, 128).T)
    d["lnbc"] = np.ascontiguousarray(g["L0_attn_ln_in_b"].reshape(NCH, 128).T)
    d["Pc"] = (np.eye(S) - 1.0 / S).astype(np.float32)
    d["ident64"] = np.eye(S, dtype=np.float32)
    d["ident128"] = np.eye(128, dtype=np.float32)
    consts = np.zeros((128, 8), np.float32)
    consts[:, 0] = EPS
    consts[:, 1] = EPS * EPS
    consts[:, 2] = -SHIFT
    d["consts"] = consts
    d["onesdiv"] = np.full((S, S), 1.0 / S, np.float32)
    d["hp_rows"] = np.ones((128, 128), np.float32)
    cmask = np.zeros((4, 128, WIN), np.float32)
    for i in range(4):
        for kp in range(128):
            cmask[i, kp, i * 128 + kp:] = 1.0
    d["cmask"] = cmask.astype(BF)
    d["moe_w1"] = g["L0_ffn_w1"]
    d["moe_w2"] = g["L0_ffn_w2"]
    d["moe_w3"] = g["L0_ffn_w3"]
    d["ws1"] = g["L0_ffn_ws1"]
    d["ws2"] = g["L0_ffn_ws2"]
    d["ws3"] = g["L0_ffn_ws3"]
    d["wr"] = g["L0_ffn_wr"]
    d["Wq_t"] = np.ascontiguousarray(g["L1_attn_Wq"].T)
    d["Wk_t"] = np.ascontiguousarray(g["L1_attn_Wk"].T)
    d["Wv_t"] = np.ascontiguousarray(g["L1_attn_Wv"].T)
    d["Wo_t"] = np.ascontiguousarray(g["L1_attn_Wo"].T)
    uvU = np.zeros((2, 4, 4, 128, 512), np.float32)
    uvV = np.zeros((2, 4, 4, 128, 512), np.float32)
    for ri, (G1k, G2k) in enumerate((("L1_ffn_G1_R", "L1_ffn_G2_R"), ("L1_ffn_G1_I", "L1_ffn_G2_I"))):
        for cc in range(4):
            Ac = g[G1k][0, :, cc, :]
            Bc = g[G2k][cc, :, 0, :]
            U = np.broadcast_to(Ac.T[:, None, :, None], (32, 16, 32, 16)).reshape(512, 512)
            V = np.broadcast_to(Bc.T[None, :, None, :], (32, 16, 32, 16)).reshape(512, 512)
            for r in range(4):
                uvU[ri, r, cc] = U[r * 128:(r + 1) * 128]
                uvV[ri, r, cc] = V[r * 128:(r + 1) * 128]
    d["uvU"] = uvU
    d["uvV"] = uvV
    d["tt_bias"] = np.ascontiguousarray(np.concatenate([g["L1_ffn_bias_real"], g["L1_ffn_bias_imag"]]).reshape(NCH, 128).T)
    return d


def kernel(**inp):
    nc = bass.Bass("TRN2", target_bir_lowering=False)
    build(nc, dbg=False)
    base = host_prep(inp)
    x = np.asarray(inp["L0_attn_x"], np.float32)
    in_maps = []
    for b in range(8):
        m = dict(base)
        m["x0"] = np.ascontiguousarray(x[b])
        in_maps.append(m)
    res = run_bass_kernel_spmd(nc, in_maps, core_ids=list(range(8)))
    return np.stack([np.asarray(res.results[i]["out"], np.float32) for i in range(8)])


# version 13 (final)
# speedup vs baseline: 1.7433x; 1.0066x over previous
import numpy as np
import ml_dtypes
import concourse.bass as bass
import concourse.mybir as mybir
from concourse.bass_utils import run_bass_kernel_spmd

F32 = mybir.dt.float32
BF16 = mybir.dt.bfloat16
AF = mybir.ActivationFunctionType
ALU = mybir.AluOpType
AX = mybir.AxisListType
BF = ml_dtypes.bfloat16

T = 4096
D = 1024
NCH = 8
WIN = 512
NW = 8
S = 64
KCH = 64
LCH = T // KCH
WARM = 256
STEPS = LCH + WARM
EPS = 1e-5
SHIFT = 12.0

NF = 18600
HIST = 0
OBF = 0
UVU, UVV, UVU2, UVV2, FACC, FTMP = 4096, 4608, 5120, 5632, 6144, 6656
SBMP = 9216
STAT = 13824
XWIN = 13888
CONVT = 17984
RTR = 18496

NB = 48200
XHT = 0
XHAT = 8224
XN2 = 12320
XGW = 20512
SGV = 24608
GSO = 25632
WBC = 26656
GWV = 27680
SCMB = 44064
YB = 0
GSL = 4096
SDW = 8224
XHAT4 = 0
XT4 = 0
HST = 4096
H2SC = 9216
MW1, MW2, MW3 = 10240, 18432, 26624
WS1, WS2, WS3, SWR = 34816, 36864, 38912, 40960
XHAT5, XT5 = 0, 4096
WQ, WK, WV = 8192, 16384, 24576
QWB, KWB, VWB, VTB = 32768, 33792, 34816, 35840
QTC, KTC = 0, 8192
SVC = 16384
PT = 24704
ATA, ATB = 26752, 34944
AWIN, SWO = 0, 8192
XHAT6, XT6, XT6N, WTR, WTI = 0, 4096, 8192, 10240, 12288


class Prog:
    def __init__(self, nc):
        self.nc = nc
        self.engines = ("tensor", "vector", "scalar", "sync", "gpsimd")
        self.ops = {e: [] for e in self.engines}
        self.semname = {"tensor": "s_t", "vector": "s_v", "scalar": "s_s",
                        "sync": "s_y", "gpsimd": "s_g"}
        self.sems = {}
        self.cnt = {}
        for s in self.semname.values():
            self.sems[s] = nc.semaphore(s).__enter__()
            self.cnt[s] = 0

    def op(self, engine, fn, deps=(), inc=1):
        sem = self.semname[engine]
        waits = {}
        for dd in deps:
            if dd is None:
                continue
            sn, val = dd
            waits[sn] = max(waits.get(sn, 0), val)
        self.cnt[sem] += inc
        tok = (sem, self.cnt[sem])
        self.ops[engine].append((fn, sorted(waits.items()), (sem, inc)))
        return tok

    def dma(self, out, in_, deps=(), queue="sync"):
        return self.op(queue, lambda e, o=out, i=in_: e.dma_start(out=o, in_=i),
                       deps=deps, inc=16)

    def dma_tr(self, out, in_, deps=(), queue="scalar"):
        return self.op(queue, lambda e, o=out, i=in_: e.dma_start(out=o, in_=i, transpose=True),
                       deps=deps, inc=16)

    def barrier(self):
        return [(self.semname[e], self.cnt[self.semname[e]]) for e in self.engines]

    def emit(self, block):
        def make(engine):
            def body(eng):
                for fn, waits, inc in self.ops[engine]:
                    for sn, val in waits:
                        eng.wait_ge(self.sems[sn], val)
                    inst = fn(eng)
                    sn, n = inc
                    inst.then_inc(self.sems[sn], n)
            return body
        block.tensor(make("tensor"))
        block.vector(make("vector"))
        block.scalar(make("scalar"))
        block.sync(make("sync"))
        block.gpsimd(make("gpsimd"))


def build(nc, dbg=False):
    dram = lambda name, shape, d=F32, kind="ExternalInput": nc.dram_tensor(name, shape, d, kind=kind).ap()
    x0 = dram("x0", [T, D])
    gate_wt = dram("gate_wt", [D, 2 * D])
    B_wt = dram("B_wt", [D, S]); C_wt = dram("C_wt", [D, S]); D_wt_h = dram("D_wt", [S, D])
    Bb_h = dram("B_b", [S, 1]); Cb_h = dram("C_b", [S, 1]); Db_h = dram("D_b", [128, NCH])
    gb_h = dram("gate_b", [128, 2 * NCH])
    taps_h = dram("conv_taps", [NCH, 128, 3])
    cvb_h = dram("conv_b", [128, NCH])
    A_h = dram("A_m", [S, S]); At_h = dram("A_mt", [S, S]); dtA_h = dram("dtA", [1, 1])
    g_h = dram("g_col", [S, 1]); beta_h = dram("beta_col", [S, 1]); grep_h = dram("g_rep64", [S, S])
    lngc_h = dram("lngc", [128, NCH]); lnbc_h = dram("lnbc", [128, NCH])
    Pc_h = dram("Pc", [S, S]); id_h = dram("ident64", [S, S])
    cst_h = dram("consts", [128, 8])
    od_h = dram("onesdiv", [S, S])
    hpr_h = dram("hp_rows", [128, 128])
    id128_h = dram("ident128", [128, 128])
    cm_h = dram("cmask", [4, 128, WIN], BF16)
    w1_h = dram("moe_w1", [8, D, 128]); w2_h = dram("moe_w2", [8, D, 128]); w3_h = dram("moe_w3", [8, 128, D])
    ws1_h = dram("ws1", [D, 256]); ws2_h = dram("ws2", [D, 256]); ws3_h = dram("ws3", [256, D]); wr_h = dram("wr", [D, 8])
    Wq_h = dram("Wq_t", [D, D]); Wk_h = dram("Wk_t", [D, D]); Wv_h = dram("Wv_t", [D, D]); Wo_h = dram("Wo_t", [D, D])
    uvU_h = dram("uvU", [2, 4, 4, 128, 512]); uvV_h = dram("uvV", [2, 4, 4, 128, 512])
    ttbr_h = dram("tt_bias", [128, NCH])
    out_h = dram("out", [T, D], kind="ExternalOutput")
    kd = "ExternalOutput" if dbg else "Internal"
    x1h = dram("x1h", [T, D], kind=kd)
    x2h = dram("x2h", [T, D], kind=kd)
    x3h = dram("x3h", [T, D], kind=kd)
    gatesh = dram("gatesh", [NCH, 128, T], d=BF16, kind=("ExternalOutput" if dbg else "Internal"))
    if dbg:
        dbpp_h = dram("dbg_bpp", [S, 4608], kind="ExternalOutput")
        dzh_h = dram("dbg_zh", [S, T], kind="ExternalOutput")
        dcm_h = dram("dbg_cm", [S, T], d=BF16, kind="ExternalOutput")
        dyb_h = dram("dbg_yb", [S, T], d=BF16, kind="ExternalOutput")
        dxht_h = dram("dbg_xht", [128, 8192], d=BF16, kind="ExternalOutput")
        dxn2_h = dram("dbg_xn2", [128, 4096], d=BF16, kind="ExternalOutput")
    qth = dram("qth", [NCH, 128, T], d=BF16, kind="Internal")
    kth = dram("kth", [NCH, 128, T], d=BF16, kind="Internal")
    vth = dram("vth", [NCH, 32, 128, 128], d=BF16, kind="Internal")
    ath = dram("ath", [NCH, 128, T], d=BF16, kind="Internal")

    sbuf = lambda name, shape, d=F32: nc.sbuf_tensor(name, shape, d).__enter__()

    cC = sbuf("cC", [128, 8])
    cOD = sbuf("cOD", [S, S])
    cID = sbuf("cID", [S, S]); cPc = sbuf("cPc", [S, S]); cID128 = sbuf("cID128", [128, 128])
    cA = sbuf("cA", [S, S]); cAt = sbuf("cAt", [S, S]); cdt = sbuf("cdt", [1, 1])
    cg = sbuf("cg", [S, 1]); cbeta = sbuf("cbeta", [S, 1]); cgrep = sbuf("cgrep", [S, S])
    cHPR = sbuf("cHPR", [128, 128])
    lngc = sbuf("lngc_s", [128, NCH]); lnbc = sbuf("lnbc_s", [128, NCH])
    staps = sbuf("staps", [128, NCH * 4])
    scvb = sbuf("scvb", [128, NCH])
    sBb = sbuf("sBb", [S, 1]); sCb = sbuf("sCb", [S, 1])
    sDb = sbuf("sDb", [128, NCH]); sgbias = sbuf("sgbias", [128, 2 * NCH])
    sttbr = sbuf("sttbr", [128, NCH])
    scanM = sbuf("scanM", [S, S])
    bacol = sbuf("bacol", [S, 1])
    eE = sbuf("eE", [S, S]); eF = sbuf("eF", [S, S]); eH = sbuf("eH", [S, S])
    eAd = sbuf("eAd", [S, S]); eAdt = sbuf("eAdt", [S, S])
    eT1 = sbuf("eT1", [S, S]); eT2 = sbuf("eT2", [S, S])
    sdtc = sbuf("sdtc", [S, 1])
    rowtmp = sbuf("rowtmp", [1, S])
    cmb = sbuf("cmb", [128, 4 * WIN], BF16)
    szh2t = sbuf("szh2t", [S, LCH])
    hcell = sbuf("hcell", [128, 64], BF16)
    craw = [sbuf("craw0", [S, KCH])]
    csqb = [sbuf("csqb0", [S, KCH])]
    ztp = [sbuf(f"ztp{h}", [S, KCH]) for h in range(2)]
    wfT = sbuf("wfT", [1, 8 * WIN])
    lrow = sbuf("lrow", [128, 2 * WIN])

    FP = sbuf("FP", [128, NF])
    BB = sbuf("BB", [128, NB], BF16)
    f = lambda a, b: FP[:, a:b]
    f6 = lambda a, b: FP[0:S, a:b]
    bb = lambda a, b: BB[:, a:b]
    b6 = lambda a, b: BB[0:S, a:b]

    p = Prog(nc)
    pb = [nc.psum_tensor(f"pb{i}", [128, 512], F32).__enter__() for i in range(8)]
    pfree = [()] * 8
    _rr = [0]

    def nb():
        i = _rr[0]; _rr[0] = (i + 1) % 8
        return i

    def bank_done(i, tok):
        pfree[i] = (tok,)

    def mm(bi, lhsT, rhs, deps=(), start=True, stop=True, tp=None, psl=None):
        dst = pb[bi][psl[0]:psl[1], psl[2]:psl[3]] if psl else pb[bi][:, :]
        dd = (list(pfree[bi]) if start else []) + list(deps)
        return p.op("tensor", lambda e, o=dst, l=lhsT, r=rhs, s0=start, s1=stop, t_=tp: e.matmul(
            o, l, r, start=s0, stop=s1, tile_position=t_, skip_group_check=True), deps=dd)

    def ptr(bi, cols, src, deps=(), rows=128):
        dst = pb[bi][0:rows, cols[0]:cols[1]]
        dd = list(pfree[bi]) + list(deps)
        return p.op("tensor", lambda e, o=dst, i=src: e.transpose(o, i, cID128[:, :]), deps=dd)

    def act(out, in_, func, deps=(), bias=None, scale=1.0):
        if bias is None:
            return p.op("scalar", lambda e, o=out, i=in_, sc=scale, fu=func: e.activation(
                out=o, in_=i, func=fu, scale=sc), deps=deps)
        return p.op("scalar", lambda e, o=out, i=in_, b=bias, sc=scale, fu=func: e.activation(
            out=o, in_=i, func=fu, bias=b, scale=sc), deps=deps)

    def vtt(out, a, b2, opn, deps=()):
        return p.op("vector", lambda e, o=out, x=a, y=b2, oo=opn: e.tensor_tensor(o, x, y, oo),
                    deps=deps)

    def vts(out, in0, s1, s2, op0, op1=None, deps=()):
        if s2 is None:
            return p.op("vector", lambda e, o=out, i=in0, a=s1, oo=op0: e.tensor_scalar(
                out=o, in0=i, scalar1=a, scalar2=None, op0=oo), deps=deps)
        return p.op("vector", lambda e, o=out, i=in0, a=s1, b2=s2, oo=op0, o1=op1: e.tensor_scalar(
            out=o, in0=i, scalar1=a, scalar2=b2, op0=oo, op1=o1), deps=deps)

    def vstt(out, in0, s1, in1, op0, op1, deps=()):
        return p.op("vector", lambda e, o=out, i=in0, a=s1, j=in1, oo=op0, o1=op1:
                    e.scalar_tensor_tensor(out=o, in0=i, scalar=a, in1=j, op0=oo, op1=o1),
                    deps=deps)

    def vcopy(out, in_, deps=()):
        return p.op("vector", lambda e, o=out, i=in_: e.tensor_copy(o, i), deps=deps)

    def vred(out, in_, opn, deps=()):
        return p.op("vector", lambda e, o=out, i=in_, oo=opn: e.tensor_reduce(
            out=o, in_=i, axis=AX.X, op=oo), deps=deps)

    def vrecip(out, in_, deps=()):
        return p.op("vector", lambda e, o=out, i=in_: e.reciprocal(out=o, in_=i), deps=deps)

    def vmemset(ap_, val, deps=()):
        return p.op("vector", lambda e, o=ap_, v=val: e.memset(o, v), deps=deps)

    def phase():
        bar = p.barrier()
        for i in range(8):
            pfree[i] = tuple(bar)
        return bar

    epsc = cC[:, 0:1]; eps2c = cC[:, 1:2]; mshiftc = cC[:, 2:3]

    ld = [p.dma(cC[:, :], cst_h), p.dma(cOD[:, :], od_h),
          p.dma(cID[:, :], id_h), p.dma(cPc[:, :], Pc_h), p.dma(cID128[:, :], id128_h),
          p.dma(cA[:, :], A_h), p.dma(cAt[:, :], At_h), p.dma(cdt[:, :], dtA_h),
          p.dma(cg[:, :], g_h), p.dma(cbeta[:, :], beta_h), p.dma(cgrep[:, :], grep_h),
          p.dma(cHPR[:, :], hpr_h),
          p.dma(lngc[:, :], lngc_h), p.dma(lnbc[:, :], lnbc_h),
          p.dma(staps.ap().rearrange("p (c k) -> p c k", k=4)[:, :, 0:3], taps_h.rearrange("c p k -> p c k")),
          p.dma(scvb[:, :], cvb_h),
          p.dma(sBb[:, :], Bb_h), p.dma(sCb[:, :], Cb_h),
          p.dma(sDb[:, :], Db_h),
          p.dma(sgbias[:, :], gb_h),
          p.dma(sttbr[:, :], ttbr_h),
          p.dma(cmb.ap().rearrange("p (j n) -> p j n", n=WIN), cm_h.rearrange("j p n -> p j n"))]

    t_t1 = []
    for c in range(NCH):
        t_t1.append(vts(staps[:, c * 4 + 3:c * 4 + 4], staps[:, c * 4 + 1:c * 4 + 2],
                        1.0, None, ALU.add, deps=ld))

    tok = mm(0, cOD[0:1, :], cdt[:, :], deps=ld, psl=(0, S, 0, 1))
    t_dt = vts(sdtc[:, :], pb[0][0:S, 0:1], float(S) * (2.0 ** -8), None, ALU.mult, deps=[tok])
    bank_done(0, t_dt)
    tA1 = vts(eAd[:, :], cA[:, :], sdtc[:, :], None, ALU.mult, deps=[t_dt] + ld)
    tA2 = vts(eAdt[:, :], cAt[:, :], sdtc[:, :], None, ALU.mult, deps=[t_dt] + ld)

    def horner(Xts, Hout, tdep):
        tH = vcopy(Hout[:, :], cID[:, :], deps=tdep + ld)
        for k in range(6, 0, -1):
            tm_ = mm(1, Xts[:, :], Hout[:, :], deps=[tH], psl=(0, S, 0, S))
            te = act(eT1[:, :], pb[1][0:S, 0:S], AF.Identity, deps=[tm_], scale=1.0 / k)
            bank_done(1, te)
            tH = vtt(Hout[:, :], eT1[:, :], cID[:, :], ALU.add, deps=[te])
        return tH

    tE = horner(eAdt, eE, [tA1, tA2])
    tF = horner(eAd, eF, [tA1, tA2])
    for _ in range(8):
        tm1 = mm(1, eF[:, :], eE[:, :], deps=[tE, tF], psl=(0, S, 0, S))
        te1 = act(eT1[:, :], pb[1][0:S, 0:S], AF.Identity, deps=[tm1])
        bank_done(1, te1)
        tm2 = mm(2, eE[:, :], eF[:, :], deps=[tE, tF], psl=(0, S, 0, S))
        te2 = act(eT2[:, :], pb[2][0:S, 0:S], AF.Identity, deps=[tm2])
        bank_done(2, te2)
        tE = vcopy(eE[:, :], eT1[:, :], deps=[te1, tm2])
        tF = vcopy(eF[:, :], eT2[:, :], deps=[te2, tE])
    tM = vtt(eH[:, :], eF[:, :], cgrep[:, :], ALU.mult, deps=[tE, tF])
    tsm = mm(1, eH[:, :], cPc[:, :], deps=[tM], psl=(0, S, 0, S))
    tsme = act(scanM[:, :], pb[1][0:S, 0:S], AF.Identity, deps=[tsm])
    bank_done(1, tsme)
    tba = mm(2, cbeta[:, :], eE[:, :], deps=[tE], psl=(0, 1, 0, S))
    tbae = vcopy(rowtmp[:, :], pb[2][0:1, 0:S], deps=[tba])
    bank_done(2, tbae)
    tbat = p.op("tensor", lambda e: e.transpose(pb[3][0:S, 0:1], rowtmp[:, :], cID128[0:1, 0:1]),
                deps=[tbae] + list(pfree[3]))
    tbac = vcopy(bacol[:, :], pb[3][0:S, 0:1], deps=[tbat])
    bank_done(3, tbac)

    tgw = [p.dma(bb(GWV + c * 2048, GWV + (c + 1) * 2048), gate_wt[c * 128:(c + 1) * 128, :],
                 queue="gpsimd", deps=ld) for c in range(NCH)]
    tbc = []
    for c in range(NCH):
        tbc.append(p.dma(bb(WBC + c * 2 * S, WBC + c * 2 * S + S), B_wt[c * 128:(c + 1) * 128, :],
                         queue="gpsimd", deps=ld))
        tbc.append(p.dma(bb(WBC + c * 2 * S + S, WBC + (c + 1) * 2 * S), C_wt[c * 128:(c + 1) * 128, :],
                         queue="gpsimd", deps=ld))
    tz = [vmemset(hcell[:, c * 4:c * 4 + 1], 0.0) for c in range(NCH)]

    def ln_window(xsrc, woff, deps, xhat_off, bf16_out=True):
        tl = p.dma(f(XWIN, XWIN + 4 * D).rearrange("p (i d) -> p i d", d=D),
                   xsrc[woff:woff + WIN, :].rearrange("(i p) d -> p i d", p=128), deps=deps)
        apl = []
        for i in range(4):
            xt = f(XWIN + i * D, XWIN + (i + 1) * D)
            st12 = f(STAT + i * 16, STAT + i * 16 + 12)
            mv = f(STAT + i * 16 + 12, STAT + i * 16 + 14)
            sc = f(STAT + i * 16 + 14, STAT + i * 16 + 15)
            t1 = p.op("vector", lambda e, o=st12[:, 0:6], i_=xt[:, 0:512]: e.bn_stats(out=o, in_=i_),
                      deps=[tl])
            t2 = p.op("vector", lambda e, o=st12[:, 6:12], i_=xt[:, 512:1024]: e.bn_stats(out=o, in_=i_),
                      deps=[t1])
            t3 = p.op("vector", lambda e, o=mv, i_=st12: e.bn_aggr(out=o, in_=i_), deps=[t2])
            if bf16_out == "composite":
                t4 = act(sc, mv[:, 1:2], AF.Sqrt, deps=[t3], bias=eps2c, scale=1.0 + EPS)
            else:
                t4 = act(sc, mv[:, 1:2], AF.Sqrt, deps=[t3], bias=epsc, scale=1.0)
            t5 = vrecip(sc, sc, deps=[t4])
            t6 = vts(f(xhat_off + i * D, xhat_off + (i + 1) * D), xt, mv[:, 0:1], sc,
                     ALU.subtract, ALU.mult, deps=[t5])
            apl.append(t6)
        return tl, apl

    def tr_window(xhat_off, dst_fn, apl, deps=()):
        tts = [[None] * NCH for _ in range(4)]
        for i in range(4):
            for c in range(NCH):
                bi = nb()
                tt = p.op("tensor", lambda e, o=pb[bi][:, 0:128],
                          i_=f(xhat_off + i * D + c * 128, xhat_off + i * D + (c + 1) * 128):
                          e.transpose(o, i_, cID128[:, :]),
                          deps=[apl[i]] + list(deps) + list(pfree[bi]))
                tv = vcopy(dst_fn(c, i), pb[bi][:, 0:128], deps=[tt])
                bank_done(bi, tv)
                tts[i][c] = tv
        return tts


    conv_done = {}
    aff_done = {}
    gates_done = {-1: []}
    bm_evac = []
    ln_prev = []

    def do_conv(w, deps_extra):
        s = w % 2
        xn2o = XN2 + s * 4096
        toks = []
        prev = None
        for c in range(NCH):
            base = XHT + (c * 2 + s) * WIN
            dat = lambda a, b: bb(base + a, base + b)
            hL = hcell[:, c * 4 + s * 2:c * 4 + s * 2 + 1]
            hR = hcell[:, c * 4 + s * 2 + 32:c * 4 + s * 2 + 33]
            cv = lambda a, b: f(4096 + c * WIN + a, 4096 + c * WIN + b)
            dd = aff_done[w] + deps_extra + gates_done.get(w - 2, [])
            t1 = vts(cv(0, WIN), dat(0, WIN), staps[:, c * 4 + 3:c * 4 + 4], None, ALU.mult, deps=dd)
            t2 = vstt(cv(1, WIN), dat(0, WIN - 1), staps[:, c * 4 + 0:c * 4 + 1],
                      cv(1, WIN), ALU.mult, ALU.add, deps=[t1])
            t2b = vstt(cv(0, 1), hL, staps[:, c * 4 + 0:c * 4 + 1],
                       cv(0, 1), ALU.mult, ALU.add, deps=[t2])
            t3 = vstt(cv(0, WIN - 1), dat(1, WIN), staps[:, c * 4 + 2:c * 4 + 3],
                      cv(0, WIN - 1), ALU.mult, ALU.add, deps=[t2b])
            t3b = vstt(cv(WIN - 1, WIN), hR, staps[:, c * 4 + 2:c * 4 + 3],
                       cv(WIN - 1, WIN), ALU.mult, ALU.add, deps=[t3])
            t4 = vts(bb(xn2o + c * WIN, xn2o + (c + 1) * WIN), cv(0, WIN),
                     scvb[:, c:c + 1], None, ALU.add, deps=[t3b])
            prev = t4
            toks.append(t4)
        conv_done[w] = toks
        return toks

    def do_gates(w):
        s = w % 2
        xn2o = XN2 + s * 4096
        ct = conv_done[w]
        igev = []
        prev_sg = {0: None, 1: None}
        for oc in range(NCH):
            bi = nb()
            tmm = None
            for c in range(NCH):
                tmm = mm(bi, bb(GWV + c * 2048 + 1024 + oc * 128, GWV + c * 2048 + 1024 + (oc + 1) * 128),
                         bb(xn2o + c * WIN, xn2o + (c + 1) * WIN),
                         deps=([ct[c]] + tgw) if c == 0 else [tmm, ct[c]], start=(c == 0), stop=(c == NCH - 1))
            sgo = SGV + (oc % 2) * WIN
            tsg = act(bb(sgo, sgo + WIN), pb[bi][:, :], AF.Sigmoid,
                      deps=[tmm] + ([prev_sg[oc % 2]] if prev_sg[oc % 2] else []),
                      bias=sgbias[:, NCH + oc:NCH + oc + 1])
            bank_done(bi, tsg)
            txg = vtt(bb(XGW + oc * WIN, XGW + (oc + 1) * WIN), bb(xn2o + oc * WIN, xn2o + (oc + 1) * WIN),
                      bb(sgo, sgo + WIN), ALU.mult, deps=[tsg])
            prev_sg[oc % 2] = txg
            igev.append(txg)
        gtoks = []
        prev_st = {0: None, 1: None}
        for oc in range(NCH):
            bi = nb()
            tmm = None
            for c in range(NCH):
                tmm = mm(bi, bb(GWV + c * 2048 + oc * 128, GWV + c * 2048 + (oc + 1) * 128),
                         bb(xn2o + c * WIN, xn2o + (c + 1) * WIN),
                         deps=([ct[c]] + tgw) if c == 0 else [tmm, ct[c]], start=(c == 0), stop=(c == NCH - 1))
            go = GSO + (oc % 2) * WIN
            tsg = act(bb(go, go + WIN), pb[bi][:, :], AF.Sigmoid,
                      deps=[tmm] + ([prev_st[oc % 2]] if prev_st[oc % 2] else []), bias=sgbias[:, oc:oc + 1])
            bank_done(bi, tsg)
            tst = p.dma(gatesh[oc, :, w * WIN:(w + 1) * WIN], bb(go, go + WIN), deps=[tsg])
            prev_st[oc % 2] = tst
            gtoks.append(tst)
        biB = nb(); biC = nb()
        tmB = None; tmC = None
        for c in range(NCH):
            tmC = mm(biC, bb(WBC + c * 2 * S + S, WBC + (c + 1) * 2 * S), bb(xn2o + c * WIN, xn2o + (c + 1) * WIN),
                     deps=([ct[c]] + tbc) if c == 0 else [tmC, ct[c]], start=(c == 0), stop=(c == NCH - 1),
                     psl=(0, S, 0, WIN))
        for c in range(NCH):
            tmB = mm(biB, bb(WBC + c * 2 * S, WBC + c * 2 * S + S), bb(XGW + c * WIN, XGW + (c + 1) * WIN),
                     deps=(igev + tbc) if c == 0 else [tmB], start=(c == 0), stop=(c == NCH - 1),
                     psl=(0, S, 0, WIN))
        tBe = act(f6(HIST + w * WIN, HIST + (w + 1) * WIN), pb[biB][0:S, 0:WIN], AF.Identity,
                  deps=[tmB], bias=sBb[:, :])
        bank_done(biB, tBe)
        tCe = act(b6(SCMB + w * WIN, SCMB + (w + 1) * WIN), pb[biC][0:S, 0:WIN], AF.Identity,
                  deps=[tmC], bias=sCb[:, :])
        bank_done(biC, tCe)
        bm_evac.extend([tBe, tCe])
        gates_done[w] = gtoks + [tBe, tCe]
        return gtoks

    halo_toks = {}
    for w in range(NW):
        ldeps = tz + t_t1 + [tA1] + (ln_prev if ln_prev else [])
        tl, apl = ln_window(x0, w * WIN, ldeps, 9216, bf16_out="composite")
        hl = []
        if w >= 2:
            s_src = w % 2; s_dst = (w - 1) % 2
            for c in range(NCH):
                base = XHT + (c * 2 + s_src) * WIN
                hl.append(vcopy(hcell[:, c * 4 + s_dst * 2:c * 4 + s_dst * 2 + 1],
                                bb(base + WIN - 1, base + WIN),
                                deps=aff_done[w - 2] + conv_done.get(w - 2, [])))
        s = w % 2
        tts = tr_window(9216, lambda c, i, s=s: bb(XHT + (c * 2 + s) * WIN + i * 128,
                                                   XHT + (c * 2 + s) * WIN + (i + 1) * 128),
                        apl, deps=hl + conv_done.get(w - 2, []) + gates_done.get(w - 2, []))
        ln_prev = [t for row in tts for t in row]
        aff = []
        for c in range(NCH):
            base = XHT + (c * 2 + s) * WIN
            aff.append(vts(bb(base, base + WIN), bb(base, base + WIN),
                           lngc[:, c:c + 1], lnbc[:, c:c + 1], ALU.mult, ALU.add,
                           deps=[tts[i][c] for i in range(4)]))
        aff_done[w] = aff
        hr = []
        if w >= 1:
            s_dst = (w - 1) % 2
            for c in range(NCH):
                base = XHT + (c * 2 + s) * WIN
                hr.append(vcopy(hcell[:, c * 4 + s_dst * 2 + 32:c * 4 + s_dst * 2 + 33],
                                bb(base, base + 1), deps=[aff[c]]))
            halo_toks[w - 1] = hl + hr
            do_conv(w - 1, hl + hr)
            do_gates(w - 1)
            if dbg and w == 1:
                td1 = p.dma(dxht_h, bb(XHT, XHT + 8192), deps=aff_done[0] + conv_done[0])
                td2 = p.dma(dxn2_h, bb(XN2, XN2 + 4096), deps=conv_done[0])
                gates_done[0] = gates_done[0] + [td1, td2]
    hl = []
    s_dst = 1
    for c in range(NCH):
        base = XHT + (c * 2 + 0) * WIN
        hl.append(vcopy(hcell[:, c * 4 + s_dst * 2:c * 4 + s_dst * 2 + 1],
                        bb(base + WIN - 1, base + WIN),
                        deps=aff_done[6] + conv_done.get(6, [])))
        hl.append(vmemset(hcell[:, c * 4 + s_dst * 2 + 32:c * 4 + s_dst * 2 + 33], 0.0,
                          deps=conv_done.get(5, [])))
    do_conv(7, hl)
    do_gates(7)

    P2B = phase()
    tzp = [vmemset(f6(SBMP, SBMP + WARM), 0.0, deps=P2B),
           vmemset(f6(SBMP + WARM + T, SBMP + 4608), 0.0, deps=P2B)]
    tb2 = vts(f6(HIST + 1, HIST + T), f6(HIST + 1, HIST + T), bacol[:, :], None, ALU.add,
              deps=P2B)
    pps = []
    for i in range(NW):
        bi = nb()
        tm_ = mm(bi, cPc[:, :], f6(HIST + i * WIN, HIST + (i + 1) * WIN), deps=[tb2],
                 psl=(0, S, 0, WIN))
        te = act(f6(SBMP + WARM + i * WIN, SBMP + WARM + (i + 1) * WIN), pb[bi][0:S, 0:WIN],
                 AF.Identity, deps=[tm_])
        bank_done(bi, te)
        pps.append(te)
    SBW = FP[0:S, SBMP:SBMP + 4608].rearrange("p (j l) -> p j l", l=KCH)
    scan_end = []
    pA = [0, 1]
    pB = [2, 3]
    tV = tq = tM2 = tR = tZ = tM1 = None
    for t in range(STEPS):
        bAP = SBW[:, t // KCH: t // KCH + KCH, t % KCH]
        if t < WARM:
            zt = ztp[t % 2][:, :]
        else:
            zt = f6(HIST + (t - WARM) * KCH, HIST + (t - WARM + 1) * KCH)
        if t == 0:
            tV = vcopy(craw[0][:, :], bAP, deps=pps + P2B + tzp)
        else:
            tV = vtt(craw[0][:, :], pb[pA[t % 2]][0:S, 0:KCH], bAP, ALU.add, deps=[tM1])
        tq = vtt(csqb[0][:, :], craw[0][:, :], craw[0][:, :], ALU.mult, deps=[tV])
        tM2 = mm(pB[t % 2], cOD[:, :], csqb[0][:, :], deps=[tq] + ([tR] if t >= 2 else []),
                 psl=(0, S, 0, KCH))
        tR = act(csqb[0][:, :], pb[pB[t % 2]][0:S, 0:KCH],
                 AF.Sqrt, deps=[tM2], bias=epsc[0:S, :])
        tRc = vrecip(csqb[0][:, :], csqb[0][:, :], deps=[tR])
        tZ = vtt(zt, craw[0][:, :], csqb[0][:, :], ALU.mult, deps=[tRc])
        if t < STEPS - 1:
            tM1 = mm(pA[(t + 1) % 2], scanM[:, :], zt, deps=[tZ] + ([tV] if t >= 1 else []),
                     psl=(0, S, 0, KCH))
        else:
            scan_end.append(tZ)

    if dbg:
        p.dma(dbpp_h, f6(SBMP, SBMP + 4608), deps=scan_end)
        p.dma(dzh_h, f6(HIST, HIST + T), deps=scan_end)
        p.dma(dcm_h, b6(SCMB, SCMB + T), deps=scan_end)

    P3B = phase()
    HISTV = FP[0:S, HIST:HIST + T].rearrange("p (t k) -> p t k", k=KCH)
    ty_prev = None
    ty_toks = []
    for j in range(KCH):
        zsrc = HISTV[:, :, j]
        t1 = vts(szh2t[:, 0:LCH], zsrc, cg[:, :], cbeta[:, :], ALU.mult, ALU.add,
                 deps=P3B + ([ty_prev] if ty_prev else []))
        t2 = vtt(b6(YB + j * LCH, YB + (j + 1) * LCH), szh2t[:, 0:LCH],
                 b6(SCMB + j * LCH, SCMB + (j + 1) * LCH), ALU.mult, deps=[t1])
        ty_prev = t2
        ty_toks.append(t2)
    tDld = p.dma(b6(SDW, SDW + D), D_wt_h, queue="gpsimd", deps=P3B)

    def out_window(w, obf_toks, xsrc, xdst, extra_ld_deps):
        tl = p.dma(f(XWIN, XWIN + 4 * D).rearrange("p (i d) -> p i d", d=D),
                   xsrc[w * WIN:(w + 1) * WIN, :].rearrange("(i p) d -> p i d", p=128),
                   deps=extra_ld_deps)
        adds = []
        for i in range(4):
            for oc in range(NCH):
                bi = nb()
                tt = ptr(bi, (0, 128), f(OBF + oc * WIN + i * 128, OBF + oc * WIN + (i + 1) * 128),
                         deps=obf_toks)
                ta = vtt(f(XWIN + i * D + oc * 128, XWIN + i * D + (oc + 1) * 128),
                         f(XWIN + i * D + oc * 128, XWIN + i * D + (oc + 1) * 128),
                         pb[bi][0:128, 0:128], ALU.add, deps=[tt, tl])
                bank_done(bi, ta)
                adds.append(ta)
        ts_ = p.dma(xdst[w * WIN:(w + 1) * WIN, :].rearrange("(i p) d -> p i d", p=128),
                    f(XWIN, XWIN + 4 * D).rearrange("p (i d) -> p i d", d=D), deps=adds)
        return ts_, adds

    prev_st = []
    tgl_prev = None
    for w in range(NW):
        tgl = p.dma(bb(GSL, GSL + 4096).rearrange("p (c n) -> p c n", n=WIN),
                    gatesh[:, :, w * WIN:(w + 1) * WIN].rearrange("c p n -> p c n"),
                    deps=P3B + ([tgl_prev] if tgl_prev else []))
        evs = []
        for oc in range(NCH):
            bi = nb()
            tm_ = mm(bi, b6(SDW + oc * 128, SDW + (oc + 1) * 128), b6(YB + w * WIN, YB + (w + 1) * WIN),
                     deps=ty_toks + [tDld], psl=(0, 128, 0, WIN))
            te = act(f(OBF + oc * WIN, OBF + (oc + 1) * WIN), pb[bi][:, 0:WIN], AF.Identity,
                     deps=[tm_] + prev_st, bias=sDb[:, oc:oc + 1])
            tg = vtt(f(OBF + oc * WIN, OBF + (oc + 1) * WIN), f(OBF + oc * WIN, OBF + (oc + 1) * WIN),
                     bb(GSL + oc * WIN, GSL + (oc + 1) * WIN), ALU.mult, deps=[te, tgl])
            bank_done(bi, tg)
            evs.append(tg)
        ts_, adds = out_window(w, evs, x0, x1h, prev_st)
        tgl_prev = evs[-1]
        prev_st = [ts_] + adds

    if dbg:
        p.dma(dyb_h, b6(YB, YB + T), deps=ty_toks + prev_st)

    wld = []
    for c in range(NCH):
        wld.append(p.dma(bb(SWR + c * 8, SWR + (c + 1) * 8), wr_h[c * 128:(c + 1) * 128, :],
                         queue="gpsimd", deps=P3B))
    for c in range(NCH):
        wld.append(p.dma(bb(WS1 + c * 256, WS1 + (c + 1) * 256), ws1_h[c * 128:(c + 1) * 128, :],
                         queue="gpsimd", deps=P3B))
        wld.append(p.dma(bb(WS2 + c * 256, WS2 + (c + 1) * 256), ws2_h[c * 128:(c + 1) * 128, :],
                         queue="gpsimd", deps=P3B))
    for e_i in range(8):
        for c in range(NCH):
            wld.append(p.dma(bb(MW1 + e_i * 1024 + c * 128, MW1 + e_i * 1024 + (c + 1) * 128),
                             w1_h[e_i, c * 128:(c + 1) * 128, :], queue="gpsimd", deps=P3B))
            wld.append(p.dma(bb(MW2 + e_i * 1024 + c * 128, MW2 + e_i * 1024 + (c + 1) * 128),
                             w2_h[e_i, c * 128:(c + 1) * 128, :], queue="gpsimd", deps=P3B))
    for k in range(2):
        wld.append(p.dma(bb(WS3 + k * 1024, WS3 + (k + 1) * 1024), ws3_h[k * 128:(k + 1) * 128, :],
                         queue="gpsimd", deps=P3B))
    for e_i in range(8):
        wld.append(p.dma(bb(MW3 + e_i * 1024, MW3 + (e_i + 1) * 1024), w3_h[e_i], queue="gpsimd", deps=P3B))

    P4B = phase()
    prev_w = P4B
    for w in range(NW):
        tl, apl = ln_window(x1h, w * WIN, prev_w, OBF)
        tts = tr_window(OBF, lambda c, i: bb(XT4 + c * WIN + i * 128, XT4 + c * WIN + (i + 1) * 128),
                        apl, deps=prev_w)
        trt = [t for row in tts for t in row]
        wf_toks = []
        for i in range(4):
            bi = nb()
            tmm = None
            for c in range(NCH):
                tmm = mm(bi, bb(XT4 + c * WIN + i * 128, XT4 + c * WIN + (i + 1) * 128),
                         bb(SWR + c * 8, SWR + (c + 1) * 8),
                         deps=(trt + wld) if c == 0 else [tmm], start=(c == 0), stop=(c == NCH - 1),
                         psl=(0, 128, 0, 8))
            pr = f(RTR + i * 24, RTR + i * 24 + 8)
            rx = f(RTR + i * 24 + 8, RTR + i * 24 + 16)
            mx = f(RTR + i * 24 + 16, RTR + i * 24 + 17)
            ms = f(RTR + i * 24 + 17, RTR + i * 24 + 18)
            m1 = f(RTR + i * 24 + 18, RTR + i * 24 + 19)
            m2 = f(RTR + i * 24 + 19, RTR + i * 24 + 20)
            t1 = vred(mx, pb[bi][:, 0:8], ALU.max, deps=[tmm])
            t2 = vts(mx, mx, -1.0, None, ALU.mult, deps=[t1])
            t3 = act(pr, pb[bi][:, 0:8], AF.Exp, deps=[t2], bias=mx)
            bank_done(bi, t3)
            t4 = vred(ms, pr, ALU.add, deps=[t3])
            t5 = vrecip(ms, ms, deps=[t4])
            t6 = vts(pr, pr, ms, None, ALU.mult, deps=[t5])
            t7 = vred(m1, pr, ALU.min, deps=[t6])
            t8 = vts(rx, pr, m1, None, ALU.is_le, deps=[t7])
            t9 = vstt(rx, rx, 2.0, pr, ALU.mult, ALU.add, deps=[t8])
            t11 = vred(m2, rx, ALU.min, deps=[t9])
            t12 = vts(rx, pr, m2, None, ALU.is_gt, deps=[t11])
            t13 = vtt(rx, rx, pr, ALU.mult, deps=[t12])
            t14 = vred(ms, rx, ALU.add, deps=[t13])
            t15 = vrecip(ms, ms, deps=[t14])
            t16 = vts(rx, rx, ms, None, ALU.mult, deps=[t15])
            for e_i in range(8):
                bj = nb()
                t17 = ptr(bj, (0, 128), rx[:, e_i:e_i + 1], deps=[t16], rows=1)
                t18 = vcopy(wfT[0:1, e_i * WIN + i * 128:e_i * WIN + (i + 1) * 128],
                            pb[bj][0:1, 0:128], deps=[t17])
                bank_done(bj, t18)
                wf_toks.append(t18)
        hw_toks = {}
        prev_h2 = {0: None, 1: None}
        for e_i in range(10):
            bi1 = nb(); bi2 = nb()
            tm1 = tm2 = None
            for c in range(NCH):
                if e_i < 8:
                    l1 = bb(MW1 + e_i * 1024 + c * 128, MW1 + e_i * 1024 + (c + 1) * 128)
                    l2 = bb(MW2 + e_i * 1024 + c * 128, MW2 + e_i * 1024 + (c + 1) * 128)
                else:
                    k = e_i - 8
                    l1 = bb(WS1 + c * 256 + k * 128, WS1 + c * 256 + (k + 1) * 128)
                    l2 = bb(WS2 + c * 256 + k * 128, WS2 + c * 256 + (k + 1) * 128)
                rhs = bb(XT4 + c * WIN, XT4 + (c + 1) * WIN)
                tm1 = mm(bi1, l1, rhs, deps=(trt + wld) if c == 0 else [tm1],
                         start=(c == 0), stop=(c == NCH - 1))
                tm2 = mm(bi2, l2, rhs, deps=(trt + wld) if c == 0 else [tm2],
                         start=(c == 0), stop=(c == NCH - 1))
            hslot = bb(HST + e_i * WIN, HST + (e_i + 1) * WIN)
            h2o = H2SC + (e_i % 2) * WIN
            ts1 = act(hslot, pb[bi1][:, :], AF.Silu, deps=[tm1])
            bank_done(bi1, ts1)
            ts2 = act(bb(h2o, h2o + WIN), pb[bi2][:, :], AF.Identity,
                      deps=[tm2] + ([prev_h2[e_i % 2]] if prev_h2[e_i % 2] else []))
            bank_done(bi2, ts2)
            th = vtt(hslot, hslot, bb(h2o, h2o + WIN), ALU.mult, deps=[ts1, ts2])
            prev_h2[e_i % 2] = th
            if e_i < 8:
                bj = nb()
                tbr = mm(bj, cHPR[0:1, :], wfT[0:1, e_i * WIN:(e_i + 1) * WIN], deps=wf_toks,
                         psl=(0, 128, 0, WIN))
                tw = vtt(hslot, hslot, pb[bj][:, :], ALU.mult, deps=[th, tbr])
                bank_done(bj, tw)
                hw_toks[e_i] = tw
            else:
                hw_toks[e_i] = th
        evs = []
        for oc in range(NCH):
            bo = nb()
            tmm = None
            first = True
            for k in range(2):
                tmm = mm(bo, bb(WS3 + k * 1024 + oc * 128, WS3 + k * 1024 + (oc + 1) * 128),
                         bb(HST + (8 + k) * WIN, HST + (9 + k) * WIN),
                         deps=([hw_toks[8], hw_toks[9]] + wld) if first else [tmm],
                         start=first, stop=False)
                first = False
            for e_i in range(8):
                tmm = mm(bo, bb(MW3 + e_i * 1024 + oc * 128, MW3 + e_i * 1024 + (oc + 1) * 128),
                         bb(HST + e_i * WIN, HST + (e_i + 1) * WIN),
                         deps=[tmm, hw_toks[e_i]], start=False, stop=(e_i == 7))
            te = act(f(OBF + oc * WIN, OBF + (oc + 1) * WIN), pb[bo][:, :], AF.Identity, deps=[tmm])
            bank_done(bo, te)
            evs.append(te)
        ts_, adds = out_window(w, evs, x1h, x2h, [])
        prev_w = [ts_] + adds

    P5aB = phase()
    wld5 = []
    for c in range(NCH):
        wld5.append(p.dma(bb(WQ + c * 1024, WQ + (c + 1) * 1024), Wq_h[c * 128:(c + 1) * 128, :],
                          queue="gpsimd", deps=P5aB))
        wld5.append(p.dma(bb(WK + c * 1024, WK + (c + 1) * 1024), Wk_h[c * 128:(c + 1) * 128, :],
                          queue="gpsimd", deps=P5aB))
        wld5.append(p.dma(bb(WV + c * 1024, WV + (c + 1) * 1024), Wv_h[c * 128:(c + 1) * 128, :],
                          queue="gpsimd", deps=P5aB))
    prev_w = P5aB
    last_use = {}
    for w in range(NW):
        tl, apl = ln_window(x2h, w * WIN, prev_w, OBF)
        tts = tr_window(OBF, lambda c, i: bb(XT5 + c * WIN + i * 128, XT5 + c * WIN + (i + 1) * 128),
                        apl, deps=prev_w)
        trt = [t for row in tts for t in row]
        wtoks = []
        for oc in range(NCH):
            for (Woff, BUF, hbm) in ((WQ, QWB, qth), (WK, KWB, kth)):
                bi = nb()
                tmm = None
                for c in range(NCH):
                    tmm = mm(bi, bb(Woff + c * 1024 + oc * 128, Woff + c * 1024 + (oc + 1) * 128),
                             bb(XT5 + c * WIN, XT5 + (c + 1) * WIN),
                             deps=(trt + wld5) if c == 0 else [tmm], start=(c == 0), stop=(c == NCH - 1))
                slot = BUF + (oc % 2) * WIN
                dd = [last_use[slot]] if slot in last_use else []
                te = act(bb(slot, slot + WIN), pb[bi][:, :], AF.Identity, deps=[tmm] + dd)
                bank_done(bi, te)
                tst = p.dma(hbm[oc, :, w * WIN:(w + 1) * WIN], bb(slot, slot + WIN), deps=[te])
                last_use[slot] = tst
                wtoks.append(tst)
            bi = nb()
            tmm = None
            for c in range(NCH):
                tmm = mm(bi, bb(WV + c * 1024 + oc * 128, WV + c * 1024 + (oc + 1) * 128),
                         bb(XT5 + c * WIN, XT5 + (c + 1) * WIN),
                         deps=(trt + wld5) if c == 0 else [tmm], start=(c == 0), stop=(c == NCH - 1))
            slot = VWB + (oc % 2) * WIN
            dd = [last_use[slot]] if slot in last_use else []
            te = act(bb(slot, slot + WIN), pb[bi][:, :], AF.Identity, deps=[tmm] + dd)
            bank_done(bi, te)
            vslot = VTB + (oc % 2) * WIN
            dv = [last_use[vslot]] if vslot in last_use else []
            ttr = [p.dma_tr(bb(vslot + i * 128, vslot + (i + 1) * 128),
                            bb(slot + i * 128, slot + (i + 1) * 128), deps=[te] + dv) for i in range(4)]
            last_use[slot] = ttr[-1]
            tst = p.dma(vth[oc, w * 4:(w + 1) * 4].rearrange("i p d -> p i d"),
                        bb(vslot, vslot + WIN).rearrange("p (i d) -> p i d", d=128), deps=ttr)
            last_use[vslot] = tst
            wtoks.append(tst)
        prev_w = wtoks

    P5bB = phase()
    chunk_done = {-1: P5bB, -2: P5bB}
    load_toks = {}

    def p5_loads(c):
        s = c % 2
        dd = list(chunk_done[c - 2])
        tq_ = p.dma(bb(QTC + s * 4096, QTC + (s + 1) * 4096), qth[c], deps=dd)
        tk_ = p.dma(bb(KTC + s * 4096, KTC + (s + 1) * 4096), kth[c], deps=dd)
        svo = SVC + s * 4160
        tms = vmemset(bb(svo, svo + 4160), 1.0, deps=dd)
        svv = BB[:, svo:svo + 4160].rearrange("p (t u) -> p t u", u=130)
        tvs = [p.dma(svv[:, :, 0:64], vth[c, :, :, 0:64].rearrange("t p d -> p t d"), deps=[tms]),
               p.dma(svv[:, :, 65:129], vth[c, :, :, 64:128].rearrange("t p d -> p t d"), deps=[tms])]
        load_toks[c] = [tq_, tk_] + tvs

    p5_loads(0)
    lrow_free = {0: None, 1: None}
    SCB = [0, 1, 5, 6]
    sc_free = {0: None, 1: None, 2: None, 3: None}
    pt_free = {0: None, 1: None, 2: None, 3: None}
    for c in range(NCH):
        if c + 1 < NCH:
            p5_loads(c + 1)
        s = c % 2
        svo = SVC + s * 4160
        ctoks = []
        po_free = {0: (chunk_done[c - 1] if c > 0 else P5bB), 1: (chunk_done[c - 1] if c > 0 else P5bB)}
        for qw in range(NW):
            nkt = 4 * (qw + 1)
            tpv_l = {}
            pend = None
            for kt in range(nkt + 1):
                cur = []
                if kt < nkt:
                    for ha in range(2):
                        hoff = ha * 64
                        tp = (64, 0) if ha else None
                        sb_ = (kt % 2) * 2 + ha
                        tsc = mm(SCB[sb_], BB[hoff:hoff + 64, KTC + s * 4096 + kt * 128: KTC + s * 4096 + (kt + 1) * 128],
                                 BB[hoff:hoff + 64, QTC + s * 4096 + qw * WIN: QTC + s * 4096 + (qw + 1) * WIN],
                                 deps=load_toks[c] + ([sc_free[sb_]] if sc_free[sb_] else []), tp=tp)
                        dd = [tsc] + ([pt_free[sb_]] if pt_free[sb_] else [])
                        te = act(bb(PT + sb_ * WIN, PT + (sb_ + 1) * WIN), pb[SCB[sb_]][:, :], AF.Exp,
                                 deps=dd, scale=0.125, bias=mshiftc)
                        sc_free[sb_] = te
                        if kt >= 4 * qw:
                            i = kt - 4 * qw
                            te = vtt(bb(PT + sb_ * WIN, PT + (sb_ + 1) * WIN),
                                     bb(PT + sb_ * WIN, PT + (sb_ + 1) * WIN),
                                     cmb[:, i * WIN:(i + 1) * WIN], ALU.mult, deps=[te])
                        cur.append((ha, sb_, te))
                if pend is not None:
                    pkt = kt - 1
                    for ha, sb_, te in pend:
                        tpv_ = mm(2 + ha, bb(svo + pkt * 130 + ha * 65, svo + pkt * 130 + (ha + 1) * 65),
                                  bb(PT + sb_ * WIN, PT + (sb_ + 1) * WIN),
                                  deps=[te] + (list(po_free[ha]) if pkt == 0 else []),
                                  start=(pkt == 0), stop=(pkt == nkt - 1), psl=(0, 65, 0, WIN))
                        pt_free[sb_] = tpv_
                        tpv_l[ha] = tpv_
                pend = cur
            tno_l = {}
            for ha in range(2):
                po = 2 + ha
                lo = ha * WIN
                tlc = vcopy(lrow[64:65, lo:lo + WIN], pb[po][64:65, 0:WIN],
                            deps=[tpv_l[ha]] + ([lrow_free[ha]] if lrow_free[ha] else []))
                tev = act(lrow[0:64, lo:lo + WIN], pb[po][0:64, 0:WIN], AF.Identity, deps=[tlc])
                trc = vrecip(lrow[64:65, lo:lo + WIN], lrow[64:65, lo:lo + WIN], deps=[tlc])
                bb_ = 4 + ha * 3
                tbc_ = mm(bb_, cHPR[64:65, 0:64], lrow[64:65, lo:lo + WIN],
                          deps=[trc] + list(pfree[bb_]), psl=(0, 64, 0, WIN), tp=(64, 0))
                AT = ATA if ha == 0 else ATB
                tno = vtt(BB[0:64, AT + s * 4096 + qw * WIN: AT + s * 4096 + (qw + 1) * WIN],
                          lrow[0:64, lo:lo + WIN], pb[bb_][0:64, 0:WIN], ALU.mult, deps=[tev, tbc_])
                pfree[bb_] = (tno,)
                po_free[ha] = [tev]
                lrow_free[ha] = tno
                ctoks.append(tno)
        tsa = p.dma(ath[c, 0:64, :], BB[0:64, ATA + s * 4096: ATA + (s + 1) * 4096], deps=ctoks)
        tsb = p.dma(ath[c, 64:128, :], BB[0:64, ATB + s * 4096: ATB + (s + 1) * 4096], deps=ctoks)
        chunk_done[c] = [tsa, tsb] + ctoks

    P5wB = phase()
    wldo = [p.dma(bb(SWO + c * 1024, SWO + (c + 1) * 1024), Wo_h[c * 128:(c + 1) * 128, :],
                  queue="gpsimd", deps=P5wB) for c in range(NCH)]
    awin_last = None
    prev_w = P5wB
    awin_last = {0: None, 1: None}
    tla_w = {}
    def awin_loads(w):
        ao = AWIN + (w % 2) * 4096
        dd = ([awin_last[w % 2]] if awin_last[w % 2] else []) + list(P5wB)
        tla_w[w] = [p.dma(bb(ao + c * WIN, ao + (c + 1) * WIN), ath[c][:, w * WIN:(w + 1) * WIN],
                          deps=dd) for c in range(NCH)]
    awin_loads(0)
    for w in range(NW):
        if w + 1 < NW:
            awin_loads(w + 1)
        ao = AWIN + (w % 2) * 4096
        evs = []
        last_mm = None
        for oc in range(NCH):
            bo = nb()
            tmm = None
            for c in range(NCH):
                tmm = mm(bo, bb(SWO + c * 1024 + oc * 128, SWO + c * 1024 + (oc + 1) * 128),
                         bb(ao + c * WIN, ao + (c + 1) * WIN),
                         deps=(tla_w[w] + wldo) if c == 0 else [tmm], start=(c == 0), stop=(c == NCH - 1))
            te = act(f(OBF + oc * WIN, OBF + (oc + 1) * WIN), pb[bo][:, :], AF.Identity, deps=[tmm])
            bank_done(bo, te)
            last_mm = tmm
            evs.append(te)
        awin_last[w % 2] = last_mm
        ts_, adds = out_window(w, evs, x2h, x3h, prev_w)
        prev_w = [ts_] + adds

    P6B = phase()
    wb_toks = []
    tprev = None
    slot_last = {0: None, 1: None}
    for ri in range(2):
        WTo = WTR if ri == 0 else WTI
        for r in range(4):
            for cc in range(4):
                slot = cc % 2
                uo = UVU if slot == 0 else UVU2
                vo = UVV if slot == 0 else UVV2
                dd = [slot_last[slot]] if slot_last[slot] else list(P6B)
                tU = p.dma(f(uo, uo + 512), uvU_h[ri, r, cc], deps=dd)
                tV = p.dma(f(vo, vo + 512), uvV_h[ri, r, cc], deps=dd)
                if cc == 0:
                    tprev = vtt(f(FACC, FACC + 512), f(uo, uo + 512), f(vo, vo + 512), ALU.mult,
                                deps=[tU, tV] + ([tprev] if tprev else []))
                else:
                    tm_ = vtt(f(FTMP, FTMP + 512), f(uo, uo + 512), f(vo, vo + 512), ALU.mult,
                              deps=[tU, tV, tprev])
                    tprev = vtt(f(FACC, FACC + 512), f(FACC, FACC + 512), f(FTMP, FTMP + 512),
                                ALU.add, deps=[tm_])
                slot_last[slot] = tprev
            tw = act(bb(WTo + r * 512, WTo + (r + 1) * 512), f(FACC, FACC + 512), AF.Identity,
                     deps=[tprev])
            tprev = tw
            wb_toks.append(tw)

    prev_w = P6B
    final_stores = []
    for w in range(NW):
        tl, apl = ln_window(x3h, w * WIN, prev_w, OBF)
        tts = tr_window(OBF, lambda c, i: bb(XT6 + c * WIN + i * 128, XT6 + c * WIN + (i + 1) * 128),
                        apl, deps=prev_w)
        trt = [t for row in tts for t in row]
        tneg = [vts(bb(XT6N + k * WIN, XT6N + (k + 1) * WIN), bb(XT6 + (4 + k) * WIN, XT6 + (5 + k) * WIN),
                    -1.0, None, ALU.mult, deps=[tts[i][4 + k] for i in range(4)]) for k in range(4)]
        evs = []
        for oc in range(NCH):
            bo = nb()
            tmm = None
            first = True
            for k in range(4):
                if oc < 4:
                    l1 = bb(WTR + k * 512 + oc * 128, WTR + k * 512 + (oc + 1) * 128)
                    l2 = bb(WTI + k * 512 + oc * 128, WTI + k * 512 + (oc + 1) * 128)
                    r1 = bb(XT6 + k * WIN, XT6 + (k + 1) * WIN)
                    r2 = bb(XT6N + k * WIN, XT6N + (k + 1) * WIN)
                else:
                    l1 = bb(WTI + k * 512 + (oc - 4) * 128, WTI + k * 512 + (oc - 3) * 128)
                    l2 = bb(WTR + k * 512 + (oc - 4) * 128, WTR + k * 512 + (oc - 3) * 128)
                    r1 = bb(XT6 + k * WIN, XT6 + (k + 1) * WIN)
                    r2 = bb(XT6 + (4 + k) * WIN, XT6 + (5 + k) * WIN)
                tmm = mm(bo, l1, r1, deps=(trt + wb_toks + tneg) if first else [tmm],
                         start=first, stop=False)
                first = False
                tmm = mm(bo, l2, r2, deps=[tmm], start=False, stop=(k == 3))
            te = act(f(OBF + oc * WIN, OBF + (oc + 1) * WIN), pb[bo][:, :], AF.Identity,
                     deps=[tmm], bias=sttbr[:, oc:oc + 1])
            bank_done(bo, te)
            evs.append(te)
        ts_, adds = out_window(w, evs, x3h, out_h, prev_w)
        prev_w = [ts_] + adds
        final_stores.append(ts_)

    vmemset(f(CONVT, CONVT + 1), 0.0, deps=final_stores)

    with nc.Block() as block:
        p.emit(block)
    return nc


def host_prep(inp):
    g = {k: np.ascontiguousarray(np.asarray(v, np.float32)) for k, v in inp.items()}
    d = {}
    d["gate_wt"] = np.ascontiguousarray(g["L0_attn_gate_w"].T)
    d["B_wt"] = np.ascontiguousarray(g["L0_attn_B_w"].T)
    d["C_wt"] = np.ascontiguousarray(g["L0_attn_C_w"].T)
    d["D_wt"] = np.ascontiguousarray(g["L0_attn_D_w"].T)
    d["B_b"] = g["L0_attn_B_b"].reshape(S, 1)
    d["C_b"] = g["L0_attn_C_b"].reshape(S, 1)
    d["D_b"] = np.ascontiguousarray(g["L0_attn_D_b"].reshape(NCH, 128).T)
    d["gate_b"] = np.ascontiguousarray(g["L0_attn_gate_b"].reshape(2 * NCH, 128).T)
    d["conv_taps"] = g["L0_attn_conv_w"].reshape(D, 3).reshape(NCH, 128, 3)
    d["conv_b"] = np.ascontiguousarray(g["L0_attn_conv_b"].reshape(NCH, 128).T)
    d["A_m"] = g["L0_attn_A"]
    d["A_mt"] = np.ascontiguousarray(g["L0_attn_A"].T)
    d["dtA"] = np.asarray(g["L0_attn_dt_A"], np.float32).reshape(1, 1)
    d["g_col"] = g["L0_attn_ln_st_g"].reshape(S, 1)
    d["beta_col"] = g["L0_attn_ln_st_b"].reshape(S, 1)
    d["g_rep64"] = np.ascontiguousarray(np.tile(g["L0_attn_ln_st_g"].reshape(1, S), (S, 1)))
    d["lngc"] = np.ascontiguousarray(g["L0_attn_ln_in_g"].reshape(NCH, 128).T)
    d["lnbc"] = np.ascontiguousarray(g["L0_attn_ln_in_b"].reshape(NCH, 128).T)
    d["Pc"] = (np.eye(S) - 1.0 / S).astype(np.float32)
    d["ident64"] = np.eye(S, dtype=np.float32)
    d["ident128"] = np.eye(128, dtype=np.float32)
    consts = np.zeros((128, 8), np.float32)
    consts[:, 0] = EPS
    consts[:, 1] = EPS * EPS
    consts[:, 2] = -SHIFT
    d["consts"] = consts
    d["onesdiv"] = np.full((S, S), 1.0 / S, np.float32)
    d["hp_rows"] = np.ones((128, 128), np.float32)
    cmask = np.zeros((4, 128, WIN), np.float32)
    for i in range(4):
        for kp in range(128):
            cmask[i, kp, i * 128 + kp:] = 1.0
    d["cmask"] = cmask.astype(BF)
    d["moe_w1"] = g["L0_ffn_w1"]
    d["moe_w2"] = g["L0_ffn_w2"]
    d["moe_w3"] = g["L0_ffn_w3"]
    d["ws1"] = g["L0_ffn_ws1"]
    d["ws2"] = g["L0_ffn_ws2"]
    d["ws3"] = g["L0_ffn_ws3"]
    d["wr"] = g["L0_ffn_wr"]
    d["Wq_t"] = np.ascontiguousarray(g["L1_attn_Wq"].T)
    d["Wk_t"] = np.ascontiguousarray(g["L1_attn_Wk"].T)
    d["Wv_t"] = np.ascontiguousarray(g["L1_attn_Wv"].T)
    d["Wo_t"] = np.ascontiguousarray(g["L1_attn_Wo"].T)
    uvU = np.zeros((2, 4, 4, 128, 512), np.float32)
    uvV = np.zeros((2, 4, 4, 128, 512), np.float32)
    for ri, (G1k, G2k) in enumerate((("L1_ffn_G1_R", "L1_ffn_G2_R"), ("L1_ffn_G1_I", "L1_ffn_G2_I"))):
        for cc in range(4):
            Ac = g[G1k][0, :, cc, :]
            Bc = g[G2k][cc, :, 0, :]
            U = np.broadcast_to(Ac.T[:, None, :, None], (32, 16, 32, 16)).reshape(512, 512)
            V = np.broadcast_to(Bc.T[None, :, None, :], (32, 16, 32, 16)).reshape(512, 512)
            for r in range(4):
                uvU[ri, r, cc] = U[r * 128:(r + 1) * 128]
                uvV[ri, r, cc] = V[r * 128:(r + 1) * 128]
    d["uvU"] = uvU
    d["uvV"] = uvV
    d["tt_bias"] = np.ascontiguousarray(np.concatenate([g["L1_ffn_bias_real"], g["L1_ffn_bias_imag"]]).reshape(NCH, 128).T)
    return d


def kernel(**inp):
    nc = bass.Bass("TRN2", target_bir_lowering=False)
    build(nc, dbg=False)
    base = host_prep(inp)
    x = np.asarray(inp["L0_attn_x"], np.float32)
    in_maps = []
    for b in range(8):
        m = dict(base)
        m["x0"] = np.ascontiguousarray(x[b])
        in_maps.append(m)
    res = run_bass_kernel_spmd(nc, in_maps, core_ids=list(range(8)))
    return np.stack([np.asarray(res.results[i]["out"], np.float32) for i in range(8)])
